# version 07
# baseline (speedup 1.0000x reference)
import os
os.environ.setdefault("TRN_ALLOW_ICI_ALL_TO_ALL", "1")
import numpy as np
from collections import defaultdict
from contextlib import ExitStack

import concourse.bass as bass
import concourse.mybir as mybir
from concourse.bass_utils import run_bass_kernel_spmd

F32 = mybir.dt.float32
BF16 = mybir.dt.bfloat16
I32 = mybir.dt.int32
AF = mybir.ActivationFunctionType
ALU = mybir.AluOpType
AX = mybir.AxisListType

DMA_DOMAINS = ("qs", "qg", "qv", "qc")
ENG_DOM = {"tensor": "te", "vector": "ve", "scalar": "se", "gpsimd": "pe",
           "sync": "sp"}
ALL_ENGINES = ["sync", "tensor", "vector", "scalar", "gpsimd"]


class EM:

    def __init__(self, nc):
        self.nc = nc
        self.cnt = defaultdict(int)
        self.strm = {e: [] for e in ALL_ENGINES}
        self.known = {e: {} for e in ALL_ENGINES}
        self.tok_know = defaultdict(list)
        self.bufs = {}
        self.pending = defaultdict(lambda: {"w": [], "r": []})
        self.nops = 0
        self.fences = []
        self.fence_applied = {e: 0 for e in ALL_ENGINES}

    def buf(self, key):
        b = self.bufs.get(key)
        if b is None:
            b = {"w": None, "r": {}}
            self.bufs[key] = b
        return b

    def _wait(self, eng, d, s):
        kn = self.known[eng]
        if kn.get(d, 0) >= s:
            return
        self.strm[eng].append(("w", d, s))
        tk = self.tok_know[d][s - 1]
        for dd, ss in tk.items():
            if kn.get(dd, 0) < ss:
                kn[dd] = ss
        if kn.get(d, 0) < s:
            kn[d] = s

    def fence(self):
        self.fences.append(dict(self.cnt))

    def op(self, eng, fn, reads=(), writes=(), dom=None, sig=True):
        dom = dom or ENG_DOM[eng]
        need = {}
        nf = len(self.fences)
        nf_forced = False
        if self.fence_applied[eng] < nf:
            nf_forced = True
            for i in range(self.fence_applied[eng], nf):
                for d, s in self.fences[i].items():
                    if need.get(d, 0) < s:
                        need[d] = s
            self.fence_applied[eng] = nf
        for k in reads:
            b = self.buf(k)
            if b["w"]:
                d, s = b["w"]
                if need.get(d, 0) < s:
                    need[d] = s
        for k in writes:
            b = self.buf(k)
            if b["w"]:
                d, s = b["w"]
                if need.get(d, 0) < s:
                    need[d] = s
            for d, s in b["r"].items():
                if need.get(d, 0) < s:
                    need[d] = s
        for d in sorted(need):
            if d == dom == "te" and not nf_forced:
                continue
            self._wait(eng, d, need[d])
        if sig and dom in DMA_DOMAINS:
            nxt = self.cnt[dom] + 1
            if nxt > 1:
                self._wait(eng, dom, nxt - 1)
        self.nops += 1
        if sig:
            self.cnt[dom] += 1
            seq = self.cnt[dom]
            self.strm[eng].append(("op", fn, dom, seq))
            kn = self.known[eng]
            know = dict(kn)
            prev = self.tok_know[dom][-1] if self.tok_know[dom] else None
            if prev:
                for dd, ss in prev.items():
                    if know.get(dd, 0) < ss:
                        know[dd] = ss
            know[dom] = seq
            self.tok_know[dom].append(know)
            t = (dom, seq)
            pend = self.pending[dom]
            wkeys = list(writes) + pend["w"]
            rkeys = list(reads) + pend["r"]
            pend["w"] = []
            pend["r"] = []
            for k in wkeys:
                b = self.buf(k)
                b["w"] = t
                b["r"] = {}
            for k in rkeys:
                b = self.buf(k)
                if b["r"].get(dom, 0) < seq:
                    b["r"][dom] = seq
            return t
        else:
            self.strm[eng].append(("op", fn, None, 0))
            pend = self.pending[dom]
            pend["w"] += list(writes)
            pend["r"] += list(reads)
            return None

    def wait_all(self, eng, toks):
        for (d, s) in toks:
            self._wait(eng, d, s)

    def wait_domain_now(self, eng, dom):
        s = self.cnt[dom]
        if s > 0:
            self._wait(eng, dom, s)

    def finalize_output_wait(self, eng="sync"):
        for dom in list(self.cnt):
            self.wait_domain_now(eng, dom)

    def realize(self, st):
        for dom, pend in self.pending.items():
            assert not pend["w"] and not pend["r"], f"pending unsignalled ops on {dom}"
        sems = {}
        for dom in list(self.cnt.keys()):
            sems[dom] = st.enter_context(self.nc.semaphore(f"sem_{dom}"))
        block = st.enter_context(self.nc.Block())
        for eng in ALL_ENGINES:
            stream = self.strm[eng]
            if not stream:
                continue
            dec = getattr(block, eng)

            def body(e, stream=stream):
                for item in stream:
                    if item[0] == "w":
                        _, d, s = item
                        e.wait_ge(sems[d], s * 16 if d in DMA_DOMAINS else s)
                    else:
                        _, fn, dom, seq = item
                        ins = fn(e)
                        if dom is not None:
                            ins.then_inc(sems[dom], 16 if dom in DMA_DOMAINS else 1)

            dec(body)


class Cfg:
    def __init__(self, D=1024, S=4096, H=16, HD=64, ncores=8):
        self.D, self.S, self.H, self.HD = D, S, H, HD
        self.ncores = ncores
        self.T = S // 2
        self.NCH = D // 128
        self.TSUB = min(512, self.T)
        self.NTS = self.T // self.TSUB
        self.KB = min(512, S)
        self.NKB = S // self.KB
        self.NQT = S // 128
        self.HC = H // 2
        self.NGC = self.HC * HD // 128
        self.NG = H * HD // 128
        self.QB = 4


def cdiv(a, b):
    return (a + b - 1) // b


class MK:
    def __init__(self, cfg: Cfg, nc, em: EM, st: ExitStack):
        self.cfg, self.nc, self.em, self.st = cfg, nc, em, st
        self.uid = 0

    def scope(self):
        from contextlib import contextmanager

        @contextmanager
        def _sc():
            old = self.st
            st2 = ExitStack()
            self.st = st2
            try:
                with st2:
                    yield
            finally:
                self.st = old
                self.em.fence()
        return _sc()

    def fresh(self, pfx):
        self.uid += 1
        return f"{pfx}_{self.uid}"

    def sb(self, name, shape, dtype):
        return self.st.enter_context(self.nc.sbuf_tensor(name, shape, dtype))

    def dram(self, name, shape, dtype=F32):
        return self.nc.dram_tensor(name, shape, dtype, kind="Internal")

    def setup(self):
        c = self.cfg
        em = self.em
        self.p512 = [self.st.enter_context(self.nc.psum_tensor(f"p512_{i}", [128, 512], F32))
                     for i in range(6)]
        self.pT = [self.st.enter_context(self.nc.psum_tensor(f"pT_{i}", [128, 1024], BF16))
                   for i in range(2)]
        self._prot = 0
        self.ident = self.sb("c_ident", [128, 128], BF16)
        self.ones_row = self.sb("c_ones_r", [1, 512], BF16)
        self.ones_row32 = self.sb("c_ones_r32", [1, 512], F32)
        self.ones128_32 = self.sb("c_ones128_32", [1, 128], F32)
        self.ones_col = self.sb("c_ones_c", [128, 1], F32)
        self.hsel32 = self.sb("c_hsel32", [2, 128], F32)
        self.cpool = self.sb("c_cpool", [128, 16], F32)
        self.cpool_vals = {}
        ident_np = np.eye(128, dtype=np.float32)
        idh = self.nc.inline_tensor(ident_np, "ident_f32")
        self.ident32 = self.sb("c_ident32", [128, 128], F32)
        em.op("sync", lambda e: e.dma_start(out=self.ident32[:, :], in_=idh.ap()),
              writes=[("c_id32",)], dom="qs")
        em.op("vector", lambda e: e.tensor_copy(self.ident[:, :], self.ident32[:, :]),
              reads=[("c_id32",)], writes=[("c_id",)])
        em.op("vector", lambda e: e.memset(self.ones_row[:, :], 1.0), writes=[("c_or",)])
        em.op("vector", lambda e: e.memset(self.ones_row32[:, :], 1.0), writes=[("c_or32",)])
        em.op("vector", lambda e: e.memset(self.ones128_32[:, :], 1.0), writes=[("c_o128",)])
        em.op("vector", lambda e: e.memset(self.ones_col[:, :], 1.0), writes=[("c_oc",)])
        hs_np = np.zeros((2, 128), np.float32)
        hs_np[0, 0:64] = 1.0
        hs_np[1, 64:128] = 1.0
        hs_h = self.nc.inline_tensor(hs_np, "hsel32_c")
        em.op("sync", lambda e: e.dma_start(out=self.hsel32[0:2, :], in_=hs_h.ap()),
              writes=[("c_hs",)], dom="qs")

    def pbank(self):
        b = self.p512[self._prot % 6]
        k = ("p512", self._prot % 6)
        self._prot += 1
        return b, k

    def constf(self, val, rows=128):
        val = float(val)
        if val in self.cpool_vals:
            i = self.cpool_vals[val]
            return self.cpool[0:rows, i:i + 1]
        i = len(self.cpool_vals)
        assert i < 16
        self.cpool_vals[val] = i
        self.em.op("vector", lambda e, i=i: e.memset(self.cpool[:, i:i + 1], val),
                   writes=[("c_cp",)])
        return self.cpool[0:rows, i:i + 1]

    def esel(self, n):
        attr = f"_esel{n}"
        if hasattr(self, attr):
            return getattr(self, attr)
        npv = np.kron(np.eye(n, dtype=np.float32), np.ones((1, 128), np.float32))
        h = self.nc.inline_tensor(npv, f"esel{n}")
        sb = self.sb(f"c_esel{n}", [n, n * 128], F32)
        self.em.op("sync", lambda e: e.dma_start(out=sb[0:n, :], in_=h.ap()),
                   writes=[(f"esel{n}",)], dom="qs")
        setattr(self, attr, (sb, (f"esel{n}",)))
        return getattr(self, attr)

    def qsel(self, n, npc):
        attr = f"_qsel{n}_{npc}"
        if hasattr(self, attr):
            return getattr(self, attr)
        npv = np.zeros((n, npc * 128), np.float32)
        for kk in range(npc):
            npv[2 * kk, kk * 128:kk * 128 + 64] = 1.0
            npv[2 * kk + 1, kk * 128 + 64:(kk + 1) * 128] = 1.0
        h = self.nc.inline_tensor(npv, f"qsel{n}_{npc}")
        sb = self.sb(f"c_qsel{n}_{npc}", [n, npc * 128], F32)
        self.em.op("sync", lambda e: e.dma_start(out=sb[0:n, :], in_=h.ap()),
                   writes=[(f"qsel{n}_{npc}",)], dom="qs")
        setattr(self, attr, (sb, (f"qsel{n}_{npc}",)))
        return getattr(self, attr)

    def bcast_selrow(self, src, n, row_idx, ncols, rkey, dst, dkey, col_off=0,
                     dst_off=0):
        em = self.em
        sel, selkey = self.esel(n)
        for t0 in range(0, ncols, 512):
            tw = min(512, ncols - t0)
            pb, pk = self.pbank()
            em.op("tensor", lambda e, t0=t0, tw=tw, pb=pb: e.matmul(
                pb[:, 0:tw], sel[0:n, row_idx * 128:(row_idx + 1) * 128],
                src[0:n, col_off + t0:col_off + t0 + tw],
                start=True, stop=True),
                reads=[selkey, rkey], writes=[pk])
            em.op("vector", lambda e, t0=t0, tw=tw, pb=pb: e.tensor_copy(
                dst[:, dst_off + t0:dst_off + t0 + tw], pb[:, 0:tw]),
                reads=[pk], writes=[dkey])

    def row_load(self, dram_ap, n, key):
        row = self.sb(self.fresh("rw"), [1, n], F32)
        self.em.op("sync", lambda e: e.dma_start(out=row[0:1, 0:n], in_=dram_ap),
                   writes=[key], dom="qs")
        return row

    def cast16(self, src, key_r, key_w, shape=None, eng="vector"):
        p, f = shape if shape else (src.shape[0], src.shape[1])
        dst = self.sb(self.fresh("b16"), [p, f], BF16)
        self.em.op(eng, lambda e: e.tensor_copy(dst[0:p, 0:f], src[0:p, 0:f]),
                   reads=[key_r], writes=[key_w])
        return dst

    def bcast_row(self, row, n, rkey, dst, dkey, dst_off=0, row_idx=0, col_off=0):
        em = self.em
        for t0 in range(0, n, 512):
            tw = min(512, n - t0)
            pb, pk = self.pbank()
            em.op("tensor", lambda e, t0=t0, tw=tw, pb=pb: e.matmul(
                pb[:, 0:tw], self.ones128_32[0:1, :],
                row[row_idx:row_idx + 1, col_off + t0:col_off + t0 + tw],
                start=True, stop=True),
                reads=[("c_o128",), rkey], writes=[pk])
            em.op("vector", lambda e, t0=t0, tw=tw, pb=pb: e.tensor_copy(
                dst[:, dst_off + t0:dst_off + t0 + tw], pb[:, 0:tw]),
                reads=[pk], writes=[dkey])

    def transpose(self, in_ap, p, f, rkeys, out_sb, out_off, wkey, pti=0, bp=0):
        em = self.em
        ptb = self.pT[pti]
        pk = ("pT", pti)
        em.op("tensor", lambda e: e.transpose(
            ptb[0:f, 0:p], in_ap, self.ident[bp:bp + p, bp:bp + p]),
            reads=list(rkeys) + [("c_id",)], writes=[pk])
        em.op("vector", lambda e: e.tensor_copy(out_sb[0:f, out_off:out_off + p],
                                                ptb[0:f, 0:p]),
              reads=[pk], writes=[wkey])

    def proj(self, wname, w_ap, K, Fdim, xn, xnkeyf, dst_evac,
             bias_ap=None, colscale=None, fgw=512, tsubs=None, xoff_fn=None,
             wq_dom="qs", cast_eng="vector", bias_scale_row=None, wbufs=None,
             bias_rhs=None, dma_cast=False, bias_pre=None):
        c = self.cfg
        em = self.em
        nk = cdiv(K, 128)
        T = c.T
        tsubs = tsubs if tsubs is not None else [(i * c.TSUB, c.TSUB) for i in range(c.NTS)]
        if xoff_fn is None:
            xoff_fn = lambda k, t0: k * T + t0
        if wbufs is not None:
            assert nk <= len(wbufs["wsl"][0]) and fgw <= wbufs["fgw"]
            stage = wbufs["stage"]
            wsl = wbufs["wsl"]
            wkp = wbufs["kp"]
            npar_eff = wbufs.get("npar", 2)
        else:
            stage = [self.sb(self.fresh(f"{wname}stg"), [128, fgw], F32) for _ in range(2)]
            wsl = [[self.sb(self.fresh(f"{wname}w"), [128, fgw], BF16) for _ in range(nk)]
                   for _ in range(2)]
            wkp = wname
        bkey = (wname, "b16")
        if bias_pre is not None:
            brow16, bkey = bias_pre
            bias_ap = True
        elif bias_ap is not None:
            if bias_scale_row is not None:
                brow = self.row_load(bias_ap, Fdim, (wname, "b"))
                srow, skey = bias_scale_row
                em.op("vector", lambda e: e.tensor_mul(brow[0:1, :], brow[0:1, :], srow),
                      reads=[(wname, "b"), skey], writes=[(wname, "b")])
                brow16 = self.cast16(brow, (wname, "b"), (wname, "b16"))
            else:
                brow16 = self.sb(self.fresh(f"{wname}b16"), [1, Fdim], BF16)
                em.op("gpsimd", lambda e: e.dma_start(
                    out=brow16[0:1, 0:Fdim], in_=bias_ap),
                    writes=[(wname, "b16")], dom="qg")
        if wbufs is None:
            npar_eff = 2
        ldeng = "sync" if wq_dom == "qs" else "gpsimd"
        for gi, f0 in enumerate(range(0, Fdim, fgw)):
            fw = min(fgw, Fdim - f0)
            pp = gi % npar_eff
            for k in range(nk):
                kk = min(128, K - k * 128)
                sp = k % 2
                if dma_cast:
                    em.op("gpsimd",
                          lambda e, k=k, kk=kk, f0=f0, fw=fw, pp=pp: e.dma_start(
                              out=wsl[pp][k][0:kk, 0:fw],
                              in_=w_ap[k * 128:k * 128 + kk, f0:f0 + fw]),
                          writes=[(wkp, "w", pp, k)], dom="qg")
                    continue
                em.op(ldeng,
                      lambda e, k=k, kk=kk, f0=f0, fw=fw, sp=sp: e.dma_start(
                          out=stage[sp][0:kk, 0:fw], in_=w_ap[k * 128:k * 128 + kk, f0:f0 + fw]),
                      writes=[(wkp, "stg", sp)], dom=wq_dom)
                if colscale is None:
                    em.op(cast_eng, lambda e, k=k, kk=kk, fw=fw, pp=pp, sp=sp: e.tensor_copy(
                        wsl[pp][k][0:kk, 0:fw], stage[sp][0:kk, 0:fw]),
                        reads=[(wkp, "stg", sp)], writes=[(wkp, "w", pp, k)])
                else:
                    srow, skey2 = colscale
                    em.op(cast_eng, lambda e, k=k, kk=kk, fw=fw, pp=pp, sp=sp:
                          e.tensor_scalar_mul(
                              wsl[pp][k][0:kk, 0:fw], stage[sp][0:kk, 0:fw], srow[k]),
                          reads=[(wkp, "stg", sp), skey2], writes=[(wkp, "w", pp, k)])
            for fc0 in range(0, fw, 128):
                fcw = min(128, fw - fc0)
                fabs = f0 + fc0
                for (t0, tw) in tsubs:
                    pb, pk = self.pbank()
                    for k in range(nk):
                        kk = min(128, K - k * 128)
                        last = (k == nk - 1) and bias_ap is None
                        em.op("tensor", lambda e, k=k, kk=kk, fc0=fc0, fcw=fcw, t0=t0,
                              tw=tw, pb=pb, pp=pp, last=last: e.matmul(
                                  pb[0:fcw, 0:tw], wsl[pp][k][0:kk, fc0:fc0 + fcw],
                                  xn[0:kk, xoff_fn(k, t0):xoff_fn(k, t0) + tw],
                                  start=(k == 0), stop=last),
                              reads=[(wkp, "w", pp, k), xnkeyf(k)],
                              writes=[pk], sig=last or (k == nk - 1))
                    if bias_ap is not None:
                        if bias_rhs is None:
                            em.op("tensor", lambda e, fabs=fabs, fcw=fcw, tw=tw,
                                  pb=pb: e.matmul(
                                      pb[0:fcw, 0:tw], brow16[0:1, fabs:fabs + fcw],
                                      self.ones_row[0:1, 0:tw], start=False, stop=True),
                                  reads=[bkey, ("c_or",)], writes=[pk])
                        else:
                            rrow, rrkey = bias_rhs
                            em.op("tensor", lambda e, fabs=fabs, fcw=fcw, t0=t0,
                                  tw=tw, pb=pb: e.matmul(
                                      pb[0:fcw, 0:tw], brow16[0:1, fabs:fabs + fcw],
                                      rrow[0:1, t0:t0 + tw], start=False, stop=True),
                                  reads=[bkey, rrkey], writes=[pk])
                    dst_evac(fabs, fcw, t0, tw, pb, pk)

    def ln_stats(self, x, xkeyf, name, l1_variant=False):
        c = self.cfg
        em = self.em
        T = c.T
        mu = self.sb(self.fresh("lnmu"), [1, T], F32)
        var = self.sb(self.fresh("lnvar"), [1, T], F32)
        rstd = self.sb(self.fresh("lnr"), [1, T], F32)
        out = {"mu": (mu, (name, "mu")), "rstd": (rstd, (name, "rstd")),
               "var": (var, (name, "var"))}
        if l1_variant:
            R = self.sb(self.fresh("lnR"), [1, T], F32)
            out["R"] = (R, (name, "R"))
        with self.scope():
            self._ln_stats_body(x, xkeyf, name, l1_variant, mu, var, rstd, out)
        return out

    def _ln_stats_body(self, x, xkeyf, name, l1_variant, mu, var, rstd, out):
        c = self.cfg
        em = self.em
        T = c.T
        sq = [self.sb(self.fresh("lnsq"), [128, c.TSUB], F32) for _ in range(2)]
        sums = self.sb(self.fresh("lnsum"), [1, T], F32)
        sumsq = self.sb(self.fresh("lnsq2"), [1, T], F32)
        kS = (name, "sums")
        kQ = (name, "sumsq")
        for ti in range(c.NTS):
            t0 = ti * c.TSUB
            pbS, pkS = self.pbank()
            pbQ, pkQ = self.pbank()
            for k in range(c.NCH):
                pp = k % 2
                em.op("scalar", lambda e, k=k, t0=t0, pp=pp: e.activation(
                    sq[pp][:, 0:c.TSUB], x[:, k * T + t0:k * T + t0 + c.TSUB], AF.Square),
                    reads=[xkeyf(k)], writes=[(name, "sq", pp)])
                em.op("tensor", lambda e, k=k, t0=t0, pbS=pbS: e.matmul(
                    pbS[0:1, 0:c.TSUB], self.ones_col[:, 0:1],
                    x[:, k * T + t0:k * T + t0 + c.TSUB],
                    start=(k == 0), stop=(k == c.NCH - 1)),
                    reads=[("c_oc",), xkeyf(k)], writes=[pkS])
                em.op("tensor", lambda e, k=k, pp=pp, pbQ=pbQ: e.matmul(
                    pbQ[0:1, 0:c.TSUB], self.ones_col[:, 0:1], sq[pp][:, 0:c.TSUB],
                    start=(k == 0), stop=(k == c.NCH - 1)),
                    reads=[("c_oc",), (name, "sq", pp)], writes=[pkQ])
            em.op("vector", lambda e, t0=t0, pbS=pbS: e.tensor_copy(
                sums[0:1, t0:t0 + c.TSUB], pbS[0:1, 0:c.TSUB]), reads=[pkS], writes=[kS])
            em.op("vector", lambda e, t0=t0, pbQ=pbQ: e.tensor_copy(
                sumsq[0:1, t0:t0 + c.TSUB], pbQ[0:1, 0:c.TSUB]), reads=[pkQ], writes=[kQ])
        D = float(c.D)
        kMu = (name, "mu")
        kVar = (name, "var")
        kR = (name, "rstd")
        em.op("vector", lambda e: e.tensor_scalar_mul(mu[0:1, :], sums[0:1, :], 1.0 / D),
              reads=[kS], writes=[kMu])
        ex2 = self.sb(self.fresh("lnex2"), [1, T], F32)
        kE = (name, "ex2")
        em.op("vector", lambda e: e.tensor_scalar_mul(ex2[0:1, :], sumsq[0:1, :], 1.0 / D),
              reads=[kQ], writes=[kE])
        em.op("vector", lambda e: e.scalar_tensor_tensor(
            var[0:1, :], mu[0:1, :], -1.0, mu[0:1, :], ALU.mult, ALU.mult),
            reads=[kMu], writes=[kVar])
        em.op("vector", lambda e: e.tensor_add(var[0:1, :], var[0:1, :], ex2[0:1, :]),
              reads=[kVar, kE], writes=[kVar])
        eps = 1e-5
        lnv = self.sb(self.fresh("lnlv"), [1, T], F32)
        kL = (name, "lnv")
        c_eps1 = self.constf(eps, 1)
        c_mh1 = self.constf(-0.5, 1)
        em.op("scalar", lambda e: e.activation(
            lnv[0:1, :], var[0:1, :], AF.Ln, bias=c_eps1),
            reads=[kVar, ("c_cp",)], writes=[kL])
        em.op("scalar", lambda e: e.activation(
            rstd[0:1, :], lnv[0:1, :], AF.Exp, scale=c_mh1),
            reads=[kL, ("c_cp",)], writes=[kR])
        if l1_variant:
            vpe = self.sb(self.fresh("lnvpe"), [1, T], F32)
            kVpe = (name, "vpe")
            em.op("vector", lambda e: e.tensor_single_scalar(
                vpe[0:1, :], var[0:1, :], eps, ALU.add), reads=[kVar], writes=[kVpe])
            a = self.sb(self.fresh("lna"), [1, T], F32)
            kA = (name, "a")
            em.op("vector", lambda e: e.tensor_tensor(
                a[0:1, :], var[0:1, :], vpe[0:1, :], ALU.divide),
                reads=[kVar, kVpe], writes=[kA])
            lnv2 = self.sb(self.fresh("lnlv2"), [1, T], F32)
            kL2 = (name, "lnv2")
            em.op("scalar", lambda e, c_eps1=c_eps1: e.activation(
                lnv2[0:1, :], a[0:1, :], AF.Ln, bias=c_eps1),
                reads=[kA, ("c_cp",)], writes=[kL2])
            rh = self.sb(self.fresh("lnrh"), [1, T], F32)
            kRh = (name, "rh")
            em.op("scalar", lambda e, c_mh1=c_mh1: e.activation(
                rh[0:1, :], lnv2[0:1, :], AF.Exp, scale=c_mh1),
                reads=[kL2, ("c_cp",)], writes=[kRh])
            R, kRR = out["R"]
            em.op("vector", lambda e: e.tensor_mul(R[0:1, :], rstd[0:1, :], rh[0:1, :]),
                  reads=[kR, kRh], writes=[kRR])

    def ln_apply(self, x, xkeyf, mu_b, kMb, r_b, kRb, dst16, dkeyf, tmp, name):
        c = self.cfg
        em = self.em
        T = c.T
        for k in range(c.NCH):
            pp = k % len(tmp)
            tk = (tmp[pp].name,)
            em.op("vector", lambda e, k=k, pp=pp: e.tensor_sub(
                tmp[pp][:, 0:T], x[:, k * T:(k + 1) * T], mu_b[:, 0:T]),
                reads=[xkeyf(k), kMb], writes=[tk])
            em.op("gpsimd", lambda e, k=k, pp=pp: e.tensor_mul(
                dst16[:, k * T:(k + 1) * T], tmp[pp][:, 0:T], r_b[:, 0:T]),
                reads=[tk, kRb], writes=[dkeyf(k)])

    def setup_meta(self, meta_ap):
        em = self.em
        T = self.cfg.T
        self.meta_sb = self.sb("meta_sb", [1, 4], I32)
        em.op("sync", lambda e: e.dma_start(out=self.meta_sb[0:1, :], in_=meta_ap),
              writes=[("meta",)], dom="qs")
        nc = self.nc
        self.dynregs = {
            ("gpsimd", "hb"): self.st.enter_context(nc.gpsimd.register("rg_hbT")),
            ("sync", "hb"): self.st.enter_context(nc.sync.register("rs_hbT")),
            ("sync", "tb"): self.st.enter_context(nc.sync.register("rs_tb")),
        }

        def init_g(e):
            with e.register() as rh:
                e.reg_load(rh, self.meta_sb[0:1, 0:1])
                return e.reg_mul(self.dynregs[("gpsimd", "hb")], rh, T)
        em.op("gpsimd", init_g, reads=[("meta",)], writes=[("regs_g",)])

        def init_s(e):
            with e.register() as rh:
                e.reg_load(rh, self.meta_sb[0:1, 0:1])
                e.reg_mul(self.dynregs[("sync", "hb")], rh, T)
                return e.reg_load(self.dynregs[("sync", "tb")],
                                  self.meta_sb[0:1, 1:2])
        em.op("sync", init_s, reads=[("meta",)], writes=[("regs_s",)])

    def dyn_load(self, eng, dst_ap, src_handle, static_off, ap_pattern,
                 rkeys, wkeys, base="hb"):
        em = self.em
        reg = self.dynregs[(eng, base)]
        regkey = ("regs_g",) if eng == "gpsimd" else ("regs_s",)

        def f(e):
            with e.register() as rt:
                e.reg_add(rt, reg, static_off)
                return e.dma_start(out=dst_ap,
                                   in_=bass.AP(src_handle, rt, ap_pattern))
        return em.op(eng, f, reads=list(rkeys) + [regkey],
                     writes=list(wkeys), dom="qg" if eng == "gpsimd" else "qs")

    def attn_headshard(self, name, ag_out_h, ag2_in_h, mode="softmax",
                       scale=0.125, maskA16=None, mask01=None,
                       decay_dram=None, gpow_src=None, post_group=None):
        c = self.cfg
        em = self.em
        S, T, D = c.S, c.T, c.D
        NK128 = S // 128
        qkv_g = self.sb(self.fresh("atqkv"), [128, 3 * S], BF16)
        q_g = qkv_g
        QO, KO, VO = 0, S, 2 * S
        if mode == "retention":
            self._ret_dec = self.sb(self.fresh("atdec"), [128, 2 * S], BF16)
            self._ret_gpow_b = (self.sb(self.fresh("atgp"),
                                        [128, 2 * (S // 128) * c.NKB], F32),
                                (name, "gpb"))
        vT = self.sb(self.fresh("atvT"), [128, S], BF16)
        P_sb = self.sb(self.fresh("atP"), [128, 4 * c.KB], BF16)
        PT_sb = self.sb(self.fresh("atPT"), [128, 4 * 128], BF16)
        outT = self.sb(self.fresh("atO"), [128, S], BF16)
        sums = self.sb(self.fresh("atsum"), [128, 4 * 2 * 8], F32)
        rrec = self.sb(self.fresh("atrr"), [128, 8], F32)
        rrT = self.sb(self.fresh("atrrT"), [8, 512], F32)
        rb_sb = self.sb(self.fresh("atrb"), [128, 512], F32)
        kQN = (name, "qn")
        kKN = (name, "kn")
        c_scale128 = self.constf(float(scale), 128) if mode == "softmax" else None
        if mode == "softmax":
            sqt = self.sb(self.fresh("atsq"), [128, 512], F32)
            rowbuf = self.sb(self.fresh("atrow"), [128, S], F32)
            qn2 = rowbuf[0:2, :]
            qn = rowbuf[0:2, :]
            kn2 = rowbuf[32:34, :]
            kn = rowbuf[32:34, :]
            cneg = rowbuf[64:66, :]
            kmax = self.sb(self.fresh("atkm"), [2, 1], F32)
            cT = self.sb(self.fresh("atcT"), [128, 2 * NK128], F32)
            hsel = self.sb(self.fresh("athsel"), [128, 2], F32)
            em.op("vector", lambda e: e.memset(hsel[:, :], 0.0), writes=[("athsel",)])
            em.op("vector", lambda e: e.memset(hsel[0:64, 0:1], 1.0), writes=[("athsel",)])
            em.op("vector", lambda e: e.memset(hsel[64:128, 1:2], 1.0), writes=[("athsel",)])
        kAg2 = (name, "ag2_in")
        kQKV = (name, "qkv")
        for g in range(c.NGC):
            dst4 = qkv_g[:, :].rearrange("p (s r t) -> p s r t", s=3, r=2)
            for rank, eng in ((0, "gpsimd"), (1, "sync")):
                self.dyn_load(
                    eng, dst4[:, :, rank, :], ag_out_h,
                    rank * 3 * D * T + g * 128 * T,
                    [[T, 128], [D * T, 3], [1, T]],
                    [("ag_out",)], [kQKV])
            if mode == "retention":
                qdec_h, kdec_h = decay_dram
                dec_sb = self._ret_dec
                for i_, dh in enumerate((qdec_h, kdec_h)):
                    em.op("sync", lambda e, i_=i_, dh=dh, g=g: e.dma_start(
                        out=dec_sb[:, i_ * S:(i_ + 1) * S],
                        in_=dh.ap()[g * 128:(g + 1) * 128, :]),
                        writes=[(name, "dec")], dom="qs")
                em.op("vector", lambda e: e.tensor_mul(
                    qkv_g[:, QO:QO + S], qkv_g[:, QO:QO + S], dec_sb[:, 0:S]),
                    reads=[kQKV, (name, "dec")], writes=[kQKV])
                em.op("vector", lambda e: e.tensor_mul(
                    qkv_g[:, KO:KO + S], qkv_g[:, KO:KO + S], dec_sb[:, S:2 * S]),
                    reads=[kQKV, (name, "dec")], writes=[kQKV])
                gpow_b2 = self._ret_gpow_b
                for h in range(2):
                    self.bcast_selrow(gpow_src[0], 2 * c.NGC, g * 2 + h,
                                      NK128 * c.NKB, gpow_src[1], gpow_b2[0],
                                      gpow_b2[1], dst_off=h * NK128 * c.NKB)
            for kt in range(NK128):
                for h in range(2):
                    self.transpose(
                        qkv_g[64 * h:64 * h + 64, VO + kt * 128:VO + (kt + 1) * 128],
                        64, 128, [kQKV], vT,
                        (h * NK128 + kt) * 64, (name, "vT"), pti=kt % 2,
                        bp=64 * h)
            if mode == "softmax":
                for t0 in range(0, S, 512):
                    em.op("scalar", lambda e, t0=t0: e.activation(
                        sqt[:, 0:512], qkv_g[:, QO + t0:QO + t0 + 512], AF.Square),
                        reads=[kQKV], writes=[(name, "sq")])
                    pb, pk = self.pbank()
                    em.op("tensor", lambda e, pb=pb: e.matmul(
                        pb[0:2, 0:512], hsel[:, 0:2], sqt[:, 0:512],
                        start=True, stop=True),
                        reads=[("athsel",), (name, "sq")], writes=[pk])
                    em.op("vector", lambda e, t0=t0, pb=pb: e.tensor_copy(
                        rowbuf[0:2, t0:t0 + 512], pb[0:2, 0:512]), reads=[pk], writes=[kQN])
                    em.op("scalar", lambda e, t0=t0: e.activation(
                        sqt[:, 0:512], qkv_g[:, KO + t0:KO + t0 + 512], AF.Square),
                        reads=[kQKV], writes=[(name, "sq")])
                    pb, pk = self.pbank()
                    em.op("tensor", lambda e, pb=pb: e.matmul(
                        pb[0:2, 0:512], hsel[:, 0:2], sqt[:, 0:512],
                        start=True, stop=True),
                        reads=[("athsel",), (name, "sq")], writes=[pk])
                    em.op("vector", lambda e, t0=t0, pb=pb: e.tensor_copy(
                        rowbuf[32:34, t0:t0 + 512], pb[0:2, 0:512]), reads=[pk], writes=[kKN])
                c_t2 = self.constf(1e-30, 2)
                c_h2 = self.constf(0.5, 2)
                em.op("scalar", lambda e: e.activation(
                    rowbuf[0:2, :], rowbuf[0:2, :], AF.Ln, bias=c_t2),
                    reads=[kQN, ("c_cp",)], writes=[(name, "qn_")])
                em.op("scalar", lambda e: e.activation(
                    rowbuf[0:2, :], rowbuf[0:2, :], AF.Exp, scale=c_h2),
                    reads=[(name, "qn_"), ("c_cp",)], writes=[(name, "qn_")])
                em.op("scalar", lambda e: e.activation(
                    rowbuf[32:34, :], rowbuf[32:34, :], AF.Ln, bias=c_t2),
                    reads=[kKN, ("c_cp",)], writes=[(name, "kn_")])
                em.op("scalar", lambda e: e.activation(
                    rowbuf[32:34, :], rowbuf[32:34, :], AF.Exp, scale=c_h2),
                    reads=[(name, "kn_"), ("c_cp",)], writes=[(name, "kn_")])
                em.op("vector", lambda e: e.tensor_reduce(
                    kmax[0:2, 0:1], rowbuf[32:34, :], AX.X, ALU.max),
                    reads=[(name, "kn_")], writes=[(name, "km")])
                em.op("vector", lambda e: e.tensor_scalar(
                    rowbuf[64:66, :], rowbuf[0:2, :], kmax[0:2, 0:1], -float(scale),
                    ALU.mult, ALU.mult),
                    reads=[(name, "qn_"), (name, "km")], writes=[(name, "c")])
                pb5 = self.p512[5]
                pk5 = ("p512", 5)
                for qt in range(NK128):
                    em.op("tensor", lambda e, qt=qt: e.transpose(
                        pb5[0:128, 0:2], rowbuf[64:66, qt * 128:(qt + 1) * 128],
                        self.ident32[64:66, 64:66]),
                        reads=[(name, "c"), ("c_id32",)], writes=[pk5])
                    em.op("vector", lambda e, qt=qt: e.tensor_copy(
                        cT[:, qt * 2:qt * 2 + 2], pb5[0:128, 0:2]),
                        reads=[pk5], writes=[(name, "cT")])
            NQB = NK128 // c.QB
            psO = self.p512[4]
            kO = ("p512", 4)
            pb5 = self.p512[5]
            pk5 = ("p512", 5)
            for qb in range(NQB):
                if mode == "softmax":
                    em.op("vector", lambda e: e.memset(sums[:, :], 0.0),
                          writes=[(name, "sums")])
                for qloc in range(c.QB):
                    qt = qb * c.QB + qloc
                    jmax = (qt * 128) // c.KB
                    for h in range(2):
                        for j in range(jmax + 1):
                            strad = (j == jmax)
                            delta = qt - j * (c.KB // 128)
                            kw = (delta + 1) * 128 if strad else c.KB
                            nkt = kw // 128
                            sb_i = h * 2 + (j % 2)
                            Sb = self.p512[sb_i]
                            kS = ("p512", sb_i)
                            em.op("tensor", lambda e, h=h, qt=qt, j=j, kw=kw, Sb=Sb:
                                  e.matmul(Sb[0:128, 0:kw],
                                           qkv_g[64 * h:64 * h + 64,
                                                 QO + qt * 128:QO + qt * 128 + 128],
                                           qkv_g[64 * h:64 * h + 64,
                                                 KO + j * c.KB:KO + j * c.KB + kw],
                                           start=True, stop=True,
                                           tile_position=(64 * h, 0)),
                                  reads=[kQKV], writes=[kS])
                            if mode == "softmax":
                                if strad:
                                    em.op("vector", lambda e, kw=kw, Sb=Sb: e.tensor_add(
                                        Sb[0:128, kw - 128:kw], maskA16[0][:, 0:128],
                                        Sb[0:128, kw - 128:kw]),
                                        reads=[maskA16[1], kS], writes=[kS])
                                pr = (h * 2 + j % 2) * c.KB
                                em.op("scalar", lambda e, h=h, qt=qt, qloc=qloc, j=j,
                                      kw=kw, Sb=Sb, pr=pr: e.activation(
                                          P_sb[:, pr:pr + kw],
                                          Sb[0:128, 0:kw], AF.Exp,
                                          bias=cT[:, qt * 2 + h:qt * 2 + h + 1],
                                          scale=c_scale128,
                                          accum_out=sums[:, (qloc * 2 + h) * 8 + j:
                                                         (qloc * 2 + h) * 8 + j + 1]),
                                      reads=[kS, (name, "cT"), ("c_cp",)],
                                      writes=[(name, "P", h, j % 2), (name, "sums")])
                            else:
                                gp = self._ret_gpow_b
                                pr = (h * 2 + j % 2) * c.KB
                                col = (h * NK128 * c.NKB + qt * c.NKB + j)
                                if strad:
                                    if delta > 0:
                                        em.op("vector", lambda e, h=h, kw=kw, Sb=Sb,
                                              col=col, pr=pr: e.tensor_scalar(
                                                  P_sb[:, pr:pr + kw - 128],
                                                  Sb[0:128, 0:kw - 128],
                                                  gp[0][:, col:col + 1], None, ALU.mult),
                                              reads=[kS, gp[1]],
                                              writes=[(name, "P", h, j % 2)])
                                    em.op("vector", lambda e, h=h, kw=kw, Sb=Sb, col=col,
                                          pr=pr: e.scalar_tensor_tensor(
                                              P_sb[:, pr + kw - 128:pr + kw],
                                              Sb[0:128, kw - 128:kw],
                                              gp[0][:, col:col + 1],
                                              mask01[0][:, 0:128], ALU.mult, ALU.mult),
                                          reads=[kS, gp[1], mask01[1]],
                                          writes=[(name, "P", h, j % 2)])
                                else:
                                    em.op("vector", lambda e, h=h, kw=kw, Sb=Sb, col=col,
                                          pr=pr: e.tensor_scalar(
                                              P_sb[:, pr:pr + kw],
                                              Sb[0:128, 0:kw],
                                              gp[0][:, col:col + 1], None, ALU.mult),
                                          reads=[kS, gp[1]], writes=[(name, "P", h, j % 2)])
                            for kt in range(nkt):
                                ktabs = j * (c.KB // 128) + kt
                                pti = (kt * 2 + h) % 2
                                ptb = self.pT[pti]
                                pkT = ("pT", pti)
                                pr = (h * 2 + j % 2) * c.KB
                                em.op("tensor", lambda e, h=h, kt=kt, ptb=ptb, pr=pr:
                                      e.transpose(
                                          ptb[0:128, 0:128],
                                          P_sb[:, pr + kt * 128:pr + kt * 128 + 128],
                                          self.ident[:, :]),
                                      reads=[(name, "P", h, j % 2), ("c_id",)], writes=[pkT])
                                pt_off = ((h * 2) + (kt % 2)) * 128
                                em.op("vector", lambda e, ptb=ptb, pt_off=pt_off:
                                      e.tensor_copy(PT_sb[:, pt_off:pt_off + 128],
                                                    ptb[0:128, 0:128]),
                                      reads=[pkT], writes=[(name, "PT", h, kt % 2)])
                                first = (j == 0 and kt == 0)
                                last = (j == jmax and kt == nkt - 1)
                                em.op("tensor", lambda e, h=h, ktabs=ktabs, qloc=qloc,
                                      pt_off=pt_off, first=first, last=last: e.matmul(
                                          psO[64 * h:64 * h + 64,
                                              qloc * 128:qloc * 128 + 128],
                                          vT[:, (h * NK128 + ktabs) * 64:
                                             (h * NK128 + ktabs) * 64 + 64],
                                          PT_sb[:, pt_off:pt_off + 128],
                                          start=first, stop=last,
                                          tile_position=(0, 64 * h)),
                                      reads=[(name, "vT"), (name, "PT", h, kt % 2)],
                                      writes=[kO], sig=last)
                if mode == "softmax":
                    em.op("vector", lambda e: e.tensor_reduce(
                        rrec[:, 0:8],
                        sums[:, :].rearrange("p (a b) -> p a b", b=8),
                        AX.X, ALU.add),
                        reads=[(name, "sums")], writes=[(name, "rr")])
                    em.op("vector", lambda e: e.reciprocal(rrec[:, 0:8], rrec[:, 0:8]),
                          reads=[(name, "rr")], writes=[(name, "rr")])
                    for qloc in range(c.QB):
                        em.op("tensor", lambda e, qloc=qloc: e.transpose(
                            pb5[0:2, 0:128], rrec[:, qloc * 2:qloc * 2 + 2],
                            self.ident32[0:128, 0:128]),
                            reads=[(name, "rr"), ("c_id32",)], writes=[pk5])
                        em.op("vector", lambda e, qloc=qloc: e.tensor_copy(
                            rrT[0:2, qloc * 128:qloc * 128 + 128], pb5[0:2, 0:128]),
                            reads=[pk5], writes=[(name, "rrT")])
                    for qloc in range(c.QB):
                        em.op("tensor", lambda e, qloc=qloc: e.matmul(
                            pb5[0:128, qloc * 128:qloc * 128 + 128],
                            self.hsel32[0:2, :],
                            rrT[0:2, qloc * 128:qloc * 128 + 128],
                            start=True, stop=True),
                            reads=[("c_hs",), (name, "rrT")], writes=[pk5])
                    em.op("vector", lambda e: e.tensor_copy(
                        rb_sb[:, 0:512], pb5[0:128, 0:512]),
                        reads=[pk5], writes=[(name, "rb")])
                    em.op("vector", lambda e, qb=qb: e.tensor_mul(
                        outT[:, qb * 512:qb * 512 + 512], psO[0:128, 0:512],
                        rb_sb[:, 0:512]),
                        reads=[kO, (name, "rb")], writes=[(name, "outT")])
                else:
                    em.op("vector", lambda e, qb=qb: e.tensor_copy(
                        outT[:, qb * 512:qb * 512 + 512], psO[0:128, 0:512]),
                        reads=[kO], writes=[(name, "outT")])
            if post_group is not None:
                post_group(g, outT, (name, "outT"))
            em.op("gpsimd", lambda e, g=g: e.dma_start(
                out=ag2_in_h.ap()[g * 128:(g + 1) * 128, :], in_=outT[:, :]),
                reads=[(name, "outT")], writes=[kAg2], dom="qg")

    def attn_qshard(self, name, qsb, kQB, ag_out_h, out_sb, kOUT, scale,
                    g_base=0, g_count=None, ag_key=("ag_out",), q_base=0,
                    shared=None):
        c = self.cfg
        em = self.em
        S, T, D = c.S, c.T, c.D
        I = D
        NG = I // 128
        if g_count is None:
            g_count = NG
        NK128 = S // 128
        NQTO = T // 128
        QBl = min(c.QB, NQTO)
        NJ = S // c.KB
        if shared is not None and "kv_g" in shared:
            kv_g = shared["kv_g"]
            vT, P_sb, PT_sb, sums = (shared["vT"], shared["P_sb"],
                                     shared["PT_sb"], shared["sums"])
            rrec, rrT, rb_sb, sqt = (shared["rrec"], shared["rrT"],
                                     shared["rb_sb"], shared["sqt"])
            rowbuf, kmax, cT, hsel = (shared["rowbuf"], shared["kmax"],
                                      shared["cT"], shared["hsel"])
            KO, VO = 0, S
            kKV = (name, "kv")
        else:
            kv_g = self.sb(self.fresh("aqkv"), [128, 2 * S], BF16)
            KO, VO = 0, S
            kKV = (name, "kv")
            vT = self.sb(self.fresh("aqvT"), [128, S], BF16)
            P_sb = self.sb(self.fresh("aqP"), [128, 4 * c.KB], BF16)
            PT_sb = self.sb(self.fresh("aqPT"), [128, 4 * 128], BF16)
            sums = self.sb(self.fresh("aqsum"), [128, 4 * 2 * NJ], F32)
            rrec = self.sb(self.fresh("aqrr"), [128, 8], F32)
            rrT = self.sb(self.fresh("aqrrT"), [8, 512], F32)
            rb_sb = self.sb(self.fresh("aqrb"), [128, 512], F32)
            sqt = self.sb(self.fresh("aqsq"), [128, 512], F32)
            rowbuf = self.sb(self.fresh("aqrow"), [128, S], F32)
            kmax = self.sb(self.fresh("aqkm"), [2, 1], F32)
            cT = self.sb(self.fresh("aqcT"), [128, 2 * NQTO], F32)
            hsel = self.sb(self.fresh("aqhsel"), [128, 2], F32)
            if shared is not None:
                shared.update(kv_g=kv_g, vT=vT, P_sb=P_sb, PT_sb=PT_sb,
                              sums=sums, rrec=rrec, rrT=rrT, rb_sb=rb_sb,
                              sqt=sqt, rowbuf=rowbuf, kmax=kmax, cT=cT,
                              hsel=hsel)
        em.op("vector", lambda e: e.memset(hsel[:, :], 0.0), writes=[(name, "hs")])
        em.op("vector", lambda e: e.memset(hsel[0:64, 0:1], 1.0), writes=[(name, "hs")])
        em.op("vector", lambda e: e.memset(hsel[64:128, 1:2], 1.0), writes=[(name, "hs")])
        kQN = (name, "qn")
        kKN = (name, "kn")
        c_scale128 = self.constf(float(scale), 128)
        c_t2 = self.constf(1e-30, 2)
        c_h2 = self.constf(0.5, 2)
        psO = self.p512[4]
        kO = ("p512", 4)
        pb5 = self.p512[5]
        pk5 = ("p512", 5)
        for g in range(g_base, g_base + g_count):
            for sec, off in ((0, KO), (1, VO)):
                for rank in range(2):
                    r0 = rank * 2 * I + sec * I + g * 128
                    em.op("gpsimd", lambda e, r0=r0, off=off, rank=rank: e.dma_start(
                        out=kv_g[:, off + rank * T:off + (rank + 1) * T],
                        in_=ag_out_h.ap()[r0:r0 + 128, :]),
                        reads=[ag_key], writes=[kKV], dom="qg")
            for kt in range(NK128):
                for h in range(2):
                    self.transpose(
                        kv_g[64 * h:64 * h + 64, VO + kt * 128:VO + (kt + 1) * 128],
                        64, 128, [kKV], vT, (h * NK128 + kt) * 64, (name, "vT"),
                        pti=kt % 2, bp=64 * h)
            for t0 in range(0, T, 512):
                tw = min(512, T - t0)
                em.op("scalar", lambda e, t0=t0, tw=tw, g=g: e.activation(
                    sqt[:, 0:tw],
                    qsb[:, (g - q_base) * T + t0:(g - q_base) * T + t0 + tw],
                    AF.Square),
                    reads=[kQB], writes=[(name, "sq")])
                em.op("tensor", lambda e, tw=tw: e.matmul(
                    pb5[0:2, 0:tw], hsel[:, 0:2], sqt[:, 0:tw],
                    start=True, stop=True),
                    reads=[(name, "hs"), (name, "sq")], writes=[pk5])
                em.op("vector", lambda e, t0=t0, tw=tw: e.tensor_copy(
                    rowbuf[0:2, t0:t0 + tw], pb5[0:2, 0:tw]),
                    reads=[pk5], writes=[kQN])
            for t0 in range(0, S, 512):
                em.op("scalar", lambda e, t0=t0: e.activation(
                    sqt[:, 0:512], kv_g[:, KO + t0:KO + t0 + 512], AF.Square),
                    reads=[kKV], writes=[(name, "sq")])
                em.op("tensor", lambda e: e.matmul(
                    pb5[0:2, 0:512], hsel[:, 0:2], sqt[:, 0:512],
                    start=True, stop=True),
                    reads=[(name, "hs"), (name, "sq")], writes=[pk5])
                em.op("vector", lambda e, t0=t0: e.tensor_copy(
                    rowbuf[32:34, t0:t0 + 512], pb5[0:2, 0:512]),
                    reads=[pk5], writes=[kKN])
            em.op("scalar", lambda e: e.activation(
                rowbuf[0:2, 0:T], rowbuf[0:2, 0:T], AF.Ln, bias=c_t2),
                reads=[kQN, ("c_cp",)], writes=[(name, "qn_")])
            em.op("scalar", lambda e: e.activation(
                rowbuf[0:2, 0:T], rowbuf[0:2, 0:T], AF.Exp, scale=c_h2),
                reads=[(name, "qn_"), ("c_cp",)], writes=[(name, "qn_")])
            em.op("scalar", lambda e: e.activation(
                rowbuf[32:34, :], rowbuf[32:34, :], AF.Ln, bias=c_t2),
                reads=[kKN, ("c_cp",)], writes=[(name, "kn_")])
            em.op("scalar", lambda e: e.activation(
                rowbuf[32:34, :], rowbuf[32:34, :], AF.Exp, scale=c_h2),
                reads=[(name, "kn_"), ("c_cp",)], writes=[(name, "kn_")])
            em.op("vector", lambda e: e.tensor_reduce(
                kmax[0:2, 0:1], rowbuf[32:34, :], AX.X, ALU.max),
                reads=[(name, "kn_")], writes=[(name, "km")])
            em.op("vector", lambda e: e.tensor_scalar(
                rowbuf[64:66, 0:T], rowbuf[0:2, 0:T], kmax[0:2, 0:1], -float(scale),
                ALU.mult, ALU.mult),
                reads=[(name, "qn_"), (name, "km")], writes=[(name, "c")])
            for qt in range(NQTO):
                em.op("tensor", lambda e, qt=qt: e.transpose(
                    pb5[0:128, 0:2], rowbuf[64:66, qt * 128:(qt + 1) * 128],
                    self.ident32[64:66, 64:66]),
                    reads=[(name, "c"), ("c_id32",)], writes=[pk5])
                em.op("vector", lambda e, qt=qt: e.tensor_copy(
                    cT[:, qt * 2:qt * 2 + 2], pb5[0:128, 0:2]),
                    reads=[pk5], writes=[(name, "cT")])
            NQB = NQTO // QBl
            for qb in range(NQB):
                em.op("vector", lambda e: e.memset(sums[:, :], 0.0),
                      writes=[(name, "sums")])
                for qloc in range(QBl):
                    qt = qb * QBl + qloc
                    for j in range(NJ):
                        for h in range(2):
                            sb_i = h * 2 + (j % 2)
                            Sb = self.p512[sb_i]
                            kS = ("p512", sb_i)
                            em.op("tensor", lambda e, h=h, qt=qt, j=j,
                                  g=g - q_base, Sb=Sb:
                                  e.matmul(Sb[0:128, 0:c.KB],
                                           qsb[64 * h:64 * h + 64,
                                               g * T + qt * 128:g * T + qt * 128 + 128],
                                           kv_g[64 * h:64 * h + 64,
                                                KO + j * c.KB:KO + (j + 1) * c.KB],
                                           start=True, stop=True,
                                           tile_position=(64 * h, 0)),
                                  reads=[kQB, kKV], writes=[kS])
                            pr = (h * 2 + j % 2) * c.KB
                            em.op("scalar", lambda e, h=h, qt=qt, qloc=qloc, j=j,
                                  Sb=Sb, pr=pr: e.activation(
                                      P_sb[:, pr:pr + c.KB],
                                      Sb[0:128, 0:c.KB], AF.Exp,
                                      bias=cT[:, qt * 2 + h:qt * 2 + h + 1],
                                      scale=c_scale128,
                                      accum_out=sums[:, (qloc * 2 + h) * NJ + j:
                                                     (qloc * 2 + h) * NJ + j + 1]),
                                  reads=[kS, (name, "cT"), ("c_cp",)],
                                  writes=[(name, "P", h, j % 2), (name, "sums")])
                            for kt in range(c.KB // 128):
                                ktabs = j * (c.KB // 128) + kt
                                pti = (kt * 2 + h) % 2
                                ptb = self.pT[pti]
                                pkT = ("pT", pti)
                                em.op("tensor", lambda e, ptb=ptb, pr=pr, kt=kt:
                                      e.transpose(
                                          ptb[0:128, 0:128],
                                          P_sb[:, pr + kt * 128:pr + kt * 128 + 128],
                                          self.ident[:, :]),
                                      reads=[(name, "P", h, j % 2), ("c_id",)],
                                      writes=[pkT])
                                pt_off = ((h * 2) + (kt % 2)) * 128
                                em.op("vector", lambda e, ptb=ptb, pt_off=pt_off:
                                      e.tensor_copy(PT_sb[:, pt_off:pt_off + 128],
                                                    ptb[0:128, 0:128]),
                                      reads=[pkT], writes=[(name, "PT", h, kt % 2)])
                                first = (j == 0 and kt == 0)
                                last = (j == NJ - 1 and kt == c.KB // 128 - 1)
                                em.op("tensor", lambda e, h=h, ktabs=ktabs, qloc=qloc,
                                      pt_off=pt_off, first=first, last=last: e.matmul(
                                          psO[64 * h:64 * h + 64,
                                              qloc * 128:qloc * 128 + 128],
                                          vT[:, (h * NK128 + ktabs) * 64:
                                             (h * NK128 + ktabs) * 64 + 64],
                                          PT_sb[:, pt_off:pt_off + 128],
                                          start=first, stop=last,
                                          tile_position=(0, 64 * h)),
                                      reads=[(name, "vT"), (name, "PT", h, kt % 2)],
                                      writes=[kO], sig=last)
                em.op("vector", lambda e: e.tensor_reduce(
                    rrec[:, 0:8],
                    sums[:, :].rearrange("p (a b) -> p a b", b=NJ),
                    AX.X, ALU.add),
                    reads=[(name, "sums")], writes=[(name, "rr")])
                em.op("vector", lambda e: e.reciprocal(rrec[:, 0:8], rrec[:, 0:8]),
                      reads=[(name, "rr")], writes=[(name, "rr")])
                for qloc in range(QBl):
                    em.op("tensor", lambda e, qloc=qloc: e.transpose(
                        pb5[0:2, 0:128], rrec[:, qloc * 2:qloc * 2 + 2],
                        self.ident32[0:128, 0:128]),
                        reads=[(name, "rr"), ("c_id32",)], writes=[pk5])
                    em.op("vector", lambda e, qloc=qloc: e.tensor_copy(
                        rrT[0:2, qloc * 128:qloc * 128 + 128], pb5[0:2, 0:128]),
                        reads=[pk5], writes=[(name, "rrT")])
                for qloc in range(QBl):
                    em.op("tensor", lambda e, qloc=qloc: e.matmul(
                        pb5[0:128, qloc * 128:qloc * 128 + 128],
                        self.hsel32[0:2, :],
                        rrT[0:2, qloc * 128:qloc * 128 + 128],
                        start=True, stop=True),
                        reads=[("c_hs",), (name, "rrT")], writes=[pk5])
                qw = QBl * 128
                em.op("vector", lambda e, qw=qw: e.tensor_copy(
                    rb_sb[:, 0:qw], pb5[0:128, 0:qw]),
                    reads=[pk5], writes=[(name, "rb")])
                em.op("vector", lambda e, qb=qb, g=g, qw=qw: e.tensor_mul(
                    out_sb[:, (g - g_base) * T + qb * qw:
                           (g - g_base) * T + qb * qw + qw],
                    psO[0:128, 0:qw], rb_sb[:, 0:qw]),
                    reads=[kO, (name, "rb")], writes=[kOUT])

    def allgather(self, in_h, out_h, ikey, okey):
        RG = [[i, i + 1] for i in range(0, self.cfg.ncores, 2)]
        self.em.op("gpsimd", lambda e: e.collective_compute(
            "AllGather", ALU.bypass, replica_groups=RG,
            ins=[in_h.ap().opt()], outs=[out_h.ap().opt()]),
            reads=[ikey], writes=[okey], dom="cc")

    def alltoall(self, in_h, out_h, ikey, okey):
        RG = [[i, i + 1] for i in range(0, self.cfg.ncores, 2)]
        self.em.op("gpsimd", lambda e: e.collective_compute(
            "AllToAll", ALU.bypass, replica_groups=RG,
            ins=[in_h.ap().opt()], outs=[out_h.ap().opt()]),
            reads=[ikey], writes=[okey], dom="cc")

    def wstream_bufs(self, maxnk, fgw=512, npar=2, with_stage=True):
        kp = self.fresh("wsb")
        wsl = [[self.sb(self.fresh("wt"), [128, fgw], BF16) for _ in range(maxnk)]
               for _ in range(npar)]
        if npar == 1:
            wsl.append(wsl[0])
        return {
            "stage": ([self.sb(self.fresh("wstg"), [128, fgw], F32) for _ in range(2)]
                      if with_stage else None),
            "wsl": wsl,
            "kp": kp, "fgw": fgw, "npar": npar,
        }

    def make_shipper(self, ag_h, row_map, name, agkey, tsubs_total=None,
                     bufs=None):
        c = self.cfg
        em = self.em
        T = c.T
        if bufs is None:
            ship = [self.sb(self.fresh("ship"), [128, T], BF16) for _ in range(2)]
            kname = name
        else:
            ship, kname = bufs
        state = {"n": 0}
        ntsub = tsubs_total if tsubs_total else c.NTS

        def evac(fabs, fcw, t0, tw, pb, pk):
            i = state["n"] // ntsub
            pp = i % 2
            em.op("vector", lambda e: e.tensor_copy(ship[pp][0:fcw, t0:t0 + tw],
                                                    pb[0:fcw, 0:tw]),
                  reads=[pk], writes=[(kname, "ship", pp)])
            state["n"] += 1
            if state["n"] % ntsub == 0:
                r0 = row_map(fabs)
                em.op("gpsimd", lambda e, r0=r0, fcw=fcw, pp=pp: e.dma_start(
                    out=ag_h.ap()[r0:r0 + fcw, :],
                    in_=ship[pp][0:fcw, :]),
                    reads=[(kname, "ship", pp)], writes=[agkey], dom="qg")
        return evac


def block_prologue(mk: MK, io, l1_variant=False, need_var=True, xqb_buf=None):
    c = mk.cfg
    em = mk.em
    T = c.T
    NCH = c.NCH
    x = mk.sb("x_stream", [128, c.NCH * T], F32)
    xkeyf = lambda k: ("x", k)
    for k in range(c.NCH):
        em.op("sync", lambda e, k=k: e.dma_start(
            out=x[:, k * T:(k + 1) * T], in_=io["x_in"].ap()[k * 128:(k + 1) * 128, :]),
            writes=[xkeyf(k)], dom="qs")
    mk.setup_meta(io["meta"].ap())
    xnb = mk.sb("xnb", [128, c.NCH * T], BF16)
    xnkeyf = lambda k: ("xnb", k)
    var_keep = mk.sb("var_keep", [1, T], F32) if need_var else None
    mu_keep = mk.sb("mu_keep", [1, T], F32) if need_var else None
    out = {"x": x, "xkeyf": xkeyf, "xnb": xnb, "xnkeyf": xnkeyf,
           "var_row": (var_keep, ("var_keep",)),
           "mu_row": (mu_keep, ("mu_keep",))}
    if l1_variant:
        xqb = xqb_buf if xqb_buf is not None else mk.sb("xqb", [128, c.NCH * T], BF16)
        out["xqb"] = xqb
        out["xqkeyf"] = lambda k: ("xqb", k)
    with mk.scope():
        PW = min(512, T)
        sqp = [mk.sb(mk.fresh("plsq"), [128, PW], F32) for _ in range(2)]
        prow = mk.sb("pl_row", [1, 4 * PW], F32)
        mub = mk.sb("pl_mub", [128, PW], F32)
        rbt = mk.sb("pl_rbt", [128, PW], F32)
        Rbt = mk.sb("pl_Rbt", [128, PW], F32) if l1_variant else None
        tmpt = [mk.sb(mk.fresh("pltm"), [128, PW], F32) for _ in range(2)]
        c_eps1 = mk.constf(1e-5, 1)
        c_mh1 = mk.constf(-0.5, 1)
        Dv = float(c.D)
        eps = 1e-5
        for (t0, tw) in [(i * PW, PW) for i in range(T // PW)]:
            pbS, pkS = mk.pbank()
            pbQ, pkQ = mk.pbank()
            for k in range(NCH):
                pp = k % 2
                kq = ("pl_sq", pp)
                em.op("scalar", lambda e, k=k, t0=t0, tw=tw, pp=pp:
                      e.activation(sqp[pp][:, 0:tw],
                                   x[:, k * T + t0:k * T + t0 + tw],
                                   AF.Square),
                      reads=[xkeyf(k)], writes=[kq])
                em.op("tensor", lambda e, k=k, t0=t0, tw=tw, pbS=pbS:
                      e.matmul(pbS[0:1, 0:tw], mk.ones_col[:, 0:1],
                               x[:, k * T + t0:k * T + t0 + tw],
                               start=(k == 0), stop=(k == NCH - 1)),
                      reads=[("c_oc",), xkeyf(k)], writes=[pkS])
                em.op("tensor", lambda e, k=k, tw=tw, pp=pp, pbQ=pbQ:
                      e.matmul(pbQ[0:1, 0:tw], mk.ones_col[:, 0:1],
                               sqp[pp][:, 0:tw],
                               start=(k == 0), stop=(k == NCH - 1)),
                      reads=[("c_oc",), kq], writes=[pkQ])
            em.op("vector", lambda e, tw=tw, pbS=pbS: e.tensor_scalar_mul(
                prow[0:1, 0:tw], pbS[0:1, 0:tw], 1.0 / Dv),
                reads=[pkS], writes=[("pl_row",)])
            em.op("vector", lambda e, tw=tw, pbQ=pbQ: e.tensor_scalar_mul(
                prow[0:1, PW:PW + tw], pbQ[0:1, 0:tw], 1.0 / Dv),
                reads=[pkQ], writes=[("pl_row",)])
            em.op("vector", lambda e, tw=tw: e.scalar_tensor_tensor(
                prow[0:1, 2 * PW:2 * PW + tw], prow[0:1, 0:tw], -1.0,
                prow[0:1, 0:tw], ALU.mult, ALU.mult),
                reads=[("pl_row",)], writes=[("pl_row",)])
            em.op("vector", lambda e, tw=tw: e.tensor_add(
                prow[0:1, 2 * PW:2 * PW + tw],
                prow[0:1, 2 * PW:2 * PW + tw], prow[0:1, PW:PW + tw]),
                reads=[("pl_row",)], writes=[("pl_row",)])
            if need_var:
                em.op("vector", lambda e, t0=t0, tw=tw: e.tensor_copy(
                    var_keep[0:1, t0:t0 + tw], prow[0:1, 2 * PW:2 * PW + tw]),
                    reads=[("pl_row",)], writes=[("var_keep",)])
                em.op("vector", lambda e, t0=t0, tw=tw: e.tensor_copy(
                    mu_keep[0:1, t0:t0 + tw], prow[0:1, 0:tw]),
                    reads=[("pl_row",)], writes=[("mu_keep",)])
            if l1_variant:
                em.op("vector", lambda e, tw=tw: e.tensor_single_scalar(
                    prow[0:1, 3 * PW:3 * PW + tw],
                    prow[0:1, 2 * PW:2 * PW + tw], eps, ALU.add),
                    reads=[("pl_row",)], writes=[("pl_row",)])
                em.op("vector", lambda e, tw=tw: e.reciprocal(
                    prow[0:1, 3 * PW:3 * PW + tw],
                    prow[0:1, 3 * PW:3 * PW + tw]),
                    reads=[("pl_row",)], writes=[("pl_row",)])
                em.op("vector", lambda e, tw=tw: e.tensor_mul(
                    prow[0:1, 3 * PW:3 * PW + tw],
                    prow[0:1, 3 * PW:3 * PW + tw],
                    prow[0:1, 2 * PW:2 * PW + tw]),
                    reads=[("pl_row",)], writes=[("pl_row",)])
            em.op("scalar", lambda e, tw=tw: e.activation(
                prow[0:1, 2 * PW:2 * PW + tw],
                prow[0:1, 2 * PW:2 * PW + tw], AF.Ln, bias=c_eps1),
                reads=[("pl_row",), ("c_cp",)], writes=[("pl_row",)])
            em.op("scalar", lambda e, tw=tw: e.activation(
                prow[0:1, 2 * PW:2 * PW + tw],
                prow[0:1, 2 * PW:2 * PW + tw], AF.Exp, scale=c_mh1),
                reads=[("pl_row",), ("c_cp",)], writes=[("pl_row",)])
            mk.bcast_row(prow, tw, ("pl_row",), mub, ("pl_mub",), col_off=0)
            mk.bcast_row(prow, tw, ("pl_row",), rbt, ("pl_rbt",),
                         col_off=2 * PW)
            if l1_variant:
                em.op("scalar", lambda e, tw=tw: e.activation(
                    prow[0:1, 3 * PW:3 * PW + tw],
                    prow[0:1, 3 * PW:3 * PW + tw], AF.Ln, bias=c_eps1),
                    reads=[("pl_row",), ("c_cp",)], writes=[("pl_row",)])
                em.op("scalar", lambda e, tw=tw: e.activation(
                    prow[0:1, 3 * PW:3 * PW + tw],
                    prow[0:1, 3 * PW:3 * PW + tw], AF.Exp, scale=c_mh1),
                    reads=[("pl_row",), ("c_cp",)], writes=[("pl_row",)])
                em.op("vector", lambda e, tw=tw: e.tensor_mul(
                    prow[0:1, 3 * PW:3 * PW + tw],
                    prow[0:1, 3 * PW:3 * PW + tw],
                    prow[0:1, 2 * PW:2 * PW + tw]),
                    reads=[("pl_row",)], writes=[("pl_row",)])
                mk.bcast_row(prow, tw, ("pl_row",), Rbt, ("pl_Rbt",),
                             col_off=3 * PW)
            for k in range(NCH):
                pp = k % 2
                tk = ("pl_tm", pp)
                em.op("vector", lambda e, k=k, t0=t0, tw=tw, pp=pp:
                      e.tensor_sub(tmpt[pp][:, 0:tw],
                                   x[:, k * T + t0:k * T + t0 + tw],
                                   mub[:, 0:tw]),
                      reads=[xkeyf(k), ("pl_mub",)], writes=[tk])
                em.op("gpsimd", lambda e, k=k, t0=t0, tw=tw, pp=pp:
                      e.tensor_mul(xnb[:, k * T + t0:k * T + t0 + tw],
                                   tmpt[pp][:, 0:tw], rbt[:, 0:tw]),
                      reads=[tk, ("pl_rbt",)], writes=[xnkeyf(k)])
                if l1_variant:
                    em.op("vector", lambda e, k=k, t0=t0, tw=tw, pp=pp:
                          e.tensor_mul(
                              out["xqb"][:, k * T + t0:k * T + t0 + tw],
                              tmpt[pp][:, 0:tw], Rbt[:, 0:tw]),
                          reads=[tk, ("pl_Rbt",)], writes=[out["xqkeyf"](k)])
    return out


def block_epilogue(mk: MK, pro, io):
    c = mk.cfg
    em = mk.em
    T = c.T
    x = pro["x"]
    for k in range(c.NCH):
        em.op("gpsimd", lambda e, k=k: e.dma_start(
            out=io["x_out"].ap()[k * 128:(k + 1) * 128, :], in_=x[:, k * T:(k + 1) * T]),
            reads=[pro["xkeyf"](k)], writes=[("x_out",)], dom="qg")
    em.finalize_output_wait("sync")


def resid_evac(mk, pro, tag):
    em = mk.em
    T = mk.cfg.T
    x = pro["x"]
    xkeyf = pro["xkeyf"]

    def evac(fabs, fcw, t0, tw, pb, pk):
        k = fabs // 128
        em.op("vector", lambda e: e.tensor_add(
            x[0:fcw, k * T + t0:k * T + t0 + tw],
            x[0:fcw, k * T + t0:k * T + t0 + tw], pb[0:fcw, 0:tw]),
            reads=[xkeyf(k), pk], writes=[xkeyf(k)])
    return evac


def emit_block0(mk: MK, io):
    c = mk.cfg
    em = mk.em
    T, D, S = c.T, c.D, c.S
    pro = block_prologue(mk, io, need_var=False)
    xnb, xnkeyf = pro["xnb"], pro["xnkeyf"]
    ag_in = mk.dram("ag_in", [3 * D, T], BF16)
    ag_out = mk.dram("ag_out", [2 * 3 * D, T], BF16)
    ag2_in = mk.dram("ag2_in", [c.NGC * 128, S], BF16)
    ag2_out = mk.dram("ag2_out", [2 * c.NGC * 128, S], BF16)
    with mk.scope():
        wbA = mk.wstream_bufs(2 * c.NCH)
        shipbufs = ([mk.sb(mk.fresh("ship"), [128, T], BF16) for _ in range(2)],
                    "shp_sh")
        for sec, (wname, bname) in enumerate((("Wq", "bq"), ("Wk", "bk"), ("Wv", "bv"))):
            rmap = (lambda fabs, sec=sec: sec * D + fabs)
            ship = mk.make_shipper(ag_in, rmap, f"shp{sec}", ("ag_in",),
                                   bufs=shipbufs)
            mk.proj(f"a0{wname}", io[wname].ap(), D, D, xnb, xnkeyf, ship,
                    bias_ap=io[bname].ap(), wbufs=wbA)
        mk.allgather(ag_in, ag_out, ("ag_in",), ("ag_out",))
    with mk.scope():
        maskA = mk.sb("maskA_sb", [128, 128], F32)
        em.op("sync", lambda e: e.dma_start(out=maskA[:, :], in_=io["maskA"].ap()),
              writes=[("maskA",)], dom="qs")
        mk.attn_headshard("a0", ag_out, ag2_in, mode="softmax", scale=1.0 / 8.0,
                          maskA16=(maskA, ("maskA",)))
        mk.allgather(ag2_in, ag2_out, ("a0", "ag2_in"), ("ag2_out",))
    wb_qkv = mk.wstream_bufs(2 * c.NCH)
    with mk.scope():
        aob = mk.sb("aob", [128, c.NCH * T], BF16)
        aokeyf = lambda k: ("aob", k)
        dst4 = aob[:, :].rearrange("p (k t) -> p k t", k=c.NCH)
        mk.dyn_load("sync", dst4, ag2_out, 0,
                    [[S, 128], [128 * S, c.NCH], [1, T]],
                    [("ag2_out",)], [aokeyf(k) for k in range(c.NCH)], base="tb")
        mk.proj("a0Wo", io["Wo"].ap(), D, D, aob, aokeyf, resid_evac(mk, pro, "wo"),
                bias_ap=io["bo"].ap(), wbufs=wb_qkv)
    with mk.scope():
        emit_l0ffn(mk, pro, io, wb_qkv)
    block_epilogue(mk, pro, io)


def emit_l0ffn(mk: MK, pro, io, wbufs):
    c = mk.cfg
    em = mk.em
    T, D = c.T, c.D
    E = 8
    HF = 2 * D
    HCH = HF // 128
    xnb, xnkeyf = pro["xnb"], pro["xnkeyf"]
    x, xkeyf = pro["x"], pro["xkeyf"]
    wg16 = []
    wg_stage = mk.sb("wg_stg", [128, E], F32)
    for k in range(c.NCH):
        s16 = mk.sb(f"wg16_{k}", [128, E], BF16)
        em.op("sync", lambda e, k=k: e.dma_start(
            out=wg_stage[:, :], in_=io["Wg"].ap()[k * 128:(k + 1) * 128, :]),
            writes=[("wg_stg",)], dom="qs")
        em.op("vector", lambda e, s16=s16: e.tensor_copy(s16[:, :], wg_stage[:, :]),
              reads=[("wg_stg",)], writes=[("wg16", k)])
        wg16.append(s16)
    wT = mk.sb("l0g_wT", [E, T], F32)
    lg = mk.sb("l0g_lg", [128, E], F32)
    m1 = mk.sb("l0g_m1", [128, 1], F32)
    eq1 = mk.sb("l0g_eq1", [128, E], F32)
    msk = mk.sb("l0g_msk", [128, E], F32)
    m2 = mk.sb("l0g_m2", [128, 1], F32)
    eq2 = mk.sb("l0g_eq2", [128, E], F32)
    d2 = mk.sb("l0g_d2", [128, 1], F32)
    den = mk.sb("l0g_den", [128, 1], F32)
    p2 = mk.sb("l0g_p2", [128, 1], F32)
    wtok = mk.sb("l0g_w", [128, E], F32)
    for tt in range(T // 128):
        t0 = tt * 128
        pb, pk = mk.pbank()
        for k in range(c.NCH):
            em.op("tensor", lambda e, k=k, t0=t0, pb=pb: e.matmul(
                pb[0:128, 0:E], xnb[:, k * T + t0:k * T + t0 + 128], wg16[k][:, :],
                start=(k == 0), stop=(k == c.NCH - 1)),
                reads=[xnkeyf(k), ("wg16", k)], writes=[pk], sig=(k == c.NCH - 1))
        em.op("vector", lambda e, pb=pb: e.tensor_copy(lg[:, :], pb[0:128, 0:E]),
              reads=[pk], writes=[("lg",)])
        em.op("vector", lambda e: e.tensor_reduce(m1[:, 0:1], lg[:, :], AX.X, ALU.max),
              reads=[("lg",)], writes=[("m1",)])
        em.op("vector", lambda e: e.tensor_scalar(
            eq1[:, :], lg[:, :], m1[:, 0:1], None, ALU.is_equal),
            reads=[("lg",), ("m1",)], writes=[("eq1",)])
        em.op("vector", lambda e: e.scalar_tensor_tensor(
            msk[:, :], eq1[:, :], -1e30, lg[:, :], ALU.mult, ALU.add),
            reads=[("eq1",), ("lg",)], writes=[("msk",)])
        em.op("vector", lambda e: e.tensor_reduce(m2[:, 0:1], msk[:, :], AX.X, ALU.max),
              reads=[("msk",)], writes=[("m2",)])
        em.op("vector", lambda e: e.tensor_scalar(
            eq2[:, :], msk[:, :], m2[:, 0:1], None, ALU.is_equal),
            reads=[("msk",), ("m2",)], writes=[("eq2",)])
        em.op("vector", lambda e: e.tensor_sub(d2[:, 0:1], m2[:, 0:1], m1[:, 0:1]),
              reads=[("m2",), ("m1",)], writes=[("d2",)])
        em.op("scalar", lambda e: e.activation(d2[:, 0:1], d2[:, 0:1], AF.Exp),
              reads=[("d2",)], writes=[("d2",)])
        em.op("vector", lambda e: e.tensor_single_scalar(
            den[:, 0:1], d2[:, 0:1], 1.0, ALU.add), reads=[("d2",)], writes=[("den",)])
        em.op("vector", lambda e: e.reciprocal(den[:, 0:1], den[:, 0:1]),
              reads=[("den",)], writes=[("den",)])
        em.op("vector", lambda e: e.tensor_mul(p2[:, 0:1], d2[:, 0:1], den[:, 0:1]),
              reads=[("d2",), ("den",)], writes=[("p2",)])
        em.op("vector", lambda e: e.tensor_scalar(
            wtok[:, :], eq1[:, :], den[:, 0:1], None, ALU.mult),
            reads=[("eq1",), ("den",)], writes=[("wtok",)])
        em.op("vector", lambda e: e.scalar_tensor_tensor(
            wtok[:, :], eq2[:, :], p2[:, 0:1], wtok[:, :], ALU.mult, ALU.add),
            reads=[("eq2",), ("p2",), ("wtok",)], writes=[("wtok",)])
        pb2, pk2 = mk.pbank()
        em.op("tensor", lambda e, pb2=pb2: e.transpose(
            pb2[0:E, 0:128], wtok[:, 0:E], mk.ident32[:, :]),
            reads=[("wtok",), ("c_id32",)], writes=[pk2])
        em.op("vector", lambda e, t0=t0, pb2=pb2: e.tensor_copy(
            wT[0:E, t0:t0 + 128], pb2[0:E, 0:128]), reads=[pk2], writes=[("wT",)])
    T2 = T // 2
    NT2 = max(1, T2 // c.TSUB)
    us16 = [mk.sb(f"l0f_us{i}", [128, T2], BF16) for i in range(2)]
    h16 = mk.sb("l0f_h", [128, HCH * T2], BF16)
    hkeyf = lambda k: ("h16", k)
    wbe = mk.sb("l0f_wb", [128, T2], F32)
    for ts2 in range(2):
        tbase = ts2 * T2
        tsubs = [(tbase + i * c.TSUB, min(c.TSUB, T2)) for i in range(NT2)]
        for e_ in range(E):
            mk.bcast_selrow(wT, E, e_, T2, ("wT",), wbe, ("wbe",), col_off=tbase)

            def evac_uv(fabs, fcw, t0, tw, pb, pk, tbase=tbase):
                fc2 = fabs // 128
                kk = fc2 // 2
                pp = kk % 2
                lo = kk * T2 + t0 - tbase
                if fc2 % 2 == 0:
                    em.op("scalar", lambda e: e.activation(
                        us16[pp][0:fcw, t0 - tbase:t0 - tbase + tw],
                        pb[0:fcw, 0:tw], AF.Silu),
                        reads=[pk], writes=[("us16", pp)])
                else:
                    em.op("vector", lambda e: e.tensor_mul(
                        h16[0:fcw, lo:lo + tw],
                        us16[pp][0:fcw, t0 - tbase:t0 - tbase + tw],
                        pb[0:fcw, 0:tw]),
                        reads=[("us16", pp), pk], writes=[("h16a", kk)])
                    em.op("gpsimd", lambda e: e.tensor_mul(
                        h16[0:fcw, lo:lo + tw], h16[0:fcw, lo:lo + tw],
                        wbe[0:fcw, t0 - tbase:t0 - tbase + tw]),
                        reads=[("h16a", kk), ("wbe",)], writes=[hkeyf(kk)])

            mk.proj(f"w12_{ts2}_{e_}", io["W12r"].ap()[e_ * D:(e_ + 1) * D, :], D,
                    2 * HF, xnb, xnkeyf, evac_uv, wbufs=wbufs, tsubs=tsubs)

            def evac_y(fabs, fcw, t0, tw, pb, pk):
                kk = fabs // 128
                em.op("vector", lambda e: e.tensor_add(
                    x[0:fcw, kk * T + t0:kk * T + t0 + tw],
                    x[0:fcw, kk * T + t0:kk * T + t0 + tw], pb[0:fcw, 0:tw]),
                    reads=[xkeyf(kk), pk], writes=[xkeyf(kk)])

            mk.proj(f"w3_{ts2}_{e_}", io["W3r"].ap()[e_ * HF:(e_ + 1) * HF, :], HF, D,
                    h16, hkeyf, evac_y, wbufs=wbufs, tsubs=tsubs,
                    xoff_fn=lambda k, t0, tbase=tbase: k * T2 + t0 - tbase)


BLOCK_INPUTS = {
    0: lambda cfg: [(n, [cfg.D, cfg.D]) for n in ("Wq", "Wk", "Wv", "Wo")]
    + [(n, [1, cfg.D]) for n in ("bq", "bk", "bv", "bo")]
    + [("Wg", [cfg.D, 8]), ("W12r", [8 * cfg.D, 4 * cfg.D]),
       ("W3r", [8 * 2 * cfg.D, cfg.D]), ("maskA", [128, 128])],
}

BLOCK_EMITTERS = {}


def build_block(cfg, blk):
    nc = bass.Bass("TRN2", target_bir_lowering=False)
    io = {}
    D, T = cfg.D, cfg.T
    io["x_in"] = nc.dram_tensor("x_in", [D, T], F32, kind="ExternalInput")
    io["x_out"] = nc.dram_tensor("x_out", [D, T], F32, kind="ExternalOutput")
    io["meta"] = nc.dram_tensor("meta", [1, 4], I32, kind="ExternalInput")
    for ent in BLOCK_INPUTS[blk](cfg):
        n, sh = ent[0], ent[1]
        dt = ent[2] if len(ent) > 2 else F32
        io[n] = nc.dram_tensor(n, sh, dt, kind="ExternalInput")
    with ExitStack() as st:
        em = EM(nc)
        mk = MK(cfg, nc, em, st)
        mk.setup()
        BLOCK_EMITTERS[blk](mk, io)
        em.realize(st)
    nc._em_debug = em
    return nc


BLOCK_EMITTERS[0] = emit_block0

BLOCK_INPUTS[1] = lambda cfg: [
    ("ln_g", [1, cfg.D]), ("ln_b", [1, cfg.D]), ("w_qkv", [cfg.D, 3 * cfg.D]),
    ("w_out", [cfg.D, cfg.D]), ("b_out", [1, cfg.D]),
    ("gate_wT", [cfg.D, 8]), ("gate_b", [1, 8]),
    ("wiT", [cfg.D, 4 * cfg.D]), ("bi1", [1, 4 * cfg.D]),
    ("woT", [4 * cfg.D, cfg.D]), ("bo1", [1, cfg.D]),
    ("loraAT", [8 * cfg.D, 16]), ("loraBTp", [3 * 128, 4 * cfg.D]),
    ("rope_cos", [128, cfg.T], BF16), ("rope_sin", [128, cfg.T], BF16),
    ("R1", [128, 128], BF16),
]

BLOCK_PERCORE = {}


def _rope1_tables(cfg, core):
    T, HD = cfg.T, cfg.HD
    half = HD // 2
    t0 = (core % 2) * T
    pos = np.arange(t0, t0 + T, dtype=np.float64)
    inv = 1.0 / 10000.0 ** (np.arange(0, HD, 2, dtype=np.float64) / HD)
    ang = pos[:, None] * inv[None, :]
    s = np.sin(ang)
    co = np.cos(ang)
    dd = np.arange(128) % HD
    C = co[:, dd % half].T.astype(np.float32)
    Sn = s[:, dd % half].T.astype(np.float32)
    return np.ascontiguousarray(C), np.ascontiguousarray(Sn)


def _rope1_R(cfg):
    HD = cfg.HD
    half = HD // 2
    M = np.zeros((HD, HD), np.float32)
    for d in range(half):
        M[d, d + half] = -1.0
        M[d + half, d] = 1.0
    Mt = M.T
    return np.ascontiguousarray(np.kron(np.eye(128 // HD, dtype=np.float32), Mt))


def _percore_block1(cfg, core, inputs=None):
    import ml_dtypes
    C, Sn = _rope1_tables(cfg, core)
    bf = ml_dtypes.bfloat16
    return {"rope_cos": C.astype(bf), "rope_sin": Sn.astype(bf),
            "R1": _rope1_R(cfg).astype(bf)}


BLOCK_PERCORE[1] = _percore_block1

BLOCK_INPUTS[2] = lambda cfg: [
    (n, [cfg.D, cfg.D]) for n in ("Wq2", "Wk2", "Wv2")] + [
    (n, [1, cfg.D]) for n in ("bq2", "bk2", "bv2")] + [
    ("maskA", [128, 128]),
    ("rope2_cos", [128, cfg.T], BF16), ("rope2_sin", [128, cfg.T], BF16),
    ("R2", [128, 128], BF16),
    ("sim", [cfg.D, 16]), ("gates", [1, 16]),
    ("qpf", [cfg.D, 1024]), ("kpf", [cfg.D, 1024]), ("vpf", [cfg.D, 1024]),
    ("opR", [1024, cfg.D]),
]


def _rope2_tables(cfg, core):
    T, HD = cfg.T, cfg.HD
    t0 = (core % 2) * T
    pos = np.arange(t0, t0 + T, dtype=np.float64)
    inv = 10000.0 ** (-2.0 * np.arange(HD // 2, dtype=np.float64) / HD)
    ang = pos[:, None] * inv[None, :]
    cosr = np.repeat(np.cos(ang), 2, axis=1)
    sinr = np.repeat(np.sin(ang), 2, axis=1)
    dd = np.arange(128) % HD
    C = cosr[:, dd].T.astype(np.float32)
    Sn = sinr[:, dd].T.astype(np.float32)
    return np.ascontiguousarray(C), np.ascontiguousarray(Sn)


def _rope2_R():
    blk = np.array([[0.0, 1.0], [-1.0, 0.0]], np.float32)
    return np.ascontiguousarray(np.kron(np.eye(64, dtype=np.float32), blk))


def _percore_block2(cfg, core, inputs=None):
    import ml_dtypes
    bf = ml_dtypes.bfloat16
    C, Sn = _rope2_tables(cfg, core)
    return {"rope2_cos": C.astype(bf), "rope2_sin": Sn.astype(bf),
            "R2": _rope2_R().astype(bf)}


BLOCK_PERCORE[2] = _percore_block2


def _interleave_w12(W1, W2):
    E, D, HF = W1.shape
    out = np.empty((E * D, 2 * HF), np.float32)
    for e in range(E):
        a = W1[e].reshape(D, HF // 128, 128)
        b = W2[e].reshape(D, HF // 128, 128)
        ab = np.stack([a, b], axis=2).reshape(D, 2 * HF)
        out[e * D:(e + 1) * D] = ab
    return np.ascontiguousarray(out)


def _mask_diag():
    p = np.arange(128)[:, None]
    f = np.arange(128)[None, :]
    return np.where(f <= p, 0.0, -1e9).astype(np.float32)


def _meta_np(cfg, core):
    parity = core % 2
    return np.array([[parity * (cfg.HC * cfg.HD), parity * cfg.T, parity, 0]],
                    dtype=np.int32)


def _prep_block_weights(cfg, blk, inputs):
    f = np.float32
    if blk == 0:
        D = cfg.D
        return {
            "Wq": np.ascontiguousarray(inputs["L0_attn_Wq"], f),
            "Wk": np.ascontiguousarray(inputs["L0_attn_Wk"], f),
            "Wv": np.ascontiguousarray(inputs["L0_attn_Wv"], f),
            "Wo": np.ascontiguousarray(inputs["L0_attn_Wo"], f),
            "bq": np.ascontiguousarray(inputs["L0_attn_bq"], f).reshape(1, D),
            "bk": np.ascontiguousarray(inputs["L0_attn_bk"], f).reshape(1, D),
            "bv": np.ascontiguousarray(inputs["L0_attn_bv"], f).reshape(1, D),
            "bo": np.ascontiguousarray(inputs["L0_attn_bo"], f).reshape(1, D),
            "Wg": np.ascontiguousarray(inputs["L0_ffn_Wg"], f),
            "W12r": _interleave_w12(np.asarray(inputs["L0_ffn_W1"], f),
                                    np.asarray(inputs["L0_ffn_W2"], f)),
            "W3r": np.ascontiguousarray(inputs["L0_ffn_W3"], f).reshape(8 * 2 * D, D),
            "maskA": _mask_diag(),
        }
    if blk == 2:
        D = cfg.D
        qp = np.asarray(inputs["L2_ffn_q_proj"], f)
        kp = np.asarray(inputs["L2_ffn_k_proj"], f)
        vp = np.asarray(inputs["L2_ffn_v_proj"], f)
        op = np.asarray(inputs["L2_ffn_o_proj"], f)
        return {
            "Wq2": np.ascontiguousarray(inputs["L2_attn_Wq"], f),
            "Wk2": np.ascontiguousarray(inputs["L2_attn_Wk"], f),
            "Wv2": np.ascontiguousarray(inputs["L2_attn_Wv"], f),
            "bq2": np.ascontiguousarray(inputs["L2_attn_bq"], f).reshape(1, D),
            "bk2": np.ascontiguousarray(inputs["L2_attn_bk"], f).reshape(1, D),
            "bv2": np.ascontiguousarray(inputs["L2_attn_bv"], f).reshape(1, D),
            "maskA": _mask_diag(),
            "sim": np.ascontiguousarray(inputs["L2_ffn_sim_matrix"], f),
            "gates": np.ascontiguousarray(inputs["L2_ffn_gates"], f).reshape(1, 16),
            "qpf": np.ascontiguousarray(qp.transpose(1, 0, 2).reshape(D, 1024)),
            "kpf": np.ascontiguousarray(kp.transpose(1, 0, 2).reshape(D, 1024)),
            "vpf": np.ascontiguousarray(vp.transpose(1, 0, 2).reshape(D, 1024)),
            "opR": np.ascontiguousarray(op.reshape(1024, D)),
        }
    if blk == 1:
        D = cfg.D
        E, RK = 8, 16
        A = np.asarray(inputs["L1_ffn_lora_A"], f)
        B = np.asarray(inputs["L1_ffn_lora_B"], f)
        DF = B.shape[1]
        lbtp = np.zeros((3 * 128, DF), f)
        for e in range(E):
            t, j = e // 3, e % 3
            lbtp[t * 128 + 32 * j:t * 128 + 32 * j + RK, :] = B[e].T
        return {
            "ln_g": np.ascontiguousarray(inputs["L1_attn_ln_gamma"], f).reshape(1, D),
            "ln_b": np.ascontiguousarray(inputs["L1_attn_ln_beta"], f).reshape(1, D),
            "w_qkv": np.ascontiguousarray(inputs["L1_attn_w_qkv"], f),
            "w_out": np.ascontiguousarray(inputs["L1_attn_w_out"], f),
            "b_out": np.ascontiguousarray(inputs["L1_attn_b_out"], f).reshape(1, D),
            "gate_wT": np.ascontiguousarray(np.asarray(inputs["L1_ffn_gate_w"], f).T),
            "gate_b": np.ascontiguousarray(inputs["L1_ffn_gate_b"], f).reshape(1, E),
            "wiT": np.ascontiguousarray(np.asarray(inputs["L1_ffn_wi"], f).T),
            "bi1": np.ascontiguousarray(inputs["L1_ffn_bi"], f).reshape(1, DF),
            "woT": np.ascontiguousarray(np.asarray(inputs["L1_ffn_wo"], f).T),
            "bo1": np.ascontiguousarray(inputs["L1_ffn_bo"], f).reshape(1, D),
            "loraAT": np.ascontiguousarray(
                A.transpose(0, 2, 1).reshape(E * D, RK)),
            "loraBTp": lbtp,
        }
    if blk == 3:
        return _prep_block3(cfg, inputs)
    raise KeyError(blk)


N_BLOCKS = 3


def kernel(**inputs):
    cfg = Cfg()
    x = np.asarray(inputs["L0_attn_X"], dtype=np.float32)
    B = x.shape[0]
    shards = []
    for core in range(cfg.ncores):
        b, half = core // 2, core % 2
        shards.append(np.ascontiguousarray(x[b, half * cfg.T:(half + 1) * cfg.T, :].T))
    for blk in range(N_BLOCKS):
        nc = build_block(cfg, blk)
        wd = _prep_block_weights(cfg, blk, inputs)
        in_maps = []
        for core in range(cfg.ncores):
            m = {"x_in": shards[core], "meta": _meta_np(cfg, core)}
            m.update(wd)
            if blk in BLOCK_PERCORE:
                m.update(BLOCK_PERCORE[blk](cfg, core, inputs))
            in_maps.append(m)
        res = run_bass_kernel_spmd(nc, in_maps, core_ids=list(range(cfg.ncores)))
        shards = [np.ascontiguousarray(r["x_out"]) for r in res.results]
    out = np.empty((B, cfg.S, cfg.D), np.float32)
    for core in range(cfg.ncores):
        b, half = core // 2, core % 2
        out[b, half * cfg.T:(half + 1) * cfg.T, :] = shards[core].T
    return out


def emit_block1(mk: MK, io):
    c = mk.cfg
    em = mk.em
    T, D, S = c.T, c.D, c.S
    I = D
    NG = I // 128
    b1u = mk.sb("b1_u", [128, NG * T], BF16)
    pro = block_prologue(mk, io, l1_variant=True, need_var=False, xqb_buf=b1u)
    x, xkeyf = pro["x"], pro["xkeyf"]
    xnb, xnkeyf = pro["xnb"], pro["xnkeyf"]
    wbp = mk.wstream_bufs(8, npar=1)
    qdr = mk.dram("b1_qdr", [NG * 128, T], BF16)
    ag_in = mk.dram("b1ag_in", [2 * I, T], BF16)
    ag_out = mk.dram("b1ag_out", [2 * 2 * I, T], BF16)
    aout = b1u[:, 0:(NG // 2) * T]
    with mk.scope():
        xqb, xqkeyf = pro["xqb"], pro["xqkeyf"]
        grow = mk.row_load(io["ln_g"].ap(), D, ("b1g",))
        brow = mk.row_load(io["ln_b"].ap(), D, ("b1b",))
        gcol = mk.sb("b1_gcol", [128, 2 * c.NCH], F32)
        pb5 = mk.p512[5]
        pk5 = ("p512", 5)
        for k in range(c.NCH):
            for i_, (row, rk) in enumerate(((grow, ("b1g",)), (brow, ("b1b",)))):
                em.op("tensor", lambda e, k=k, row=row: e.transpose(
                    pb5[0:128, 0:1], row[0:1, k * 128:(k + 1) * 128],
                    mk.ident32[0:1, 0:1]),
                    reads=[rk, ("c_id32",)], writes=[pk5])
                em.op("vector", lambda e, k=k, i_=i_: e.tensor_copy(
                    gcol[:, 2 * k + i_:2 * k + i_ + 1], pb5[0:128, 0:1]),
                    reads=[pk5], writes=[("b1gcol",)])
        for k in range(c.NCH):
            em.op("vector", lambda e, k=k: e.tensor_scalar(
                xqb[:, k * T:(k + 1) * T], xqb[:, k * T:(k + 1) * T],
                gcol[:, 2 * k:2 * k + 1], gcol[:, 2 * k + 1:2 * k + 2],
                ALU.mult, ALU.add),
                reads=[xqkeyf(k), ("b1gcol",)], writes=[xqkeyf(k)])
        rcos = mk.sb("b1_rcos", [128, T], BF16)
        rsin = mk.sb("b1_rsin", [128, T], BF16)
        R1 = mk.sb("b1_R1", [128, 128], BF16)
        for nm, dst in (("rope_cos", rcos), ("rope_sin", rsin), ("R1", R1)):
            em.op("sync", lambda e, nm=nm, dst=dst: e.dma_start(
                out=dst[:, :], in_=io[nm].ap()), writes=[(nm,)], dom="qs")
        stg = [mk.sb(f"b1_stg{i}", [128, T], BF16) for i in range(2)]
        rop = mk.sb("b1_rop", [128, T], BF16)
        ship = mk.sb("b1_ship", [128, T], BF16)
        state = {"fc": -1}

        def finish_chunk():
            fc = state["fc"]
            if fc < 0:
                return
            gch = fc % NG
            sec = fc // NG
            pp = state["pp"]
            kSt = ("b1stg", pp)
            if sec <= 1:
                for t0 in range(0, T, 512):
                    tw = min(512, T - t0)
                    em.op("tensor", lambda e, t0=t0, tw=tw, pp=pp: e.matmul(
                        pb5[0:128, 0:tw], R1[:, :], stg[pp][:, t0:t0 + tw],
                        start=True, stop=True),
                        reads=[("R1",), kSt], writes=[pk5])
                    em.op("vector", lambda e, t0=t0, tw=tw, pp=pp: e.tensor_mul(
                        rop[:, t0:t0 + tw], pb5[0:128, 0:tw],
                        rsin[:, t0:t0 + tw]),
                        reads=[pk5, ("rope_sin",)], writes=[("b1rop",)])
                em.op("vector", lambda e, pp=pp: e.tensor_mul(
                    stg[pp][:, :], stg[pp][:, :], rcos[:, :]),
                    reads=[kSt, ("rope_cos",)], writes=[kSt])
                em.op("vector", lambda e, pp=pp: e.tensor_add(
                    ship[:, :], stg[pp][:, :], rop[:, :]),
                    reads=[kSt, ("b1rop",)], writes=[("b1ship",)])
                if sec == 0:
                    em.op("gpsimd", lambda e, gch=gch: e.dma_start(
                        out=qdr.ap()[gch * 128:(gch + 1) * 128, :],
                        in_=ship[:, :]),
                        reads=[("b1ship",)], writes=[("b1qdr",)], dom="qg")
                else:
                    em.op("gpsimd", lambda e, gch=gch: e.dma_start(
                        out=ag_in.ap()[gch * 128:(gch + 1) * 128, :],
                        in_=ship[:, :]),
                        reads=[("b1ship",)], writes=[("b1ag_in",)], dom="qg")
            else:
                em.op("gpsimd", lambda e, pp=pp, gch=gch: e.dma_start(
                    out=ag_in.ap()[I + gch * 128:I + (gch + 1) * 128, :],
                    in_=stg[pp][:, :]),
                    reads=[kSt], writes=[("b1ag_in",)], dom="qg")

        def evac_qkv(fabs, fcw, t0, tw, pb, pk):
            fc = fabs // 128
            newchunk = fc != state["fc"]
            prev = state["fc"]
            if newchunk:
                state["fc"] = fc
                state["pp"] = fc % 2
            pp = state["pp"]
            em.op("vector", lambda e, pp=pp, fcw=fcw, t0=t0, tw=tw, pb=pb:
                  e.tensor_copy(stg[pp][0:fcw, t0:t0 + tw], pb[0:fcw, 0:tw]),
                  reads=[pk], writes=[("b1stg", pp)])
            if newchunk and prev >= 0:
                cur = state["fc"], state["pp"]
                state["fc"], state["pp"] = prev, prev % 2
                finish_chunk()
                state["fc"], state["pp"] = cur

        mk.proj("b1qkv", io["w_qkv"].ap(), D, 3 * I, xqb, xqkeyf, evac_qkv,
                wbufs=wbp)
        finish_chunk()
        mk.allgather(ag_in, ag_out, ("b1ag_in",), ("ag_out", "b1a"))
    NGH = NG // 2
    for half in range(2):
        with mk.scope():
            qbh = b1u[:, NGH * T:NG * T]
            kQBH = ("b1qbh", half)
            for i in range(NGH):
                em.op("sync", lambda e, i=i, half=half: e.dma_start(
                    out=qbh[:, i * T:(i + 1) * T],
                    in_=qdr.ap()[(half * NGH + i) * 128:
                                 (half * NGH + i + 1) * 128, :]),
                    reads=[("b1qdr",)], writes=[kQBH], dom="qs")
            mk.attn_qshard(f"b1h{half}", qbh, kQBH, ag_out, aout, ("b1aout",),
                           scale=1.0 / 8.0, g_base=half * NGH, g_count=NGH,
                           ag_key=("ag_out", "b1a"), q_base=half * NGH)
        with mk.scope():
            aokeyf = lambda k: ("b1aout",)
            mk.proj(f"b1wo{half}", io["w_out"].ap()[half * NGH * 128:, :],
                    NGH * 128, D, aout, aokeyf,
                    resid_evac(mk, pro, "b1wo"),
                    bias_ap=(io["b_out"].ap() if half == 0 else None), wbufs=wbp)
    with mk.scope():
        emit_l1ffn(mk, pro, io, wbp)
    block_epilogue(mk, pro, io)




def emit_l1ffn(mk: MK, pro, io, wbufs):
    c = mk.cfg
    em = mk.em
    T, D = c.T, c.D
    E, RK, DF = 8, 16, 4 * c.D
    HCH = DF // 128
    NCH = c.NCH
    xnb, xnkeyf = pro["xnb"], pro["xnkeyf"]
    x, xkeyf = pro["x"], pro["xkeyf"]
    pb5 = mk.p512[5]
    pk5 = ("p512", 5)
    gw16 = []
    gw_stage = mk.sb("f1g_stg", [128, E], F32)
    for k in range(NCH):
        s16 = mk.sb(f"f1gw_{k}", [128, E], BF16)
        em.op("sync", lambda e, k=k: e.dma_start(
            out=gw_stage[:, :], in_=io["gate_wT"].ap()[k * 128:(k + 1) * 128, :]),
            writes=[("f1gstg",)], dom="qs")
        em.op("vector", lambda e, s16=s16: e.tensor_copy(s16[:, :], gw_stage[:, :]),
              reads=[("f1gstg",)], writes=[("f1gw", k)])
        gw16.append(s16)
    gbrow = mk.row_load(io["gate_b"].ap(), E, ("f1gb",))
    wT = mk.sb("f1_wT", [E, T], F32)
    vs2row = mk.sb("f1_vs2", [1, T], F32)
    vs2row16 = mk.sb("f1_vs2b", [1, T], BF16)
    lg = mk.sb("f1_lg", [128, E], F32)
    sc = mk.sb("f1_sc", [128, E], F32)
    m1 = mk.sb("f1_m1", [128, 1], F32)
    eq1 = mk.sb("f1_eq1", [128, E], F32)
    msk = mk.sb("f1_msk", [128, E], F32)
    m2 = mk.sb("f1_m2", [128, 1], F32)
    vs = mk.sb("f1_vs", [128, 1], F32)
    wtok = mk.sb("f1_w", [128, E], F32)
    for tt in range(T // 128):
        t0 = tt * 128
        pb, pk = mk.pbank()
        for k in range(NCH):
            em.op("tensor", lambda e, k=k, t0=t0, pb=pb: e.matmul(
                pb[0:128, 0:E], xnb[:, k * T + t0:k * T + t0 + 128], gw16[k][:, :],
                start=(k == 0), stop=False),
                reads=[xnkeyf(k), ("f1gw", k)], writes=[pk], sig=False)
        em.op("tensor", lambda e, pb=pb: e.matmul(
            pb[0:128, 0:E], mk.ones128_32[0:1, :], gbrow[0:1, 0:E],
            start=False, stop=True),
            reads=[("c_o128",), ("f1gb",)], writes=[pk])
        em.op("vector", lambda e, pb=pb: e.tensor_copy(lg[:, :], pb[0:128, 0:E]),
              reads=[pk], writes=[("lg1",)])
        em.op("vector", lambda e: e.tensor_reduce(m1[:, 0:1], lg[:, :], AX.X, ALU.max),
              reads=[("lg1",)], writes=[("m11",)])
        em.op("vector", lambda e: e.tensor_scalar_mul(m2[:, 0:1], m1[:, 0:1], -1.0),
              reads=[("m11",)], writes=[("m21",)])
        em.op("scalar", lambda e: e.activation(sc[:, :], lg[:, :], AF.Exp,
                                               bias=m2[:, 0:1]),
              reads=[("lg1",), ("m21",)], writes=[("sc1",)])
        em.op("vector", lambda e: e.tensor_reduce(m1[:, 0:1], sc[:, :], AX.X, ALU.add),
              reads=[("sc1",)], writes=[("m11",)])
        em.op("vector", lambda e: e.reciprocal(m1[:, 0:1], m1[:, 0:1]),
              reads=[("m11",)], writes=[("m11",)])
        em.op("vector", lambda e: e.tensor_scalar(
            sc[:, :], sc[:, :], m1[:, 0:1], None, ALU.mult),
            reads=[("sc1",), ("m11",)], writes=[("sc1",)])
        em.op("vector", lambda e: e.tensor_reduce(m1[:, 0:1], sc[:, :], AX.X, ALU.max),
              reads=[("sc1",)], writes=[("m11",)])
        em.op("vector", lambda e: e.tensor_scalar(
            eq1[:, :], sc[:, :], m1[:, 0:1], None, ALU.is_equal),
            reads=[("sc1",), ("m11",)], writes=[("eq11",)])
        em.op("vector", lambda e: e.scalar_tensor_tensor(
            msk[:, :], eq1[:, :], -1e30, sc[:, :], ALU.mult, ALU.add),
            reads=[("eq11",), ("sc1",)], writes=[("msk1",)])
        em.op("vector", lambda e: e.tensor_reduce(m2[:, 0:1], msk[:, :], AX.X, ALU.max),
              reads=[("msk1",)], writes=[("m21",)])
        em.op("vector", lambda e: e.tensor_add(vs[:, 0:1], m1[:, 0:1], m2[:, 0:1]),
              reads=[("m11",), ("m21",)], writes=[("vs1",)])
        em.op("vector", lambda e: e.tensor_scalar(
            wtok[:, :], msk[:, :], m2[:, 0:1], None, ALU.is_equal),
            reads=[("msk1",), ("m21",)], writes=[("w1",)])
        em.op("vector", lambda e: e.tensor_add(wtok[:, :], wtok[:, :], eq1[:, :]),
              reads=[("w1",), ("eq11",)], writes=[("w1",)])
        em.op("vector", lambda e: e.tensor_scalar(
            wtok[:, :], wtok[:, :], vs[:, 0:1], None, ALU.mult),
            reads=[("w1",), ("vs1",)], writes=[("w1",)])
        em.op("vector", lambda e: e.tensor_scalar_mul(vs[:, 0:1], vs[:, 0:1], 2.0),
              reads=[("vs1",)], writes=[("vs1",)])
        em.op("tensor", lambda e, pb=pb: e.transpose(
            pb5[0:E, 0:128], wtok[:, 0:E], mk.ident32[0:128, 0:128]),
            reads=[("w1",), ("c_id32",)], writes=[pk5])
        em.op("vector", lambda e, t0=t0: e.tensor_copy(
            wT[0:E, t0:t0 + 128], pb5[0:E, 0:128]), reads=[pk5], writes=[("f1wT",)])
        em.op("tensor", lambda e: e.transpose(
            pb5[0:1, 0:128], vs[:, 0:1], mk.ident32[0:128, 0:128]),
            reads=[("vs1",), ("c_id32",)], writes=[pk5])
        em.op("vector", lambda e, t0=t0: e.tensor_copy(
            vs2row[0:1, t0:t0 + 128], pb5[0:1, 0:128]),
            reads=[pk5], writes=[("f1vs2",)])
    em.op("vector", lambda e: e.tensor_copy(vs2row16[0:1, :], vs2row[0:1, :]),
          reads=[("f1vs2",)], writes=[("f1vs2b",)])
    lat_stage = mk.sb("f1lat_stg", [128, NCH * RK], F32)
    lat16 = mk.sb("f1lat", [128, NCH * E * RK], BF16)
    for e_ in range(E):
        src_ap = io["loraAT"].ap()[e_ * D:(e_ + 1) * D, :].rearrange(
            "(k p) r -> p k r", p=128)
        em.op("sync", lambda e, src_ap=src_ap: e.dma_start(
            out=lat_stage[:, :].rearrange("p (k r) -> p k r", r=RK), in_=src_ap),
            writes=[("f1latstg",)], dom="qs")
        for k in range(NCH):
            em.op("vector", lambda e, e_=e_, k=k: e.tensor_copy(
                lat16[:, (k * E + e_) * RK:(k * E + e_) * RK + RK],
                lat_stage[:, k * RK:(k + 1) * RK]),
                reads=[("f1latstg",)], writes=[("f1lat",)])
    NG1 = c.D // 128
    U_ok1 = NG1 * T >= 3 * DF + 3 * (T // 4) + T
    mid = [mk.sb(f"f1mid{t}", [128, T], BF16) for t in range(2)]
    if U_ok1:
        mid.append(pro["xqb"][:, 3 * DF + 3 * (T // 4):
                              3 * DF + 3 * (T // 4) + T])
    else:
        mid.append(mk.sb("f1mid2", [128, T], BF16))
    for e_ in range(E):
        ti, j = e_ // 3, (e_ % 3)
        for ts in range(c.NTS):
            t0 = ts * c.TSUB
            pb, pk = mk.pbank()
            for k in range(NCH):
                em.op("tensor", lambda e, k=k, e_=e_, t0=t0, pb=pb: e.matmul(
                    pb[0:RK, 0:c.TSUB],
                    lat16[:, (k * E + e_) * RK:(k * E + e_) * RK + RK],
                    xnb[:, k * T + t0:k * T + t0 + c.TSUB],
                    start=(k == 0), stop=(k == NCH - 1)),
                    reads=[("f1lat",), xnkeyf(k)], writes=[pk],
                    sig=(k == NCH - 1))
            em.op("vector", lambda e, ti=ti, j=j, t0=t0, pb=pb: e.tensor_copy(
                mid[ti][32 * j:32 * j + RK, t0:t0 + c.TSUB], pb[0:RK, 0:c.TSUB]),
                reads=[pk], writes=[("f1mid", ti)])
    U_ok = NG1 * T >= 3 * DF + 3 * (T // 4) + T
    if U_ok:
        lbt = [pro["xqb"][:, t * DF:(t + 1) * DF] for t in range(3)]
    else:
        lbt = [mk.sb(f"f1lbt{t}", [128, DF], BF16) for t in range(3)]
    for t in range(3):
        em.op("gpsimd", lambda e, t=t: e.dma_start(
            out=lbt[t][:, :], in_=io["loraBTp"].ap()[t * 128:(t + 1) * 128, :]),
            writes=[("f1lbt", t)], dom="qg")
    T4 = T // 4
    bi116 = mk.sb("f1_bi116", [1, DF], BF16)
    em.op("gpsimd", lambda e: e.dma_start(out=bi116[0:1, 0:DF],
                                          in_=io["bi1"].ap()),
          writes=[("f1bi116",)], dom="qg")
    bo116 = mk.sb("f1_bo116", [1, D], BF16)
    em.op("gpsimd", lambda e: e.dma_start(out=bo116[0:1, 0:D],
                                          in_=io["bo1"].ap()),
          writes=[("f1bo116",)], dom="qg")
    if U_ok1:
        base16 = pro["xqb"][:, 3 * DF:3 * DF + T4]
        z16 = pro["xqb"][:, 3 * DF + T4:3 * DF + 2 * T4]
        zg16 = pro["xqb"][:, 3 * DF + 2 * T4:3 * DF + 3 * T4]
    else:
        base16 = mk.sb("f1_base", [128, T4], BF16)
        z16 = mk.sb("f1_z", [128, T4], BF16)
        zg16 = mk.sb("f1_zg", [128, T4], BF16)
    HCQ = HCH // 4
    hacc16 = mk.sb("f1_hacc", [128, HCQ * T4], BF16)
    wbe8 = mk.sb("f1_wbe", [128, E * T4], BF16)
    for ts4 in range(T // T4):
        tbase = ts4 * T4
        for e_ in range(E):
            mk.bcast_selrow(wT, E, e_, T4, ("f1wT",), wbe8, ("f1wbe",),
                            dst_off=e_ * T4, col_off=tbase)

        for q4 in range(4):
            def evac_base(fabs, fcw, t0, tw, pb, pk, tbase=tbase, q4=q4):
                hcl = fabs // 128
                hc = q4 * HCQ + hcl
                em.op("vector", lambda e, pb=pb: e.tensor_copy(
                    base16[0:128, 0:T4], pb[0:128, 0:T4]),
                    reads=[pk], writes=[("f1base",)])
                pbA, pkA = mk.p512[4], ("p512", 4)
                for e_ in range(E):
                    ti, j = e_ // 3, e_ % 3
                    pbE, pkE = mk.p512[e_ % 2], ("p512", e_ % 2)
                    em.op("tensor", lambda e, pbE=pbE: e.matmul(
                        pbE[0:128, 0:T4], mk.ident[:, :], base16[0:128, 0:T4],
                        start=True, stop=False),
                        reads=[("c_id",), ("f1base",)], writes=[pkE], sig=False)
                    em.op("tensor", lambda e, ti=ti, j=j, hc=hc, tbase=tbase,
                          pbE=pbE: e.matmul(
                              pbE[0:128, 0:T4],
                              lbt[ti][32 * j:32 * j + RK,
                                      hc * 128:hc * 128 + 128],
                              mid[ti][32 * j:32 * j + RK, tbase:tbase + T4],
                              start=False, stop=True, tile_position=(32 * j, 0)),
                          reads=[("f1lbt", ti), ("f1mid", ti)], writes=[pkE])
                    em.op("scalar", lambda e, pbE=pbE: e.activation(
                        z16[0:128, 0:T4], pbE[0:128, 0:T4], AF.Relu),
                        reads=[pkE], writes=[("f1z",)])
                    em.op("vector", lambda e, e_=e_: e.tensor_mul(
                        zg16[0:128, 0:T4], z16[0:128, 0:T4],
                        wbe8[:, e_ * T4:(e_ + 1) * T4]),
                        reads=[("f1z",), ("f1wbe",)], writes=[("f1zg",)])
                    em.op("tensor", lambda e, e_=e_, pbA=pbA: e.matmul(
                        pbA[0:128, 0:T4], mk.ident[:, :], zg16[0:128, 0:T4],
                        start=(e_ == 0), stop=(e_ == E - 1)),
                        reads=[("c_id",), ("f1zg",)], writes=[pkA],
                        sig=(e_ == E - 1))
                em.op("vector", lambda e, hcl=hcl, pbA=pbA: e.tensor_copy(
                    hacc16[:, hcl * T4:(hcl + 1) * T4], pbA[0:128, 0:T4]),
                    reads=[pkA], writes=[("f1hacc", hcl)])

            mk.proj(f"f1wi_{ts4}_{q4}",
                    io["wiT"].ap()[:, q4 * (DF // 4):(q4 + 1) * (DF // 4)],
                    D, DF // 4, xnb, xnkeyf, evac_base,
                    wbufs=wbufs, tsubs=[(tbase, T4)],
                    bias_pre=(bi116[0:1, q4 * (DF // 4):(q4 + 1) * (DF // 4)],
                              ("f1bi116",)))

            def evac_wo(fabs, fcw, t0, tw, pb, pk):
                kk = fabs // 128
                em.op("vector", lambda e: e.tensor_add(
                    x[0:fcw, kk * T + t0:kk * T + t0 + tw],
                    x[0:fcw, kk * T + t0:kk * T + t0 + tw], pb[0:fcw, 0:tw]),
                    reads=[xkeyf(kk), pk], writes=[xkeyf(kk)])
            mk.proj(f"f1wo_{ts4}_{q4}",
                    io["woT"].ap()[q4 * (DF // 4):(q4 + 1) * (DF // 4), :],
                    DF // 4, D, hacc16,
                    lambda k: ("f1hacc", k),
                    evac_wo, wbufs=wbufs, tsubs=[(tbase, T4)],
                    xoff_fn=lambda k, t0, tbase=tbase: k * T4 + t0 - tbase,
                    bias_pre=((bo116, ("f1bo116",)) if q4 == 0 else None),
                    bias_rhs=(vs2row16, ("f1vs2b",)))


BLOCK_EMITTERS[1] = emit_block1



def emit_l2gate_raw(mk: MK, pro, io):
    c = mk.cfg
    em = mk.em
    T, D = c.T, c.D
    NE = 16
    NCH = c.NCH
    x, xkeyf = pro["x"], pro["xkeyf"]
    mu_row, kMu2 = pro["mu_row"]
    pb5 = mk.p512[5]
    pk5 = ("p512", 5)
    NTT = T // 128
    rawG = mk.sb("f2_rawG", [128, NTT * NE], F32)
    kRaw = ("f2rawG",)
    with mk.scope():
        simsq = mk.sb("f2sim_sq", [128, NE], F32)
        simnF = [mk.sb(f"f2simn_{k}", [128, NE], F32) for k in range(NCH)]
        cn = mk.sb("f2_cn", [1, NE], F32)
        cnb = mk.sb("f2_cnb", [128, NE], F32)
        ncs = mk.sb("f2_ncs", [1, NE], F32)
        c_t1g = mk.constf(1e-30, 1)
        c_mh1g = mk.constf(-0.5, 1)
        for k in range(NCH):
            em.op("sync", lambda e, k=k: e.dma_start(
                out=simnF[k][:, :], in_=io["sim"].ap()[k * 128:(k + 1) * 128, :]),
                writes=[("f2simn", k)], dom="qs")
            em.op("scalar", lambda e, k=k: e.activation(
                simsq[:, :], simnF[k][:, :], AF.Square),
                reads=[("f2simn", k)], writes=[("f2simsq",)])
            em.op("tensor", lambda e, k=k: e.matmul(
                pb5[0:1, 0:NE], mk.ones_col[:, 0:1], simsq[:, :],
                start=(k == 0), stop=(k == NCH - 1)),
                reads=[("c_oc",), ("f2simsq",)], writes=[pk5])
        em.op("vector", lambda e: e.tensor_copy(cn[0:1, :], pb5[0:1, 0:NE]),
              reads=[pk5], writes=[("f2cn",)])
        em.op("scalar", lambda e: e.activation(cn[0:1, :], cn[0:1, :], AF.Ln,
                                               bias=c_t1g),
              reads=[("f2cn",), ("c_cp",)], writes=[("f2cn",)])
        em.op("scalar", lambda e: e.activation(cn[0:1, :], cn[0:1, :], AF.Exp,
                                               scale=c_mh1g),
              reads=[("f2cn",), ("c_cp",)], writes=[("f2cn",)])
        mk.bcast_row(cn, NE, ("f2cn",), cnb, ("f2cnb",))
        for k in range(NCH):
            em.op("vector", lambda e, k=k: e.tensor_mul(
                simnF[k][:, :], simnF[k][:, :], cnb[:, 0:NE]),
                reads=[("f2simn", k), ("f2cnb",)], writes=[("f2simn", k)])
        for k in range(NCH):
            em.op("tensor", lambda e, k=k: e.matmul(
                pb5[0:1, 0:NE], mk.ones_col[:, 0:1], simnF[k][:, :],
                start=(k == 0), stop=(k == NCH - 1)),
                reads=[("c_oc",), ("f2simn", k)], writes=[pk5])
        em.op("vector", lambda e: e.tensor_scalar_mul(
            ncs[0:1, :], pb5[0:1, 0:NE], -1.0), reads=[pk5], writes=[("f2ncs",)])
        for tt in range(NTT):
            t0 = tt * 128
            pb, pk = mk.pbank()
            for k in range(NCH):
                em.op("tensor", lambda e, k=k, t0=t0, pb=pb: e.matmul(
                    pb[0:128, 0:NE], x[:, k * T + t0:k * T + t0 + 128],
                    simnF[k][:, :],
                    start=(k == 0), stop=False),
                    reads=[xkeyf(k), ("f2simn", k)], writes=[pk], sig=False)
            em.op("tensor", lambda e, t0=t0, pb=pb: e.matmul(
                pb[0:128, 0:NE], mu_row[0:1, t0:t0 + 128], ncs[0:1, 0:NE],
                start=False, stop=True),
                reads=[kMu2, ("f2ncs",)], writes=[pk])
            em.op("vector", lambda e, tt=tt, pb=pb: e.tensor_copy(
                rawG[:, tt * NE:(tt + 1) * NE], pb[0:128, 0:NE]),
                reads=[pk], writes=[kRaw])
    return rawG, kRaw


def emit_block2(mk: MK, io):
    c = mk.cfg
    em = mk.em
    T, D, S = c.T, c.D, c.S
    NG = D // 128
    NGH = NG // 2
    pro = block_prologue(mk, io)
    x, xkeyf = pro["x"], pro["xkeyf"]
    xnb, xnkeyf = pro["xnb"], pro["xnkeyf"]
    pro["rawG"] = emit_l2gate_raw(mk, pro, io)
    qdr = mk.dram("b2_qdr", [NG * 128, T], BF16)
    ag_in = mk.dram("b2ag_in", [2 * D, T], BF16)
    ag_out = mk.dram("b2ag_out", [2 * 2 * D, T], BF16)
    pb5 = mk.p512[5]
    pk5 = ("p512", 5)
    with mk.scope():
        wbp = mk.wstream_bufs(max(c.NCH, 8), npar=1)
        rcos = mk.sb("b2_rcos", [128, T], BF16)
        rsin = mk.sb("b2_rsin", [128, T], BF16)
        R2 = mk.sb("b2_R2", [128, 128], BF16)
        for nm, dst in (("rope2_cos", rcos), ("rope2_sin", rsin), ("R2", R2)):
            em.op("sync", lambda e, nm=nm, dst=dst: e.dma_start(
                out=dst[:, :], in_=io[nm].ap()), writes=[(nm,)], dom="qs")
        stg = [mk.sb(f"b2_stg{i}", [128, T], BF16) for i in range(2)]
        rop = mk.sb("b2_rop", [128, T], BF16)
        ship = mk.sb("b2_ship", [128, T], BF16)
        state = {"fc": -1}

        def finish_chunk():
            fc = state["fc"]
            if fc < 0:
                return
            gch = fc % c.NCH
            sec = fc // c.NCH
            pp = state["pp"]
            kSt = ("b2stg", pp)
            if sec <= 1:
                for t0 in range(0, T, 512):
                    tw = min(512, T - t0)
                    em.op("tensor", lambda e, t0=t0, tw=tw, pp=pp: e.matmul(
                        pb5[0:128, 0:tw], R2[:, :], stg[pp][:, t0:t0 + tw],
                        start=True, stop=True),
                        reads=[("R2",), kSt], writes=[pk5])
                    em.op("vector", lambda e, t0=t0, tw=tw: e.tensor_mul(
                        rop[:, t0:t0 + tw], pb5[0:128, 0:tw],
                        rsin[:, t0:t0 + tw]),
                        reads=[pk5, ("rope2_sin",)], writes=[("b2rop",)])
                em.op("vector", lambda e, pp=pp: e.tensor_mul(
                    stg[pp][:, :], stg[pp][:, :], rcos[:, :]),
                    reads=[kSt, ("rope2_cos",)], writes=[kSt])
                em.op("vector", lambda e, pp=pp: e.tensor_add(
                    ship[:, :], stg[pp][:, :], rop[:, :]),
                    reads=[kSt, ("b2rop",)], writes=[("b2ship",)])
                if sec == 0:
                    em.op("gpsimd", lambda e, gch=gch: e.dma_start(
                        out=qdr.ap()[gch * 128:(gch + 1) * 128, :],
                        in_=ship[:, :]),
                        reads=[("b2ship",)], writes=[("b2qdr",)], dom="qg")
                else:
                    em.op("gpsimd", lambda e, gch=gch: e.dma_start(
                        out=ag_in.ap()[gch * 128:(gch + 1) * 128, :],
                        in_=ship[:, :]),
                        reads=[("b2ship",)], writes=[("b2ag_in",)], dom="qg")
            else:
                em.op("gpsimd", lambda e, gch=gch, pp=pp: e.dma_start(
                    out=ag_in.ap()[D + gch * 128:D + (gch + 1) * 128, :],
                    in_=stg[pp][:, :]),
                    reads=[kSt], writes=[("b2ag_in",)], dom="qg")

        def evac_qkv(fabs, fcw, t0, tw, pb, pk):
            fc = state["base"] + fabs // 128
            newchunk = fc != state["fc"]
            prev = state["fc"]
            if newchunk:
                state["fc"] = fc
                state["pp"] = fc % 2
            pp = state["pp"]
            em.op("vector", lambda e, pp=pp, fcw=fcw, t0=t0, tw=tw, pb=pb:
                  e.tensor_copy(stg[pp][0:fcw, t0:t0 + tw], pb[0:fcw, 0:tw]),
                  reads=[pk], writes=[("b2stg", pp)])
            if newchunk and prev >= 0:
                cur = state["fc"], state["pp"]
                state["fc"], state["pp"] = prev, prev % 2
                finish_chunk()
                state["fc"], state["pp"] = cur

        for sec, (wname, bname) in enumerate((("Wq2", "bq2"), ("Wk2", "bk2"),
                                              ("Wv2", "bv2"))):
            state["base"] = sec * c.NCH
            mk.proj(f"b2{wname}", io[wname].ap(), D, D, xnb, xnkeyf, evac_qkv,
                    bias_ap=io[bname].ap(), wbufs=wbp)
            finish_chunk()
            state["fc"] = -1
        mk.allgather(ag_in, ag_out, ("b2ag_in",), ("ag_out", "b2a"))
    with mk.scope():
        qbh = mk.sb("b2_qbh", [128, NGH * T], BF16)
        aout = mk.sb("b2_aout", [128, NGH * T], BF16)
        kQBH = ("b2qbh",)
        aq_shared = {}
        for half in range(2):
            for i in range(NGH):
                em.op("sync", lambda e, i=i, half=half: e.dma_start(
                    out=qbh[:, i * T:(i + 1) * T],
                    in_=qdr.ap()[(half * NGH + i) * 128:
                                 (half * NGH + i + 1) * 128, :]),
                    reads=[("b2qdr",)], writes=[kQBH], dom="qs")
            mk.attn_qshard("b2h", qbh, kQBH, ag_out, aout,
                           ("b2aout",), scale=1.0 / 8.0,
                           g_base=half * NGH, g_count=NGH,
                           ag_key=("ag_out", "b2a"), q_base=half * NGH,
                           shared=aq_shared)
            for i in range(NGH):
                g = half * NGH + i
                for t0 in range(0, T, 512):
                    tw = min(512, T - t0)
                    em.op("vector", lambda e, g=g, i=i, t0=t0, tw=tw:
                          e.tensor_add(
                              x[:, g * T + t0:g * T + t0 + tw],
                              x[:, g * T + t0:g * T + t0 + tw],
                              aout[:, i * T + t0:i * T + t0 + tw]),
                          reads=[xkeyf(g), ("b2aout",)], writes=[xkeyf(g)])
    with mk.scope():
        wbp2 = mk.wstream_bufs(max(c.NCH, 8), npar=1)
        emit_l2ffn(mk, pro, io, wbp2)
    block_epilogue(mk, pro, io)


def emit_l2ffn(mk: MK, pro, io, wbufs):
    c = mk.cfg
    em = mk.em
    T, D, S = c.T, c.D, c.S
    NE = 16
    HD = 64
    NCH = c.NCH
    x, xkeyf = pro["x"], pro["xkeyf"]
    xnb, xnkeyf = pro["xnb"], pro["xnkeyf"]
    var_row, kVar = pro["var_row"]
    rawG, kRaw = pro["rawG"]
    pb5 = mk.p512[5]
    pk5 = ("p512", 5)
    rwT = mk.sb("f2_rwT", [NE, T], F32)
    with mk.scope():
        rowinv = mk.sb("f2_rowinv", [1, T], F32)
        em.op("vector", lambda e: e.tensor_scalar_mul(
            rowinv[0:1, :], var_row[0:1, :], float(D)),
            reads=[kVar], writes=[("f2ri",)])
        c_t1 = mk.constf(1e-30, 1)
        c_mh1b = mk.constf(-0.5, 1)
        em.op("scalar", lambda e: e.activation(
            rowinv[0:1, :], rowinv[0:1, :], AF.Ln, bias=c_t1),
            reads=[("f2ri",), ("c_cp",)], writes=[("f2ri",)])
        em.op("scalar", lambda e: e.activation(
            rowinv[0:1, :], rowinv[0:1, :], AF.Exp, scale=c_mh1b),
            reads=[("f2ri",), ("c_cp",)], writes=[("f2ri",)])
        sg = mk.sb("f2_sg", [1, NE], F32)
        sgb = mk.sb("f2_sgb", [128, NE], F32)
        em.op("sync", lambda e: e.dma_start(out=sg[0:1, :], in_=io["gates"].ap()),
              writes=[("f2sg",)], dom="qs")
        em.op("scalar", lambda e: e.activation(sg[0:1, :], sg[0:1, :], AF.Sigmoid),
              reads=[("f2sg",)], writes=[("f2sg",)])
        mk.bcast_row(sg, NE, ("f2sg",), sgb, ("f2sgb",))
        lg = mk.sb("f2_lg", [128, NE], F32)
        gtd = mk.sb("f2_gtd", [128, NE], F32)
        mpos = mk.sb("f2_mpos", [128, NE], F32)
        cnt = mk.sb("f2_cnt", [128, 1], F32)
        act = mk.sb("f2_act", [128, 1], F32)
        m1 = mk.sb("f2_m1", [128, 1], F32)
        eq1 = mk.sb("f2_eq1", [128, NE], F32)
        msk2 = mk.sb("f2_msk2", [128, NE], F32)
        m2 = mk.sb("f2_m2", [128, 1], F32)
        fb = mk.sb("f2_fb", [128, NE], F32)
        mfin = mk.sb("f2_mfin", [128, NE], F32)
        zz = mk.sb("f2_zz", [128, NE], F32)
        riT = mk.sb("f2_riT", [128, 1], F32)
        for tt in range(T // 128):
            t0 = tt * 128
            em.op("tensor", lambda e, t0=t0: e.transpose(
                pb5[0:128, 0:1], rowinv[0:1, t0:t0 + 128], mk.ident32[0:1, 0:1]),
                reads=[("f2ri",), ("c_id32",)], writes=[pk5])
            em.op("vector", lambda e: e.tensor_copy(riT[:, 0:1], pb5[0:128, 0:1]),
                  reads=[pk5], writes=[("f2riT",)])
            em.op("vector", lambda e, tt=tt: e.tensor_scalar(
                lg[:, :], rawG[:, tt * NE:(tt + 1) * NE], riT[:, 0:1], None,
                ALU.mult),
                reads=[kRaw, ("f2riT",)], writes=[("f2lg",)])
            em.op("vector", lambda e: e.tensor_sub(lg[:, :], lg[:, :], sgb[:, 0:NE]),
                  reads=[("f2lg",), ("f2sgb",)], writes=[("f2lg",)])
            em.op("vector", lambda e: e.tensor_scalar_max(gtd[:, :], lg[:, :], 0.0),
                  reads=[("f2lg",)], writes=[("f2gtd",)])
            em.op("vector", lambda e: e.tensor_single_scalar(
                mpos[:, :], lg[:, :], 0.0, ALU.is_gt),
                reads=[("f2lg",)], writes=[("f2mpos",)])
            em.op("vector", lambda e: e.tensor_reduce(
                cnt[:, 0:1], mpos[:, :], AX.X, ALU.add),
                reads=[("f2mpos",)], writes=[("f2cnt",)])
            em.op("vector", lambda e: e.tensor_single_scalar(
                act[:, 0:1], cnt[:, 0:1], 0.0, ALU.is_gt),
                reads=[("f2cnt",)], writes=[("f2act",)])
            em.op("vector", lambda e: e.tensor_reduce(m1[:, 0:1], lg[:, :], AX.X,
                                                      ALU.max),
                  reads=[("f2lg",)], writes=[("f2m1",)])
            em.op("vector", lambda e: e.tensor_scalar(
                eq1[:, :], lg[:, :], m1[:, 0:1], None, ALU.is_equal),
                reads=[("f2lg",), ("f2m1",)], writes=[("f2eq1",)])
            em.op("vector", lambda e: e.scalar_tensor_tensor(
                msk2[:, :], eq1[:, :], -1e30, lg[:, :], ALU.mult, ALU.add),
                reads=[("f2eq1",), ("f2lg",)], writes=[("f2msk2",)])
            em.op("vector", lambda e: e.tensor_reduce(m2[:, 0:1], msk2[:, :], AX.X,
                                                      ALU.max),
                  reads=[("f2msk2",)], writes=[("f2m2",)])
            em.op("vector", lambda e: e.tensor_scalar(
                fb[:, :], msk2[:, :], m2[:, 0:1], None, ALU.is_equal),
                reads=[("f2msk2",), ("f2m2",)], writes=[("f2fb",)])
            em.op("vector", lambda e: e.tensor_add(fb[:, :], fb[:, :], eq1[:, :]),
                  reads=[("f2fb",), ("f2eq1",)], writes=[("f2fb",)])
            em.op("vector", lambda e: e.tensor_scalar(
                mfin[:, :], mpos[:, :], act[:, 0:1], None, ALU.mult),
                reads=[("f2mpos",), ("f2act",)], writes=[("f2mfin",)])
            em.op("vector", lambda e: e.tensor_scalar_mul(act[:, 0:1], act[:, 0:1],
                                                          -1.0),
                  reads=[("f2act",)], writes=[("f2act",)])
            em.op("vector", lambda e: e.tensor_single_scalar(
                act[:, 0:1], act[:, 0:1], 1.0, ALU.add),
                reads=[("f2act",)], writes=[("f2act",)])
            em.op("vector", lambda e: e.tensor_scalar(
                zz[:, :], fb[:, :], act[:, 0:1], None, ALU.mult),
                reads=[("f2fb",), ("f2act",)], writes=[("f2zz",)])
            em.op("vector", lambda e: e.tensor_add(mfin[:, :], mfin[:, :], zz[:, :]),
                  reads=[("f2mfin",), ("f2zz",)], writes=[("f2mfin",)])
            em.op("vector", lambda e: e.tensor_mul(zz[:, :], gtd[:, :], mfin[:, :]),
                  reads=[("f2gtd",), ("f2mfin",)], writes=[("f2zz",)])
            em.op("vector", lambda e: e.scalar_tensor_tensor(
                zz[:, :], mfin[:, :], 1e4, zz[:, :], ALU.mult, ALU.add),
                reads=[("f2mfin",), ("f2zz",)], writes=[("f2zz",)])
            em.op("vector", lambda e: e.tensor_single_scalar(
                zz[:, :], zz[:, :], -1e4, ALU.add),
                reads=[("f2zz",)], writes=[("f2zz",)])
            em.op("vector", lambda e: e.tensor_reduce(m1[:, 0:1], zz[:, :], AX.X,
                                                      ALU.max),
                  reads=[("f2zz",)], writes=[("f2m1",)])
            em.op("vector", lambda e: e.tensor_scalar_mul(m2[:, 0:1], m1[:, 0:1], -1.0),
                  reads=[("f2m1",)], writes=[("f2m2",)])
            em.op("scalar", lambda e: e.activation(zz[:, :], zz[:, :], AF.Exp,
                                                   bias=m2[:, 0:1]),
                  reads=[("f2zz",), ("f2m2",)], writes=[("f2zz",)])
            em.op("vector", lambda e: e.tensor_reduce(m1[:, 0:1], zz[:, :], AX.X,
                                                      ALU.add),
                  reads=[("f2zz",)], writes=[("f2m1",)])
            em.op("vector", lambda e: e.reciprocal(m1[:, 0:1], m1[:, 0:1]),
                  reads=[("f2m1",)], writes=[("f2m1",)])
            em.op("vector", lambda e: e.tensor_scalar(
                zz[:, :], zz[:, :], m1[:, 0:1], None, ALU.mult),
                reads=[("f2zz",), ("f2m1",)], writes=[("f2zz",)])
            em.op("tensor", lambda e: e.transpose(
                pb5[0:NE, 0:128], zz[:, 0:NE], mk.ident32[0:128, 0:128]),
                reads=[("f2zz",), ("c_id32",)], writes=[pk5])
            em.op("vector", lambda e, t0=t0: e.tensor_copy(
                rwT[0:NE, t0:t0 + 128], pb5[0:NE, 0:128]),
                reads=[pk5], writes=[("f2rwT",)])
    NPC = NE * HD // 128
    mk.qsel(NE, NPC)
    ag3_in = mk.dram("b2ag3_in", [3 * 64, T], BF16)
    ag3_out = mk.dram("b2ag3_out", [2 * 3 * 64, T], BF16)
    with mk.scope():
        fold_np = np.zeros((128, 64), np.float32)
        for d_ in range(64):
            fold_np[d_, d_] = 1.0
            fold_np[64 + d_, d_] = 1.0
        fold_h = mk.nc.inline_tensor(fold_np, "fold64")
        fold64 = mk.sb("f2_fold64", [128, 64], F32)
        em.op("sync", lambda e: e.dma_start(out=fold64[:, :], in_=fold_h.ap()),
              writes=[("f2fold",)], dom="qs")
        Pbuf = mk.sb("f2_P", [128, NPC * T], BF16)
        acc = mk.sb("f2_acc", [128, T], F32)
        rwb = mk.sb("f2_rwb", [128, T], F32)
        qkvm = mk.sb("f2_qkvm", [128, 3 * T], BF16)

        def evac_P(fabs, fcw, t0, tw, pb, pk):
            kk = fabs // 128
            em.op("vector", lambda e, kk=kk, t0=t0, tw=tw, fcw=fcw, pb=pb:
                  e.tensor_copy(
                      Pbuf[0:fcw, kk * T + t0:kk * T + t0 + tw], pb[0:fcw, 0:tw]),
                  reads=[pk], writes=[("f2P", kk)])

        qsel, qselkey = mk.qsel(NE, NPC)
        for sec, wname in enumerate(("qpf", "kpf", "vpf")):
            mk.proj(f"f2{wname}", io[wname].ap(), D, NE * HD, xnb, xnkeyf,
                    evac_P, wbufs=wbufs)
            em.op("vector", lambda e: e.memset(acc[:, :], 0.0),
                  writes=[("f2acc",)])
            for kk in range(NPC):
                for t0 in range(0, T, 512):
                    tw = min(512, T - t0)
                    pbq, pkq = mk.pbank()
                    em.op("tensor", lambda e, kk=kk, t0=t0, tw=tw, pbq=pbq:
                          e.matmul(
                              pbq[:, 0:tw], qsel[0:NE, kk * 128:(kk + 1) * 128],
                              rwT[0:NE, t0:t0 + tw], start=True, stop=True),
                          reads=[qselkey, ("f2rwT",)], writes=[pkq])
                    em.op("vector", lambda e, t0=t0, tw=tw, pbq=pbq:
                          e.tensor_copy(rwb[:, t0:t0 + tw], pbq[:, 0:tw]),
                          reads=[pkq], writes=[("f2rwb",)])
                em.op("vector", lambda e, kk=kk: e.tensor_mul(
                    Pbuf[:, kk * T:(kk + 1) * T], Pbuf[:, kk * T:(kk + 1) * T],
                    rwb[:, :]),
                    reads=[("f2P", kk), ("f2rwb",)], writes=[("f2P", kk)])
                em.op("vector", lambda e, kk=kk: e.tensor_add(
                    acc[:, 0:T], acc[:, 0:T],
                    Pbuf[:, kk * T:(kk + 1) * T]),
                    reads=[("f2acc",), ("f2P", kk)], writes=[("f2acc",)])
            for t0 in range(0, T, 512):
                tw = min(512, T - t0)
                pbq, pkq = mk.pbank()
                em.op("tensor", lambda e, t0=t0, tw=tw, pbq=pbq: e.matmul(
                    pbq[0:64, 0:tw], fold64[:, 0:64], acc[:, t0:t0 + tw],
                    start=True, stop=True),
                    reads=[("f2fold",), ("f2acc",)], writes=[pkq])
                em.op("vector", lambda e, sec=sec, t0=t0, tw=tw, pbq=pbq:
                      e.tensor_copy(
                          qkvm[0:64, sec * T + t0:sec * T + t0 + tw],
                          pbq[0:64, 0:tw]),
                      reads=[pkq], writes=[("f2qkvm",)])
        for sec in range(3):
            em.op("gpsimd", lambda e, sec=sec: e.dma_start(
                out=ag3_in.ap()[sec * 64:(sec + 1) * 64, :],
                in_=qkvm[0:64, sec * T:(sec + 1) * T]),
                reads=[("f2qkvm",)], writes=[("f2ag3in",)], dom="qg")
        mk.allgather(ag3_in, ag3_out, ("f2ag3in",), ("f2ag3out",))
    emit_l2ffn_attn(mk, pro, io, ag3_out, rwT, wbufs)


def emit_l2ffn_attn(mk: MK, pro, io, ag3_out, rwT, wbufs):
    c = mk.cfg
    em = mk.em
    T, D, S = c.T, c.D, c.S
    NE, HD = 16, 64
    NPC = NE * HD // 128
    x, xkeyf = pro["x"], pro["xkeyf"]
    pb5 = mk.p512[5]
    pk5 = ("p512", 5)
    NK128 = S // 128
    NQT = S // 128
    NJ = S // c.KB
    ao = mk.sb("f2a_ao", [128, T], BF16)
    aodr = mk.dram("f2a_aodr", [64, S], BF16)
    with mk.scope():
        qf = mk.sb("f2a_q", [64, S], BF16)
        kf = mk.sb("f2a_k", [64, S], BF16)
        vf = mk.sb("f2a_v", [64, S], BF16)
        vT = mk.sb("f2a_vT", [128, NK128 * 64], BF16)
        P_sb = mk.sb("f2a_P", [128, 2 * c.KB], BF16)
        PT_sb = mk.sb("f2a_PT", [128, 2 * 128], BF16)
        sums = mk.sb("f2a_sum", [128, 4 * NJ], F32)
        rrec = mk.sb("f2a_rr", [128, 4], F32)
        rrT1 = mk.sb("f2a_rrT", [1, 512], F32)
        rb1 = mk.sb("f2a_rb", [64, 512], F32)
        mxq = mk.sb("f2a_mx", [128, 2], F32)
        nm1 = mk.sb("f2a_nm", [128, 1], F32)
        aoS = mk.sb("f2a_aoS", [64, S], BF16)
        for sec, dst, dkey in ((0, qf, ("f2aq",)), (1, kf, ("f2ak",)),
                               (2, vf, ("f2av",))):
            for rank in range(2):
                r0 = rank * 192 + sec * 64
                em.op("gpsimd", lambda e, r0=r0, rank=rank, dst=dst: e.dma_start(
                    out=dst[0:64, rank * T:(rank + 1) * T],
                    in_=ag3_out.ap()[r0:r0 + 64, :]),
                    reads=[("f2ag3out",)], writes=[dkey], dom="qg")
        for kt in range(NK128):
            mk.transpose(vf[0:64, kt * 128:(kt + 1) * 128], 64, 128, [("f2av",)],
                         vT, kt * 64, ("f2avT",), pti=kt % 2, bp=0)
        c_sc = mk.constf(0.125, 128)
        psO = mk.p512[4]
        kO = ("p512", 4)
        maskA2 = ("maskA2",)
        maskA = mk.sb("f2a_maskA", [128, 128], F32)
        em.op("sync", lambda e: e.dma_start(out=maskA[:, :], in_=io["maskA"].ap()),
              writes=[("f2amA",)], dom="qs")
        NQB = NQT // c.QB
        for qb in range(NQB):
            em.op("vector", lambda e: e.memset(sums[:, :], 0.0), writes=[("f2asums",)])
            for qloc in range(c.QB):
                qt = qb * c.QB + qloc
                jmax = (qt * 128) // c.KB
                for j in range(jmax + 1):
                    strad = (j == jmax)
                    delta = qt - j * (c.KB // 128)
                    kw = (delta + 1) * 128 if strad else c.KB
                    sb_i = j % 2
                    Sb = mk.p512[sb_i]
                    kS = ("p512", sb_i)
                    em.op("tensor", lambda e, qt=qt, j=j, kw=kw, Sb=Sb: e.matmul(
                        Sb[0:128, 0:kw], qf[0:64, qt * 128:qt * 128 + 128],
                        kf[0:64, j * c.KB:j * c.KB + kw],
                        start=True, stop=True, tile_position=(0, 0)),
                        reads=[("f2aq",), ("f2ak",)], writes=[kS])
                    if strad:
                        em.op("vector", lambda e, kw=kw, Sb=Sb: e.tensor_add(
                            Sb[0:128, kw - 128:kw], maskA[:, 0:128],
                            Sb[0:128, kw - 128:kw]),
                            reads=[("f2amA",), kS], writes=[kS])
                    if j == 0:
                        em.op("vector", lambda e, kw=kw, Sb=Sb: e.tensor_reduce(
                            mxq[:, 0:1], Sb[0:128, 0:kw], AX.X, ALU.max),
                            reads=[kS], writes=[("f2amx",)])
                    else:
                        em.op("vector", lambda e, kw=kw, Sb=Sb: e.tensor_reduce(
                            mxq[:, 1:2], Sb[0:128, 0:kw], AX.X, ALU.max),
                            reads=[kS], writes=[("f2amx",)])
                        em.op("vector", lambda e: e.tensor_tensor(
                            mxq[:, 0:1], mxq[:, 0:1], mxq[:, 1:2], ALU.max),
                            reads=[("f2amx",)], writes=[("f2amx",)])
                em.op("vector", lambda e: e.tensor_scalar_mul(
                    nm1[:, 0:1], mxq[:, 0:1], -0.125),
                    reads=[("f2amx",)], writes=[("f2anm",)])
                for j in range(jmax + 1):
                    strad = (j == jmax)
                    delta = qt - j * (c.KB // 128)
                    kw = (delta + 1) * 128 if strad else c.KB
                    nkt = kw // 128
                    sb_i = j % 2
                    Sb = mk.p512[sb_i]
                    kS = ("p512", sb_i)
                    em.op("tensor", lambda e, qt=qt, j=j, kw=kw, Sb=Sb: e.matmul(
                        Sb[0:128, 0:kw], qf[0:64, qt * 128:qt * 128 + 128],
                        kf[0:64, j * c.KB:j * c.KB + kw],
                        start=True, stop=True, tile_position=(0, 0)),
                        reads=[("f2aq",), ("f2ak",)], writes=[kS])
                    if strad:
                        em.op("vector", lambda e, kw=kw, Sb=Sb: e.tensor_add(
                            Sb[0:128, kw - 128:kw], maskA[:, 0:128],
                            Sb[0:128, kw - 128:kw]),
                            reads=[("f2amA",), kS], writes=[kS])
                    pr = (j % 2) * c.KB
                    em.op("scalar", lambda e, qt=qt, qloc=qloc, j=j, kw=kw, Sb=Sb,
                          pr=pr: e.activation(
                              P_sb[:, pr:pr + kw], Sb[0:128, 0:kw], AF.Exp,
                              bias=nm1[:, 0:1], scale=c_sc,
                              accum_out=sums[:, qloc * NJ + j:qloc * NJ + j + 1]),
                          reads=[kS, ("f2anm",), ("c_cp",)],
                          writes=[("f2aP", j % 2), ("f2asums",)])
                    for kt in range(nkt):
                        ktabs = j * (c.KB // 128) + kt
                        pti = kt % 2
                        ptb = mk.pT[pti]
                        pkT = ("pT", pti)
                        em.op("tensor", lambda e, ptb=ptb, pr=pr, kt=kt: e.transpose(
                            ptb[0:128, 0:128],
                            P_sb[:, pr + kt * 128:pr + kt * 128 + 128],
                            mk.ident[:, :]),
                            reads=[("f2aP", j % 2), ("c_id",)], writes=[pkT])
                        pt_off = (kt % 2) * 128
                        em.op("vector", lambda e, ptb=ptb, pt_off=pt_off: e.tensor_copy(
                            PT_sb[:, pt_off:pt_off + 128], ptb[0:128, 0:128]),
                            reads=[pkT], writes=[("f2aPT", kt % 2)])
                        first = (j == 0 and kt == 0)
                        last = (j == jmax and kt == nkt - 1)
                        em.op("tensor", lambda e, ktabs=ktabs, qloc=qloc, pt_off=pt_off,
                              first=first, last=last: e.matmul(
                                  psO[0:64, qloc * 128:qloc * 128 + 128],
                                  vT[:, ktabs * 64:ktabs * 64 + 64],
                                  PT_sb[:, pt_off:pt_off + 128],
                                  start=first, stop=last, tile_position=(0, 0)),
                              reads=[("f2avT",), ("f2aPT", kt % 2)],
                              writes=[kO], sig=last)
            em.op("vector", lambda e: e.tensor_reduce(
                rrec[:, 0:4], sums[:, :].rearrange("p (a b) -> p a b", b=NJ),
                AX.X, ALU.add),
                reads=[("f2asums",)], writes=[("f2arr",)])
            em.op("vector", lambda e: e.reciprocal(rrec[:, 0:4], rrec[:, 0:4]),
                  reads=[("f2arr",)], writes=[("f2arr",)])
            for qloc in range(c.QB):
                em.op("tensor", lambda e, qloc=qloc: e.transpose(
                    pb5[0:1, 0:128], rrec[:, qloc:qloc + 1],
                    mk.ident32[0:128, 0:128]),
                    reads=[("f2arr",), ("c_id32",)], writes=[pk5])
                em.op("vector", lambda e, qloc=qloc: e.tensor_copy(
                    rrT1[0:1, qloc * 128:qloc * 128 + 128], pb5[0:1, 0:128]),
                    reads=[pk5], writes=[("f2arrT",)])
            em.op("tensor", lambda e: e.matmul(
                pb5[0:64, 0:512], mk.ones128_32[0:1, 0:64], rrT1[0:1, 0:512],
                start=True, stop=True),
                reads=[("c_o128",), ("f2arrT",)], writes=[pk5])
            em.op("vector", lambda e: e.tensor_copy(rb1[0:64, 0:512],
                                                    pb5[0:64, 0:512]),
                  reads=[pk5], writes=[("f2arb",)])
            em.op("vector", lambda e, qb=qb: e.tensor_mul(
                aoS[0:64, qb * 512:qb * 512 + 512], psO[0:64, 0:512],
                rb1[0:64, 0:512]),
                reads=[kO, ("f2arb",)], writes=[("f2aoS",)])
        em.op("gpsimd", lambda e: e.dma_start(out=aodr.ap(), in_=aoS[0:64, :]),
              reads=[("f2aoS",)], writes=[("f2aodr",)], dom="qg")
    mk.dyn_load("sync", ao[0:64, 0:T], aodr, 0, [[S, 64], [1, T]],
                [("f2aodr",)], [("f2ao",)], base="tb")
    mk.dyn_load("sync", ao[64:128, 0:T], aodr, 0, [[S, 64], [1, T]],
                [("f2aodr",)], [("f2ao",)], base="tb")
    NPQ = 2
    Pbuf = mk.sb("f2z_P", [128, NPQ * T], BF16)
    rwb = mk.sb("f2z_rwb", [128, T], F32)
    qsel, qselkey = mk.qsel(NE, NPC)
    for q in range(NPC // NPQ):
        for kl in range(NPQ):
            kk = q * NPQ + kl
            for t0 in range(0, T, 512):
                tw = min(512, T - t0)
                pbq, pkq = mk.pbank()
                em.op("tensor", lambda e, kk=kk, t0=t0, tw=tw, pbq=pbq: e.matmul(
                    pbq[:, 0:tw], qsel[0:NE, kk * 128:(kk + 1) * 128],
                    rwT[0:NE, t0:t0 + tw], start=True, stop=True),
                    reads=[qselkey, ("f2rwT",)], writes=[pkq])
                em.op("vector", lambda e, t0=t0, tw=tw, pbq=pbq: e.tensor_copy(
                    rwb[:, t0:t0 + tw], pbq[:, 0:tw]),
                    reads=[pkq], writes=[("f2zrwb",)])
            em.op("vector", lambda e, kl=kl: e.tensor_mul(
                Pbuf[0:64, kl * T:(kl + 1) * T], ao[0:64, 0:T], rwb[0:64, 0:T]),
                reads=[("f2ao",), ("f2zrwb",)], writes=[("f2zP", kl)])
            em.op("vector", lambda e, kl=kl: e.tensor_mul(
                Pbuf[64:128, kl * T:(kl + 1) * T], ao[64:128, 0:T],
                rwb[64:128, 0:T]),
                reads=[("f2ao",), ("f2zrwb",)], writes=[("f2zP", kl)])
        mk.proj(f"f2op{q}", io["opR"].ap()[q * NPQ * 128:(q + 1) * NPQ * 128, :],
                NPQ * 128, D, Pbuf,
                lambda k: ("f2zP", k), resid_evac(mk, pro, "f2o"), wbufs=wbufs)

BLOCK_EMITTERS[2] = emit_block2


def emit_block3(mk: MK, io):
    c = mk.cfg
    em = mk.em
    T, D, S = c.T, c.D, c.S
    NCH = c.NCH
    pb5 = mk.p512[5]
    pk5 = ("p512", 5)
    xsp = mk.dram("b3_xsp", [D, T], F32)
    xnsp = mk.dram("b3_xnsp", [D, T], F32)
    mk.setup_meta(io["meta"].ap())
    ag2_out = mk.dram("b3ag2_out", [2 * c.NGC * 128, S], BF16)
    with mk.scope():
        xnb = mk.sb("b3_xnb", [128, NCH * T], BF16)
        xnkeyf = lambda k: ("b3xnb", k)
        x = mk.sb("b3_x", [128, NCH * T], F32)
        xkeyf = lambda k: ("b3x", k)
        for k in range(NCH):
            em.op("sync", lambda e, k=k: e.dma_start(
                out=x[:, k * T:(k + 1) * T],
                in_=io["x_in"].ap()[k * 128:(k + 1) * 128, :]),
                writes=[xkeyf(k)], dom="qs")
        with mk.scope():
            PW = min(512, T)
            sqp = [mk.sb(mk.fresh("b3psq"), [128, PW], F32) for _ in range(2)]
            prow = mk.sb("b3prow", [1, 3 * PW], F32)
            mub = mk.sb("b3pmub", [128, PW], F32)
            rbt = mk.sb("b3prbt", [128, PW], F32)
            tmpt = [mk.sb(mk.fresh("b3ptmp"), [128, PW], F32) for _ in range(2)]
            Dv = float(D)
            for (t0, tw) in [(i * PW, PW) for i in range(T // PW)]:
                pbS, pkS = mk.pbank()
                pbQ, pkQ = mk.pbank()
                for k in range(NCH):
                    pp = k % 2
                    kq = ("b3psq", pp)
                    em.op("scalar", lambda e, k=k, t0=t0, tw=tw, pp=pp:
                          e.activation(sqp[pp][:, 0:tw],
                                       x[:, k * T + t0:k * T + t0 + tw],
                                       AF.Square),
                          reads=[xkeyf(k)], writes=[kq])
                    em.op("tensor", lambda e, k=k, t0=t0, tw=tw, pbS=pbS:
                          e.matmul(pbS[0:1, 0:tw], mk.ones_col[:, 0:1],
                                   x[:, k * T + t0:k * T + t0 + tw],
                                   start=(k == 0), stop=(k == NCH - 1)),
                          reads=[("c_oc",), xkeyf(k)], writes=[pkS])
                    em.op("tensor", lambda e, k=k, tw=tw, pp=pp, pbQ=pbQ:
                          e.matmul(pbQ[0:1, 0:tw], mk.ones_col[:, 0:1],
                                   sqp[pp][:, 0:tw],
                                   start=(k == 0), stop=(k == NCH - 1)),
                          reads=[("c_oc",), kq], writes=[pkQ])
                em.op("vector", lambda e, tw=tw, pbS=pbS: e.tensor_scalar_mul(
                    prow[0:1, 0:tw], pbS[0:1, 0:tw], 1.0 / Dv),
                    reads=[pkS], writes=[("b3pr",)])
                em.op("vector", lambda e, tw=tw, pbQ=pbQ: e.tensor_scalar_mul(
                    prow[0:1, PW:PW + tw], pbQ[0:1, 0:tw], 1.0 / Dv),
                    reads=[pkQ], writes=[("b3pr",)])
                em.op("vector", lambda e, tw=tw: e.scalar_tensor_tensor(
                    prow[0:1, 2 * PW:2 * PW + tw], prow[0:1, 0:tw], -1.0,
                    prow[0:1, 0:tw], ALU.mult, ALU.mult),
                    reads=[("b3pr",)], writes=[("b3pr",)])
                em.op("vector", lambda e, tw=tw: e.tensor_add(
                    prow[0:1, 2 * PW:2 * PW + tw],
                    prow[0:1, 2 * PW:2 * PW + tw], prow[0:1, PW:PW + tw]),
                    reads=[("b3pr",)], writes=[("b3pr",)])
                c_eps1p = mk.constf(1e-5, 1)
                c_mh1p = mk.constf(-0.5, 1)
                em.op("scalar", lambda e, tw=tw, c_eps1p=c_eps1p: e.activation(
                    prow[0:1, 2 * PW:2 * PW + tw],
                    prow[0:1, 2 * PW:2 * PW + tw], AF.Ln, bias=c_eps1p),
                    reads=[("b3pr",), ("c_cp",)], writes=[("b3pr",)])
                em.op("scalar", lambda e, tw=tw, c_mh1p=c_mh1p: e.activation(
                    prow[0:1, 2 * PW:2 * PW + tw],
                    prow[0:1, 2 * PW:2 * PW + tw], AF.Exp, scale=c_mh1p),
                    reads=[("b3pr",), ("c_cp",)], writes=[("b3pr",)])
                mk.bcast_row(prow, tw, ("b3pr",), mub, ("b3mub",), col_off=0)
                mk.bcast_row(prow, tw, ("b3pr",), rbt, ("b3rbt",),
                             col_off=2 * PW)
                for k in range(NCH):
                    pp = k % 2
                    tk = ("b3ptm", pp)
                    em.op("vector", lambda e, k=k, t0=t0, tw=tw, pp=pp:
                          e.tensor_sub(tmpt[pp][:, 0:tw],
                                       x[:, k * T + t0:k * T + t0 + tw],
                                       mub[:, 0:tw]),
                          reads=[xkeyf(k), ("b3mub",)], writes=[tk])
                    em.op("gpsimd", lambda e, k=k, t0=t0, tw=tw, pp=pp:
                          e.tensor_mul(xnb[:, k * T + t0:k * T + t0 + tw],
                                       tmpt[pp][:, 0:tw], rbt[:, 0:tw]),
                          reads=[tk, ("b3rbt",)], writes=[xnkeyf(k)])
                    em.op("vector", lambda e, k=k, t0=t0, tw=tw, pp=pp:
                          e.tensor_mul(tmpt[pp][:, 0:tw], tmpt[pp][:, 0:tw],
                                       rbt[:, 0:tw]),
                          reads=[tk, ("b3rbt",)], writes=[tk])
                    em.op("sync", lambda e, k=k, t0=t0, tw=tw, pp=pp:
                          e.dma_start(
                              out=xnsp.ap()[k * 128:(k + 1) * 128, t0:t0 + tw],
                              in_=tmpt[pp][:, 0:tw]),
                          reads=[tk], writes=[("b3xnsp",)], dom="qs")
        pro = {"x": x, "xkeyf": xkeyf, "xnb": xnb, "xnkeyf": xnkeyf}
        wbp = mk.wstream_bufs(max(NCH, 8), npar=1)
        ag_in = mk.dram("b3ag_in", [3 * D, T], BF16)
        ag_out = mk.dram("b3ag_out", [2 * 3 * D, T], BF16)
        ag2_in = mk.dram("b3ag2_in", [c.NGC * 128, S], BF16)
        with mk.scope():
            wstrip = mk.sb("b3_wstrip", [128, 128], F32)
            wstrip16 = mk.sb("b3_ws16", [128, 128], BF16)
            shipb = mk.sb("b3_shipb", [128, T], BF16)
            for sec, wn in enumerate(("wqBD", "wkBD", "wvBD")):
                for k in range(NCH):
                    em.op("sync", lambda e, wn=wn, k=k: e.dma_start(
                        out=wstrip[:, :],
                        in_=io[wn].ap()[k * 128:(k + 1) * 128, :]),
                        writes=[("b3ws",)], dom="qs")
                    em.op("vector", lambda e: e.tensor_copy(wstrip16[:, :],
                                                            wstrip[:, :]),
                          reads=[("b3ws",)], writes=[("b3ws16",)])
                    for ti in range(c.NTS):
                        t0 = ti * c.TSUB
                        pb, pk = mk.pbank()
                        em.op("tensor", lambda e, k=k, t0=t0, pb=pb: e.matmul(
                            pb[0:128, 0:c.TSUB], wstrip16[:, :],
                            xnb[:, k * T + t0:k * T + t0 + c.TSUB],
                            start=True, stop=True),
                            reads=[("b3ws16",), xnkeyf(k)], writes=[pk])
                        em.op("vector", lambda e, t0=t0, pb=pb: e.tensor_copy(
                            shipb[:, t0:t0 + c.TSUB], pb[0:128, 0:c.TSUB]),
                            reads=[pk], writes=[("b3shipb",)])
                    em.op("gpsimd", lambda e, sec=sec, k=k: e.dma_start(
                        out=ag_in.ap()[sec * D + k * 128:sec * D + (k + 1) * 128, :],
                        in_=shipb[:, :]),
                        reads=[("b3shipb",)], writes=[("b3ag_in",)], dom="qg")
            mk.allgather(ag_in, ag_out, ("b3ag_in",), ("ag_out",))
        with mk.scope():
            mask01 = mk.sb("b3_m01", [128, 128], F32)
            em.op("sync", lambda e: e.dma_start(out=mask01[:, :],
                                                in_=io["mask01"].ap()),
                  writes=[("b3m01",)], dom="qs")
            gprow = mk.sb("b3_gprow", [2 * c.NGC, (S // 128) * c.NKB], F32)
            em.op("sync", lambda e: e.dma_start(out=gprow[0:2 * c.NGC, :],
                                                in_=io["gpow"].ap()),
                  writes=[("b3gpr",)], dom="qs")
            gnsc = mk.sb("b3_gnsc", [128, 2 * c.NGC], F32)
            em.op("sync", lambda e: e.dma_start(out=gnsc[:, :],
                                                in_=io["gncols"].ap()),
                  writes=[("b3gnsc",)], dom="qs")
            rowg = mk.sb("b3_rowg", [2, 2 * 512], F32)
            sqg = mk.sb("b3_sqg", [128, 512], BF16)
            muv = mk.sb("b3_muv", [128, 512], F32)
            hselg = mk.sb("b3_hselg", [128, 2], BF16)
            em.op("vector", lambda e: e.memset(hselg[:, :], 0.0), writes=[("b3hsg",)])
            em.op("vector", lambda e: e.memset(hselg[0:64, 0:1], 1.0), writes=[("b3hsg",)])
            em.op("vector", lambda e: e.memset(hselg[64:128, 1:2], 1.0), writes=[("b3hsg",)])
            rstb = mk.sb("b3_rstb", [128, 512], F32)
            c_gt = mk.constf(1e-5, 2)
            c_gmh = mk.constf(-0.5, 2)

            def post_group(g, outT, kOT):
                for t0 in range(0, S, 512):
                    em.op("scalar", lambda e, t0=t0, outT=outT: e.activation(
                        sqg[:, 0:512], outT[:, t0:t0 + 512], AF.Square),
                        reads=[kOT], writes=[("b3sqg",)])
                    em.op("tensor", lambda e, t0=t0, outT=outT: e.matmul(
                        pb5[0:2, 0:512], hselg[:, 0:2], outT[:, t0:t0 + 512],
                        start=True, stop=True),
                        reads=[("b3hsg",), kOT], writes=[pk5])
                    em.op("vector", lambda e: e.tensor_copy(
                        rowg[0:2, 0:512], pb5[0:2, 0:512]),
                        reads=[pk5], writes=[("b3rs",)])
                    em.op("tensor", lambda e: e.matmul(
                        pb5[0:2, 0:512], hselg[:, 0:2], sqg[:, 0:512],
                        start=True, stop=True),
                        reads=[("b3hsg",), ("b3sqg",)], writes=[pk5])
                    em.op("vector", lambda e: e.tensor_copy(
                        rowg[0:2, 512:1024], pb5[0:2, 0:512]),
                        reads=[pk5], writes=[("b3rs",)])
                    em.op("vector", lambda e: e.tensor_scalar_mul(
                        rowg[0:2, 0:512], rowg[0:2, 0:512], 1.0 / 64.0),
                        reads=[("b3rs",)], writes=[("b3rs",)])
                    em.op("tensor", lambda e: e.matmul(
                        pb5[0:128, 0:512], mk.hsel32[0:2, :],
                        rowg[0:2, 0:512], start=True, stop=True),
                        reads=[("c_hs",), ("b3rs",)], writes=[pk5])
                    em.op("vector", lambda e: e.tensor_copy(
                        muv[:, 0:512], pb5[0:128, 0:512]),
                        reads=[pk5], writes=[("b3muv",)])
                    em.op("vector", lambda e: e.tensor_scalar_mul(
                        rowg[0:2, 512:1024], rowg[0:2, 512:1024], 1.0 / 64.0),
                        reads=[("b3rs",)], writes=[("b3rs",)])
                    em.op("vector", lambda e: e.scalar_tensor_tensor(
                        rowg[0:2, 0:512], rowg[0:2, 0:512], -1.0,
                        rowg[0:2, 0:512], ALU.mult, ALU.mult),
                        reads=[("b3rs",)], writes=[("b3rs",)])
                    em.op("vector", lambda e: e.tensor_add(
                        rowg[0:2, 0:512], rowg[0:2, 0:512],
                        rowg[0:2, 512:1024]),
                        reads=[("b3rs",)], writes=[("b3rs",)])
                    em.op("scalar", lambda e: e.activation(
                        rowg[0:2, 0:512], rowg[0:2, 0:512], AF.Ln, bias=c_gt),
                        reads=[("b3rs",), ("c_cp",)], writes=[("b3rs",)])
                    em.op("scalar", lambda e: e.activation(
                        rowg[0:2, 0:512], rowg[0:2, 0:512], AF.Exp,
                        scale=c_gmh),
                        reads=[("b3rs",), ("c_cp",)], writes=[("b3rs",)])
                    em.op("vector", lambda e, t0=t0, outT=outT: e.tensor_sub(
                        muv[:, 0:512], outT[:, t0:t0 + 512], muv[:, 0:512]),
                        reads=[kOT, ("b3muv",)], writes=[("b3muv",)])
                    em.op("tensor", lambda e: e.matmul(
                        pb5[0:128, 0:512], mk.hsel32[0:2, :],
                        rowg[0:2, 0:512], start=True, stop=True),
                        reads=[("c_hs",), ("b3rs",)], writes=[pk5])
                    em.op("vector", lambda e: e.tensor_copy(
                        rstb[:, 0:512], pb5[0:128, 0:512]),
                        reads=[pk5], writes=[("b3rstb",)])
                    em.op("vector", lambda e: e.tensor_mul(
                        muv[:, 0:512], muv[:, 0:512], rstb[:, 0:512]),
                        reads=[("b3muv",), ("b3rstb",)], writes=[("b3muv",)])
                    em.op("vector", lambda e, t0=t0, g=g, outT=outT:
                          e.tensor_scalar(
                              outT[:, t0:t0 + 512], muv[:, 0:512],
                              gnsc[:, 2 * g:2 * g + 1],
                              gnsc[:, 2 * g + 1:2 * g + 2],
                              ALU.mult, ALU.add),
                          reads=[("b3muv",), ("b3gnsc",)], writes=[kOT])

            mk.attn_headshard("b3a", ag_out, ag2_in, mode="retention",
                              mask01=(mask01, ("b3m01",)),
                              decay_dram=(io["qdec"], io["kdec"]),
                              gpow_src=(gprow, ("b3gpr",)),
                              post_group=post_group)
            mk.allgather(ag2_in, ag2_out, ("b3a", "ag2_in"), ("b3ag2o",))
        with mk.scope():
            g16 = mk.sb("b3_g16", [128, NCH * T], BF16)
            def evac_gate(fabs, fcw, t0, tw, pb, pk):
                kk = fabs // 128
                em.op("scalar", lambda e, kk=kk, t0=t0, tw=tw, fcw=fcw, pb=pb:
                      e.activation(g16[0:fcw, kk * T + t0:kk * T + t0 + tw],
                                   pb[0:fcw, 0:tw], AF.Silu),
                      reads=[pk], writes=[("b3g16", kk)])
            mk.proj("b3w1", io["w1b3"].ap(), D, D, xnb, xnkeyf, evac_gate,
                    wbufs=wbp)
            aob = mk.sb("b3_aob", [128, NCH * T], BF16)
            dst4 = aob[:, :].rearrange("p (k t) -> p k t", k=NCH)
            mk.dyn_load("sync", dst4, ag2_out, 0,
                        [[S, 128], [128 * S, NCH], [1, T]],
                        [("b3ag2o",)], [("b3aob", k) for k in range(NCH)],
                        base="tb")
            for k in range(NCH):
                em.op("vector", lambda e, k=k: e.tensor_add(
                    g16[:, k * T:(k + 1) * T], g16[:, k * T:(k + 1) * T],
                    aob[:, k * T:(k + 1) * T]),
                    reads=[("b3g16", k), ("b3aob", k)], writes=[("b3g16", k)])
            mk.proj("b3w2", io["w2b3"].ap(), D, D, g16,
                    lambda k: ("b3g16", k), resid_evac(mk, pro, "b3w2"),
                    wbufs=wbp)
        for k in range(NCH):
            em.op("gpsimd", lambda e, k=k: e.dma_start(
                out=xsp.ap()[k * 128:(k + 1) * 128, :],
                in_=x[:, k * T:(k + 1) * T]),
                reads=[xkeyf(k)], writes=[("b3xsp",)], dom="qg")
    emit_l3ffn(mk, io, xnsp, xsp)




def _dih(cfg):
    return cdiv(int(cfg.D * 4 * 2 / 3), 128) * 128


def emit_l3ffn(mk: MK, io, xnsp, xsp):
    c = mk.cfg
    em = mk.em
    T, D, S = c.T, c.D, c.S
    NCH = c.NCH
    TH = 1 + T
    x3 = mk.sb("b3f_x3", [128, NCH * TH], F32)
    x3keyf = lambda k: ("x3", k)
    for k in range(NCH):
        em.op("sync", lambda e, k=k: e.dma_start(
            out=x3[:, k * TH + 1:k * TH + 1 + T],
            in_=xnsp.ap()[k * 128:(k + 1) * 128, :]),
            reads=[("b3xnsp",)], writes=[x3keyf(k)], dom="qs")
    THB = 258 + T
    EXT = 256 + T
    NG3 = D // 128
    NW = T // 256
    DIH = _dih(c)
    NKH = DIH // 128
    JB = 3
    NDEP = 6
    pb5 = mk.p512[5]
    pk5 = ("p512", 5)
    O_QKV = 0
    O_RP = 3 * EXT
    O_P3 = O_RP + 512
    O_PT = O_P3 + 1024
    O_VT = O_PT + 512
    O_OT = O_VT + 512
    ACOLS = max(O_OT + T, JB * (T + D))
    arena = mk.sb("b3f_arena", [128, ACOLS], BF16)
    O_HJ = 0
    O_W2 = JB * T
    x3b = mk.sb("b3f_x3b", [128, NCH * THB], BF16)
    x3bkeyf = lambda k: ("x3b", k)
    wo_sb = mk.sb("b3f_wo", [128, D], BF16)
    wbp = mk.wstream_bufs(NCH, npar=1)
    w1sl = [[mk.sb(mk.fresh("w1s"), [128, 256], BF16) for _ in range(NCH)]
            for _ in range(2)]
    hv = mk.sb("b3f_hv", [128, 1], F32)
    xq_cos = mk.sb("b3f_xqc", [128, T], BF16)
    xq_sin = mk.sb("b3f_xqs", [128, T], BF16)
    xk_cos = mk.sb("b3f_xkc", [128, EXT], BF16)
    xk_sin = mk.sb("b3f_xks", [128, EXT], BF16)
    R3 = mk.sb("b3f_R3", [128, 128], BF16)
    maskL3 = mk.sb("b3f_mL", [128, 2 * 512], BF16)
    maskW0 = mk.sb("b3f_mW0", [128, 512], BF16)
    qscols = mk.sb("b3f_qsc", [128, 2 * NDEP], F32)
    gbcols = mk.sb("b3f_gbc", [128, 2 * NDEP * NCH], F32)
    b1cols = mk.sb("b3f_b1c", [128, 2 * NDEP * NKH], F32)
    b2cols = mk.sb("b3f_b2c", [128, NDEP * NCH], F32)
    for nm, dst in (("halo_valid", hv), ("xq_cos", xq_cos), ("xq_sin", xq_sin),
                    ("xk_cos", xk_cos), ("xk_sin", xk_sin), ("R3", R3),
                    ("maskL3", maskL3), ("maskW0", maskW0), ("qscols", qscols),
                    ("gbcols", gbcols), ("b1cols", b1cols), ("b2cols", b2cols)):
        em.op("sync", lambda e, nm=nm, dst=dst: e.dma_start(
            out=dst[:, :], in_=io[nm].ap()), writes=[("t_" + nm,)], dom="qs")
    kHV = ("t_halo_valid",)
    hsel3 = mk.sb("b3f_hs3", [128, 2], BF16)
    em.op("vector", lambda e: e.memset(hsel3[:, :], 0.0), writes=[("b3hs3",)])
    em.op("vector", lambda e: e.memset(hsel3[0:64, 0:1], 1.0), writes=[("b3hs3",)])
    em.op("vector", lambda e: e.memset(hsel3[64:128, 1:2], 1.0), writes=[("b3hs3",)])
    rn3 = mk.sb("b3f_rn3", [2, 512], F32)
    rbb = mk.sb("b3f_rbb", [128, 512], F32)
    rb2 = mk.sb("b3f_rb2", [128, 512], F32)
    tmp3 = mk.sb("b3f_tmp3", [128, 512], F32)
    abuf = [mk.sb(mk.fresh("b3fab"), [128, 512], BF16) for _ in range(2)]
    sums3 = mk.sb("b3f_sm3", [128, 8], F32)
    nm3 = mk.sb("b3f_nm3", [128, 4], F32)
    STW = min(512, T)
    frow = mk.sb("b3f_frow", [1, 3 * STW], F32)
    c_t24 = mk.constf(1e-24, 2)
    c_mh2 = mk.constf(-0.5, 2)
    c_8 = mk.constf(8.0, 128)
    c_eps1 = mk.constf(1e-5, 1)
    c_mh1 = mk.constf(-0.5, 1)
    HW = min(258, T)
    HO = 258 - HW
    agh_in = mk.dram("b3agh_in", [D, HW], F32)
    agh_out = mk.dram("b3agh_out", [2 * D, HW], F32)
    agf_in = mk.dram("b3agf_in", [D, 1], F32)
    agf_out = mk.dram("b3agf_out", [2 * D, 1], F32)
    for k in range(NCH):
        if HO:
            em.op("vector", lambda e, k=k: e.memset(
                x3b[:, k * THB:k * THB + HO], 0.0), writes=[x3bkeyf(k)])
        em.op("vector", lambda e, k=k: e.tensor_copy(
            x3b[:, k * THB + 258:k * THB + 258 + T],
            x3[:, k * TH + 1:k * TH + 1 + T]),
            reads=[x3keyf(k)], writes=[x3bkeyf(k)])
    stat_tiles = [(i * STW, STW) for i in range(T // STW)]
    ext_tsubs = [(2, 256)] + [(258 + t0, tw) for (t0, tw) in stat_tiles]
    kloc_tiles = [(0, 256)] + [(256 + t0, tw) for (t0, tw) in stat_tiles]
    qloc_tiles = [(256 + t0, tw) for (t0, tw) in stat_tiles]

    for li in range(NDEP):
        for k in range(NCH):
            em.op("gpsimd", lambda e, k=k: e.dma_start(
                out=agh_in.ap()[k * 128:(k + 1) * 128, :],
                in_=x3[:, k * TH + 1 + T - HW:k * TH + 1 + T]),
                reads=[x3keyf(k)], writes=[("aghi",)], dom="qg")
        mk.allgather(agh_in, agh_out, ("aghi",), ("agho",))
        for k in range(NCH):
            em.op("gpsimd", lambda e, k=k: e.dma_start(
                out=x3b[:, k * THB + HO:k * THB + 258],
                in_=agh_out.ap()[k * 128:(k + 1) * 128, :]),
                reads=[("agho",)], writes=[x3bkeyf(k)], dom="qg")
            em.op("vector", lambda e, k=k: e.tensor_scalar(
                x3b[:, k * THB + HO:k * THB + 258],
                x3b[:, k * THB + HO:k * THB + 258],
                hv[:, 0:1], None, ALU.mult),
                reads=[x3bkeyf(k), kHV], writes=[x3bkeyf(k)])
        for g in range(NG3):
            def evac_qkv3(fabs, fcw, t0, tw, pb, pk):
                sec = fabs // 128
                em.op("vector", lambda e, sec=sec, t0=t0, tw=tw, fcw=fcw, pb=pb:
                      e.tensor_copy(
                          arena[0:fcw, O_QKV + sec * EXT + t0 - 2:
                                O_QKV + sec * EXT + t0 - 2 + tw],
                          pb[0:fcw, 0:tw]),
                      reads=[pk], writes=[("a_qkv", sec)])
            mk.proj(f"b3f_qkv{li}_{g}",
                    io["WqkvR"].ap()[li * D:(li + 1) * D, g * 384:(g + 1) * 384],
                    D, 384, x3b, x3bkeyf, evac_qkv3, wbufs=wbp,
                    tsubs=ext_tsubs,
                    xoff_fn=lambda k, t0: k * THB + t0 - (1 if k >= NCH // 2 else 0))
            for sec in (0, 1):
                tiles = qloc_tiles if sec == 0 else kloc_tiles
                toff = 256 if sec == 0 else 0
                ctab, ck = (xq_cos, ("t_xq_cos",)) if sec == 0 else (xk_cos, ("t_xk_cos",))
                stab, sk = (xq_sin, ("t_xq_sin",)) if sec == 0 else (xk_sin, ("t_xk_sin",))
                for (t0, tw) in tiles:
                    q0 = O_QKV + sec * EXT + t0
                    em.op("scalar", lambda e, q0=q0, tw=tw: e.activation(
                        arena[:, O_RP:O_RP + tw], arena[:, q0:q0 + tw], AF.Square),
                        reads=[("a_qkv", sec)], writes=[("a_rp",)])
                    em.op("tensor", lambda e, tw=tw: e.matmul(
                        pb5[0:2, 0:tw], hsel3[:, 0:2], arena[:, O_RP:O_RP + tw],
                        start=True, stop=True),
                        reads=[("b3hs3",), ("a_rp",)], writes=[pk5])
                    em.op("vector", lambda e, tw=tw: e.tensor_copy(
                        rn3[0:2, 0:tw], pb5[0:2, 0:tw]), reads=[pk5],
                        writes=[("b3rn",)])
                    em.op("scalar", lambda e, tw=tw: e.activation(
                        rn3[0:2, 0:tw], rn3[0:2, 0:tw], AF.Ln, bias=c_t24),
                        reads=[("b3rn",), ("c_cp",)], writes=[("b3rn",)])
                    em.op("scalar", lambda e, tw=tw: e.activation(
                        rn3[0:2, 0:tw], rn3[0:2, 0:tw], AF.Exp, scale=c_mh2),
                        reads=[("b3rn",), ("c_cp",)], writes=[("b3rn",)])
                    em.op("tensor", lambda e, tw=tw: e.matmul(
                        pb5[0:128, 0:tw], mk.hsel32[0:2, :], rn3[0:2, 0:tw],
                        start=True, stop=True),
                        reads=[("c_hs",), ("b3rn",)], writes=[pk5])
                    em.op("vector", lambda e, tw=tw, sec=sec, li=li: e.tensor_scalar(
                        rbb[:, 0:tw], pb5[0:128, 0:tw],
                        qscols[:, 2 * li + sec:2 * li + sec + 1], None, ALU.mult),
                        reads=[pk5, ("t_qscols",)], writes=[("b3rbb",)])
                    em.op("vector", lambda e, q0=q0, tw=tw, sec=sec: e.tensor_mul(
                        arena[:, q0:q0 + tw], arena[:, q0:q0 + tw], rbb[:, 0:tw]),
                        reads=[("a_qkv", sec), ("b3rbb",)], writes=[("a_qkv", sec)])
                    em.op("tensor", lambda e, q0=q0, tw=tw, sec=sec: e.matmul(
                        pb5[0:128, 0:tw], R3[:, :], arena[:, q0:q0 + tw],
                        start=True, stop=True),
                        reads=[("t_R3",), ("a_qkv", sec)], writes=[pk5])
                    em.op("vector", lambda e, t0=t0, tw=tw, toff=toff, stab=stab,
                          sk=sk: e.tensor_mul(
                              arena[:, O_RP:O_RP + tw], pb5[0:128, 0:tw],
                              stab[:, t0 - toff:t0 - toff + tw]),
                          reads=[pk5, sk], writes=[("a_rp",)])
                    em.op("vector", lambda e, q0=q0, tw=tw, t0=t0, toff=toff,
                          ctab=ctab, ck=ck, sec=sec: e.tensor_mul(
                              arena[:, q0:q0 + tw], arena[:, q0:q0 + tw],
                              ctab[:, t0 - toff:t0 - toff + tw]),
                          reads=[("a_qkv", sec), ck], writes=[("a_qkv", sec)])
                    em.op("vector", lambda e, q0=q0, tw=tw, sec=sec: e.tensor_add(
                        arena[:, q0:q0 + tw], arena[:, q0:q0 + tw],
                        arena[:, O_RP:O_RP + tw]),
                        reads=[("a_qkv", sec), ("a_rp",)], writes=[("a_qkv", sec)])
            for w in range(NW):
                kb0 = w * 256
                kas = list(range(4)) if w == 0 else [w * 2 + 2, w * 2 + 3]
                for ka in kas:
                    slot = ka % 4
                    for h in range(2):
                        mk.transpose(
                            arena[64 * h:64 * h + 64,
                                  O_QKV + 2 * EXT + ka * 128:
                                  O_QKV + 2 * EXT + (ka + 1) * 128],
                            64, 128, [("a_qkv", 2)], arena,
                            O_VT + (h * 4 + slot) * 64,
                            ("a_vT", h, slot), pti=ka % 2, bp=64 * h)
                for qt2 in range(2):
                    ql0 = O_QKV + 256 + w * 256 + qt2 * 128
                    for h in range(2):
                        si = h * 2 + qt2
                        Sb, kS = mk.p512[si], ("p512", si)
                        scol = qt2 * 2 + h
                        em.op("tensor", lambda e, h=h, ql0=ql0, kb0=kb0, Sb=Sb:
                              e.matmul(
                                  Sb[0:128, 0:512],
                                  arena[64 * h:64 * h + 64, ql0:ql0 + 128],
                                  arena[64 * h:64 * h + 64,
                                        O_QKV + EXT + kb0:O_QKV + EXT + kb0 + 512],
                                  start=True, stop=True,
                                  tile_position=(64 * h, 0)),
                              reads=[("a_qkv", 0), ("a_qkv", 1)], writes=[kS])
                        em.op("vector", lambda e, qt2=qt2, Sb=Sb: e.tensor_add(
                            Sb[0:128, 0:512], maskL3[:, qt2 * 512:(qt2 + 1) * 512],
                            Sb[0:128, 0:512]),
                            reads=[("t_maskL3",), kS], writes=[kS])
                        if w == 0:
                            em.op("vector", lambda e, Sb=Sb: e.tensor_add(
                                Sb[0:128, 0:512], maskW0[:, 0:512],
                                Sb[0:128, 0:512]),
                                reads=[("t_maskW0",), kS], writes=[kS])
                        em.op("vector", lambda e, scol=scol, Sb=Sb: e.tensor_reduce(
                            sums3[:, scol:scol + 1], Sb[0:128, 0:512],
                            AX.X, ALU.max),
                            reads=[kS], writes=[("b3mx",)])
                        em.op("vector", lambda e, scol=scol: e.tensor_scalar_mul(
                            nm3[:, scol:scol + 1], sums3[:, scol:scol + 1], -8.0),
                            reads=[("b3mx",)], writes=[("b3nm",)])
                        em.op("scalar", lambda e, h=h, scol=scol, Sb=Sb:
                              e.activation(
                                  arena[:, O_P3 + h * 512:O_P3 + h * 512 + 512],
                                  Sb[0:128, 0:512], AF.Exp,
                                  bias=nm3[:, scol:scol + 1], scale=c_8,
                                  accum_out=sums3[:, 4 + scol:4 + scol + 1]),
                              reads=[kS, ("b3nm",), ("c_cp",)],
                              writes=[("a_P", h), ("b3sm",)])
                        for kt in range(4):
                            ka = w * 2 + kt
                            slot = ka % 4
                            pti = (kt + h) % 2
                            ptb, pkT = mk.pT[pti], ("pT", pti)
                            em.op("tensor", lambda e, h=h, kt=kt, ptb=ptb:
                                  e.transpose(
                                      ptb[0:128, 0:128],
                                      arena[:, O_P3 + h * 512 + kt * 128:
                                            O_P3 + h * 512 + (kt + 1) * 128],
                                      mk.ident[:, :]),
                                  reads=[("a_P", h), ("c_id",)], writes=[pkT])
                            pt_off = O_PT + (h * 2 + kt % 2) * 128
                            em.op("vector", lambda e, ptb=ptb, pt_off=pt_off:
                                  e.tensor_copy(arena[:, pt_off:pt_off + 128],
                                                ptb[0:128, 0:128]),
                                  reads=[pkT], writes=[("a_PT", h, kt % 2)])
                            em.op("tensor", lambda e, h=h, kt=kt, qt2=qt2,
                                  slot=slot, pt_off=pt_off: e.matmul(
                                      mk.p512[4][64 * h:64 * h + 64,
                                                 qt2 * 128:qt2 * 128 + 128],
                                      arena[:, O_VT + (h * 4 + slot) * 64:
                                            O_VT + (h * 4 + slot) * 64 + 64],
                                      arena[:, pt_off:pt_off + 128],
                                      start=(kt == 0), stop=(kt == 3),
                                      tile_position=(0, 64 * h)),
                                  reads=[("a_vT", h, slot), ("a_PT", h, kt % 2)],
                                  writes=[("p512", 4)])
                em.op("vector", lambda e: e.reciprocal(sums3[:, 4:8],
                                                       sums3[:, 4:8]),
                      reads=[("b3sm",)], writes=[("b3sm",)])
                for qt2 in range(2):
                    em.op("tensor", lambda e, qt2=qt2: e.transpose(
                        pb5[0:2, qt2 * 128:qt2 * 128 + 128],
                        sums3[:, 4 + qt2 * 2:4 + qt2 * 2 + 2],
                        mk.ident32[:, :]),
                        reads=[("b3sm",), ("c_id32",)], writes=[pk5])
                    em.op("vector", lambda e, qt2=qt2: e.tensor_copy(
                        rn3[0:2, qt2 * 128:qt2 * 128 + 128],
                        pb5[0:2, qt2 * 128:qt2 * 128 + 128]),
                        reads=[pk5], writes=[("b3rn",)])
                for qt2 in range(2):
                    em.op("tensor", lambda e, qt2=qt2: e.matmul(
                        pb5[0:128, qt2 * 128:qt2 * 128 + 128], mk.hsel32[0:2, :],
                        rn3[0:2, qt2 * 128:qt2 * 128 + 128],
                        start=True, stop=True),
                        reads=[("c_hs",), ("b3rn",)], writes=[pk5])
                em.op("vector", lambda e: e.tensor_copy(
                    rbb[:, 0:256], pb5[0:128, 0:256]),
                    reads=[pk5], writes=[("b3rbb",)])
                em.op("vector", lambda e, w=w: e.tensor_mul(
                    arena[:, O_OT + w * 256:O_OT + w * 256 + 256],
                    mk.p512[4][0:128, 0:256], rbb[:, 0:256]),
                    reads=[("p512", 4), ("b3rbb",)], writes=[("a_oT",)])
            em.op("gpsimd", lambda e, li=li, g=g: e.dma_start(
                out=wo_sb[:, :],
                in_=io["Wo3"].ap()[li * D + g * 128:li * D + (g + 1) * 128, :]),
                writes=[("b3wo",)], dom="qg")
            for kk in range(NCH):
                for ti, (t0, tw) in enumerate(stat_tiles):
                    pi = (kk * len(stat_tiles) + ti) % 2
                    pbF, pkF = mk.p512[pi], ("p512", pi)
                    em.op("tensor", lambda e, kk=kk, t0=t0, tw=tw, pbF=pbF:
                          e.matmul(pbF[0:128, 0:tw], wo_sb[:, kk * 128:(kk + 1) * 128],
                                   arena[:, O_OT + t0:O_OT + t0 + tw],
                                   start=True, stop=True),
                          reads=[("b3wo",), ("a_oT",)], writes=[pkF])
                    em.op("vector", lambda e, kk=kk, t0=t0, tw=tw, pbF=pbF:
                          e.tensor_add(
                              x3[:, kk * TH + 1 + t0:kk * TH + 1 + t0 + tw],
                              x3[:, kk * TH + 1 + t0:kk * TH + 1 + t0 + tw],
                              pbF[0:128, 0:tw]),
                          reads=[x3keyf(kk), pkF], writes=[x3keyf(kk)])
        for k in range(NCH):
            em.op("gpsimd", lambda e, k=k: e.dma_start(
                out=agf_in.ap()[k * 128:(k + 1) * 128, :],
                in_=x3[:, k * TH + T:k * TH + T + 1]),
                reads=[x3keyf(k)], writes=[("agfi",)], dom="qg")
        mk.allgather(agf_in, agf_out, ("agfi",), ("agfo",))
        for k in range(NCH):
            em.op("sync", lambda e, k=k: e.dma_start(
                out=x3[:, k * TH:k * TH + 1],
                in_=agf_out.ap()[k * 128:(k + 1) * 128, :]),
                reads=[("agfo",)], writes=[x3keyf(k)], dom="qs")
            em.op("vector", lambda e, k=k: e.tensor_scalar(
                x3[:, k * TH:k * TH + 1], x3[:, k * TH:k * TH + 1],
                hv[:, 0:1], None, ALU.mult),
                reads=[x3keyf(k), kHV], writes=[x3keyf(k)])
        em.fence()
        emit_l3ffn_mlp(mk, io, li, x3, x3keyf, x3b, x3bkeyf, arena,
                       dict(wbp=wbp, w1sl=w1sl, abuf=abuf, frow=frow,
                            rbb=rbb, rb2=rb2, tmp3=tmp3,
                            stat_tiles=stat_tiles, STW=STW, JB=JB,
                            NKH=NKH, DIH=DIH, O_HJ=O_HJ, O_W2=O_W2,
                            c_eps1=c_eps1, c_mh1=c_mh1,
                            gbcols=gbcols, b1cols=b1cols, b2cols=b2cols))
        for k in range(NCH):
            em.op("vector", lambda e, k=k: e.tensor_copy(
                x3b[:, k * THB + 258:k * THB + 258 + T],
                x3[:, k * TH + 1:k * TH + 1 + T]),
                reads=[x3keyf(k)], writes=[x3bkeyf(k)])
        em.fence()
    for k in range(NCH):
        for ti, (t0, tw) in enumerate(stat_tiles):
            bb = rbb if ti % 2 == 0 else rb2
            kb = ("b3rbb",) if ti % 2 == 0 else ("b3rb2",)
            em.op("sync", lambda e, k=k, t0=t0, tw=tw, bb=bb: e.dma_start(
                out=bb[:, 0:tw], in_=xsp.ap()[k * 128:(k + 1) * 128, t0:t0 + tw]),
                reads=[("b3xsp",)], writes=[kb], dom="qs")
            em.op("vector", lambda e, k=k, t0=t0, tw=tw, bb=bb: e.tensor_add(
                bb[:, 0:tw], bb[:, 0:tw],
                x3[:, k * TH + 1 + t0:k * TH + 1 + t0 + tw]),
                reads=[kb, x3keyf(k)], writes=[kb])
            em.op("gpsimd", lambda e, k=k, t0=t0, tw=tw, bb=bb: e.dma_start(
                out=io["x_out"].ap()[k * 128:(k + 1) * 128, t0:t0 + tw],
                in_=bb[:, 0:tw]),
                reads=[kb], writes=[("x_out",)], dom="qg")
    em.finalize_output_wait("sync")


def emit_l3ffn_mlp(mk: MK, io, li, x3, x3keyf, x3b, x3bkeyf, arena, sh):
    c = mk.cfg
    em = mk.em
    T, D = c.T, c.D
    NCH = c.NCH
    TH = 1 + T
    THB = 258 + T
    JB, NKH, DIH = sh["JB"], sh["NKH"], sh["DIH"]
    O_HJ, O_W2 = sh["O_HJ"], sh["O_W2"]
    STW = sh["STW"]
    stat_tiles = sh["stat_tiles"]
    frow, rbb, rb2, tmp3 = sh["frow"], sh["rbb"], sh["rb2"], sh["tmp3"]
    abuf, w1sl, wbp = sh["abuf"], sh["w1sl"], sh["wbp"]
    gbcols, b1cols, b2cols = sh["gbcols"], sh["b1cols"], sh["b2cols"]
    stage = wbp["stage"]
    Dv = float(D)
    for (t0, tw) in stat_tiles:
        pbS, pkS = mk.pbank()
        pbQ, pkQ = mk.pbank()
        for k in range(NCH):
            sOFF = k * TH + 1 + t0 - (1 if k >= NCH // 2 else 0)
            sqb = rbb if k % 2 == 0 else rb2
            kq = ("b3rbb",) if k % 2 == 0 else ("b3rb2",)
            em.op("scalar", lambda e, sOFF=sOFF, tw=tw, sqb=sqb: e.activation(
                sqb[:, 0:tw], x3[:, sOFF:sOFF + tw], AF.Square),
                reads=[x3keyf(k)], writes=[kq])
            em.op("tensor", lambda e, sOFF=sOFF, tw=tw, k=k, pbS=pbS: e.matmul(
                pbS[0:1, 0:tw], mk.ones_col[:, 0:1], x3[:, sOFF:sOFF + tw],
                start=(k == 0), stop=(k == NCH - 1)),
                reads=[("c_oc",), x3keyf(k)], writes=[pkS])
            em.op("tensor", lambda e, tw=tw, k=k, pbQ=pbQ, sqb=sqb: e.matmul(
                pbQ[0:1, 0:tw], mk.ones_col[:, 0:1], sqb[:, 0:tw],
                start=(k == 0), stop=(k == NCH - 1)),
                reads=[("c_oc",), kq], writes=[pkQ])
        em.op("vector", lambda e, tw=tw, pbS=pbS: e.tensor_scalar_mul(
            frow[0:1, 0:tw], pbS[0:1, 0:tw], 1.0 / Dv),
            reads=[pkS], writes=[("b3fr",)])
        em.op("vector", lambda e, tw=tw, pbQ=pbQ: e.tensor_scalar_mul(
            frow[0:1, STW:STW + tw], pbQ[0:1, 0:tw], 1.0 / Dv),
            reads=[pkQ], writes=[("b3fr",)])
        em.op("vector", lambda e, tw=tw: e.scalar_tensor_tensor(
            frow[0:1, 2 * STW:2 * STW + tw], frow[0:1, 0:tw], -1.0,
            frow[0:1, 0:tw], ALU.mult, ALU.mult),
            reads=[("b3fr",)], writes=[("b3fr",)])
        em.op("vector", lambda e, tw=tw: e.tensor_add(
            frow[0:1, 2 * STW:2 * STW + tw], frow[0:1, 2 * STW:2 * STW + tw],
            frow[0:1, STW:STW + tw]),
            reads=[("b3fr",)], writes=[("b3fr",)])
        em.op("scalar", lambda e, tw=tw: e.activation(
            frow[0:1, 2 * STW:2 * STW + tw], frow[0:1, 2 * STW:2 * STW + tw],
            AF.Ln, bias=sh["c_eps1"]),
            reads=[("b3fr",), ("c_cp",)], writes=[("b3fr",)])
        em.op("scalar", lambda e, tw=tw: e.activation(
            frow[0:1, 2 * STW:2 * STW + tw], frow[0:1, 2 * STW:2 * STW + tw],
            AF.Exp, scale=sh["c_mh1"]),
            reads=[("b3fr",), ("c_cp",)], writes=[("b3fr",)])
        mk.bcast_row(frow, tw, ("b3fr",), rbb, ("b3rbb",), col_off=0)
        mk.bcast_row(frow, tw, ("b3fr",), rb2, ("b3rb2",), col_off=2 * STW)
        for k in range(NCH):
            sOFF = k * TH + 1 + t0 - (1 if k >= NCH // 2 else 0)
            gc = (li * NCH + k) * 2
            em.op("vector", lambda e, sOFF=sOFF, tw=tw: e.tensor_sub(
                tmp3[:, 0:tw], x3[:, sOFF:sOFF + tw], rbb[:, 0:tw]),
                reads=[x3keyf(k), ("b3rbb",)], writes=[("b3tmp3",)])
            em.op("vector", lambda e, tw=tw: e.tensor_mul(
                tmp3[:, 0:tw], tmp3[:, 0:tw], rb2[:, 0:tw]),
                reads=[("b3tmp3",), ("b3rb2",)], writes=[("b3tmp3",)])
            em.op("gpsimd", lambda e, k=k, t0=t0, tw=tw, gc=gc: e.tensor_scalar(
                x3b[:, k * THB + 258 + t0:k * THB + 258 + t0 + tw],
                tmp3[:, 0:tw], gbcols[:, gc:gc + 1], gbcols[:, gc + 1:gc + 2],
                ALU.mult, ALU.add),
                reads=[("b3tmp3",), ("t_gbcols",)], writes=[x3bkeyf(k)])
    jblocks = [list(range(j0, min(j0 + JB, NKH))) for j0 in range(0, NKH, JB)]
    for js in jblocks:
        for ji, j in enumerate(js):
            jp = j % 2
            for k in range(NCH):
                sp = k % 2
                em.op("sync", lambda e, li=li, j=j, k=k, sp=sp: e.dma_start(
                    out=stage[sp][0:128, 0:256],
                    in_=io["W1p"].ap()[li * D + k * 128:li * D + (k + 1) * 128,
                                       j * 256:(j + 1) * 256]),
                    writes=[("w1stg", sp)], dom="qs")
                em.op("gpsimd", lambda e, k=k, sp=sp, jp=jp: e.tensor_copy(
                    w1sl[jp][k][:, 0:256], stage[sp][0:128, 0:256]),
                    reads=[("w1stg", sp)], writes=[("w1w", jp, k)])
            bcol = (li * NKH + j) * 2
            for ti, (t0, tw) in enumerate(stat_tiles):
                pbA, pkA = mk.pbank()
                pbG, pkG = mk.pbank()
                for k in range(NCH):
                    em.op("tensor", lambda e, k=k, jp=jp, t0=t0, tw=tw, pbA=pbA:
                          e.matmul(pbA[0:128, 0:tw], w1sl[jp][k][:, 0:128],
                                   x3b[:, k * THB + 258 + t0:
                                       k * THB + 258 + t0 + tw],
                                   start=(k == 0), stop=(k == NCH - 1)),
                          reads=[("w1w", jp, k), x3bkeyf(k)], writes=[pkA])
                    em.op("tensor", lambda e, k=k, jp=jp, t0=t0, tw=tw, pbG=pbG:
                          e.matmul(pbG[0:128, 0:tw], w1sl[jp][k][:, 128:256],
                                   x3b[:, k * THB + 258 + t0:
                                       k * THB + 258 + t0 + tw],
                                   start=(k == 0), stop=(k == NCH - 1)),
                          reads=[("w1w", jp, k), x3bkeyf(k)], writes=[pkG])
                pa = ti % 2
                em.op("vector", lambda e, tw=tw, bcol=bcol, pa=pa, pbA=pbA:
                      e.tensor_scalar(abuf[pa][:, 0:tw], pbA[0:128, 0:tw],
                                      b1cols[:, bcol:bcol + 1], None,
                                      ALU.add),
                      reads=[pkA, ("t_b1cols",)], writes=[("b3ab", pa)])
                em.op("scalar", lambda e, ji=ji, t0=t0, tw=tw, bcol=bcol,
                      pbG=pbG: e.activation(
                          arena[:, O_HJ + ji * T + t0:O_HJ + ji * T + t0 + tw],
                          pbG[0:128, 0:tw], AF.Gelu,
                          bias=b1cols[:, bcol + 1:bcol + 2]),
                      reads=[pkG, ("t_b1cols",)], writes=[("a_hj", ji)])
                em.op("vector", lambda e, ji=ji, t0=t0, tw=tw, pa=pa:
                      e.tensor_mul(
                          arena[:, O_HJ + ji * T + t0:O_HJ + ji * T + t0 + tw],
                          arena[:, O_HJ + ji * T + t0:O_HJ + ji * T + t0 + tw],
                          abuf[pa][:, 0:tw]),
                      reads=[("a_hj", ji), ("b3ab", pa)], writes=[("a_hj", ji)])
            em.op("gpsimd", lambda e, li=li, j=j, ji=ji: e.dma_start(
                out=arena[:, O_W2 + ji * D:O_W2 + (ji + 1) * D],
                in_=io["W2p"].ap()[li * DIH + j * 128:li * DIH + (j + 1) * 128, :]),
                writes=[("a_w2", ji)], dom="qg")
        for kk in range(NCH):
            for (t0, tw) in stat_tiles:
                pb, pk = mk.pbank()
                for ji in range(len(js)):
                    em.op("tensor", lambda e, kk=kk, ji=ji, t0=t0, tw=tw, pb=pb,
                          nj=len(js): e.matmul(
                              pb[0:128, 0:tw],
                              arena[:, O_W2 + ji * D + kk * 128:
                                    O_W2 + ji * D + (kk + 1) * 128],
                              arena[:, O_HJ + ji * T + t0:O_HJ + ji * T + t0 + tw],
                              start=(ji == 0), stop=(ji == nj - 1)),
                          reads=[("a_w2", ji), ("a_hj", ji)], writes=[pk])
                em.op("vector", lambda e, kk=kk, t0=t0, tw=tw, pb=pb:
                      e.tensor_add(
                          x3[:, kk * TH + 1 + t0:kk * TH + 1 + t0 + tw],
                          x3[:, kk * TH + 1 + t0:kk * TH + 1 + t0 + tw],
                          pb[0:128, 0:tw]),
                      reads=[pk, x3keyf(kk)], writes=[x3keyf(kk)])
    for kk in range(NCH):
        for (t0, tw) in stat_tiles:
            em.op("vector", lambda e, kk=kk, t0=t0, tw=tw, li=li: e.tensor_scalar(
                x3[:, kk * TH + 1 + t0:kk * TH + 1 + t0 + tw],
                x3[:, kk * TH + 1 + t0:kk * TH + 1 + t0 + tw],
                b2cols[:, li * NCH + kk:li * NCH + kk + 1], None, ALU.add),
                reads=[x3keyf(kk), ("t_b2cols",)], writes=[x3keyf(kk)])


BLOCK_INPUTS[3] = lambda cfg: [
    ("wqBD", [cfg.NCH * 128, 128]), ("wkBD", [cfg.NCH * 128, 128]),
    ("wvBD", [cfg.NCH * 128, 128]),
    ("w1b3", [cfg.D, cfg.D]), ("w2b3", [cfg.D, cfg.D]),
    ("mask01", [128, 128]),
    ("gpow", [2 * cfg.NGC, (cfg.S // 128) * cfg.NKB]),
    ("qdec", [cfg.NGC * 128, cfg.S], BF16),
    ("kdec", [cfg.NGC * 128, cfg.S], BF16),
    ("gncols", [128, 2 * cfg.NGC]),
    ("WqkvR", [6 * cfg.D, (cfg.D // 128) * 384]),
    ("Wo3", [6 * cfg.D, cfg.D]),
    ("W1p", [6 * cfg.D, 2 * _dih(cfg)]),
    ("W2p", [6 * _dih(cfg), cfg.D]),
    ("qscols", [128, 12]), ("gbcols", [128, 12 * cfg.NCH]),
    ("b1cols", [128, 12 * (_dih(cfg) // 128)]), ("b2cols", [128, 6 * cfg.NCH]),
    ("halo_valid", [128, 1]),
    ("maskW0", [128, 512], BF16),
    ("maskL3", [128, 1024], BF16),
    ("R3", [128, 128], BF16),
    ("xq_cos", [128, cfg.T], BF16), ("xq_sin", [128, cfg.T], BF16),
    ("xk_cos", [128, 256 + cfg.T], BF16), ("xk_sin", [128, 256 + cfg.T], BF16),
]

BLOCK_EMITTERS[3] = emit_block3


def _gammas_ret(H):
    lin = np.linspace(np.log(1.0 / 512.0), np.log(1.0 / 32.0), H)
    return 1.0 - np.exp(lin)


def _bd_strips(w):
    H = w.shape[0]
    out = np.zeros((H // 2 * 128, 128), np.float32)
    for k in range(H // 2):
        out[k * 128:k * 128 + 64, 0:64] = w[2 * k]
        out[k * 128 + 64:(k + 1) * 128, 64:128] = w[2 * k + 1]
    return np.ascontiguousarray(out)


def _mask01_np():
    p = np.arange(128)[:, None]
    f = np.arange(128)[None, :]
    return np.where(p >= f, 1.0, 0.0).astype(np.float32)


def _maskL3_np():
    import ml_dtypes
    p = np.arange(128)[:, None]
    f = np.arange(512)[None, :]
    out = np.zeros((128, 1024), np.float32)
    for qt2 in range(2):
        il = qt2 * 128 + p
        valid = (il <= f) & (f <= il + 256)
        out[:, qt2 * 512:(qt2 + 1) * 512] = np.where(valid, 0.0, -1e9)
    return out.astype(ml_dtypes.bfloat16)


def _xpos_tables(cfg, core):
    import ml_dtypes
    S, T = cfg.S, cfg.T
    HD = 64
    tb = (core % 2) * T
    half = np.arange(0, HD, 2, dtype=np.float64)
    inv = 1.0 / 10000.0 ** (half / HD)
    base = (half + 0.4 * HD) / (1.4 * HD)

    def tabs(pos):
        fr = pos[:, None] * inv[None, :]
        fr = np.concatenate([fr, fr], -1)
        power = (pos - S // 2) / (256 // 2)
        sc = base[None, :] ** power[:, None]
        sc = np.concatenate([sc, sc], -1)
        return np.cos(fr), np.sin(fr), sc
    dd = np.arange(128) % HD
    posq = np.arange(tb, tb + T, dtype=np.float64)
    cq, sq, scq = tabs(posq)
    posk = np.clip(np.arange(tb - 256, tb + T, dtype=np.float64), 0, None)
    ck, sk, sck = tabs(posk)
    bf = ml_dtypes.bfloat16
    return (np.ascontiguousarray((cq * scq)[:, dd].T).astype(bf),
            np.ascontiguousarray((sq * scq)[:, dd].T).astype(bf),
            np.ascontiguousarray((ck / sck)[:, dd].T).astype(bf),
            np.ascontiguousarray((sk / sck)[:, dd].T).astype(bf))


def _percore_block3(cfg, core, inputs=None):
    import ml_dtypes
    bf = ml_dtypes.bfloat16
    S, T, NGC, NKB = cfg.S, cfg.T, cfg.NGC, cfg.NKB
    parity = core % 2
    g_all = _gammas_ret(cfg.H)
    heads = parity * cfg.HC + np.arange(cfg.HC)
    lg = np.log(g_all[heads])
    t = np.arange(S, dtype=np.float64)
    qdec = np.zeros((NGC * 128, S))
    kdec = np.zeros((NGC * 128, S))
    NQT128 = S // 128
    gpow = np.zeros((2 * NGC, NQT128 * NKB))
    for g in range(NGC):
        for hh in range(2):
            lgh = lg[g * 2 + hh]
            rows = slice(g * 128 + hh * 64, g * 128 + (hh + 1) * 64)
            qdec[rows, :] = np.exp(lgh * (t % 128))[None, :]
            kdec[rows, :] = np.exp(lgh * (-(t % 512)))[None, :]
            qt = np.arange(NQT128)[:, None]
            jb = np.arange(NKB)[None, :]
            ex = np.clip(lgh * (128.0 * qt - 512.0 * jb), -300.0, 80.0)
            gpow[g * 2 + hh, :] = np.exp(ex).reshape(-1)
    gnsc = np.asarray(inputs["L3_attn_gn_scale"], np.float64).reshape(-1)
    gnb = np.asarray(inputs["L3_attn_gn_bias"], np.float64).reshape(-1)
    gncols = np.zeros((128, 2 * NGC), np.float32)
    for g in range(NGC):
        for hh in range(2):
            ch = heads[g * 2 + hh] * 64 + np.arange(64)
            gncols[hh * 64:(hh + 1) * 64, 2 * g] = gnsc[ch]
            gncols[hh * 64:(hh + 1) * 64, 2 * g + 1] = gnb[ch]
    xqc, xqs, xkc, xks = _xpos_tables(cfg, core)
    mW0 = np.zeros((128, 512), np.float32)
    if parity == 0:
        mW0[:, 0:256] = -1e9
    hvv = np.full((128, 1), float(parity), np.float32)
    return {
        "gpow": gpow.astype(np.float32),
        "qdec": qdec.astype(bf), "kdec": kdec.astype(bf),
        "gncols": gncols,
        "halo_valid": hvv,
        "maskW0": mW0.astype(bf),
        "xq_cos": xqc, "xq_sin": xqs, "xk_cos": xkc, "xk_sin": xks,
    }


BLOCK_PERCORE[3] = _percore_block3


def _prep_block3(cfg, inputs):
    import ml_dtypes
    f = np.float32
    D = cfg.D
    NCH = cfg.NCH
    DIH = _dih(cfg)
    NKH = DIH // 128
    DI = int(D * 4 * 2 / 3)
    NG3 = D // 128
    Wqkv = np.asarray(inputs["L3_ffn_Wqkv"], f)
    WqkvR = np.zeros((6 * D, NG3 * 384), f)
    for li in range(6):
        for g in range(NG3):
            for sec in range(3):
                WqkvR[li * D:(li + 1) * D, g * 384 + sec * 128:
                      g * 384 + (sec + 1) * 128] = \
                    Wqkv[li][:, sec * D + g * 128:sec * D + (g + 1) * 128]
    W1 = np.asarray(inputs["L3_ffn_W1"], f)
    b1 = np.asarray(inputs["L3_ffn_b1"], f)
    W1p = np.zeros((6 * D, 2 * DIH), f)
    b1cols = np.zeros((128, 12 * NKH), f)
    for li in range(6):
        a = np.zeros((D, DIH), f)
        a[:, :DI] = W1[li][:, :DI]
        g_ = np.zeros((D, DIH), f)
        g_[:, :DI] = W1[li][:, DI:]
        W1p[li * D:(li + 1) * D] = np.stack(
            [a.reshape(D, NKH, 128), g_.reshape(D, NKH, 128)],
            axis=2).reshape(D, 2 * DIH)
        ab = np.zeros(DIH, f)
        ab[:DI] = b1[li][:DI]
        gb = np.zeros(DIH, f)
        gb[:DI] = b1[li][DI:]
        for j in range(NKH):
            b1cols[:, (li * NKH + j) * 2] = ab[j * 128:(j + 1) * 128]
            b1cols[:, (li * NKH + j) * 2 + 1] = gb[j * 128:(j + 1) * 128]
    W2 = np.asarray(inputs["L3_ffn_W2"], f)
    b2 = np.asarray(inputs["L3_ffn_b2"], f)
    W2p = np.zeros((6 * DIH, D), f)
    b2cols = np.zeros((128, 6 * NCH), f)
    for li in range(6):
        W2p[li * DIH:li * DIH + DI] = W2[li]
        for k in range(NCH):
            b2cols[:, li * NCH + k] = b2[li][k * 128:(k + 1) * 128]
    lng = np.asarray(inputs["L3_ffn_ln_g"], f)
    lnb = np.asarray(inputs["L3_ffn_ln_b"], f)
    gbcols = np.zeros((128, 12 * NCH), f)
    for li in range(6):
        for k in range(NCH):
            gbcols[:, (li * NCH + k) * 2] = lng[li][k * 128:(k + 1) * 128]
            gbcols[:, (li * NCH + k) * 2 + 1] = lnb[li][k * 128:(k + 1) * 128]
    qs_ = np.asarray(inputs["L3_ffn_q_scale"], f)
    ks_ = np.asarray(inputs["L3_ffn_k_scale"], f)
    qscols = np.zeros((128, 12), f)
    for li in range(6):
        qscols[:, 2 * li] = np.tile(qs_[li], 2)
        qscols[:, 2 * li + 1] = np.tile(ks_[li], 2)
    return {
        "wqBD": _bd_strips(np.asarray(inputs["L3_attn_wq"], f)),
        "wkBD": _bd_strips(np.asarray(inputs["L3_attn_wk"], f)),
        "wvBD": _bd_strips(np.asarray(inputs["L3_attn_wv"], f)),
        "w1b3": np.ascontiguousarray(inputs["L3_attn_w1"], dtype=f),
        "w2b3": np.ascontiguousarray(inputs["L3_attn_w2"], dtype=f),
        "mask01": _mask01_np(),
        "WqkvR": WqkvR,
        "Wo3": np.ascontiguousarray(np.asarray(inputs["L3_ffn_Wo"], f)
                                    .reshape(6 * D, D)),
        "W1p": W1p, "W2p": W2p,
        "qscols": qscols, "gbcols": gbcols,
        "b1cols": b1cols, "b2cols": b2cols,
        "maskL3": _maskL3_np(),
        "R3": _r3_bf(cfg),
    }


def _r3_bf(cfg):
    import ml_dtypes
    return _rope1_R(cfg).astype(ml_dtypes.bfloat16)


N_BLOCKS = 4


# version 09
# speedup vs baseline: 1.1775x; 1.1775x over previous
import os
os.environ.setdefault("TRN_ALLOW_ICI_ALL_TO_ALL", "1")
import numpy as np
from collections import defaultdict
from contextlib import ExitStack

import concourse.bass as bass
import concourse.mybir as mybir
from concourse.bass_utils import run_bass_kernel_spmd

F32 = mybir.dt.float32
BF16 = mybir.dt.bfloat16
I32 = mybir.dt.int32
AF = mybir.ActivationFunctionType
ALU = mybir.AluOpType
AX = mybir.AxisListType

DMA_DOMAINS = ("qs", "qg", "qv", "qc")
ENG_DOM = {"tensor": "te", "vector": "ve", "scalar": "se", "gpsimd": "pe",
           "sync": "sp"}
ALL_ENGINES = ["sync", "tensor", "vector", "scalar", "gpsimd"]


class EM:

    def __init__(self, nc):
        self.nc = nc
        self.cnt = defaultdict(int)
        self.strm = {e: [] for e in ALL_ENGINES}
        self.known = {e: {} for e in ALL_ENGINES}
        self.tok_know = defaultdict(list)
        self.bufs = {}
        self.pending = defaultdict(lambda: {"w": [], "r": []})
        self.nops = 0
        self.fences = []
        self.fence_applied = {e: 0 for e in ALL_ENGINES}

    def buf(self, key):
        b = self.bufs.get(key)
        if b is None:
            b = {"w": None, "r": {}}
            self.bufs[key] = b
        return b

    def _wait(self, eng, d, s):
        kn = self.known[eng]
        if kn.get(d, 0) >= s:
            return
        self.strm[eng].append(("w", d, s))
        tk = self.tok_know[d][s - 1]
        for dd, ss in tk.items():
            if kn.get(dd, 0) < ss:
                kn[dd] = ss
        if kn.get(d, 0) < s:
            kn[d] = s

    def fence(self):
        self.fences.append(dict(self.cnt))

    def op(self, eng, fn, reads=(), writes=(), dom=None, sig=True):
        dom = dom or ENG_DOM[eng]
        need = {}
        nf = len(self.fences)
        nf_forced = False
        if self.fence_applied[eng] < nf:
            nf_forced = True
            for i in range(self.fence_applied[eng], nf):
                for d, s in self.fences[i].items():
                    if need.get(d, 0) < s:
                        need[d] = s
            self.fence_applied[eng] = nf
        for k in reads:
            b = self.buf(k)
            if b["w"]:
                d, s = b["w"]
                if need.get(d, 0) < s:
                    need[d] = s
        for k in writes:
            b = self.buf(k)
            if b["w"]:
                d, s = b["w"]
                if need.get(d, 0) < s:
                    need[d] = s
            for d, s in b["r"].items():
                if need.get(d, 0) < s:
                    need[d] = s
        for d in sorted(need):
            if d == dom == "te" and not nf_forced:
                continue
            self._wait(eng, d, need[d])
        if sig and dom in DMA_DOMAINS:
            nxt = self.cnt[dom] + 1
            if nxt > 1:
                self._wait(eng, dom, nxt - 1)
        self.nops += 1
        if sig:
            self.cnt[dom] += 1
            seq = self.cnt[dom]
            self.strm[eng].append(("op", fn, dom, seq))
            kn = self.known[eng]
            know = dict(kn)
            prev = self.tok_know[dom][-1] if self.tok_know[dom] else None
            if prev:
                for dd, ss in prev.items():
                    if know.get(dd, 0) < ss:
                        know[dd] = ss
            know[dom] = seq
            self.tok_know[dom].append(know)
            t = (dom, seq)
            pend = self.pending[dom]
            wkeys = list(writes) + pend["w"]
            rkeys = list(reads) + pend["r"]
            pend["w"] = []
            pend["r"] = []
            for k in wkeys:
                b = self.buf(k)
                b["w"] = t
                b["r"] = {}
            for k in rkeys:
                b = self.buf(k)
                if b["r"].get(dom, 0) < seq:
                    b["r"][dom] = seq
            return t
        else:
            self.strm[eng].append(("op", fn, None, 0))
            pend = self.pending[dom]
            pend["w"] += list(writes)
            pend["r"] += list(reads)
            return None

    def wait_all(self, eng, toks):
        for (d, s) in toks:
            self._wait(eng, d, s)

    def wait_domain_now(self, eng, dom):
        s = self.cnt[dom]
        if s > 0:
            self._wait(eng, dom, s)

    def finalize_output_wait(self, eng="sync"):
        for dom in list(self.cnt):
            self.wait_domain_now(eng, dom)

    def realize(self, st):
        for dom, pend in self.pending.items():
            assert not pend["w"] and not pend["r"], f"pending unsignalled ops on {dom}"
        sems = {}
        for dom in list(self.cnt.keys()):
            sems[dom] = st.enter_context(self.nc.semaphore(f"sem_{dom}"))
        block = st.enter_context(self.nc.Block())
        for eng in ALL_ENGINES:
            stream = self.strm[eng]
            if not stream:
                continue
            dec = getattr(block, eng)

            def body(e, stream=stream):
                for item in stream:
                    if item[0] == "w":
                        _, d, s = item
                        e.wait_ge(sems[d], s * 16 if d in DMA_DOMAINS else s)
                    else:
                        _, fn, dom, seq = item
                        ins = fn(e)
                        if dom is not None:
                            ins.then_inc(sems[dom], 16 if dom in DMA_DOMAINS else 1)

            dec(body)


class Cfg:
    def __init__(self, D=1024, S=4096, H=16, HD=64, ncores=8):
        self.D, self.S, self.H, self.HD = D, S, H, HD
        self.ncores = ncores
        self.T = S // 2
        self.NCH = D // 128
        self.TSUB = min(512, self.T)
        self.NTS = self.T // self.TSUB
        self.KB = min(512, S)
        self.NKB = S // self.KB
        self.NQT = S // 128
        self.HC = H // 2
        self.NGC = self.HC * HD // 128
        self.NG = H * HD // 128
        self.QB = 4


def cdiv(a, b):
    return (a + b - 1) // b


class MK:
    def __init__(self, cfg: Cfg, nc, em: EM, st: ExitStack):
        self.cfg, self.nc, self.em, self.st = cfg, nc, em, st
        self.root_st = st
        self.uid = 0

    def scope(self):
        from contextlib import contextmanager

        @contextmanager
        def _sc():
            old = self.st
            st2 = ExitStack()
            self.st = st2
            try:
                with st2:
                    yield
            finally:
                self.st = old
                self.em.fence()
        return _sc()

    def fresh(self, pfx):
        self.uid += 1
        return f"{pfx}_{self.uid}"

    def sb(self, name, shape, dtype):
        if not hasattr(self, "_sbnames"):
            self._sbnames = set()
        if name in self._sbnames:
            self.uid += 1
            name = f"{name}__{self.uid}"
        self._sbnames.add(name)
        return self.st.enter_context(self.nc.sbuf_tensor(name, shape, dtype))

    def dram(self, name, shape, dtype=F32):
        return self.nc.dram_tensor(name, shape, dtype, kind="Internal")

    def setup(self):
        c = self.cfg
        em = self.em
        self.p512 = [self.st.enter_context(self.nc.psum_tensor(f"p512_{i}", [128, 512], F32))
                     for i in range(6)]
        self.pT = [self.st.enter_context(self.nc.psum_tensor(f"pT_{i}", [128, 1024], BF16))
                   for i in range(2)]
        self._prot = 0
        self.ident = self.sb("c_ident", [128, 128], BF16)
        self.ones_row = self.sb("c_ones_r", [1, 512], BF16)
        self.ones_row32 = self.sb("c_ones_r32", [1, 512], F32)
        self.ones128_32 = self.sb("c_ones128_32", [1, 128], F32)
        self.ones_col = self.sb("c_ones_c", [128, 1], F32)
        self.hsel32 = self.sb("c_hsel32", [2, 128], F32)
        self.cpool = self.sb("c_cpool", [128, 16], F32)
        self.cpool_vals = {}
        ident_np = np.eye(128, dtype=np.float32)
        idh = self.nc.inline_tensor(ident_np, "ident_f32")
        self.ident32 = self.sb("c_ident32", [128, 128], F32)
        em.op("sync", lambda e: e.dma_start(out=self.ident32[:, :], in_=idh.ap()),
              writes=[("c_id32",)], dom="qs")
        em.op("vector", lambda e: e.tensor_copy(self.ident[:, :], self.ident32[:, :]),
              reads=[("c_id32",)], writes=[("c_id",)])
        em.op("vector", lambda e: e.memset(self.ones_row[:, :], 1.0), writes=[("c_or",)])
        em.op("vector", lambda e: e.memset(self.ones_row32[:, :], 1.0), writes=[("c_or32",)])
        em.op("vector", lambda e: e.memset(self.ones128_32[:, :], 1.0), writes=[("c_o128",)])
        em.op("vector", lambda e: e.memset(self.ones_col[:, :], 1.0), writes=[("c_oc",)])
        hs_np = np.zeros((2, 128), np.float32)
        hs_np[0, 0:64] = 1.0
        hs_np[1, 64:128] = 1.0
        hs_h = self.nc.inline_tensor(hs_np, "hsel32_c")
        em.op("sync", lambda e: e.dma_start(out=self.hsel32[0:2, :], in_=hs_h.ap()),
              writes=[("c_hs",)], dom="qs")

    def pbank(self):
        b = self.p512[self._prot % 6]
        k = ("p512", self._prot % 6)
        self._prot += 1
        return b, k

    def constf(self, val, rows=128):
        val = float(val)
        if val in self.cpool_vals:
            i = self.cpool_vals[val]
            return self.cpool[0:rows, i:i + 1]
        i = len(self.cpool_vals)
        assert i < 16
        self.cpool_vals[val] = i
        self.em.op("vector", lambda e, i=i: e.memset(self.cpool[:, i:i + 1], val),
                   writes=[("c_cp",)])
        return self.cpool[0:rows, i:i + 1]

    def esel(self, n):
        attr = f"_esel{n}"
        if hasattr(self, attr):
            return getattr(self, attr)
        npv = np.kron(np.eye(n, dtype=np.float32), np.ones((1, 128), np.float32))
        h = self.nc.inline_tensor(npv, self.fresh(f"esel{n}"))
        sb = self.sb(f"c_esel{n}", [n, n * 128], F32)
        self.em.op("sync", lambda e: e.dma_start(out=sb[0:n, :], in_=h.ap()),
                   writes=[(f"esel{n}",)], dom="qs")
        setattr(self, attr, (sb, (f"esel{n}",)))
        return getattr(self, attr)

    def qsel(self, n, npc):
        attr = f"_qsel{n}_{npc}"
        if hasattr(self, attr):
            return getattr(self, attr)
        npv = np.zeros((n, npc * 128), np.float32)
        for kk in range(npc):
            npv[2 * kk, kk * 128:kk * 128 + 64] = 1.0
            npv[2 * kk + 1, kk * 128 + 64:(kk + 1) * 128] = 1.0
        h = self.nc.inline_tensor(npv, self.fresh(f"qsel{n}_{npc}"))
        sb = self.sb(f"c_qsel{n}_{npc}", [n, npc * 128], F32)
        self.em.op("sync", lambda e: e.dma_start(out=sb[0:n, :], in_=h.ap()),
                   writes=[(f"qsel{n}_{npc}",)], dom="qs")
        setattr(self, attr, (sb, (f"qsel{n}_{npc}",)))
        return getattr(self, attr)

    def bcast_selrow(self, src, n, row_idx, ncols, rkey, dst, dkey, col_off=0,
                     dst_off=0):
        em = self.em
        sel, selkey = self.esel(n)
        for t0 in range(0, ncols, 512):
            tw = min(512, ncols - t0)
            pb, pk = self.pbank()
            em.op("tensor", lambda e, t0=t0, tw=tw, pb=pb: e.matmul(
                pb[:, 0:tw], sel[0:n, row_idx * 128:(row_idx + 1) * 128],
                src[0:n, col_off + t0:col_off + t0 + tw],
                start=True, stop=True),
                reads=[selkey, rkey], writes=[pk])
            em.op("vector", lambda e, t0=t0, tw=tw, pb=pb: e.tensor_copy(
                dst[:, dst_off + t0:dst_off + t0 + tw], pb[:, 0:tw]),
                reads=[pk], writes=[dkey])

    def row_load(self, dram_ap, n, key):
        row = self.sb(self.fresh("rw"), [1, n], F32)
        self.em.op("sync", lambda e: e.dma_start(out=row[0:1, 0:n], in_=dram_ap),
                   writes=[key], dom="qs")
        return row

    def cast16(self, src, key_r, key_w, shape=None, eng="vector"):
        p, f = shape if shape else (src.shape[0], src.shape[1])
        dst = self.sb(self.fresh("b16"), [p, f], BF16)
        self.em.op(eng, lambda e: e.tensor_copy(dst[0:p, 0:f], src[0:p, 0:f]),
                   reads=[key_r], writes=[key_w])
        return dst

    def bcast_row(self, row, n, rkey, dst, dkey, dst_off=0, row_idx=0, col_off=0):
        em = self.em
        for t0 in range(0, n, 512):
            tw = min(512, n - t0)
            pb, pk = self.pbank()
            em.op("tensor", lambda e, t0=t0, tw=tw, pb=pb: e.matmul(
                pb[:, 0:tw], self.ones128_32[0:1, :],
                row[row_idx:row_idx + 1, col_off + t0:col_off + t0 + tw],
                start=True, stop=True),
                reads=[("c_o128",), rkey], writes=[pk])
            em.op("vector", lambda e, t0=t0, tw=tw, pb=pb: e.tensor_copy(
                dst[:, dst_off + t0:dst_off + t0 + tw], pb[:, 0:tw]),
                reads=[pk], writes=[dkey])

    def transpose(self, in_ap, p, f, rkeys, out_sb, out_off, wkey, pti=0, bp=0):
        em = self.em
        ptb = self.pT[pti]
        pk = ("pT", pti)
        em.op("tensor", lambda e: e.transpose(
            ptb[0:f, 0:p], in_ap, self.ident[bp:bp + p, bp:bp + p]),
            reads=list(rkeys) + [("c_id",)], writes=[pk])
        em.op("vector", lambda e: e.tensor_copy(out_sb[0:f, out_off:out_off + p],
                                                ptb[0:f, 0:p]),
              reads=[pk], writes=[wkey])

    def proj(self, wname, w_ap, K, Fdim, xn, xnkeyf, dst_evac,
             bias_ap=None, colscale=None, fgw=512, tsubs=None, xoff_fn=None,
             wq_dom="qs", cast_eng="vector", bias_scale_row=None, wbufs=None,
             bias_rhs=None, dma_cast=False, bias_pre=None):
        c = self.cfg
        em = self.em
        nk = cdiv(K, 128)
        T = c.T
        tsubs = tsubs if tsubs is not None else [(i * c.TSUB, c.TSUB) for i in range(c.NTS)]
        if xoff_fn is None:
            xoff_fn = lambda k, t0: k * T + t0
        if wbufs is not None:
            assert nk <= len(wbufs["wsl"][0]) and fgw <= wbufs["fgw"]
            stage = wbufs["stage"]
            wsl = wbufs["wsl"]
            wkp = wbufs["kp"]
            npar_eff = wbufs.get("npar", 2)
        else:
            stage = [self.sb(self.fresh(f"{wname}stg"), [128, fgw], F32) for _ in range(2)]
            wsl = [[self.sb(self.fresh(f"{wname}w"), [128, fgw], BF16) for _ in range(nk)]
                   for _ in range(2)]
            wkp = wname
        bkey = (wname, "b16")
        if bias_pre is not None:
            brow16, bkey = bias_pre
            bias_ap = True
        elif bias_ap is not None:
            if bias_scale_row is not None:
                brow = self.row_load(bias_ap, Fdim, (wname, "b"))
                srow, skey = bias_scale_row
                em.op("vector", lambda e: e.tensor_mul(brow[0:1, :], brow[0:1, :], srow),
                      reads=[(wname, "b"), skey], writes=[(wname, "b")])
                brow16 = self.cast16(brow, (wname, "b"), (wname, "b16"))
            else:
                brow16 = self.sb(self.fresh(f"{wname}b16"), [1, Fdim], BF16)
                em.op("gpsimd", lambda e: e.dma_start(
                    out=brow16[0:1, 0:Fdim], in_=bias_ap),
                    writes=[(wname, "b16")], dom="qg")
        if wbufs is None:
            npar_eff = 2
        ldeng = "sync" if wq_dom == "qs" else "gpsimd"
        for gi, f0 in enumerate(range(0, Fdim, fgw)):
            fw = min(fgw, Fdim - f0)
            pp = gi % npar_eff
            for k in range(nk):
                kk = min(128, K - k * 128)
                sp = k % 2
                if dma_cast:
                    em.op("gpsimd",
                          lambda e, k=k, kk=kk, f0=f0, fw=fw, pp=pp: e.dma_start(
                              out=wsl[pp][k][0:kk, 0:fw],
                              in_=w_ap[k * 128:k * 128 + kk, f0:f0 + fw]),
                          writes=[(wkp, "w", pp, k)], dom="qg")
                    continue
                em.op(ldeng,
                      lambda e, k=k, kk=kk, f0=f0, fw=fw, sp=sp: e.dma_start(
                          out=stage[sp][0:kk, 0:fw], in_=w_ap[k * 128:k * 128 + kk, f0:f0 + fw]),
                      writes=[(wkp, "stg", sp)], dom=wq_dom)
                if colscale is None:
                    em.op(cast_eng, lambda e, k=k, kk=kk, fw=fw, pp=pp, sp=sp: e.tensor_copy(
                        wsl[pp][k][0:kk, 0:fw], stage[sp][0:kk, 0:fw]),
                        reads=[(wkp, "stg", sp)], writes=[(wkp, "w", pp, k)])
                else:
                    srow, skey2 = colscale
                    em.op(cast_eng, lambda e, k=k, kk=kk, fw=fw, pp=pp, sp=sp:
                          e.tensor_scalar_mul(
                              wsl[pp][k][0:kk, 0:fw], stage[sp][0:kk, 0:fw], srow[k]),
                          reads=[(wkp, "stg", sp), skey2], writes=[(wkp, "w", pp, k)])
            for fc0 in range(0, fw, 128):
                fcw = min(128, fw - fc0)
                fabs = f0 + fc0
                for (t0, tw) in tsubs:
                    pb, pk = self.pbank()
                    for k in range(nk):
                        kk = min(128, K - k * 128)
                        last = (k == nk - 1) and bias_ap is None
                        em.op("tensor", lambda e, k=k, kk=kk, fc0=fc0, fcw=fcw, t0=t0,
                              tw=tw, pb=pb, pp=pp, last=last: e.matmul(
                                  pb[0:fcw, 0:tw], wsl[pp][k][0:kk, fc0:fc0 + fcw],
                                  xn[0:kk, xoff_fn(k, t0):xoff_fn(k, t0) + tw],
                                  start=(k == 0), stop=last),
                              reads=[(wkp, "w", pp, k), xnkeyf(k)],
                              writes=[pk], sig=last or (k == nk - 1))
                    if bias_ap is not None:
                        if bias_rhs is None:
                            em.op("tensor", lambda e, fabs=fabs, fcw=fcw, tw=tw,
                                  pb=pb: e.matmul(
                                      pb[0:fcw, 0:tw], brow16[0:1, fabs:fabs + fcw],
                                      self.ones_row[0:1, 0:tw], start=False, stop=True),
                                  reads=[bkey, ("c_or",)], writes=[pk])
                        else:
                            rrow, rrkey = bias_rhs
                            em.op("tensor", lambda e, fabs=fabs, fcw=fcw, t0=t0,
                                  tw=tw, pb=pb: e.matmul(
                                      pb[0:fcw, 0:tw], brow16[0:1, fabs:fabs + fcw],
                                      rrow[0:1, t0:t0 + tw], start=False, stop=True),
                                  reads=[bkey, rrkey], writes=[pk])
                    dst_evac(fabs, fcw, t0, tw, pb, pk)

    def ln_stats(self, x, xkeyf, name, l1_variant=False):
        c = self.cfg
        em = self.em
        T = c.T
        mu = self.sb(self.fresh("lnmu"), [1, T], F32)
        var = self.sb(self.fresh("lnvar"), [1, T], F32)
        rstd = self.sb(self.fresh("lnr"), [1, T], F32)
        out = {"mu": (mu, (name, "mu")), "rstd": (rstd, (name, "rstd")),
               "var": (var, (name, "var"))}
        if l1_variant:
            R = self.sb(self.fresh("lnR"), [1, T], F32)
            out["R"] = (R, (name, "R"))
        with self.scope():
            self._ln_stats_body(x, xkeyf, name, l1_variant, mu, var, rstd, out)
        return out

    def _ln_stats_body(self, x, xkeyf, name, l1_variant, mu, var, rstd, out):
        c = self.cfg
        em = self.em
        T = c.T
        sq = [self.sb(self.fresh("lnsq"), [128, c.TSUB], F32) for _ in range(2)]
        sums = self.sb(self.fresh("lnsum"), [1, T], F32)
        sumsq = self.sb(self.fresh("lnsq2"), [1, T], F32)
        kS = (name, "sums")
        kQ = (name, "sumsq")
        for ti in range(c.NTS):
            t0 = ti * c.TSUB
            pbS, pkS = self.pbank()
            pbQ, pkQ = self.pbank()
            for k in range(c.NCH):
                pp = k % 2
                em.op("scalar", lambda e, k=k, t0=t0, pp=pp: e.activation(
                    sq[pp][:, 0:c.TSUB], x[:, k * T + t0:k * T + t0 + c.TSUB], AF.Square),
                    reads=[xkeyf(k)], writes=[(name, "sq", pp)])
                em.op("tensor", lambda e, k=k, t0=t0, pbS=pbS: e.matmul(
                    pbS[0:1, 0:c.TSUB], self.ones_col[:, 0:1],
                    x[:, k * T + t0:k * T + t0 + c.TSUB],
                    start=(k == 0), stop=(k == c.NCH - 1)),
                    reads=[("c_oc",), xkeyf(k)], writes=[pkS])
                em.op("tensor", lambda e, k=k, pp=pp, pbQ=pbQ: e.matmul(
                    pbQ[0:1, 0:c.TSUB], self.ones_col[:, 0:1], sq[pp][:, 0:c.TSUB],
                    start=(k == 0), stop=(k == c.NCH - 1)),
                    reads=[("c_oc",), (name, "sq", pp)], writes=[pkQ])
            em.op("vector", lambda e, t0=t0, pbS=pbS: e.tensor_copy(
                sums[0:1, t0:t0 + c.TSUB], pbS[0:1, 0:c.TSUB]), reads=[pkS], writes=[kS])
            em.op("vector", lambda e, t0=t0, pbQ=pbQ: e.tensor_copy(
                sumsq[0:1, t0:t0 + c.TSUB], pbQ[0:1, 0:c.TSUB]), reads=[pkQ], writes=[kQ])
        D = float(c.D)
        kMu = (name, "mu")
        kVar = (name, "var")
        kR = (name, "rstd")
        em.op("vector", lambda e: e.tensor_scalar_mul(mu[0:1, :], sums[0:1, :], 1.0 / D),
              reads=[kS], writes=[kMu])
        ex2 = self.sb(self.fresh("lnex2"), [1, T], F32)
        kE = (name, "ex2")
        em.op("vector", lambda e: e.tensor_scalar_mul(ex2[0:1, :], sumsq[0:1, :], 1.0 / D),
              reads=[kQ], writes=[kE])
        em.op("vector", lambda e: e.scalar_tensor_tensor(
            var[0:1, :], mu[0:1, :], -1.0, mu[0:1, :], ALU.mult, ALU.mult),
            reads=[kMu], writes=[kVar])
        em.op("vector", lambda e: e.tensor_add(var[0:1, :], var[0:1, :], ex2[0:1, :]),
              reads=[kVar, kE], writes=[kVar])
        eps = 1e-5
        lnv = self.sb(self.fresh("lnlv"), [1, T], F32)
        kL = (name, "lnv")
        c_eps1 = self.constf(eps, 1)
        c_mh1 = self.constf(-0.5, 1)
        em.op("scalar", lambda e: e.activation(
            lnv[0:1, :], var[0:1, :], AF.Ln, bias=c_eps1),
            reads=[kVar, ("c_cp",)], writes=[kL])
        em.op("scalar", lambda e: e.activation(
            rstd[0:1, :], lnv[0:1, :], AF.Exp, scale=c_mh1),
            reads=[kL, ("c_cp",)], writes=[kR])
        if l1_variant:
            vpe = self.sb(self.fresh("lnvpe"), [1, T], F32)
            kVpe = (name, "vpe")
            em.op("vector", lambda e: e.tensor_single_scalar(
                vpe[0:1, :], var[0:1, :], eps, ALU.add), reads=[kVar], writes=[kVpe])
            a = self.sb(self.fresh("lna"), [1, T], F32)
            kA = (name, "a")
            em.op("vector", lambda e: e.tensor_tensor(
                a[0:1, :], var[0:1, :], vpe[0:1, :], ALU.divide),
                reads=[kVar, kVpe], writes=[kA])
            lnv2 = self.sb(self.fresh("lnlv2"), [1, T], F32)
            kL2 = (name, "lnv2")
            em.op("scalar", lambda e, c_eps1=c_eps1: e.activation(
                lnv2[0:1, :], a[0:1, :], AF.Ln, bias=c_eps1),
                reads=[kA, ("c_cp",)], writes=[kL2])
            rh = self.sb(self.fresh("lnrh"), [1, T], F32)
            kRh = (name, "rh")
            em.op("scalar", lambda e, c_mh1=c_mh1: e.activation(
                rh[0:1, :], lnv2[0:1, :], AF.Exp, scale=c_mh1),
                reads=[kL2, ("c_cp",)], writes=[kRh])
            R, kRR = out["R"]
            em.op("vector", lambda e: e.tensor_mul(R[0:1, :], rstd[0:1, :], rh[0:1, :]),
                  reads=[kR, kRh], writes=[kRR])

    def ln_apply(self, x, xkeyf, mu_b, kMb, r_b, kRb, dst16, dkeyf, tmp, name):
        c = self.cfg
        em = self.em
        T = c.T
        for k in range(c.NCH):
            pp = k % len(tmp)
            tk = (tmp[pp].name,)
            em.op("vector", lambda e, k=k, pp=pp: e.tensor_sub(
                tmp[pp][:, 0:T], x[:, k * T:(k + 1) * T], mu_b[:, 0:T]),
                reads=[xkeyf(k), kMb], writes=[tk])
            em.op("gpsimd", lambda e, k=k, pp=pp: e.tensor_mul(
                dst16[:, k * T:(k + 1) * T], tmp[pp][:, 0:T], r_b[:, 0:T]),
                reads=[tk, kRb], writes=[dkeyf(k)])

    def setup_meta(self, meta_ap):
        if getattr(self, "_meta_done", False):
            return
        self._meta_done = True
        em = self.em
        T = self.cfg.T
        self.meta_sb = self.sb("meta_sb", [1, 4], I32)
        em.op("sync", lambda e: e.dma_start(out=self.meta_sb[0:1, :], in_=meta_ap),
              writes=[("meta",)], dom="qs")
        nc = self.nc
        self.dynregs = {
            ("gpsimd", "hb"): self.root_st.enter_context(
                nc.gpsimd.register("rg_hbT")),
            ("sync", "hb"): self.root_st.enter_context(
                nc.sync.register("rs_hbT")),
            ("sync", "tb"): self.root_st.enter_context(
                nc.sync.register("rs_tb")),
        }

        def init_g(e):
            with e.register() as rh:
                e.reg_load(rh, self.meta_sb[0:1, 0:1])
                return e.reg_mul(self.dynregs[("gpsimd", "hb")], rh, T)
        em.op("gpsimd", init_g, reads=[("meta",)], writes=[("regs_g",)])

        def init_s(e):
            with e.register() as rh:
                e.reg_load(rh, self.meta_sb[0:1, 0:1])
                e.reg_mul(self.dynregs[("sync", "hb")], rh, T)
                return e.reg_load(self.dynregs[("sync", "tb")],
                                  self.meta_sb[0:1, 1:2])
        em.op("sync", init_s, reads=[("meta",)], writes=[("regs_s",)])

    def dyn_load(self, eng, dst_ap, src_handle, static_off, ap_pattern,
                 rkeys, wkeys, base="hb"):
        em = self.em
        reg = self.dynregs[(eng, base)]
        regkey = ("regs_g",) if eng == "gpsimd" else ("regs_s",)

        def f(e):
            with e.register() as rt:
                e.reg_add(rt, reg, static_off)
                return e.dma_start(out=dst_ap,
                                   in_=bass.AP(src_handle, rt, ap_pattern))
        return em.op(eng, f, reads=list(rkeys) + [regkey],
                     writes=list(wkeys), dom="qg" if eng == "gpsimd" else "qs")

    def attn_headshard(self, name, ag_out_h, ag2_in_h, mode="softmax",
                       scale=0.125, maskA16=None, mask01=None,
                       decay_dram=None, gpow_src=None, post_group=None):
        c = self.cfg
        em = self.em
        S, T, D = c.S, c.T, c.D
        NK128 = S // 128
        qkv_g = self.sb(self.fresh("atqkv"), [128, 3 * S], BF16)
        q_g = qkv_g
        QO, KO, VO = 0, S, 2 * S
        if mode == "retention":
            self._ret_dec = self.sb(self.fresh("atdec"), [128, 2 * S], BF16)
            self._ret_gpow_b = (self.sb(self.fresh("atgp"),
                                        [128, 2 * (S // 128) * c.NKB], F32),
                                (name, "gpb"))
        vT = self.sb(self.fresh("atvT"), [128, S], BF16)
        P_sb = self.sb(self.fresh("atP"), [128, 4 * c.KB], BF16)
        PT_sb = self.sb(self.fresh("atPT"), [128, 4 * 128], BF16)
        outT = self.sb(self.fresh("atO"), [128, S], BF16)
        sums = self.sb(self.fresh("atsum"), [128, 4 * 2 * 8], F32)
        rrec = self.sb(self.fresh("atrr"), [128, 8], F32)
        rrT = self.sb(self.fresh("atrrT"), [8, 512], F32)
        rb_sb = self.sb(self.fresh("atrb"), [128, 512], F32)
        kQN = (name, "qn")
        kKN = (name, "kn")
        c_scale128 = self.constf(float(scale), 128) if mode == "softmax" else None
        if mode == "softmax":
            sqt = self.sb(self.fresh("atsq"), [128, 512], F32)
            rowbuf = self.sb(self.fresh("atrow"), [128, S], F32)
            qn2 = rowbuf[0:2, :]
            qn = rowbuf[0:2, :]
            kn2 = rowbuf[32:34, :]
            kn = rowbuf[32:34, :]
            cneg = rowbuf[64:66, :]
            kmax = self.sb(self.fresh("atkm"), [2, 1], F32)
            cT = self.sb(self.fresh("atcT"), [128, 2 * NK128], F32)
            hsel = self.sb(self.fresh("athsel"), [128, 2], F32)
            em.op("vector", lambda e: e.memset(hsel[:, :], 0.0), writes=[("athsel",)])
            em.op("vector", lambda e: e.memset(hsel[0:64, 0:1], 1.0), writes=[("athsel",)])
            em.op("vector", lambda e: e.memset(hsel[64:128, 1:2], 1.0), writes=[("athsel",)])
        kAg2 = (name, "ag2_in")
        kQKV = (name, "qkv")
        for g in range(c.NGC):
            dst4 = qkv_g[:, :].rearrange("p (s r t) -> p s r t", s=3, r=2)
            for rank, eng in ((0, "gpsimd"), (1, "sync")):
                self.dyn_load(
                    eng, dst4[:, :, rank, :], ag_out_h,
                    rank * 3 * D * T + g * 128 * T,
                    [[T, 128], [D * T, 3], [1, T]],
                    [("ag_out",)], [kQKV])
            if mode == "retention":
                qdec_h, kdec_h = decay_dram
                dec_sb = self._ret_dec
                for i_, dh in enumerate((qdec_h, kdec_h)):
                    em.op("sync", lambda e, i_=i_, dh=dh, g=g: e.dma_start(
                        out=dec_sb[:, i_ * S:(i_ + 1) * S],
                        in_=dh.ap()[g * 128:(g + 1) * 128, :]),
                        writes=[(name, "dec")], dom="qs")
                em.op("vector", lambda e: e.tensor_mul(
                    qkv_g[:, QO:QO + S], qkv_g[:, QO:QO + S], dec_sb[:, 0:S]),
                    reads=[kQKV, (name, "dec")], writes=[kQKV])
                em.op("vector", lambda e: e.tensor_mul(
                    qkv_g[:, KO:KO + S], qkv_g[:, KO:KO + S], dec_sb[:, S:2 * S]),
                    reads=[kQKV, (name, "dec")], writes=[kQKV])
                gpow_b2 = self._ret_gpow_b
                for h in range(2):
                    self.bcast_selrow(gpow_src[0], 2 * c.NGC, g * 2 + h,
                                      NK128 * c.NKB, gpow_src[1], gpow_b2[0],
                                      gpow_b2[1], dst_off=h * NK128 * c.NKB)
            for kt in range(NK128):
                for h in range(2):
                    self.transpose(
                        qkv_g[64 * h:64 * h + 64, VO + kt * 128:VO + (kt + 1) * 128],
                        64, 128, [kQKV], vT,
                        (h * NK128 + kt) * 64, (name, "vT"), pti=kt % 2,
                        bp=64 * h)
            if mode == "softmax":
                for t0 in range(0, S, 512):
                    em.op("scalar", lambda e, t0=t0: e.activation(
                        sqt[:, 0:512], qkv_g[:, QO + t0:QO + t0 + 512], AF.Square),
                        reads=[kQKV], writes=[(name, "sq")])
                    pb, pk = self.pbank()
                    em.op("tensor", lambda e, pb=pb: e.matmul(
                        pb[0:2, 0:512], hsel[:, 0:2], sqt[:, 0:512],
                        start=True, stop=True),
                        reads=[("athsel",), (name, "sq")], writes=[pk])
                    em.op("vector", lambda e, t0=t0, pb=pb: e.tensor_copy(
                        rowbuf[0:2, t0:t0 + 512], pb[0:2, 0:512]), reads=[pk], writes=[kQN])
                    em.op("scalar", lambda e, t0=t0: e.activation(
                        sqt[:, 0:512], qkv_g[:, KO + t0:KO + t0 + 512], AF.Square),
                        reads=[kQKV], writes=[(name, "sq")])
                    pb, pk = self.pbank()
                    em.op("tensor", lambda e, pb=pb: e.matmul(
                        pb[0:2, 0:512], hsel[:, 0:2], sqt[:, 0:512],
                        start=True, stop=True),
                        reads=[("athsel",), (name, "sq")], writes=[pk])
                    em.op("vector", lambda e, t0=t0, pb=pb: e.tensor_copy(
                        rowbuf[32:34, t0:t0 + 512], pb[0:2, 0:512]), reads=[pk], writes=[kKN])
                c_t2 = self.constf(1e-30, 2)
                c_h2 = self.constf(0.5, 2)
                em.op("scalar", lambda e: e.activation(
                    rowbuf[0:2, :], rowbuf[0:2, :], AF.Ln, bias=c_t2),
                    reads=[kQN, ("c_cp",)], writes=[(name, "qn_")])
                em.op("scalar", lambda e: e.activation(
                    rowbuf[0:2, :], rowbuf[0:2, :], AF.Exp, scale=c_h2),
                    reads=[(name, "qn_"), ("c_cp",)], writes=[(name, "qn_")])
                em.op("scalar", lambda e: e.activation(
                    rowbuf[32:34, :], rowbuf[32:34, :], AF.Ln, bias=c_t2),
                    reads=[kKN, ("c_cp",)], writes=[(name, "kn_")])
                em.op("scalar", lambda e: e.activation(
                    rowbuf[32:34, :], rowbuf[32:34, :], AF.Exp, scale=c_h2),
                    reads=[(name, "kn_"), ("c_cp",)], writes=[(name, "kn_")])
                em.op("vector", lambda e: e.tensor_reduce(
                    kmax[0:2, 0:1], rowbuf[32:34, :], AX.X, ALU.max),
                    reads=[(name, "kn_")], writes=[(name, "km")])
                em.op("vector", lambda e: e.tensor_scalar(
                    rowbuf[64:66, :], rowbuf[0:2, :], kmax[0:2, 0:1], -float(scale),
                    ALU.mult, ALU.mult),
                    reads=[(name, "qn_"), (name, "km")], writes=[(name, "c")])
                pb5 = self.p512[5]
                pk5 = ("p512", 5)
                for qt in range(NK128):
                    em.op("tensor", lambda e, qt=qt: e.transpose(
                        pb5[0:128, 0:2], rowbuf[64:66, qt * 128:(qt + 1) * 128],
                        self.ident32[64:66, 64:66]),
                        reads=[(name, "c"), ("c_id32",)], writes=[pk5])
                    em.op("vector", lambda e, qt=qt: e.tensor_copy(
                        cT[:, qt * 2:qt * 2 + 2], pb5[0:128, 0:2]),
                        reads=[pk5], writes=[(name, "cT")])
            NQB = NK128 // c.QB
            psO = self.p512[4]
            kO = ("p512", 4)
            pb5 = self.p512[5]
            pk5 = ("p512", 5)
            for qb in range(NQB):
                if mode == "softmax":
                    em.op("vector", lambda e: e.memset(sums[:, :], 0.0),
                          writes=[(name, "sums")])
                for qloc in range(c.QB):
                    qt = qb * c.QB + qloc
                    jmax = (qt * 128) // c.KB
                    for h in range(2):
                        for j in range(jmax + 1):
                            strad = (j == jmax)
                            delta = qt - j * (c.KB // 128)
                            kw = (delta + 1) * 128 if strad else c.KB
                            nkt = kw // 128
                            sb_i = h * 2 + (j % 2)
                            Sb = self.p512[sb_i]
                            kS = ("p512", sb_i)
                            em.op("tensor", lambda e, h=h, qt=qt, j=j, kw=kw, Sb=Sb:
                                  e.matmul(Sb[0:128, 0:kw],
                                           qkv_g[64 * h:64 * h + 64,
                                                 QO + qt * 128:QO + qt * 128 + 128],
                                           qkv_g[64 * h:64 * h + 64,
                                                 KO + j * c.KB:KO + j * c.KB + kw],
                                           start=True, stop=True,
                                           tile_position=(64 * h, 0)),
                                  reads=[kQKV], writes=[kS])
                            if mode == "softmax":
                                if strad:
                                    em.op("vector", lambda e, kw=kw, Sb=Sb: e.tensor_add(
                                        Sb[0:128, kw - 128:kw], maskA16[0][:, 0:128],
                                        Sb[0:128, kw - 128:kw]),
                                        reads=[maskA16[1], kS], writes=[kS])
                                pr = (h * 2 + j % 2) * c.KB
                                em.op("scalar", lambda e, h=h, qt=qt, qloc=qloc, j=j,
                                      kw=kw, Sb=Sb, pr=pr: e.activation(
                                          P_sb[:, pr:pr + kw],
                                          Sb[0:128, 0:kw], AF.Exp,
                                          bias=cT[:, qt * 2 + h:qt * 2 + h + 1],
                                          scale=c_scale128,
                                          accum_out=sums[:, (qloc * 2 + h) * 8 + j:
                                                         (qloc * 2 + h) * 8 + j + 1]),
                                      reads=[kS, (name, "cT"), ("c_cp",)],
                                      writes=[(name, "P", h, j % 2), (name, "sums")])
                            else:
                                gp = self._ret_gpow_b
                                pr = (h * 2 + j % 2) * c.KB
                                col = (h * NK128 * c.NKB + qt * c.NKB + j)
                                if strad:
                                    if delta > 0:
                                        em.op("vector", lambda e, h=h, kw=kw, Sb=Sb,
                                              col=col, pr=pr: e.tensor_scalar(
                                                  P_sb[:, pr:pr + kw - 128],
                                                  Sb[0:128, 0:kw - 128],
                                                  gp[0][:, col:col + 1], None, ALU.mult),
                                              reads=[kS, gp[1]],
                                              writes=[(name, "P", h, j % 2)])
                                    em.op("vector", lambda e, h=h, kw=kw, Sb=Sb, col=col,
                                          pr=pr: e.scalar_tensor_tensor(
                                              P_sb[:, pr + kw - 128:pr + kw],
                                              Sb[0:128, kw - 128:kw],
                                              gp[0][:, col:col + 1],
                                              mask01[0][:, 0:128], ALU.mult, ALU.mult),
                                          reads=[kS, gp[1], mask01[1]],
                                          writes=[(name, "P", h, j % 2)])
                                else:
                                    em.op("vector", lambda e, h=h, kw=kw, Sb=Sb, col=col,
                                          pr=pr: e.tensor_scalar(
                                              P_sb[:, pr:pr + kw],
                                              Sb[0:128, 0:kw],
                                              gp[0][:, col:col + 1], None, ALU.mult),
                                          reads=[kS, gp[1]], writes=[(name, "P", h, j % 2)])
                            for kt in range(nkt):
                                ktabs = j * (c.KB // 128) + kt
                                pti = (kt * 2 + h) % 2
                                ptb = self.pT[pti]
                                pkT = ("pT", pti)
                                pr = (h * 2 + j % 2) * c.KB
                                em.op("tensor", lambda e, h=h, kt=kt, ptb=ptb, pr=pr:
                                      e.transpose(
                                          ptb[0:128, 0:128],
                                          P_sb[:, pr + kt * 128:pr + kt * 128 + 128],
                                          self.ident[:, :]),
                                      reads=[(name, "P", h, j % 2), ("c_id",)], writes=[pkT])
                                pt_off = ((h * 2) + (kt % 2)) * 128
                                em.op("vector", lambda e, ptb=ptb, pt_off=pt_off:
                                      e.tensor_copy(PT_sb[:, pt_off:pt_off + 128],
                                                    ptb[0:128, 0:128]),
                                      reads=[pkT], writes=[(name, "PT", h, kt % 2)])
                                first = (j == 0 and kt == 0)
                                last = (j == jmax and kt == nkt - 1)
                                em.op("tensor", lambda e, h=h, ktabs=ktabs, qloc=qloc,
                                      pt_off=pt_off, first=first, last=last: e.matmul(
                                          psO[64 * h:64 * h + 64,
                                              qloc * 128:qloc * 128 + 128],
                                          vT[:, (h * NK128 + ktabs) * 64:
                                             (h * NK128 + ktabs) * 64 + 64],
                                          PT_sb[:, pt_off:pt_off + 128],
                                          start=first, stop=last,
                                          tile_position=(0, 64 * h)),
                                      reads=[(name, "vT"), (name, "PT", h, kt % 2)],
                                      writes=[kO], sig=last)
                if mode == "softmax":
                    em.op("vector", lambda e: e.tensor_reduce(
                        rrec[:, 0:8],
                        sums[:, :].rearrange("p (a b) -> p a b", b=8),
                        AX.X, ALU.add),
                        reads=[(name, "sums")], writes=[(name, "rr")])
                    em.op("vector", lambda e: e.reciprocal(rrec[:, 0:8], rrec[:, 0:8]),
                          reads=[(name, "rr")], writes=[(name, "rr")])
                    for qloc in range(c.QB):
                        em.op("tensor", lambda e, qloc=qloc: e.transpose(
                            pb5[0:2, 0:128], rrec[:, qloc * 2:qloc * 2 + 2],
                            self.ident32[0:128, 0:128]),
                            reads=[(name, "rr"), ("c_id32",)], writes=[pk5])
                        em.op("vector", lambda e, qloc=qloc: e.tensor_copy(
                            rrT[0:2, qloc * 128:qloc * 128 + 128], pb5[0:2, 0:128]),
                            reads=[pk5], writes=[(name, "rrT")])
                    for qloc in range(c.QB):
                        em.op("tensor", lambda e, qloc=qloc: e.matmul(
                            pb5[0:128, qloc * 128:qloc * 128 + 128],
                            self.hsel32[0:2, :],
                            rrT[0:2, qloc * 128:qloc * 128 + 128],
                            start=True, stop=True),
                            reads=[("c_hs",), (name, "rrT")], writes=[pk5])
                    em.op("vector", lambda e: e.tensor_copy(
                        rb_sb[:, 0:512], pb5[0:128, 0:512]),
                        reads=[pk5], writes=[(name, "rb")])
                    em.op("vector", lambda e, qb=qb: e.tensor_mul(
                        outT[:, qb * 512:qb * 512 + 512], psO[0:128, 0:512],
                        rb_sb[:, 0:512]),
                        reads=[kO, (name, "rb")], writes=[(name, "outT")])
                else:
                    em.op("vector", lambda e, qb=qb: e.tensor_copy(
                        outT[:, qb * 512:qb * 512 + 512], psO[0:128, 0:512]),
                        reads=[kO], writes=[(name, "outT")])
            if post_group is not None:
                post_group(g, outT, (name, "outT"))
            em.op("gpsimd", lambda e, g=g: e.dma_start(
                out=ag2_in_h.ap()[g * 128:(g + 1) * 128, :], in_=outT[:, :]),
                reads=[(name, "outT")], writes=[kAg2], dom="qg")

    def attn_qshard(self, name, qsb, kQB, ag_out_h, out_sb, kOUT, scale,
                    g_base=0, g_count=None, ag_key=("ag_out",), q_base=0,
                    shared=None):
        c = self.cfg
        em = self.em
        S, T, D = c.S, c.T, c.D
        I = D
        NG = I // 128
        if g_count is None:
            g_count = NG
        NK128 = S // 128
        NQTO = T // 128
        QBl = min(c.QB, NQTO)
        NJ = S // c.KB
        if shared is not None and "kv_g" in shared:
            kv_g = shared["kv_g"]
            vT, P_sb, PT_sb, sums = (shared["vT"], shared["P_sb"],
                                     shared["PT_sb"], shared["sums"])
            rrec, rrT, rb_sb, sqt = (shared["rrec"], shared["rrT"],
                                     shared["rb_sb"], shared["sqt"])
            rowbuf, kmax, cT, hsel = (shared["rowbuf"], shared["kmax"],
                                      shared["cT"], shared["hsel"])
            KO, VO = 0, S
            kKV = (name, "kv")
        else:
            kv_g = self.sb(self.fresh("aqkv"), [128, 2 * S], BF16)
            KO, VO = 0, S
            kKV = (name, "kv")
            vT = self.sb(self.fresh("aqvT"), [128, S], BF16)
            P_sb = self.sb(self.fresh("aqP"), [128, 4 * c.KB], BF16)
            PT_sb = self.sb(self.fresh("aqPT"), [128, 4 * 128], BF16)
            sums = self.sb(self.fresh("aqsum"), [128, 4 * 2 * NJ], F32)
            rrec = self.sb(self.fresh("aqrr"), [128, 8], F32)
            rrT = self.sb(self.fresh("aqrrT"), [8, 512], F32)
            rb_sb = self.sb(self.fresh("aqrb"), [128, 512], F32)
            sqt = self.sb(self.fresh("aqsq"), [128, 512], F32)
            rowbuf = self.sb(self.fresh("aqrow"), [128, S], F32)
            kmax = self.sb(self.fresh("aqkm"), [2, 1], F32)
            cT = self.sb(self.fresh("aqcT"), [128, 2 * NQTO], F32)
            hsel = self.sb(self.fresh("aqhsel"), [128, 2], F32)
            if shared is not None:
                shared.update(kv_g=kv_g, vT=vT, P_sb=P_sb, PT_sb=PT_sb,
                              sums=sums, rrec=rrec, rrT=rrT, rb_sb=rb_sb,
                              sqt=sqt, rowbuf=rowbuf, kmax=kmax, cT=cT,
                              hsel=hsel)
        em.op("vector", lambda e: e.memset(hsel[:, :], 0.0), writes=[(name, "hs")])
        em.op("vector", lambda e: e.memset(hsel[0:64, 0:1], 1.0), writes=[(name, "hs")])
        em.op("vector", lambda e: e.memset(hsel[64:128, 1:2], 1.0), writes=[(name, "hs")])
        kQN = (name, "qn")
        kKN = (name, "kn")
        c_scale128 = self.constf(float(scale), 128)
        c_t2 = self.constf(1e-30, 2)
        c_h2 = self.constf(0.5, 2)
        psO = self.p512[4]
        kO = ("p512", 4)
        pb5 = self.p512[5]
        pk5 = ("p512", 5)
        for g in range(g_base, g_base + g_count):
            for sec, off in ((0, KO), (1, VO)):
                for rank in range(2):
                    r0 = rank * 2 * I + sec * I + g * 128
                    em.op("gpsimd", lambda e, r0=r0, off=off, rank=rank: e.dma_start(
                        out=kv_g[:, off + rank * T:off + (rank + 1) * T],
                        in_=ag_out_h.ap()[r0:r0 + 128, :]),
                        reads=[ag_key], writes=[kKV], dom="qg")
            for kt in range(NK128):
                for h in range(2):
                    self.transpose(
                        kv_g[64 * h:64 * h + 64, VO + kt * 128:VO + (kt + 1) * 128],
                        64, 128, [kKV], vT, (h * NK128 + kt) * 64, (name, "vT"),
                        pti=kt % 2, bp=64 * h)
            for t0 in range(0, T, 512):
                tw = min(512, T - t0)
                em.op("scalar", lambda e, t0=t0, tw=tw, g=g: e.activation(
                    sqt[:, 0:tw],
                    qsb[:, (g - q_base) * T + t0:(g - q_base) * T + t0 + tw],
                    AF.Square),
                    reads=[kQB], writes=[(name, "sq")])
                em.op("tensor", lambda e, tw=tw: e.matmul(
                    pb5[0:2, 0:tw], hsel[:, 0:2], sqt[:, 0:tw],
                    start=True, stop=True),
                    reads=[(name, "hs"), (name, "sq")], writes=[pk5])
                em.op("vector", lambda e, t0=t0, tw=tw: e.tensor_copy(
                    rowbuf[0:2, t0:t0 + tw], pb5[0:2, 0:tw]),
                    reads=[pk5], writes=[kQN])
            for t0 in range(0, S, 512):
                em.op("scalar", lambda e, t0=t0: e.activation(
                    sqt[:, 0:512], kv_g[:, KO + t0:KO + t0 + 512], AF.Square),
                    reads=[kKV], writes=[(name, "sq")])
                em.op("tensor", lambda e: e.matmul(
                    pb5[0:2, 0:512], hsel[:, 0:2], sqt[:, 0:512],
                    start=True, stop=True),
                    reads=[(name, "hs"), (name, "sq")], writes=[pk5])
                em.op("vector", lambda e, t0=t0: e.tensor_copy(
                    rowbuf[32:34, t0:t0 + 512], pb5[0:2, 0:512]),
                    reads=[pk5], writes=[kKN])
            em.op("scalar", lambda e: e.activation(
                rowbuf[0:2, 0:T], rowbuf[0:2, 0:T], AF.Ln, bias=c_t2),
                reads=[kQN, ("c_cp",)], writes=[(name, "qn_")])
            em.op("scalar", lambda e: e.activation(
                rowbuf[0:2, 0:T], rowbuf[0:2, 0:T], AF.Exp, scale=c_h2),
                reads=[(name, "qn_"), ("c_cp",)], writes=[(name, "qn_")])
            em.op("scalar", lambda e: e.activation(
                rowbuf[32:34, :], rowbuf[32:34, :], AF.Ln, bias=c_t2),
                reads=[kKN, ("c_cp",)], writes=[(name, "kn_")])
            em.op("scalar", lambda e: e.activation(
                rowbuf[32:34, :], rowbuf[32:34, :], AF.Exp, scale=c_h2),
                reads=[(name, "kn_"), ("c_cp",)], writes=[(name, "kn_")])
            em.op("vector", lambda e: e.tensor_reduce(
                kmax[0:2, 0:1], rowbuf[32:34, :], AX.X, ALU.max),
                reads=[(name, "kn_")], writes=[(name, "km")])
            em.op("vector", lambda e: e.tensor_scalar(
                rowbuf[64:66, 0:T], rowbuf[0:2, 0:T], kmax[0:2, 0:1], -float(scale),
                ALU.mult, ALU.mult),
                reads=[(name, "qn_"), (name, "km")], writes=[(name, "c")])
            for qt in range(NQTO):
                em.op("tensor", lambda e, qt=qt: e.transpose(
                    pb5[0:128, 0:2], rowbuf[64:66, qt * 128:(qt + 1) * 128],
                    self.ident32[64:66, 64:66]),
                    reads=[(name, "c"), ("c_id32",)], writes=[pk5])
                em.op("vector", lambda e, qt=qt: e.tensor_copy(
                    cT[:, qt * 2:qt * 2 + 2], pb5[0:128, 0:2]),
                    reads=[pk5], writes=[(name, "cT")])
            NQB = NQTO // QBl
            for qb in range(NQB):
                em.op("vector", lambda e: e.memset(sums[:, :], 0.0),
                      writes=[(name, "sums")])
                for qloc in range(QBl):
                    qt = qb * QBl + qloc
                    for h in range(2):
                        for j in range(NJ):
                            sb_i = h * 2 + (j % 2)
                            Sb = self.p512[sb_i]
                            kS = ("p512", sb_i)
                            em.op("tensor", lambda e, h=h, qt=qt, j=j,
                                  g=g - q_base, Sb=Sb:
                                  e.matmul(Sb[0:128, 0:c.KB],
                                           qsb[64 * h:64 * h + 64,
                                               g * T + qt * 128:g * T + qt * 128 + 128],
                                           kv_g[64 * h:64 * h + 64,
                                                KO + j * c.KB:KO + (j + 1) * c.KB],
                                           start=True, stop=True,
                                           tile_position=(64 * h, 0)),
                                  reads=[kQB, kKV], writes=[kS])
                            pr = (h * 2 + j % 2) * c.KB
                            em.op("scalar", lambda e, h=h, qt=qt, qloc=qloc, j=j,
                                  Sb=Sb, pr=pr: e.activation(
                                      P_sb[:, pr:pr + c.KB],
                                      Sb[0:128, 0:c.KB], AF.Exp,
                                      bias=cT[:, qt * 2 + h:qt * 2 + h + 1],
                                      scale=c_scale128,
                                      accum_out=sums[:, (qloc * 2 + h) * NJ + j:
                                                     (qloc * 2 + h) * NJ + j + 1]),
                                  reads=[kS, (name, "cT"), ("c_cp",)],
                                  writes=[(name, "P", h, j % 2), (name, "sums")])
                            for kt in range(c.KB // 128):
                                ktabs = j * (c.KB // 128) + kt
                                pti = (kt * 2 + h) % 2
                                ptb = self.pT[pti]
                                pkT = ("pT", pti)
                                em.op("tensor", lambda e, ptb=ptb, pr=pr, kt=kt:
                                      e.transpose(
                                          ptb[0:128, 0:128],
                                          P_sb[:, pr + kt * 128:pr + kt * 128 + 128],
                                          self.ident[:, :]),
                                      reads=[(name, "P", h, j % 2), ("c_id",)],
                                      writes=[pkT])
                                pt_off = ((h * 2) + (kt % 2)) * 128
                                em.op("vector", lambda e, ptb=ptb, pt_off=pt_off:
                                      e.tensor_copy(PT_sb[:, pt_off:pt_off + 128],
                                                    ptb[0:128, 0:128]),
                                      reads=[pkT], writes=[(name, "PT", h, kt % 2)])
                                first = (j == 0 and kt == 0)
                                last = (j == NJ - 1 and kt == c.KB // 128 - 1)
                                em.op("tensor", lambda e, h=h, ktabs=ktabs, qloc=qloc,
                                      pt_off=pt_off, first=first, last=last: e.matmul(
                                          psO[64 * h:64 * h + 64,
                                              qloc * 128:qloc * 128 + 128],
                                          vT[:, (h * NK128 + ktabs) * 64:
                                             (h * NK128 + ktabs) * 64 + 64],
                                          PT_sb[:, pt_off:pt_off + 128],
                                          start=first, stop=last,
                                          tile_position=(0, 64 * h)),
                                      reads=[(name, "vT"), (name, "PT", h, kt % 2)],
                                      writes=[kO], sig=last)
                em.op("vector", lambda e: e.tensor_reduce(
                    rrec[:, 0:8],
                    sums[:, :].rearrange("p (a b) -> p a b", b=NJ),
                    AX.X, ALU.add),
                    reads=[(name, "sums")], writes=[(name, "rr")])
                em.op("vector", lambda e: e.reciprocal(rrec[:, 0:8], rrec[:, 0:8]),
                      reads=[(name, "rr")], writes=[(name, "rr")])
                for qloc in range(QBl):
                    em.op("tensor", lambda e, qloc=qloc: e.transpose(
                        pb5[0:2, 0:128], rrec[:, qloc * 2:qloc * 2 + 2],
                        self.ident32[0:128, 0:128]),
                        reads=[(name, "rr"), ("c_id32",)], writes=[pk5])
                    em.op("vector", lambda e, qloc=qloc: e.tensor_copy(
                        rrT[0:2, qloc * 128:qloc * 128 + 128], pb5[0:2, 0:128]),
                        reads=[pk5], writes=[(name, "rrT")])
                for qloc in range(QBl):
                    em.op("tensor", lambda e, qloc=qloc: e.matmul(
                        pb5[0:128, qloc * 128:qloc * 128 + 128],
                        self.hsel32[0:2, :],
                        rrT[0:2, qloc * 128:qloc * 128 + 128],
                        start=True, stop=True),
                        reads=[("c_hs",), (name, "rrT")], writes=[pk5])
                qw = QBl * 128
                em.op("vector", lambda e, qw=qw: e.tensor_copy(
                    rb_sb[:, 0:qw], pb5[0:128, 0:qw]),
                    reads=[pk5], writes=[(name, "rb")])
                em.op("vector", lambda e, qb=qb, g=g, qw=qw: e.tensor_mul(
                    out_sb[:, (g - g_base) * T + qb * qw:
                           (g - g_base) * T + qb * qw + qw],
                    psO[0:128, 0:qw], rb_sb[:, 0:qw]),
                    reads=[kO, (name, "rb")], writes=[kOUT])

    def allgather(self, in_h, out_h, ikey, okey):
        RG = [[i, i + 1] for i in range(0, self.cfg.ncores, 2)]
        self.em.op("gpsimd", lambda e: e.collective_compute(
            "AllGather", ALU.bypass, replica_groups=RG,
            ins=[in_h.ap().opt()], outs=[out_h.ap().opt()]),
            reads=[ikey], writes=[okey], dom="cc")

    def alltoall(self, in_h, out_h, ikey, okey):
        RG = [[i, i + 1] for i in range(0, self.cfg.ncores, 2)]
        self.em.op("gpsimd", lambda e: e.collective_compute(
            "AllToAll", ALU.bypass, replica_groups=RG,
            ins=[in_h.ap().opt()], outs=[out_h.ap().opt()]),
            reads=[ikey], writes=[okey], dom="cc")

    def wstream_bufs(self, maxnk, fgw=512, npar=2, with_stage=True):
        kp = self.fresh("wsb")
        wsl = [[self.sb(self.fresh("wt"), [128, fgw], BF16) for _ in range(maxnk)]
               for _ in range(npar)]
        if npar == 1:
            wsl.append(wsl[0])
        return {
            "stage": ([self.sb(self.fresh("wstg"), [128, fgw], F32) for _ in range(2)]
                      if with_stage else None),
            "wsl": wsl,
            "kp": kp, "fgw": fgw, "npar": npar,
        }

    def make_shipper(self, ag_h, row_map, name, agkey, tsubs_total=None,
                     bufs=None):
        c = self.cfg
        em = self.em
        T = c.T
        if bufs is None:
            ship = [self.sb(self.fresh("ship"), [128, T], BF16) for _ in range(2)]
            kname = name
        else:
            ship, kname = bufs
        state = {"n": 0}
        ntsub = tsubs_total if tsubs_total else c.NTS

        def evac(fabs, fcw, t0, tw, pb, pk):
            i = state["n"] // ntsub
            pp = i % 2
            em.op("vector", lambda e: e.tensor_copy(ship[pp][0:fcw, t0:t0 + tw],
                                                    pb[0:fcw, 0:tw]),
                  reads=[pk], writes=[(kname, "ship", pp)])
            state["n"] += 1
            if state["n"] % ntsub == 0:
                r0 = row_map(fabs)
                em.op("gpsimd", lambda e, r0=r0, fcw=fcw, pp=pp: e.dma_start(
                    out=ag_h.ap()[r0:r0 + fcw, :],
                    in_=ship[pp][0:fcw, :]),
                    reads=[(kname, "ship", pp)], writes=[agkey], dom="qg")
        return evac


def block_prologue(mk: MK, io, l1_variant=False, need_var=True, xqb_buf=None):
    c = mk.cfg
    em = mk.em
    T = c.T
    NCH = c.NCH
    x = mk.sb("x_stream", [128, c.NCH * T], F32)
    xkeyf = lambda k: ("x", k)
    for k in range(c.NCH):
        em.op("sync", lambda e, k=k: e.dma_start(
            out=x[:, k * T:(k + 1) * T], in_=io["x_in"].ap()[k * 128:(k + 1) * 128, :]),
            writes=[xkeyf(k)], dom="qs")
    mk.setup_meta(io["meta"].ap())
    xnb = mk.sb("xnb", [128, c.NCH * T], BF16)
    xnkeyf = lambda k: ("xnb", k)
    var_keep = mk.sb("var_keep", [1, T], F32) if need_var else None
    mu_keep = mk.sb("mu_keep", [1, T], F32) if need_var else None
    out = {"x": x, "xkeyf": xkeyf, "xnb": xnb, "xnkeyf": xnkeyf,
           "var_row": (var_keep, ("var_keep",)),
           "mu_row": (mu_keep, ("mu_keep",))}
    if l1_variant:
        xqb = xqb_buf if xqb_buf is not None else mk.sb("xqb", [128, c.NCH * T], BF16)
        out["xqb"] = xqb
        out["xqkeyf"] = lambda k: ("xqb", k)
    with mk.scope():
        PW = min(512, T)
        sqp = [mk.sb(mk.fresh("plsq"), [128, PW], F32) for _ in range(2)]
        prow = mk.sb("pl_row", [1, 4 * PW], F32)
        mub = mk.sb("pl_mub", [128, PW], F32)
        rbt = mk.sb("pl_rbt", [128, PW], F32)
        Rbt = mk.sb("pl_Rbt", [128, PW], F32) if l1_variant else None
        tmpt = [mk.sb(mk.fresh("pltm"), [128, PW], F32) for _ in range(2)]
        c_eps1 = mk.constf(1e-5, 1)
        c_mh1 = mk.constf(-0.5, 1)
        Dv = float(c.D)
        eps = 1e-5
        for (t0, tw) in [(i * PW, PW) for i in range(T // PW)]:
            pbS, pkS = mk.pbank()
            pbQ, pkQ = mk.pbank()
            for k in range(NCH):
                pp = k % 2
                kq = ("pl_sq", pp)
                em.op("scalar", lambda e, k=k, t0=t0, tw=tw, pp=pp:
                      e.activation(sqp[pp][:, 0:tw],
                                   x[:, k * T + t0:k * T + t0 + tw],
                                   AF.Square),
                      reads=[xkeyf(k)], writes=[kq])
                em.op("tensor", lambda e, k=k, t0=t0, tw=tw, pbS=pbS:
                      e.matmul(pbS[0:1, 0:tw], mk.ones_col[:, 0:1],
                               x[:, k * T + t0:k * T + t0 + tw],
                               start=(k == 0), stop=(k == NCH - 1)),
                      reads=[("c_oc",), xkeyf(k)], writes=[pkS])
                em.op("tensor", lambda e, k=k, tw=tw, pp=pp, pbQ=pbQ:
                      e.matmul(pbQ[0:1, 0:tw], mk.ones_col[:, 0:1],
                               sqp[pp][:, 0:tw],
                               start=(k == 0), stop=(k == NCH - 1)),
                      reads=[("c_oc",), kq], writes=[pkQ])
            em.op("vector", lambda e, tw=tw, pbS=pbS: e.tensor_scalar_mul(
                prow[0:1, 0:tw], pbS[0:1, 0:tw], 1.0 / Dv),
                reads=[pkS], writes=[("pl_row",)])
            em.op("vector", lambda e, tw=tw, pbQ=pbQ: e.tensor_scalar_mul(
                prow[0:1, PW:PW + tw], pbQ[0:1, 0:tw], 1.0 / Dv),
                reads=[pkQ], writes=[("pl_row",)])
            em.op("vector", lambda e, tw=tw: e.scalar_tensor_tensor(
                prow[0:1, 2 * PW:2 * PW + tw], prow[0:1, 0:tw], -1.0,
                prow[0:1, 0:tw], ALU.mult, ALU.mult),
                reads=[("pl_row",)], writes=[("pl_row",)])
            em.op("vector", lambda e, tw=tw: e.tensor_add(
                prow[0:1, 2 * PW:2 * PW + tw],
                prow[0:1, 2 * PW:2 * PW + tw], prow[0:1, PW:PW + tw]),
                reads=[("pl_row",)], writes=[("pl_row",)])
            if need_var:
                em.op("vector", lambda e, t0=t0, tw=tw: e.tensor_copy(
                    var_keep[0:1, t0:t0 + tw], prow[0:1, 2 * PW:2 * PW + tw]),
                    reads=[("pl_row",)], writes=[("var_keep",)])
                em.op("vector", lambda e, t0=t0, tw=tw: e.tensor_copy(
                    mu_keep[0:1, t0:t0 + tw], prow[0:1, 0:tw]),
                    reads=[("pl_row",)], writes=[("mu_keep",)])
            if l1_variant:
                em.op("vector", lambda e, tw=tw: e.tensor_single_scalar(
                    prow[0:1, 3 * PW:3 * PW + tw],
                    prow[0:1, 2 * PW:2 * PW + tw], eps, ALU.add),
                    reads=[("pl_row",)], writes=[("pl_row",)])
                em.op("vector", lambda e, tw=tw: e.reciprocal(
                    prow[0:1, 3 * PW:3 * PW + tw],
                    prow[0:1, 3 * PW:3 * PW + tw]),
                    reads=[("pl_row",)], writes=[("pl_row",)])
                em.op("vector", lambda e, tw=tw: e.tensor_mul(
                    prow[0:1, 3 * PW:3 * PW + tw],
                    prow[0:1, 3 * PW:3 * PW + tw],
                    prow[0:1, 2 * PW:2 * PW + tw]),
                    reads=[("pl_row",)], writes=[("pl_row",)])
            em.op("scalar", lambda e, tw=tw: e.activation(
                prow[0:1, 2 * PW:2 * PW + tw],
                prow[0:1, 2 * PW:2 * PW + tw], AF.Ln, bias=c_eps1),
                reads=[("pl_row",), ("c_cp",)], writes=[("pl_row",)])
            em.op("scalar", lambda e, tw=tw: e.activation(
                prow[0:1, 2 * PW:2 * PW + tw],
                prow[0:1, 2 * PW:2 * PW + tw], AF.Exp, scale=c_mh1),
                reads=[("pl_row",), ("c_cp",)], writes=[("pl_row",)])
            mk.bcast_row(prow, tw, ("pl_row",), mub, ("pl_mub",), col_off=0)
            mk.bcast_row(prow, tw, ("pl_row",), rbt, ("pl_rbt",),
                         col_off=2 * PW)
            if l1_variant:
                em.op("scalar", lambda e, tw=tw: e.activation(
                    prow[0:1, 3 * PW:3 * PW + tw],
                    prow[0:1, 3 * PW:3 * PW + tw], AF.Ln, bias=c_eps1),
                    reads=[("pl_row",), ("c_cp",)], writes=[("pl_row",)])
                em.op("scalar", lambda e, tw=tw: e.activation(
                    prow[0:1, 3 * PW:3 * PW + tw],
                    prow[0:1, 3 * PW:3 * PW + tw], AF.Exp, scale=c_mh1),
                    reads=[("pl_row",), ("c_cp",)], writes=[("pl_row",)])
                em.op("vector", lambda e, tw=tw: e.tensor_mul(
                    prow[0:1, 3 * PW:3 * PW + tw],
                    prow[0:1, 3 * PW:3 * PW + tw],
                    prow[0:1, 2 * PW:2 * PW + tw]),
                    reads=[("pl_row",)], writes=[("pl_row",)])
                mk.bcast_row(prow, tw, ("pl_row",), Rbt, ("pl_Rbt",),
                             col_off=3 * PW)
            for k in range(NCH):
                pp = k % 2
                tk = ("pl_tm", pp)
                em.op("vector", lambda e, k=k, t0=t0, tw=tw, pp=pp:
                      e.tensor_sub(tmpt[pp][:, 0:tw],
                                   x[:, k * T + t0:k * T + t0 + tw],
                                   mub[:, 0:tw]),
                      reads=[xkeyf(k), ("pl_mub",)], writes=[tk])
                em.op("gpsimd", lambda e, k=k, t0=t0, tw=tw, pp=pp:
                      e.tensor_mul(xnb[:, k * T + t0:k * T + t0 + tw],
                                   tmpt[pp][:, 0:tw], rbt[:, 0:tw]),
                      reads=[tk, ("pl_rbt",)], writes=[xnkeyf(k)])
                if l1_variant:
                    em.op("vector", lambda e, k=k, t0=t0, tw=tw, pp=pp:
                          e.tensor_mul(
                              out["xqb"][:, k * T + t0:k * T + t0 + tw],
                              tmpt[pp][:, 0:tw], Rbt[:, 0:tw]),
                          reads=[tk, ("pl_Rbt",)], writes=[out["xqkeyf"](k)])
    return out


def block_epilogue(mk: MK, pro, io):
    c = mk.cfg
    em = mk.em
    T = c.T
    x = pro["x"]
    for k in range(c.NCH):
        em.op("gpsimd", lambda e, k=k: e.dma_start(
            out=io["x_out"].ap()[k * 128:(k + 1) * 128, :], in_=x[:, k * T:(k + 1) * T]),
            reads=[pro["xkeyf"](k)], writes=[("x_out",)], dom="qg")
    em.finalize_output_wait("sync")


def resid_evac(mk, pro, tag):
    em = mk.em
    T = mk.cfg.T
    x = pro["x"]
    xkeyf = pro["xkeyf"]

    def evac(fabs, fcw, t0, tw, pb, pk):
        k = fabs // 128
        em.op("vector", lambda e: e.tensor_add(
            x[0:fcw, k * T + t0:k * T + t0 + tw],
            x[0:fcw, k * T + t0:k * T + t0 + tw], pb[0:fcw, 0:tw]),
            reads=[xkeyf(k), pk], writes=[xkeyf(k)])
    return evac


def emit_block0(mk: MK, io):
    c = mk.cfg
    em = mk.em
    T, D, S = c.T, c.D, c.S
    pro = block_prologue(mk, io, need_var=False)
    xnb, xnkeyf = pro["xnb"], pro["xnkeyf"]
    ag_in = mk.dram("ag_in", [3 * D, T], BF16)
    ag_out = mk.dram("ag_out", [2 * 3 * D, T], BF16)
    ag2_in = mk.dram("ag2_in", [c.NGC * 128, S], BF16)
    ag2_out = mk.dram("ag2_out", [2 * c.NGC * 128, S], BF16)
    with mk.scope():
        wbA = mk.wstream_bufs(2 * c.NCH)
        shipbufs = ([mk.sb(mk.fresh("ship"), [128, T], BF16) for _ in range(2)],
                    "shp_sh")
        for sec, (wname, bname) in enumerate((("Wq", "bq"), ("Wk", "bk"), ("Wv", "bv"))):
            rmap = (lambda fabs, sec=sec: sec * D + fabs)
            ship = mk.make_shipper(ag_in, rmap, f"shp{sec}", ("ag_in",),
                                   bufs=shipbufs)
            mk.proj(f"a0{wname}", io[wname].ap(), D, D, xnb, xnkeyf, ship,
                    bias_ap=io[bname].ap(), wbufs=wbA)
        mk.allgather(ag_in, ag_out, ("ag_in",), ("ag_out",))
    with mk.scope():
        maskA = mk.sb("maskA_sb", [128, 128], F32)
        em.op("sync", lambda e: e.dma_start(out=maskA[:, :], in_=io["maskA"].ap()),
              writes=[("maskA",)], dom="qs")
        mk.attn_headshard("a0", ag_out, ag2_in, mode="softmax", scale=1.0 / 8.0,
                          maskA16=(maskA, ("maskA",)))
        mk.allgather(ag2_in, ag2_out, ("a0", "ag2_in"), ("ag2_out",))
    wb_qkv = mk.wstream_bufs(2 * c.NCH)
    with mk.scope():
        aob = mk.sb("aob", [128, c.NCH * T], BF16)
        aokeyf = lambda k: ("aob", k)
        dst4 = aob[:, :].rearrange("p (k t) -> p k t", k=c.NCH)
        mk.dyn_load("sync", dst4, ag2_out, 0,
                    [[S, 128], [128 * S, c.NCH], [1, T]],
                    [("ag2_out",)], [aokeyf(k) for k in range(c.NCH)], base="tb")
        mk.proj("a0Wo", io["Wo"].ap(), D, D, aob, aokeyf, resid_evac(mk, pro, "wo"),
                bias_ap=io["bo"].ap(), wbufs=wb_qkv)
    with mk.scope():
        emit_l0ffn(mk, pro, io, wb_qkv)
    block_epilogue(mk, pro, io)


def emit_l0ffn(mk: MK, pro, io, wbufs):
    c = mk.cfg
    em = mk.em
    T, D = c.T, c.D
    E = 8
    HF = 2 * D
    HCH = HF // 128
    xnb, xnkeyf = pro["xnb"], pro["xnkeyf"]
    x, xkeyf = pro["x"], pro["xkeyf"]
    wg16 = []
    wg_stage = mk.sb("wg_stg", [128, E], F32)
    for k in range(c.NCH):
        s16 = mk.sb(f"wg16_{k}", [128, E], BF16)
        em.op("sync", lambda e, k=k: e.dma_start(
            out=wg_stage[:, :], in_=io["Wg"].ap()[k * 128:(k + 1) * 128, :]),
            writes=[("wg_stg",)], dom="qs")
        em.op("vector", lambda e, s16=s16: e.tensor_copy(s16[:, :], wg_stage[:, :]),
              reads=[("wg_stg",)], writes=[("wg16", k)])
        wg16.append(s16)
    wT = mk.sb("l0g_wT", [E, T], F32)
    lg = mk.sb("l0g_lg", [128, E], F32)
    m1 = mk.sb("l0g_m1", [128, 1], F32)
    eq1 = mk.sb("l0g_eq1", [128, E], F32)
    msk = mk.sb("l0g_msk", [128, E], F32)
    m2 = mk.sb("l0g_m2", [128, 1], F32)
    eq2 = mk.sb("l0g_eq2", [128, E], F32)
    d2 = mk.sb("l0g_d2", [128, 1], F32)
    den = mk.sb("l0g_den", [128, 1], F32)
    p2 = mk.sb("l0g_p2", [128, 1], F32)
    wtok = mk.sb("l0g_w", [128, E], F32)
    for tt in range(T // 128):
        t0 = tt * 128
        pb, pk = mk.pbank()
        for k in range(c.NCH):
            em.op("tensor", lambda e, k=k, t0=t0, pb=pb: e.matmul(
                pb[0:128, 0:E], xnb[:, k * T + t0:k * T + t0 + 128], wg16[k][:, :],
                start=(k == 0), stop=(k == c.NCH - 1)),
                reads=[xnkeyf(k), ("wg16", k)], writes=[pk], sig=(k == c.NCH - 1))
        em.op("vector", lambda e, pb=pb: e.tensor_copy(lg[:, :], pb[0:128, 0:E]),
              reads=[pk], writes=[("lg",)])
        em.op("vector", lambda e: e.tensor_reduce(m1[:, 0:1], lg[:, :], AX.X, ALU.max),
              reads=[("lg",)], writes=[("m1",)])
        em.op("vector", lambda e: e.tensor_scalar(
            eq1[:, :], lg[:, :], m1[:, 0:1], None, ALU.is_equal),
            reads=[("lg",), ("m1",)], writes=[("eq1",)])
        em.op("vector", lambda e: e.scalar_tensor_tensor(
            msk[:, :], eq1[:, :], -1e30, lg[:, :], ALU.mult, ALU.add),
            reads=[("eq1",), ("lg",)], writes=[("msk",)])
        em.op("vector", lambda e: e.tensor_reduce(m2[:, 0:1], msk[:, :], AX.X, ALU.max),
              reads=[("msk",)], writes=[("m2",)])
        em.op("vector", lambda e: e.tensor_scalar(
            eq2[:, :], msk[:, :], m2[:, 0:1], None, ALU.is_equal),
            reads=[("msk",), ("m2",)], writes=[("eq2",)])
        em.op("vector", lambda e: e.tensor_sub(d2[:, 0:1], m2[:, 0:1], m1[:, 0:1]),
              reads=[("m2",), ("m1",)], writes=[("d2",)])
        em.op("scalar", lambda e: e.activation(d2[:, 0:1], d2[:, 0:1], AF.Exp),
              reads=[("d2",)], writes=[("d2",)])
        em.op("vector", lambda e: e.tensor_single_scalar(
            den[:, 0:1], d2[:, 0:1], 1.0, ALU.add), reads=[("d2",)], writes=[("den",)])
        em.op("vector", lambda e: e.reciprocal(den[:, 0:1], den[:, 0:1]),
              reads=[("den",)], writes=[("den",)])
        em.op("vector", lambda e: e.tensor_mul(p2[:, 0:1], d2[:, 0:1], den[:, 0:1]),
              reads=[("d2",), ("den",)], writes=[("p2",)])
        em.op("vector", lambda e: e.tensor_scalar(
            wtok[:, :], eq1[:, :], den[:, 0:1], None, ALU.mult),
            reads=[("eq1",), ("den",)], writes=[("wtok",)])
        em.op("vector", lambda e: e.scalar_tensor_tensor(
            wtok[:, :], eq2[:, :], p2[:, 0:1], wtok[:, :], ALU.mult, ALU.add),
            reads=[("eq2",), ("p2",), ("wtok",)], writes=[("wtok",)])
        pb2, pk2 = mk.pbank()
        em.op("tensor", lambda e, pb2=pb2: e.transpose(
            pb2[0:E, 0:128], wtok[:, 0:E], mk.ident32[:, :]),
            reads=[("wtok",), ("c_id32",)], writes=[pk2])
        em.op("vector", lambda e, t0=t0, pb2=pb2: e.tensor_copy(
            wT[0:E, t0:t0 + 128], pb2[0:E, 0:128]), reads=[pk2], writes=[("wT",)])
    T2 = T // 2
    NT2 = max(1, T2 // c.TSUB)
    us16 = [mk.sb(f"l0f_us{i}", [128, T2], BF16) for i in range(2)]
    h16 = mk.sb("l0f_h", [128, HCH * T2], BF16)
    hkeyf = lambda k: ("h16", k)
    wbe = mk.sb("l0f_wb", [128, T2], F32)
    for ts2 in range(2):
        tbase = ts2 * T2
        tsubs = [(tbase + i * c.TSUB, min(c.TSUB, T2)) for i in range(NT2)]
        for e_ in range(E):
            mk.bcast_selrow(wT, E, e_, T2, ("wT",), wbe, ("wbe",), col_off=tbase)

            def evac_uv(fabs, fcw, t0, tw, pb, pk, tbase=tbase):
                fc2 = fabs // 128
                kk = fc2 // 2
                pp = kk % 2
                lo = kk * T2 + t0 - tbase
                if fc2 % 2 == 0:
                    em.op("scalar", lambda e: e.activation(
                        us16[pp][0:fcw, t0 - tbase:t0 - tbase + tw],
                        pb[0:fcw, 0:tw], AF.Silu),
                        reads=[pk], writes=[("us16", pp)])
                else:
                    em.op("vector", lambda e: e.tensor_mul(
                        h16[0:fcw, lo:lo + tw],
                        us16[pp][0:fcw, t0 - tbase:t0 - tbase + tw],
                        pb[0:fcw, 0:tw]),
                        reads=[("us16", pp), pk], writes=[("h16a", kk)])
                    em.op("gpsimd", lambda e: e.tensor_mul(
                        h16[0:fcw, lo:lo + tw], h16[0:fcw, lo:lo + tw],
                        wbe[0:fcw, t0 - tbase:t0 - tbase + tw]),
                        reads=[("h16a", kk), ("wbe",)], writes=[hkeyf(kk)])

            mk.proj(f"w12_{ts2}_{e_}", io["W12r"].ap()[e_ * D:(e_ + 1) * D, :], D,
                    2 * HF, xnb, xnkeyf, evac_uv, wbufs=wbufs, tsubs=tsubs)

            def evac_y(fabs, fcw, t0, tw, pb, pk):
                kk = fabs // 128
                em.op("vector", lambda e: e.tensor_add(
                    x[0:fcw, kk * T + t0:kk * T + t0 + tw],
                    x[0:fcw, kk * T + t0:kk * T + t0 + tw], pb[0:fcw, 0:tw]),
                    reads=[xkeyf(kk), pk], writes=[xkeyf(kk)])

            mk.proj(f"w3_{ts2}_{e_}", io["W3r"].ap()[e_ * HF:(e_ + 1) * HF, :], HF, D,
                    h16, hkeyf, evac_y, wbufs=wbufs, tsubs=tsubs,
                    xoff_fn=lambda k, t0, tbase=tbase: k * T2 + t0 - tbase)


BLOCK_INPUTS = {
    0: lambda cfg: [(n, [cfg.D, cfg.D]) for n in ("Wq", "Wk", "Wv", "Wo")]
    + [(n, [1, cfg.D]) for n in ("bq", "bk", "bv", "bo")]
    + [("Wg", [cfg.D, 8]), ("W12r", [8 * cfg.D, 4 * cfg.D]),
       ("W3r", [8 * 2 * cfg.D, cfg.D]), ("maskA", [128, 128])],
}

BLOCK_EMITTERS = {}


def build_block(cfg, blk):
    nc = bass.Bass("TRN2", target_bir_lowering=False)
    io = {}
    D, T = cfg.D, cfg.T
    io["x_in"] = nc.dram_tensor("x_in", [D, T], F32, kind="ExternalInput")
    io["x_out"] = nc.dram_tensor("x_out", [D, T], F32, kind="ExternalOutput")
    io["meta"] = nc.dram_tensor("meta", [1, 4], I32, kind="ExternalInput")
    for ent in BLOCK_INPUTS[blk](cfg):
        n, sh = ent[0], ent[1]
        dt = ent[2] if len(ent) > 2 else F32
        io[n] = nc.dram_tensor(n, sh, dt, kind="ExternalInput")
    with ExitStack() as st:
        em = EM(nc)
        mk = MK(cfg, nc, em, st)
        mk.setup()
        BLOCK_EMITTERS[blk](mk, io)
        em.realize(st)
    nc._em_debug = em
    return nc


BLOCK_EMITTERS[0] = emit_block0

BLOCK_INPUTS[1] = lambda cfg: [
    ("ln_g", [1, cfg.D]), ("ln_b", [1, cfg.D]), ("w_qkv", [cfg.D, 3 * cfg.D]),
    ("w_out", [cfg.D, cfg.D]), ("b_out", [1, cfg.D]),
    ("gate_wT", [cfg.D, 8]), ("gate_b", [1, 8]),
    ("wiT", [cfg.D, 4 * cfg.D]), ("bi1", [1, 4 * cfg.D]),
    ("woT", [4 * cfg.D, cfg.D]), ("bo1", [1, cfg.D]),
    ("loraAT", [8 * cfg.D, 16]), ("loraBTp", [3 * 128, 4 * cfg.D]),
    ("rope_cos", [128, cfg.T], BF16), ("rope_sin", [128, cfg.T], BF16),
    ("R1", [128, 128], BF16),
]

BLOCK_PERCORE = {}


def _rope1_tables(cfg, core):
    T, HD = cfg.T, cfg.HD
    half = HD // 2
    t0 = (core % 2) * T
    pos = np.arange(t0, t0 + T, dtype=np.float64)
    inv = 1.0 / 10000.0 ** (np.arange(0, HD, 2, dtype=np.float64) / HD)
    ang = pos[:, None] * inv[None, :]
    s = np.sin(ang)
    co = np.cos(ang)
    dd = np.arange(128) % HD
    C = co[:, dd % half].T.astype(np.float32)
    Sn = s[:, dd % half].T.astype(np.float32)
    return np.ascontiguousarray(C), np.ascontiguousarray(Sn)


def _rope1_R(cfg):
    HD = cfg.HD
    half = HD // 2
    M = np.zeros((HD, HD), np.float32)
    for d in range(half):
        M[d, d + half] = -1.0
        M[d + half, d] = 1.0
    Mt = M.T
    return np.ascontiguousarray(np.kron(np.eye(128 // HD, dtype=np.float32), Mt))


def _percore_block1(cfg, core, inputs=None):
    import ml_dtypes
    C, Sn = _rope1_tables(cfg, core)
    bf = ml_dtypes.bfloat16
    return {"rope_cos": C.astype(bf), "rope_sin": Sn.astype(bf),
            "R1": _rope1_R(cfg).astype(bf)}


BLOCK_PERCORE[1] = _percore_block1

BLOCK_INPUTS[2] = lambda cfg: [
    (n, [cfg.D, cfg.D]) for n in ("Wq2", "Wk2", "Wv2")] + [
    (n, [1, cfg.D]) for n in ("bq2", "bk2", "bv2")] + [
    ("maskA", [128, 128]),
    ("rope2_cos", [128, cfg.T], BF16), ("rope2_sin", [128, cfg.T], BF16),
    ("R2", [128, 128], BF16),
    ("sim", [cfg.D, 16]), ("gates", [1, 16]),
    ("qpf", [cfg.D, 1024]), ("kpf", [cfg.D, 1024]), ("vpf", [cfg.D, 1024]),
    ("opR", [1024, cfg.D]),
]


def _rope2_tables(cfg, core):
    T, HD = cfg.T, cfg.HD
    t0 = (core % 2) * T
    pos = np.arange(t0, t0 + T, dtype=np.float64)
    inv = 10000.0 ** (-2.0 * np.arange(HD // 2, dtype=np.float64) / HD)
    ang = pos[:, None] * inv[None, :]
    cosr = np.repeat(np.cos(ang), 2, axis=1)
    sinr = np.repeat(np.sin(ang), 2, axis=1)
    dd = np.arange(128) % HD
    C = cosr[:, dd].T.astype(np.float32)
    Sn = sinr[:, dd].T.astype(np.float32)
    return np.ascontiguousarray(C), np.ascontiguousarray(Sn)


def _rope2_R():
    blk = np.array([[0.0, 1.0], [-1.0, 0.0]], np.float32)
    return np.ascontiguousarray(np.kron(np.eye(64, dtype=np.float32), blk))


def _percore_block2(cfg, core, inputs=None):
    import ml_dtypes
    bf = ml_dtypes.bfloat16
    C, Sn = _rope2_tables(cfg, core)
    return {"rope2_cos": C.astype(bf), "rope2_sin": Sn.astype(bf),
            "R2": _rope2_R().astype(bf)}


BLOCK_PERCORE[2] = _percore_block2


def _interleave_w12(W1, W2):
    E, D, HF = W1.shape
    out = np.empty((E * D, 2 * HF), np.float32)
    for e in range(E):
        a = W1[e].reshape(D, HF // 128, 128)
        b = W2[e].reshape(D, HF // 128, 128)
        ab = np.stack([a, b], axis=2).reshape(D, 2 * HF)
        out[e * D:(e + 1) * D] = ab
    return np.ascontiguousarray(out)


def _mask_diag():
    p = np.arange(128)[:, None]
    f = np.arange(128)[None, :]
    return np.where(f <= p, 0.0, -1e9).astype(np.float32)


def _meta_np(cfg, core):
    parity = core % 2
    return np.array([[parity * (cfg.HC * cfg.HD), parity * cfg.T, parity, 0]],
                    dtype=np.int32)


def _prep_block_weights(cfg, blk, inputs):
    f = np.float32
    if blk == 0:
        D = cfg.D
        return {
            "Wq": np.ascontiguousarray(inputs["L0_attn_Wq"], f),
            "Wk": np.ascontiguousarray(inputs["L0_attn_Wk"], f),
            "Wv": np.ascontiguousarray(inputs["L0_attn_Wv"], f),
            "Wo": np.ascontiguousarray(inputs["L0_attn_Wo"], f),
            "bq": np.ascontiguousarray(inputs["L0_attn_bq"], f).reshape(1, D),
            "bk": np.ascontiguousarray(inputs["L0_attn_bk"], f).reshape(1, D),
            "bv": np.ascontiguousarray(inputs["L0_attn_bv"], f).reshape(1, D),
            "bo": np.ascontiguousarray(inputs["L0_attn_bo"], f).reshape(1, D),
            "Wg": np.ascontiguousarray(inputs["L0_ffn_Wg"], f),
            "W12r": _interleave_w12(np.asarray(inputs["L0_ffn_W1"], f),
                                    np.asarray(inputs["L0_ffn_W2"], f)),
            "W3r": np.ascontiguousarray(inputs["L0_ffn_W3"], f).reshape(8 * 2 * D, D),
            "maskA": _mask_diag(),
        }
    if blk == 2:
        D = cfg.D
        qp = np.asarray(inputs["L2_ffn_q_proj"], f)
        kp = np.asarray(inputs["L2_ffn_k_proj"], f)
        vp = np.asarray(inputs["L2_ffn_v_proj"], f)
        op = np.asarray(inputs["L2_ffn_o_proj"], f)
        return {
            "Wq2": np.ascontiguousarray(inputs["L2_attn_Wq"], f),
            "Wk2": np.ascontiguousarray(inputs["L2_attn_Wk"], f),
            "Wv2": np.ascontiguousarray(inputs["L2_attn_Wv"], f),
            "bq2": np.ascontiguousarray(inputs["L2_attn_bq"], f).reshape(1, D),
            "bk2": np.ascontiguousarray(inputs["L2_attn_bk"], f).reshape(1, D),
            "bv2": np.ascontiguousarray(inputs["L2_attn_bv"], f).reshape(1, D),
            "maskA": _mask_diag(),
            "sim": np.ascontiguousarray(inputs["L2_ffn_sim_matrix"], f),
            "gates": np.ascontiguousarray(inputs["L2_ffn_gates"], f).reshape(1, 16),
            "qpf": np.ascontiguousarray(qp.transpose(1, 0, 2).reshape(D, 1024)),
            "kpf": np.ascontiguousarray(kp.transpose(1, 0, 2).reshape(D, 1024)),
            "vpf": np.ascontiguousarray(vp.transpose(1, 0, 2).reshape(D, 1024)),
            "opR": np.ascontiguousarray(op.reshape(1024, D)),
        }
    if blk == 1:
        D = cfg.D
        E, RK = 8, 16
        A = np.asarray(inputs["L1_ffn_lora_A"], f)
        B = np.asarray(inputs["L1_ffn_lora_B"], f)
        DF = B.shape[1]
        lbtp = np.zeros((3 * 128, DF), f)
        for e in range(E):
            t, j = e // 3, e % 3
            lbtp[t * 128 + 32 * j:t * 128 + 32 * j + RK, :] = B[e].T
        return {
            "ln_g": np.ascontiguousarray(inputs["L1_attn_ln_gamma"], f).reshape(1, D),
            "ln_b": np.ascontiguousarray(inputs["L1_attn_ln_beta"], f).reshape(1, D),
            "w_qkv": np.ascontiguousarray(inputs["L1_attn_w_qkv"], f),
            "w_out": np.ascontiguousarray(inputs["L1_attn_w_out"], f),
            "b_out": np.ascontiguousarray(inputs["L1_attn_b_out"], f).reshape(1, D),
            "gate_wT": np.ascontiguousarray(np.asarray(inputs["L1_ffn_gate_w"], f).T),
            "gate_b": np.ascontiguousarray(inputs["L1_ffn_gate_b"], f).reshape(1, E),
            "wiT": np.ascontiguousarray(np.asarray(inputs["L1_ffn_wi"], f).T),
            "bi1": np.ascontiguousarray(inputs["L1_ffn_bi"], f).reshape(1, DF),
            "woT": np.ascontiguousarray(np.asarray(inputs["L1_ffn_wo"], f).T),
            "bo1": np.ascontiguousarray(inputs["L1_ffn_bo"], f).reshape(1, D),
            "loraAT": np.ascontiguousarray(
                A.transpose(0, 2, 1).reshape(E * D, RK)),
            "loraBTp": lbtp,
        }
    if blk == 3:
        return _prep_block3(cfg, inputs)
    raise KeyError(blk)


N_BLOCKS = 3


def build_fused(cfg):
    nc = bass.Bass("TRN2", target_bir_lowering=False)
    D, T = cfg.D, cfg.T
    x_in = nc.dram_tensor("x_in", [D, T], F32, kind="ExternalInput")
    x_out = nc.dram_tensor("x_out", [D, T], F32, kind="ExternalOutput")
    meta = nc.dram_tensor("meta", [1, 4], I32, kind="ExternalInput")
    ios = []
    for blk in range(N_BLOCKS):
        io = {"meta": meta}
        for ent in BLOCK_INPUTS[blk](cfg):
            n, sh = ent[0], ent[1]
            dt = ent[2] if len(ent) > 2 else F32
            io[n] = nc.dram_tensor(f"i{blk}_{n}", sh, dt, kind="ExternalInput")
        ios.append(io)
    with ExitStack() as st:
        em = EM(nc)
        mk = MK(cfg, nc, em, st)
        mk.setup()
        xmid = [nc.dram_tensor(f"xmid{b}", [D, T], F32, kind="Internal")
                for b in range(N_BLOCKS - 1)]
        for blk in range(N_BLOCKS):
            io = ios[blk]
            io["x_in"] = x_in if blk == 0 else xmid[blk - 1]
            io["x_out"] = x_out if blk == N_BLOCKS - 1 else xmid[blk]
            for attr in [a for a in list(vars(mk))
                         if a.startswith("_esel") or a.startswith("_qsel")]:
                delattr(mk, attr)
            with mk.scope():
                BLOCK_EMITTERS[blk](mk, io)
        em.realize(st)
    nc._em_debug = em
    return nc


def _fused_inmaps(cfg, inputs):
    wd_all = {}
    for blk in range(N_BLOCKS):
        for n, a in _prep_block_weights(cfg, blk, inputs).items():
            wd_all[f"i{blk}_{n}"] = a
    x = np.asarray(inputs["L0_attn_X"], dtype=np.float32)
    in_maps = []
    for core in range(cfg.ncores):
        b, half = core // 2, core % 2
        m = {"x_in": np.ascontiguousarray(
                x[b, half * cfg.T:(half + 1) * cfg.T, :].T),
             "meta": _meta_np(cfg, core)}
        m.update(wd_all)
        for blk in sorted(BLOCK_PERCORE):
            if blk >= N_BLOCKS:
                continue
            for n, a in BLOCK_PERCORE[blk](cfg, core, inputs).items():
                m[f"i{blk}_{n}"] = a
        in_maps.append(m)
    return in_maps


def kernel(**inputs):
    cfg = Cfg()
    x = np.asarray(inputs["L0_attn_X"], dtype=np.float32)
    B = x.shape[0]
    nc = build_fused(cfg)
    in_maps = _fused_inmaps(cfg, inputs)
    res = run_bass_kernel_spmd(nc, in_maps, core_ids=list(range(cfg.ncores)))
    out = np.empty((B, cfg.S, cfg.D), np.float32)
    for core in range(cfg.ncores):
        b, half = core // 2, core % 2
        out[b, half * cfg.T:(half + 1) * cfg.T, :] = np.ascontiguousarray(
            res.results[core]["x_out"]).T
    return out


def emit_block1(mk: MK, io):
    c = mk.cfg
    em = mk.em
    T, D, S = c.T, c.D, c.S
    I = D
    NG = I // 128
    b1u = mk.sb("b1_u", [128, NG * T], BF16)
    pro = block_prologue(mk, io, l1_variant=True, need_var=False, xqb_buf=b1u)
    x, xkeyf = pro["x"], pro["xkeyf"]
    xnb, xnkeyf = pro["xnb"], pro["xnkeyf"]
    wbp = mk.wstream_bufs(8, npar=1)
    qdr = mk.dram("b1_qdr", [NG * 128, T], BF16)
    ag_in = mk.dram("b1ag_in", [2 * I, T], BF16)
    ag_out = mk.dram("b1ag_out", [2 * 2 * I, T], BF16)
    aout = b1u[:, 0:(NG // 2) * T]
    with mk.scope():
        xqb, xqkeyf = pro["xqb"], pro["xqkeyf"]
        grow = mk.row_load(io["ln_g"].ap(), D, ("b1g",))
        brow = mk.row_load(io["ln_b"].ap(), D, ("b1b",))
        gcol = mk.sb("b1_gcol", [128, 2 * c.NCH], F32)
        pb5 = mk.p512[5]
        pk5 = ("p512", 5)
        for k in range(c.NCH):
            for i_, (row, rk) in enumerate(((grow, ("b1g",)), (brow, ("b1b",)))):
                em.op("tensor", lambda e, k=k, row=row: e.transpose(
                    pb5[0:128, 0:1], row[0:1, k * 128:(k + 1) * 128],
                    mk.ident32[0:1, 0:1]),
                    reads=[rk, ("c_id32",)], writes=[pk5])
                em.op("vector", lambda e, k=k, i_=i_: e.tensor_copy(
                    gcol[:, 2 * k + i_:2 * k + i_ + 1], pb5[0:128, 0:1]),
                    reads=[pk5], writes=[("b1gcol",)])
        for k in range(c.NCH):
            em.op("vector", lambda e, k=k: e.tensor_scalar(
                xqb[:, k * T:(k + 1) * T], xqb[:, k * T:(k + 1) * T],
                gcol[:, 2 * k:2 * k + 1], gcol[:, 2 * k + 1:2 * k + 2],
                ALU.mult, ALU.add),
                reads=[xqkeyf(k), ("b1gcol",)], writes=[xqkeyf(k)])
        rcos = mk.sb("b1_rcos", [128, T], BF16)
        rsin = mk.sb("b1_rsin", [128, T], BF16)
        R1 = mk.sb("b1_R1", [128, 128], BF16)
        for nm, dst in (("rope_cos", rcos), ("rope_sin", rsin), ("R1", R1)):
            em.op("sync", lambda e, nm=nm, dst=dst: e.dma_start(
                out=dst[:, :], in_=io[nm].ap()), writes=[(nm,)], dom="qs")
        stg = [mk.sb(f"b1_stg{i}", [128, T], BF16) for i in range(2)]
        rop = mk.sb("b1_rop", [128, T], BF16)
        ship = mk.sb("b1_ship", [128, T], BF16)
        state = {"fc": -1}

        def finish_chunk():
            fc = state["fc"]
            if fc < 0:
                return
            gch = fc % NG
            sec = fc // NG
            pp = state["pp"]
            kSt = ("b1stg", pp)
            if sec <= 1:
                for t0 in range(0, T, 512):
                    tw = min(512, T - t0)
                    em.op("tensor", lambda e, t0=t0, tw=tw, pp=pp: e.matmul(
                        pb5[0:128, 0:tw], R1[:, :], stg[pp][:, t0:t0 + tw],
                        start=True, stop=True),
                        reads=[("R1",), kSt], writes=[pk5])
                    em.op("vector", lambda e, t0=t0, tw=tw, pp=pp: e.tensor_mul(
                        rop[:, t0:t0 + tw], pb5[0:128, 0:tw],
                        rsin[:, t0:t0 + tw]),
                        reads=[pk5, ("rope_sin",)], writes=[("b1rop",)])
                em.op("vector", lambda e, pp=pp: e.tensor_mul(
                    stg[pp][:, :], stg[pp][:, :], rcos[:, :]),
                    reads=[kSt, ("rope_cos",)], writes=[kSt])
                em.op("vector", lambda e, pp=pp: e.tensor_add(
                    ship[:, :], stg[pp][:, :], rop[:, :]),
                    reads=[kSt, ("b1rop",)], writes=[("b1ship",)])
                if sec == 0:
                    em.op("gpsimd", lambda e, gch=gch: e.dma_start(
                        out=qdr.ap()[gch * 128:(gch + 1) * 128, :],
                        in_=ship[:, :]),
                        reads=[("b1ship",)], writes=[("b1qdr",)], dom="qg")
                else:
                    em.op("gpsimd", lambda e, gch=gch: e.dma_start(
                        out=ag_in.ap()[gch * 128:(gch + 1) * 128, :],
                        in_=ship[:, :]),
                        reads=[("b1ship",)], writes=[("b1ag_in",)], dom="qg")
            else:
                em.op("gpsimd", lambda e, pp=pp, gch=gch: e.dma_start(
                    out=ag_in.ap()[I + gch * 128:I + (gch + 1) * 128, :],
                    in_=stg[pp][:, :]),
                    reads=[kSt], writes=[("b1ag_in",)], dom="qg")

        def evac_qkv(fabs, fcw, t0, tw, pb, pk):
            fc = fabs // 128
            newchunk = fc != state["fc"]
            prev = state["fc"]
            if newchunk:
                state["fc"] = fc
                state["pp"] = fc % 2
            pp = state["pp"]
            em.op("vector", lambda e, pp=pp, fcw=fcw, t0=t0, tw=tw, pb=pb:
                  e.tensor_copy(stg[pp][0:fcw, t0:t0 + tw], pb[0:fcw, 0:tw]),
                  reads=[pk], writes=[("b1stg", pp)])
            if newchunk and prev >= 0:
                cur = state["fc"], state["pp"]
                state["fc"], state["pp"] = prev, prev % 2
                finish_chunk()
                state["fc"], state["pp"] = cur

        mk.proj("b1qkv", io["w_qkv"].ap(), D, 3 * I, xqb, xqkeyf, evac_qkv,
                wbufs=wbp)
        finish_chunk()
        mk.allgather(ag_in, ag_out, ("b1ag_in",), ("ag_out", "b1a"))
    NGH = NG // 2
    for half in range(2):
        with mk.scope():
            qbh = b1u[:, NGH * T:NG * T]
            kQBH = ("b1qbh", half)
            for i in range(NGH):
                em.op("sync", lambda e, i=i, half=half: e.dma_start(
                    out=qbh[:, i * T:(i + 1) * T],
                    in_=qdr.ap()[(half * NGH + i) * 128:
                                 (half * NGH + i + 1) * 128, :]),
                    reads=[("b1qdr",)], writes=[kQBH], dom="qs")
            mk.attn_qshard(f"b1h{half}", qbh, kQBH, ag_out, aout, ("b1aout",),
                           scale=1.0 / 8.0, g_base=half * NGH, g_count=NGH,
                           ag_key=("ag_out", "b1a"), q_base=half * NGH)
        with mk.scope():
            aokeyf = lambda k: ("b1aout",)
            mk.proj(f"b1wo{half}", io["w_out"].ap()[half * NGH * 128:, :],
                    NGH * 128, D, aout, aokeyf,
                    resid_evac(mk, pro, "b1wo"),
                    bias_ap=(io["b_out"].ap() if half == 0 else None), wbufs=wbp)
    with mk.scope():
        emit_l1ffn(mk, pro, io, wbp)
    block_epilogue(mk, pro, io)




def emit_l1ffn(mk: MK, pro, io, wbufs):
    c = mk.cfg
    em = mk.em
    T, D = c.T, c.D
    E, RK, DF = 8, 16, 4 * c.D
    HCH = DF // 128
    NCH = c.NCH
    xnb, xnkeyf = pro["xnb"], pro["xnkeyf"]
    x, xkeyf = pro["x"], pro["xkeyf"]
    pb5 = mk.p512[5]
    pk5 = ("p512", 5)
    gw16 = []
    gw_stage = mk.sb("f1g_stg", [128, E], F32)
    for k in range(NCH):
        s16 = mk.sb(f"f1gw_{k}", [128, E], BF16)
        em.op("sync", lambda e, k=k: e.dma_start(
            out=gw_stage[:, :], in_=io["gate_wT"].ap()[k * 128:(k + 1) * 128, :]),
            writes=[("f1gstg",)], dom="qs")
        em.op("vector", lambda e, s16=s16: e.tensor_copy(s16[:, :], gw_stage[:, :]),
              reads=[("f1gstg",)], writes=[("f1gw", k)])
        gw16.append(s16)
    gbrow = mk.row_load(io["gate_b"].ap(), E, ("f1gb",))
    wT = mk.sb("f1_wT", [E, T], F32)
    vs2row = mk.sb("f1_vs2", [1, T], F32)
    vs2row16 = mk.sb("f1_vs2b", [1, T], BF16)
    lg = mk.sb("f1_lg", [128, E], F32)
    sc = mk.sb("f1_sc", [128, E], F32)
    m1 = mk.sb("f1_m1", [128, 1], F32)
    eq1 = mk.sb("f1_eq1", [128, E], F32)
    msk = mk.sb("f1_msk", [128, E], F32)
    m2 = mk.sb("f1_m2", [128, 1], F32)
    vs = mk.sb("f1_vs", [128, 1], F32)
    wtok = mk.sb("f1_w", [128, E], F32)
    for tt in range(T // 128):
        t0 = tt * 128
        pb, pk = mk.pbank()
        for k in range(NCH):
            em.op("tensor", lambda e, k=k, t0=t0, pb=pb: e.matmul(
                pb[0:128, 0:E], xnb[:, k * T + t0:k * T + t0 + 128], gw16[k][:, :],
                start=(k == 0), stop=False),
                reads=[xnkeyf(k), ("f1gw", k)], writes=[pk], sig=False)
        em.op("tensor", lambda e, pb=pb: e.matmul(
            pb[0:128, 0:E], mk.ones128_32[0:1, :], gbrow[0:1, 0:E],
            start=False, stop=True),
            reads=[("c_o128",), ("f1gb",)], writes=[pk])
        em.op("vector", lambda e, pb=pb: e.tensor_copy(lg[:, :], pb[0:128, 0:E]),
              reads=[pk], writes=[("lg1",)])
        em.op("vector", lambda e: e.tensor_reduce(m1[:, 0:1], lg[:, :], AX.X, ALU.max),
              reads=[("lg1",)], writes=[("m11",)])
        em.op("vector", lambda e: e.tensor_scalar_mul(m2[:, 0:1], m1[:, 0:1], -1.0),
              reads=[("m11",)], writes=[("m21",)])
        em.op("scalar", lambda e: e.activation(sc[:, :], lg[:, :], AF.Exp,
                                               bias=m2[:, 0:1]),
              reads=[("lg1",), ("m21",)], writes=[("sc1",)])
        em.op("vector", lambda e: e.tensor_reduce(m1[:, 0:1], sc[:, :], AX.X, ALU.add),
              reads=[("sc1",)], writes=[("m11",)])
        em.op("vector", lambda e: e.reciprocal(m1[:, 0:1], m1[:, 0:1]),
              reads=[("m11",)], writes=[("m11",)])
        em.op("vector", lambda e: e.tensor_scalar(
            sc[:, :], sc[:, :], m1[:, 0:1], None, ALU.mult),
            reads=[("sc1",), ("m11",)], writes=[("sc1",)])
        em.op("vector", lambda e: e.tensor_reduce(m1[:, 0:1], sc[:, :], AX.X, ALU.max),
              reads=[("sc1",)], writes=[("m11",)])
        em.op("vector", lambda e: e.tensor_scalar(
            eq1[:, :], sc[:, :], m1[:, 0:1], None, ALU.is_equal),
            reads=[("sc1",), ("m11",)], writes=[("eq11",)])
        em.op("vector", lambda e: e.scalar_tensor_tensor(
            msk[:, :], eq1[:, :], -1e30, sc[:, :], ALU.mult, ALU.add),
            reads=[("eq11",), ("sc1",)], writes=[("msk1",)])
        em.op("vector", lambda e: e.tensor_reduce(m2[:, 0:1], msk[:, :], AX.X, ALU.max),
              reads=[("msk1",)], writes=[("m21",)])
        em.op("vector", lambda e: e.tensor_add(vs[:, 0:1], m1[:, 0:1], m2[:, 0:1]),
              reads=[("m11",), ("m21",)], writes=[("vs1",)])
        em.op("vector", lambda e: e.tensor_scalar(
            wtok[:, :], msk[:, :], m2[:, 0:1], None, ALU.is_equal),
            reads=[("msk1",), ("m21",)], writes=[("w1",)])
        em.op("vector", lambda e: e.tensor_add(wtok[:, :], wtok[:, :], eq1[:, :]),
              reads=[("w1",), ("eq11",)], writes=[("w1",)])
        em.op("vector", lambda e: e.tensor_scalar(
            wtok[:, :], wtok[:, :], vs[:, 0:1], None, ALU.mult),
            reads=[("w1",), ("vs1",)], writes=[("w1",)])
        em.op("vector", lambda e: e.tensor_scalar_mul(vs[:, 0:1], vs[:, 0:1], 2.0),
              reads=[("vs1",)], writes=[("vs1",)])
        em.op("tensor", lambda e, pb=pb: e.transpose(
            pb5[0:E, 0:128], wtok[:, 0:E], mk.ident32[0:128, 0:128]),
            reads=[("w1",), ("c_id32",)], writes=[pk5])
        em.op("vector", lambda e, t0=t0: e.tensor_copy(
            wT[0:E, t0:t0 + 128], pb5[0:E, 0:128]), reads=[pk5], writes=[("f1wT",)])
        em.op("tensor", lambda e: e.transpose(
            pb5[0:1, 0:128], vs[:, 0:1], mk.ident32[0:128, 0:128]),
            reads=[("vs1",), ("c_id32",)], writes=[pk5])
        em.op("vector", lambda e, t0=t0: e.tensor_copy(
            vs2row[0:1, t0:t0 + 128], pb5[0:1, 0:128]),
            reads=[pk5], writes=[("f1vs2",)])
    em.op("vector", lambda e: e.tensor_copy(vs2row16[0:1, :], vs2row[0:1, :]),
          reads=[("f1vs2",)], writes=[("f1vs2b",)])
    lat_stage = mk.sb("f1lat_stg", [128, NCH * RK], F32)
    lat16 = mk.sb("f1lat", [128, NCH * E * RK], BF16)
    for e_ in range(E):
        src_ap = io["loraAT"].ap()[e_ * D:(e_ + 1) * D, :].rearrange(
            "(k p) r -> p k r", p=128)
        em.op("sync", lambda e, src_ap=src_ap: e.dma_start(
            out=lat_stage[:, :].rearrange("p (k r) -> p k r", r=RK), in_=src_ap),
            writes=[("f1latstg",)], dom="qs")
        for k in range(NCH):
            em.op("vector", lambda e, e_=e_, k=k: e.tensor_copy(
                lat16[:, (k * E + e_) * RK:(k * E + e_) * RK + RK],
                lat_stage[:, k * RK:(k + 1) * RK]),
                reads=[("f1latstg",)], writes=[("f1lat",)])
    NG1 = c.D // 128
    U_ok1 = NG1 * T >= 3 * DF + 3 * (T // 4) + T
    mid = [mk.sb(f"f1mid{t}", [128, T], BF16) for t in range(2)]
    if U_ok1:
        mid.append(pro["xqb"][:, 3 * DF + 3 * (T // 4):
                              3 * DF + 3 * (T // 4) + T])
    else:
        mid.append(mk.sb("f1mid2", [128, T], BF16))
    for e_ in range(E):
        ti, j = e_ // 3, (e_ % 3)
        for ts in range(c.NTS):
            t0 = ts * c.TSUB
            pb, pk = mk.pbank()
            for k in range(NCH):
                em.op("tensor", lambda e, k=k, e_=e_, t0=t0, pb=pb: e.matmul(
                    pb[0:RK, 0:c.TSUB],
                    lat16[:, (k * E + e_) * RK:(k * E + e_) * RK + RK],
                    xnb[:, k * T + t0:k * T + t0 + c.TSUB],
                    start=(k == 0), stop=(k == NCH - 1)),
                    reads=[("f1lat",), xnkeyf(k)], writes=[pk],
                    sig=(k == NCH - 1))
            em.op("vector", lambda e, ti=ti, j=j, t0=t0, pb=pb: e.tensor_copy(
                mid[ti][32 * j:32 * j + RK, t0:t0 + c.TSUB], pb[0:RK, 0:c.TSUB]),
                reads=[pk], writes=[("f1mid", ti)])
    U_ok = NG1 * T >= 3 * DF + 3 * (T // 4) + T
    if U_ok:
        lbt = [pro["xqb"][:, t * DF:(t + 1) * DF] for t in range(3)]
    else:
        lbt = [mk.sb(f"f1lbt{t}", [128, DF], BF16) for t in range(3)]
    for t in range(3):
        em.op("gpsimd", lambda e, t=t: e.dma_start(
            out=lbt[t][:, :], in_=io["loraBTp"].ap()[t * 128:(t + 1) * 128, :]),
            writes=[("f1lbt", t)], dom="qg")
    T4 = T // 4
    bi116 = mk.sb("f1_bi116", [1, DF], BF16)
    em.op("gpsimd", lambda e: e.dma_start(out=bi116[0:1, 0:DF],
                                          in_=io["bi1"].ap()),
          writes=[("f1bi116",)], dom="qg")
    bo116 = mk.sb("f1_bo116", [1, D], BF16)
    em.op("gpsimd", lambda e: e.dma_start(out=bo116[0:1, 0:D],
                                          in_=io["bo1"].ap()),
          writes=[("f1bo116",)], dom="qg")
    if U_ok1:
        base16 = pro["xqb"][:, 3 * DF:3 * DF + T4]
        z16 = pro["xqb"][:, 3 * DF + T4:3 * DF + 2 * T4]
        zg16 = pro["xqb"][:, 3 * DF + 2 * T4:3 * DF + 3 * T4]
    else:
        base16 = mk.sb("f1_base", [128, T4], BF16)
        z16 = mk.sb("f1_z", [128, T4], BF16)
        zg16 = mk.sb("f1_zg", [128, T4], BF16)
    HCQ = HCH // 4
    hacc16 = mk.sb("f1_hacc", [128, HCQ * T4], BF16)
    wbe8 = mk.sb("f1_wbe", [128, E * T4], BF16)
    for ts4 in range(T // T4):
        tbase = ts4 * T4
        for e_ in range(E):
            mk.bcast_selrow(wT, E, e_, T4, ("f1wT",), wbe8, ("f1wbe",),
                            dst_off=e_ * T4, col_off=tbase)

        for q4 in range(4):
            def evac_base(fabs, fcw, t0, tw, pb, pk, tbase=tbase, q4=q4):
                hcl = fabs // 128
                hc = q4 * HCQ + hcl
                em.op("vector", lambda e, pb=pb: e.tensor_copy(
                    base16[0:128, 0:T4], pb[0:128, 0:T4]),
                    reads=[pk], writes=[("f1base",)])
                pbA, pkA = mk.p512[4], ("p512", 4)
                for e_ in range(E):
                    ti, j = e_ // 3, e_ % 3
                    pbE, pkE = mk.p512[e_ % 2], ("p512", e_ % 2)
                    em.op("tensor", lambda e, pbE=pbE: e.matmul(
                        pbE[0:128, 0:T4], mk.ident[:, :], base16[0:128, 0:T4],
                        start=True, stop=False),
                        reads=[("c_id",), ("f1base",)], writes=[pkE], sig=False)
                    em.op("tensor", lambda e, ti=ti, j=j, hc=hc, tbase=tbase,
                          pbE=pbE: e.matmul(
                              pbE[0:128, 0:T4],
                              lbt[ti][32 * j:32 * j + RK,
                                      hc * 128:hc * 128 + 128],
                              mid[ti][32 * j:32 * j + RK, tbase:tbase + T4],
                              start=False, stop=True, tile_position=(32 * j, 0)),
                          reads=[("f1lbt", ti), ("f1mid", ti)], writes=[pkE])
                    em.op("scalar", lambda e, pbE=pbE: e.activation(
                        z16[0:128, 0:T4], pbE[0:128, 0:T4], AF.Relu),
                        reads=[pkE], writes=[("f1z",)])
                    em.op("vector", lambda e, e_=e_: e.tensor_mul(
                        zg16[0:128, 0:T4], z16[0:128, 0:T4],
                        wbe8[:, e_ * T4:(e_ + 1) * T4]),
                        reads=[("f1z",), ("f1wbe",)], writes=[("f1zg",)])
                    em.op("tensor", lambda e, e_=e_, pbA=pbA: e.matmul(
                        pbA[0:128, 0:T4], mk.ident[:, :], zg16[0:128, 0:T4],
                        start=(e_ == 0), stop=(e_ == E - 1)),
                        reads=[("c_id",), ("f1zg",)], writes=[pkA],
                        sig=(e_ == E - 1))
                em.op("vector", lambda e, hcl=hcl, pbA=pbA: e.tensor_copy(
                    hacc16[:, hcl * T4:(hcl + 1) * T4], pbA[0:128, 0:T4]),
                    reads=[pkA], writes=[("f1hacc", hcl)])

            mk.proj(f"f1wi_{ts4}_{q4}",
                    io["wiT"].ap()[:, q4 * (DF // 4):(q4 + 1) * (DF // 4)],
                    D, DF // 4, xnb, xnkeyf, evac_base,
                    wbufs=wbufs, tsubs=[(tbase, T4)],
                    bias_pre=(bi116[0:1, q4 * (DF // 4):(q4 + 1) * (DF // 4)],
                              ("f1bi116",)))

            def evac_wo(fabs, fcw, t0, tw, pb, pk):
                kk = fabs // 128
                em.op("vector", lambda e: e.tensor_add(
                    x[0:fcw, kk * T + t0:kk * T + t0 + tw],
                    x[0:fcw, kk * T + t0:kk * T + t0 + tw], pb[0:fcw, 0:tw]),
                    reads=[xkeyf(kk), pk], writes=[xkeyf(kk)])
            mk.proj(f"f1wo_{ts4}_{q4}",
                    io["woT"].ap()[q4 * (DF // 4):(q4 + 1) * (DF // 4), :],
                    DF // 4, D, hacc16,
                    lambda k: ("f1hacc", k),
                    evac_wo, wbufs=wbufs, tsubs=[(tbase, T4)],
                    xoff_fn=lambda k, t0, tbase=tbase: k * T4 + t0 - tbase,
                    bias_pre=((bo116, ("f1bo116",)) if q4 == 0 else None),
                    bias_rhs=(vs2row16, ("f1vs2b",)))


BLOCK_EMITTERS[1] = emit_block1



def emit_l2gate_raw(mk: MK, pro, io):
    c = mk.cfg
    em = mk.em
    T, D = c.T, c.D
    NE = 16
    NCH = c.NCH
    x, xkeyf = pro["x"], pro["xkeyf"]
    mu_row, kMu2 = pro["mu_row"]
    pb5 = mk.p512[5]
    pk5 = ("p512", 5)
    NTT = T // 128
    rawG = mk.sb("f2_rawG", [128, NTT * NE], F32)
    kRaw = ("f2rawG",)
    with mk.scope():
        simsq = mk.sb("f2sim_sq", [128, NE], F32)
        simnF = [mk.sb(f"f2simn_{k}", [128, NE], F32) for k in range(NCH)]
        cn = mk.sb("f2_cn", [1, NE], F32)
        cnb = mk.sb("f2_cnb", [128, NE], F32)
        ncs = mk.sb("f2_ncs", [1, NE], F32)
        c_t1g = mk.constf(1e-30, 1)
        c_mh1g = mk.constf(-0.5, 1)
        for k in range(NCH):
            em.op("sync", lambda e, k=k: e.dma_start(
                out=simnF[k][:, :], in_=io["sim"].ap()[k * 128:(k + 1) * 128, :]),
                writes=[("f2simn", k)], dom="qs")
            em.op("scalar", lambda e, k=k: e.activation(
                simsq[:, :], simnF[k][:, :], AF.Square),
                reads=[("f2simn", k)], writes=[("f2simsq",)])
            em.op("tensor", lambda e, k=k: e.matmul(
                pb5[0:1, 0:NE], mk.ones_col[:, 0:1], simsq[:, :],
                start=(k == 0), stop=(k == NCH - 1)),
                reads=[("c_oc",), ("f2simsq",)], writes=[pk5])
        em.op("vector", lambda e: e.tensor_copy(cn[0:1, :], pb5[0:1, 0:NE]),
              reads=[pk5], writes=[("f2cn",)])
        em.op("scalar", lambda e: e.activation(cn[0:1, :], cn[0:1, :], AF.Ln,
                                               bias=c_t1g),
              reads=[("f2cn",), ("c_cp",)], writes=[("f2cn",)])
        em.op("scalar", lambda e: e.activation(cn[0:1, :], cn[0:1, :], AF.Exp,
                                               scale=c_mh1g),
              reads=[("f2cn",), ("c_cp",)], writes=[("f2cn",)])
        mk.bcast_row(cn, NE, ("f2cn",), cnb, ("f2cnb",))
        for k in range(NCH):
            em.op("vector", lambda e, k=k: e.tensor_mul(
                simnF[k][:, :], simnF[k][:, :], cnb[:, 0:NE]),
                reads=[("f2simn", k), ("f2cnb",)], writes=[("f2simn", k)])
        for k in range(NCH):
            em.op("tensor", lambda e, k=k: e.matmul(
                pb5[0:1, 0:NE], mk.ones_col[:, 0:1], simnF[k][:, :],
                start=(k == 0), stop=(k == NCH - 1)),
                reads=[("c_oc",), ("f2simn", k)], writes=[pk5])
        em.op("vector", lambda e: e.tensor_scalar_mul(
            ncs[0:1, :], pb5[0:1, 0:NE], -1.0), reads=[pk5], writes=[("f2ncs",)])
        for tt in range(NTT):
            t0 = tt * 128
            pb, pk = mk.pbank()
            for k in range(NCH):
                em.op("tensor", lambda e, k=k, t0=t0, pb=pb: e.matmul(
                    pb[0:128, 0:NE], x[:, k * T + t0:k * T + t0 + 128],
                    simnF[k][:, :],
                    start=(k == 0), stop=False),
                    reads=[xkeyf(k), ("f2simn", k)], writes=[pk], sig=False)
            em.op("tensor", lambda e, t0=t0, pb=pb: e.matmul(
                pb[0:128, 0:NE], mu_row[0:1, t0:t0 + 128], ncs[0:1, 0:NE],
                start=False, stop=True),
                reads=[kMu2, ("f2ncs",)], writes=[pk])
            em.op("vector", lambda e, tt=tt, pb=pb: e.tensor_copy(
                rawG[:, tt * NE:(tt + 1) * NE], pb[0:128, 0:NE]),
                reads=[pk], writes=[kRaw])
    return rawG, kRaw


def emit_block2(mk: MK, io):
    c = mk.cfg
    em = mk.em
    T, D, S = c.T, c.D, c.S
    NG = D // 128
    NGH = NG // 2
    pro = block_prologue(mk, io)
    x, xkeyf = pro["x"], pro["xkeyf"]
    xnb, xnkeyf = pro["xnb"], pro["xnkeyf"]
    pro["rawG"] = emit_l2gate_raw(mk, pro, io)
    qdr = mk.dram("b2_qdr", [NG * 128, T], BF16)
    ag_in = mk.dram("b2ag_in", [2 * D, T], BF16)
    ag_out = mk.dram("b2ag_out", [2 * 2 * D, T], BF16)
    pb5 = mk.p512[5]
    pk5 = ("p512", 5)
    with mk.scope():
        wbp = mk.wstream_bufs(max(c.NCH, 8), npar=1)
        rcos = mk.sb("b2_rcos", [128, T], BF16)
        rsin = mk.sb("b2_rsin", [128, T], BF16)
        R2 = mk.sb("b2_R2", [128, 128], BF16)
        for nm, dst in (("rope2_cos", rcos), ("rope2_sin", rsin), ("R2", R2)):
            em.op("sync", lambda e, nm=nm, dst=dst: e.dma_start(
                out=dst[:, :], in_=io[nm].ap()), writes=[(nm,)], dom="qs")
        stg = [mk.sb(f"b2_stg{i}", [128, T], BF16) for i in range(2)]
        rop = mk.sb("b2_rop", [128, T], BF16)
        ship = mk.sb("b2_ship", [128, T], BF16)
        state = {"fc": -1}

        def finish_chunk():
            fc = state["fc"]
            if fc < 0:
                return
            gch = fc % c.NCH
            sec = fc // c.NCH
            pp = state["pp"]
            kSt = ("b2stg", pp)
            if sec <= 1:
                for t0 in range(0, T, 512):
                    tw = min(512, T - t0)
                    em.op("tensor", lambda e, t0=t0, tw=tw, pp=pp: e.matmul(
                        pb5[0:128, 0:tw], R2[:, :], stg[pp][:, t0:t0 + tw],
                        start=True, stop=True),
                        reads=[("R2",), kSt], writes=[pk5])
                    em.op("vector", lambda e, t0=t0, tw=tw: e.tensor_mul(
                        rop[:, t0:t0 + tw], pb5[0:128, 0:tw],
                        rsin[:, t0:t0 + tw]),
                        reads=[pk5, ("rope2_sin",)], writes=[("b2rop",)])
                em.op("vector", lambda e, pp=pp: e.tensor_mul(
                    stg[pp][:, :], stg[pp][:, :], rcos[:, :]),
                    reads=[kSt, ("rope2_cos",)], writes=[kSt])
                em.op("vector", lambda e, pp=pp: e.tensor_add(
                    ship[:, :], stg[pp][:, :], rop[:, :]),
                    reads=[kSt, ("b2rop",)], writes=[("b2ship",)])
                if sec == 0:
                    em.op("gpsimd", lambda e, gch=gch: e.dma_start(
                        out=qdr.ap()[gch * 128:(gch + 1) * 128, :],
                        in_=ship[:, :]),
                        reads=[("b2ship",)], writes=[("b2qdr",)], dom="qg")
                else:
                    em.op("gpsimd", lambda e, gch=gch: e.dma_start(
                        out=ag_in.ap()[gch * 128:(gch + 1) * 128, :],
                        in_=ship[:, :]),
                        reads=[("b2ship",)], writes=[("b2ag_in",)], dom="qg")
            else:
                em.op("gpsimd", lambda e, gch=gch, pp=pp: e.dma_start(
                    out=ag_in.ap()[D + gch * 128:D + (gch + 1) * 128, :],
                    in_=stg[pp][:, :]),
                    reads=[kSt], writes=[("b2ag_in",)], dom="qg")

        def evac_qkv(fabs, fcw, t0, tw, pb, pk):
            fc = state["base"] + fabs // 128
            newchunk = fc != state["fc"]
            prev = state["fc"]
            if newchunk:
                state["fc"] = fc
                state["pp"] = fc % 2
            pp = state["pp"]
            em.op("vector", lambda e, pp=pp, fcw=fcw, t0=t0, tw=tw, pb=pb:
                  e.tensor_copy(stg[pp][0:fcw, t0:t0 + tw], pb[0:fcw, 0:tw]),
                  reads=[pk], writes=[("b2stg", pp)])
            if newchunk and prev >= 0:
                cur = state["fc"], state["pp"]
                state["fc"], state["pp"] = prev, prev % 2
                finish_chunk()
                state["fc"], state["pp"] = cur

        for sec, (wname, bname) in enumerate((("Wq2", "bq2"), ("Wk2", "bk2"),
                                              ("Wv2", "bv2"))):
            state["base"] = sec * c.NCH
            mk.proj(f"b2{wname}", io[wname].ap(), D, D, xnb, xnkeyf, evac_qkv,
                    bias_ap=io[bname].ap(), wbufs=wbp)
            finish_chunk()
            state["fc"] = -1
        mk.allgather(ag_in, ag_out, ("b2ag_in",), ("ag_out", "b2a"))
    with mk.scope():
        qbh = mk.sb("b2_qbh", [128, NGH * T], BF16)
        aout = mk.sb("b2_aout", [128, NGH * T], BF16)
        kQBH = ("b2qbh",)
        aq_shared = {}
        for half in range(2):
            for i in range(NGH):
                em.op("sync", lambda e, i=i, half=half: e.dma_start(
                    out=qbh[:, i * T:(i + 1) * T],
                    in_=qdr.ap()[(half * NGH + i) * 128:
                                 (half * NGH + i + 1) * 128, :]),
                    reads=[("b2qdr",)], writes=[kQBH], dom="qs")
            mk.attn_qshard("b2h", qbh, kQBH, ag_out, aout,
                           ("b2aout",), scale=1.0 / 8.0,
                           g_base=half * NGH, g_count=NGH,
                           ag_key=("ag_out", "b2a"), q_base=half * NGH,
                           shared=aq_shared)
            for i in range(NGH):
                g = half * NGH + i
                for t0 in range(0, T, 512):
                    tw = min(512, T - t0)
                    em.op("vector", lambda e, g=g, i=i, t0=t0, tw=tw:
                          e.tensor_add(
                              x[:, g * T + t0:g * T + t0 + tw],
                              x[:, g * T + t0:g * T + t0 + tw],
                              aout[:, i * T + t0:i * T + t0 + tw]),
                          reads=[xkeyf(g), ("b2aout",)], writes=[xkeyf(g)])
    with mk.scope():
        wbp2 = mk.wstream_bufs(max(c.NCH, 8), npar=1)
        emit_l2ffn(mk, pro, io, wbp2)
    block_epilogue(mk, pro, io)


def emit_l2ffn(mk: MK, pro, io, wbufs):
    c = mk.cfg
    em = mk.em
    T, D, S = c.T, c.D, c.S
    NE = 16
    HD = 64
    NCH = c.NCH
    x, xkeyf = pro["x"], pro["xkeyf"]
    xnb, xnkeyf = pro["xnb"], pro["xnkeyf"]
    var_row, kVar = pro["var_row"]
    rawG, kRaw = pro["rawG"]
    pb5 = mk.p512[5]
    pk5 = ("p512", 5)
    rwT = mk.sb("f2_rwT", [NE, T], F32)
    with mk.scope():
        rowinv = mk.sb("f2_rowinv", [1, T], F32)
        em.op("vector", lambda e: e.tensor_scalar_mul(
            rowinv[0:1, :], var_row[0:1, :], float(D)),
            reads=[kVar], writes=[("f2ri",)])
        c_t1 = mk.constf(1e-30, 1)
        c_mh1b = mk.constf(-0.5, 1)
        em.op("scalar", lambda e: e.activation(
            rowinv[0:1, :], rowinv[0:1, :], AF.Ln, bias=c_t1),
            reads=[("f2ri",), ("c_cp",)], writes=[("f2ri",)])
        em.op("scalar", lambda e: e.activation(
            rowinv[0:1, :], rowinv[0:1, :], AF.Exp, scale=c_mh1b),
            reads=[("f2ri",), ("c_cp",)], writes=[("f2ri",)])
        sg = mk.sb("f2_sg", [1, NE], F32)
        sgb = mk.sb("f2_sgb", [128, NE], F32)
        em.op("sync", lambda e: e.dma_start(out=sg[0:1, :], in_=io["gates"].ap()),
              writes=[("f2sg",)], dom="qs")
        em.op("scalar", lambda e: e.activation(sg[0:1, :], sg[0:1, :], AF.Sigmoid),
              reads=[("f2sg",)], writes=[("f2sg",)])
        mk.bcast_row(sg, NE, ("f2sg",), sgb, ("f2sgb",))
        lg = mk.sb("f2_lg", [128, NE], F32)
        gtd = mk.sb("f2_gtd", [128, NE], F32)
        mpos = mk.sb("f2_mpos", [128, NE], F32)
        cnt = mk.sb("f2_cnt", [128, 1], F32)
        act = mk.sb("f2_act", [128, 1], F32)
        m1 = mk.sb("f2_m1", [128, 1], F32)
        eq1 = mk.sb("f2_eq1", [128, NE], F32)
        msk2 = mk.sb("f2_msk2", [128, NE], F32)
        m2 = mk.sb("f2_m2", [128, 1], F32)
        fb = mk.sb("f2_fb", [128, NE], F32)
        mfin = mk.sb("f2_mfin", [128, NE], F32)
        zz = mk.sb("f2_zz", [128, NE], F32)
        riT = mk.sb("f2_riT", [128, 1], F32)
        for tt in range(T // 128):
            t0 = tt * 128
            em.op("tensor", lambda e, t0=t0: e.transpose(
                pb5[0:128, 0:1], rowinv[0:1, t0:t0 + 128], mk.ident32[0:1, 0:1]),
                reads=[("f2ri",), ("c_id32",)], writes=[pk5])
            em.op("vector", lambda e: e.tensor_copy(riT[:, 0:1], pb5[0:128, 0:1]),
                  reads=[pk5], writes=[("f2riT",)])
            em.op("vector", lambda e, tt=tt: e.tensor_scalar(
                lg[:, :], rawG[:, tt * NE:(tt + 1) * NE], riT[:, 0:1], None,
                ALU.mult),
                reads=[kRaw, ("f2riT",)], writes=[("f2lg",)])
            em.op("vector", lambda e: e.tensor_sub(lg[:, :], lg[:, :], sgb[:, 0:NE]),
                  reads=[("f2lg",), ("f2sgb",)], writes=[("f2lg",)])
            em.op("vector", lambda e: e.tensor_scalar_max(gtd[:, :], lg[:, :], 0.0),
                  reads=[("f2lg",)], writes=[("f2gtd",)])
            em.op("vector", lambda e: e.tensor_single_scalar(
                mpos[:, :], lg[:, :], 0.0, ALU.is_gt),
                reads=[("f2lg",)], writes=[("f2mpos",)])
            em.op("vector", lambda e: e.tensor_reduce(
                cnt[:, 0:1], mpos[:, :], AX.X, ALU.add),
                reads=[("f2mpos",)], writes=[("f2cnt",)])
            em.op("vector", lambda e: e.tensor_single_scalar(
                act[:, 0:1], cnt[:, 0:1], 0.0, ALU.is_gt),
                reads=[("f2cnt",)], writes=[("f2act",)])
            em.op("vector", lambda e: e.tensor_reduce(m1[:, 0:1], lg[:, :], AX.X,
                                                      ALU.max),
                  reads=[("f2lg",)], writes=[("f2m1",)])
            em.op("vector", lambda e: e.tensor_scalar(
                eq1[:, :], lg[:, :], m1[:, 0:1], None, ALU.is_equal),
                reads=[("f2lg",), ("f2m1",)], writes=[("f2eq1",)])
            em.op("vector", lambda e: e.scalar_tensor_tensor(
                msk2[:, :], eq1[:, :], -1e30, lg[:, :], ALU.mult, ALU.add),
                reads=[("f2eq1",), ("f2lg",)], writes=[("f2msk2",)])
            em.op("vector", lambda e: e.tensor_reduce(m2[:, 0:1], msk2[:, :], AX.X,
                                                      ALU.max),
                  reads=[("f2msk2",)], writes=[("f2m2",)])
            em.op("vector", lambda e: e.tensor_scalar(
                fb[:, :], msk2[:, :], m2[:, 0:1], None, ALU.is_equal),
                reads=[("f2msk2",), ("f2m2",)], writes=[("f2fb",)])
            em.op("vector", lambda e: e.tensor_add(fb[:, :], fb[:, :], eq1[:, :]),
                  reads=[("f2fb",), ("f2eq1",)], writes=[("f2fb",)])
            em.op("vector", lambda e: e.tensor_scalar(
                mfin[:, :], mpos[:, :], act[:, 0:1], None, ALU.mult),
                reads=[("f2mpos",), ("f2act",)], writes=[("f2mfin",)])
            em.op("vector", lambda e: e.tensor_scalar_mul(act[:, 0:1], act[:, 0:1],
                                                          -1.0),
                  reads=[("f2act",)], writes=[("f2act",)])
            em.op("vector", lambda e: e.tensor_single_scalar(
                act[:, 0:1], act[:, 0:1], 1.0, ALU.add),
                reads=[("f2act",)], writes=[("f2act",)])
            em.op("vector", lambda e: e.tensor_scalar(
                zz[:, :], fb[:, :], act[:, 0:1], None, ALU.mult),
                reads=[("f2fb",), ("f2act",)], writes=[("f2zz",)])
            em.op("vector", lambda e: e.tensor_add(mfin[:, :], mfin[:, :], zz[:, :]),
                  reads=[("f2mfin",), ("f2zz",)], writes=[("f2mfin",)])
            em.op("vector", lambda e: e.tensor_mul(zz[:, :], gtd[:, :], mfin[:, :]),
                  reads=[("f2gtd",), ("f2mfin",)], writes=[("f2zz",)])
            em.op("vector", lambda e: e.scalar_tensor_tensor(
                zz[:, :], mfin[:, :], 1e4, zz[:, :], ALU.mult, ALU.add),
                reads=[("f2mfin",), ("f2zz",)], writes=[("f2zz",)])
            em.op("vector", lambda e: e.tensor_single_scalar(
                zz[:, :], zz[:, :], -1e4, ALU.add),
                reads=[("f2zz",)], writes=[("f2zz",)])
            em.op("vector", lambda e: e.tensor_reduce(m1[:, 0:1], zz[:, :], AX.X,
                                                      ALU.max),
                  reads=[("f2zz",)], writes=[("f2m1",)])
            em.op("vector", lambda e: e.tensor_scalar_mul(m2[:, 0:1], m1[:, 0:1], -1.0),
                  reads=[("f2m1",)], writes=[("f2m2",)])
            em.op("scalar", lambda e: e.activation(zz[:, :], zz[:, :], AF.Exp,
                                                   bias=m2[:, 0:1]),
                  reads=[("f2zz",), ("f2m2",)], writes=[("f2zz",)])
            em.op("vector", lambda e: e.tensor_reduce(m1[:, 0:1], zz[:, :], AX.X,
                                                      ALU.add),
                  reads=[("f2zz",)], writes=[("f2m1",)])
            em.op("vector", lambda e: e.reciprocal(m1[:, 0:1], m1[:, 0:1]),
                  reads=[("f2m1",)], writes=[("f2m1",)])
            em.op("vector", lambda e: e.tensor_scalar(
                zz[:, :], zz[:, :], m1[:, 0:1], None, ALU.mult),
                reads=[("f2zz",), ("f2m1",)], writes=[("f2zz",)])
            em.op("tensor", lambda e: e.transpose(
                pb5[0:NE, 0:128], zz[:, 0:NE], mk.ident32[0:128, 0:128]),
                reads=[("f2zz",), ("c_id32",)], writes=[pk5])
            em.op("vector", lambda e, t0=t0: e.tensor_copy(
                rwT[0:NE, t0:t0 + 128], pb5[0:NE, 0:128]),
                reads=[pk5], writes=[("f2rwT",)])
    NPC = NE * HD // 128
    mk.qsel(NE, NPC)
    ag3_in = mk.dram("b2ag3_in", [3 * 64, T], BF16)
    ag3_out = mk.dram("b2ag3_out", [2 * 3 * 64, T], BF16)
    with mk.scope():
        fold_np = np.zeros((128, 64), np.float32)
        for d_ in range(64):
            fold_np[d_, d_] = 1.0
            fold_np[64 + d_, d_] = 1.0
        fold_h = mk.nc.inline_tensor(fold_np, "fold64")
        fold64 = mk.sb("f2_fold64", [128, 64], F32)
        em.op("sync", lambda e: e.dma_start(out=fold64[:, :], in_=fold_h.ap()),
              writes=[("f2fold",)], dom="qs")
        Pbuf = mk.sb("f2_P", [128, NPC * T], BF16)
        acc = mk.sb("f2_acc", [128, T], F32)
        rwb = mk.sb("f2_rwb", [128, T], F32)
        qkvm = mk.sb("f2_qkvm", [128, 3 * T], BF16)

        def evac_P(fabs, fcw, t0, tw, pb, pk):
            kk = fabs // 128
            em.op("vector", lambda e, kk=kk, t0=t0, tw=tw, fcw=fcw, pb=pb:
                  e.tensor_copy(
                      Pbuf[0:fcw, kk * T + t0:kk * T + t0 + tw], pb[0:fcw, 0:tw]),
                  reads=[pk], writes=[("f2P", kk)])

        qsel, qselkey = mk.qsel(NE, NPC)
        for sec, wname in enumerate(("qpf", "kpf", "vpf")):
            mk.proj(f"f2{wname}", io[wname].ap(), D, NE * HD, xnb, xnkeyf,
                    evac_P, wbufs=wbufs)
            em.op("vector", lambda e: e.memset(acc[:, :], 0.0),
                  writes=[("f2acc",)])
            for kk in range(NPC):
                for t0 in range(0, T, 512):
                    tw = min(512, T - t0)
                    pbq, pkq = mk.pbank()
                    em.op("tensor", lambda e, kk=kk, t0=t0, tw=tw, pbq=pbq:
                          e.matmul(
                              pbq[:, 0:tw], qsel[0:NE, kk * 128:(kk + 1) * 128],
                              rwT[0:NE, t0:t0 + tw], start=True, stop=True),
                          reads=[qselkey, ("f2rwT",)], writes=[pkq])
                    em.op("vector", lambda e, t0=t0, tw=tw, pbq=pbq:
                          e.tensor_copy(rwb[:, t0:t0 + tw], pbq[:, 0:tw]),
                          reads=[pkq], writes=[("f2rwb",)])
                em.op("vector", lambda e, kk=kk: e.tensor_mul(
                    Pbuf[:, kk * T:(kk + 1) * T], Pbuf[:, kk * T:(kk + 1) * T],
                    rwb[:, :]),
                    reads=[("f2P", kk), ("f2rwb",)], writes=[("f2P", kk)])
                em.op("vector", lambda e, kk=kk: e.tensor_add(
                    acc[:, 0:T], acc[:, 0:T],
                    Pbuf[:, kk * T:(kk + 1) * T]),
                    reads=[("f2acc",), ("f2P", kk)], writes=[("f2acc",)])
            for t0 in range(0, T, 512):
                tw = min(512, T - t0)
                pbq, pkq = mk.pbank()
                em.op("tensor", lambda e, t0=t0, tw=tw, pbq=pbq: e.matmul(
                    pbq[0:64, 0:tw], fold64[:, 0:64], acc[:, t0:t0 + tw],
                    start=True, stop=True),
                    reads=[("f2fold",), ("f2acc",)], writes=[pkq])
                em.op("vector", lambda e, sec=sec, t0=t0, tw=tw, pbq=pbq:
                      e.tensor_copy(
                          qkvm[0:64, sec * T + t0:sec * T + t0 + tw],
                          pbq[0:64, 0:tw]),
                      reads=[pkq], writes=[("f2qkvm",)])
        for sec in range(3):
            em.op("gpsimd", lambda e, sec=sec: e.dma_start(
                out=ag3_in.ap()[sec * 64:(sec + 1) * 64, :],
                in_=qkvm[0:64, sec * T:(sec + 1) * T]),
                reads=[("f2qkvm",)], writes=[("f2ag3in",)], dom="qg")
        mk.allgather(ag3_in, ag3_out, ("f2ag3in",), ("f2ag3out",))
    emit_l2ffn_attn(mk, pro, io, ag3_out, rwT, wbufs)


def emit_l2ffn_attn(mk: MK, pro, io, ag3_out, rwT, wbufs):
    c = mk.cfg
    em = mk.em
    T, D, S = c.T, c.D, c.S
    NE, HD = 16, 64
    NPC = NE * HD // 128
    x, xkeyf = pro["x"], pro["xkeyf"]
    pb5 = mk.p512[5]
    pk5 = ("p512", 5)
    NK128 = S // 128
    NQT = S // 128
    NJ = S // c.KB
    ao = mk.sb("f2a_ao", [128, T], BF16)
    aodr = mk.dram("f2a_aodr", [64, S], BF16)
    with mk.scope():
        qf = mk.sb("f2a_q", [64, S], BF16)
        kf = mk.sb("f2a_k", [64, S], BF16)
        vf = mk.sb("f2a_v", [64, S], BF16)
        vT = mk.sb("f2a_vT", [128, NK128 * 64], BF16)
        P_sb = mk.sb("f2a_P", [128, 2 * c.KB], BF16)
        PT_sb = mk.sb("f2a_PT", [128, 2 * 128], BF16)
        sums = mk.sb("f2a_sum", [128, 4 * NJ], F32)
        rrec = mk.sb("f2a_rr", [128, 4], F32)
        rrT1 = mk.sb("f2a_rrT", [1, 512], F32)
        rb1 = mk.sb("f2a_rb", [64, 512], F32)
        mxq = mk.sb("f2a_mx", [128, 2], F32)
        nm1 = mk.sb("f2a_nm", [128, 1], F32)
        aoS = mk.sb("f2a_aoS", [64, S], BF16)
        for sec, dst, dkey in ((0, qf, ("f2aq",)), (1, kf, ("f2ak",)),
                               (2, vf, ("f2av",))):
            for rank in range(2):
                r0 = rank * 192 + sec * 64
                em.op("gpsimd", lambda e, r0=r0, rank=rank, dst=dst: e.dma_start(
                    out=dst[0:64, rank * T:(rank + 1) * T],
                    in_=ag3_out.ap()[r0:r0 + 64, :]),
                    reads=[("f2ag3out",)], writes=[dkey], dom="qg")
        for kt in range(NK128):
            mk.transpose(vf[0:64, kt * 128:(kt + 1) * 128], 64, 128, [("f2av",)],
                         vT, kt * 64, ("f2avT",), pti=kt % 2, bp=0)
        c_sc = mk.constf(0.125, 128)
        psO = mk.p512[4]
        kO = ("p512", 4)
        maskA2 = ("maskA2",)
        maskA = mk.sb("f2a_maskA", [128, 128], F32)
        em.op("sync", lambda e: e.dma_start(out=maskA[:, :], in_=io["maskA"].ap()),
              writes=[("f2amA",)], dom="qs")
        NQB = NQT // c.QB
        for qb in range(NQB):
            em.op("vector", lambda e: e.memset(sums[:, :], 0.0), writes=[("f2asums",)])
            for qloc in range(c.QB):
                qt = qb * c.QB + qloc
                jmax = (qt * 128) // c.KB
                for j in range(jmax + 1):
                    strad = (j == jmax)
                    delta = qt - j * (c.KB // 128)
                    kw = (delta + 1) * 128 if strad else c.KB
                    sb_i = j % 2
                    Sb = mk.p512[sb_i]
                    kS = ("p512", sb_i)
                    em.op("tensor", lambda e, qt=qt, j=j, kw=kw, Sb=Sb: e.matmul(
                        Sb[0:128, 0:kw], qf[0:64, qt * 128:qt * 128 + 128],
                        kf[0:64, j * c.KB:j * c.KB + kw],
                        start=True, stop=True, tile_position=(0, 0)),
                        reads=[("f2aq",), ("f2ak",)], writes=[kS])
                    if strad:
                        em.op("vector", lambda e, kw=kw, Sb=Sb: e.tensor_add(
                            Sb[0:128, kw - 128:kw], maskA[:, 0:128],
                            Sb[0:128, kw - 128:kw]),
                            reads=[("f2amA",), kS], writes=[kS])
                    if j == 0:
                        em.op("vector", lambda e, kw=kw, Sb=Sb: e.tensor_reduce(
                            mxq[:, 0:1], Sb[0:128, 0:kw], AX.X, ALU.max),
                            reads=[kS], writes=[("f2amx",)])
                    else:
                        em.op("vector", lambda e, kw=kw, Sb=Sb: e.tensor_reduce(
                            mxq[:, 1:2], Sb[0:128, 0:kw], AX.X, ALU.max),
                            reads=[kS], writes=[("f2amx",)])
                        em.op("vector", lambda e: e.tensor_tensor(
                            mxq[:, 0:1], mxq[:, 0:1], mxq[:, 1:2], ALU.max),
                            reads=[("f2amx",)], writes=[("f2amx",)])
                em.op("vector", lambda e: e.tensor_scalar_mul(
                    nm1[:, 0:1], mxq[:, 0:1], -0.125),
                    reads=[("f2amx",)], writes=[("f2anm",)])
                for j in range(jmax + 1):
                    strad = (j == jmax)
                    delta = qt - j * (c.KB // 128)
                    kw = (delta + 1) * 128 if strad else c.KB
                    nkt = kw // 128
                    sb_i = j % 2
                    Sb = mk.p512[sb_i]
                    kS = ("p512", sb_i)
                    em.op("tensor", lambda e, qt=qt, j=j, kw=kw, Sb=Sb: e.matmul(
                        Sb[0:128, 0:kw], qf[0:64, qt * 128:qt * 128 + 128],
                        kf[0:64, j * c.KB:j * c.KB + kw],
                        start=True, stop=True, tile_position=(0, 0)),
                        reads=[("f2aq",), ("f2ak",)], writes=[kS])
                    if strad:
                        em.op("vector", lambda e, kw=kw, Sb=Sb: e.tensor_add(
                            Sb[0:128, kw - 128:kw], maskA[:, 0:128],
                            Sb[0:128, kw - 128:kw]),
                            reads=[("f2amA",), kS], writes=[kS])
                    pr = (j % 2) * c.KB
                    em.op("scalar", lambda e, qt=qt, qloc=qloc, j=j, kw=kw, Sb=Sb,
                          pr=pr: e.activation(
                              P_sb[:, pr:pr + kw], Sb[0:128, 0:kw], AF.Exp,
                              bias=nm1[:, 0:1], scale=c_sc,
                              accum_out=sums[:, qloc * NJ + j:qloc * NJ + j + 1]),
                          reads=[kS, ("f2anm",), ("c_cp",)],
                          writes=[("f2aP", j % 2), ("f2asums",)])
                    for kt in range(nkt):
                        ktabs = j * (c.KB // 128) + kt
                        pti = kt % 2
                        ptb = mk.pT[pti]
                        pkT = ("pT", pti)
                        em.op("tensor", lambda e, ptb=ptb, pr=pr, kt=kt: e.transpose(
                            ptb[0:128, 0:128],
                            P_sb[:, pr + kt * 128:pr + kt * 128 + 128],
                            mk.ident[:, :]),
                            reads=[("f2aP", j % 2), ("c_id",)], writes=[pkT])
                        pt_off = (kt % 2) * 128
                        em.op("vector", lambda e, ptb=ptb, pt_off=pt_off: e.tensor_copy(
                            PT_sb[:, pt_off:pt_off + 128], ptb[0:128, 0:128]),
                            reads=[pkT], writes=[("f2aPT", kt % 2)])
                        first = (j == 0 and kt == 0)
                        last = (j == jmax and kt == nkt - 1)
                        em.op("tensor", lambda e, ktabs=ktabs, qloc=qloc, pt_off=pt_off,
                              first=first, last=last: e.matmul(
                                  psO[0:64, qloc * 128:qloc * 128 + 128],
                                  vT[:, ktabs * 64:ktabs * 64 + 64],
                                  PT_sb[:, pt_off:pt_off + 128],
                                  start=first, stop=last, tile_position=(0, 0)),
                              reads=[("f2avT",), ("f2aPT", kt % 2)],
                              writes=[kO], sig=last)
            em.op("vector", lambda e: e.tensor_reduce(
                rrec[:, 0:4], sums[:, :].rearrange("p (a b) -> p a b", b=NJ),
                AX.X, ALU.add),
                reads=[("f2asums",)], writes=[("f2arr",)])
            em.op("vector", lambda e: e.reciprocal(rrec[:, 0:4], rrec[:, 0:4]),
                  reads=[("f2arr",)], writes=[("f2arr",)])
            for qloc in range(c.QB):
                em.op("tensor", lambda e, qloc=qloc: e.transpose(
                    pb5[0:1, 0:128], rrec[:, qloc:qloc + 1],
                    mk.ident32[0:128, 0:128]),
                    reads=[("f2arr",), ("c_id32",)], writes=[pk5])
                em.op("vector", lambda e, qloc=qloc: e.tensor_copy(
                    rrT1[0:1, qloc * 128:qloc * 128 + 128], pb5[0:1, 0:128]),
                    reads=[pk5], writes=[("f2arrT",)])
            em.op("tensor", lambda e: e.matmul(
                pb5[0:64, 0:512], mk.ones128_32[0:1, 0:64], rrT1[0:1, 0:512],
                start=True, stop=True),
                reads=[("c_o128",), ("f2arrT",)], writes=[pk5])
            em.op("vector", lambda e: e.tensor_copy(rb1[0:64, 0:512],
                                                    pb5[0:64, 0:512]),
                  reads=[pk5], writes=[("f2arb",)])
            em.op("vector", lambda e, qb=qb: e.tensor_mul(
                aoS[0:64, qb * 512:qb * 512 + 512], psO[0:64, 0:512],
                rb1[0:64, 0:512]),
                reads=[kO, ("f2arb",)], writes=[("f2aoS",)])
        em.op("gpsimd", lambda e: e.dma_start(out=aodr.ap(), in_=aoS[0:64, :]),
              reads=[("f2aoS",)], writes=[("f2aodr",)], dom="qg")
    mk.dyn_load("sync", ao[0:64, 0:T], aodr, 0, [[S, 64], [1, T]],
                [("f2aodr",)], [("f2ao",)], base="tb")
    mk.dyn_load("sync", ao[64:128, 0:T], aodr, 0, [[S, 64], [1, T]],
                [("f2aodr",)], [("f2ao",)], base="tb")
    NPQ = 2
    Pbuf = mk.sb("f2z_P", [128, NPQ * T], BF16)
    rwb = mk.sb("f2z_rwb", [128, T], F32)
    qsel, qselkey = mk.qsel(NE, NPC)
    for q in range(NPC // NPQ):
        for kl in range(NPQ):
            kk = q * NPQ + kl
            for t0 in range(0, T, 512):
                tw = min(512, T - t0)
                pbq, pkq = mk.pbank()
                em.op("tensor", lambda e, kk=kk, t0=t0, tw=tw, pbq=pbq: e.matmul(
                    pbq[:, 0:tw], qsel[0:NE, kk * 128:(kk + 1) * 128],
                    rwT[0:NE, t0:t0 + tw], start=True, stop=True),
                    reads=[qselkey, ("f2rwT",)], writes=[pkq])
                em.op("vector", lambda e, t0=t0, tw=tw, pbq=pbq: e.tensor_copy(
                    rwb[:, t0:t0 + tw], pbq[:, 0:tw]),
                    reads=[pkq], writes=[("f2zrwb",)])
            em.op("vector", lambda e, kl=kl: e.tensor_mul(
                Pbuf[0:64, kl * T:(kl + 1) * T], ao[0:64, 0:T], rwb[0:64, 0:T]),
                reads=[("f2ao",), ("f2zrwb",)], writes=[("f2zP", kl)])
            em.op("vector", lambda e, kl=kl: e.tensor_mul(
                Pbuf[64:128, kl * T:(kl + 1) * T], ao[64:128, 0:T],
                rwb[64:128, 0:T]),
                reads=[("f2ao",), ("f2zrwb",)], writes=[("f2zP", kl)])
        mk.proj(f"f2op{q}", io["opR"].ap()[q * NPQ * 128:(q + 1) * NPQ * 128, :],
                NPQ * 128, D, Pbuf,
                lambda k: ("f2zP", k), resid_evac(mk, pro, "f2o"), wbufs=wbufs)

BLOCK_EMITTERS[2] = emit_block2


def emit_block3(mk: MK, io):
    c = mk.cfg
    em = mk.em
    T, D, S = c.T, c.D, c.S
    NCH = c.NCH
    pb5 = mk.p512[5]
    pk5 = ("p512", 5)
    xsp = mk.dram("b3_xsp", [D, T], F32)
    xnsp = mk.dram("b3_xnsp", [D, T], F32)
    mk.setup_meta(io["meta"].ap())
    ag2_out = mk.dram("b3ag2_out", [2 * c.NGC * 128, S], BF16)
    with mk.scope():
        xnb = mk.sb("b3_xnb", [128, NCH * T], BF16)
        xnkeyf = lambda k: ("b3xnb", k)
        x = mk.sb("b3_x", [128, NCH * T], F32)
        xkeyf = lambda k: ("b3x", k)
        for k in range(NCH):
            em.op("sync", lambda e, k=k: e.dma_start(
                out=x[:, k * T:(k + 1) * T],
                in_=io["x_in"].ap()[k * 128:(k + 1) * 128, :]),
                writes=[xkeyf(k)], dom="qs")
        with mk.scope():
            PW = min(512, T)
            sqp = [mk.sb(mk.fresh("b3psq"), [128, PW], F32) for _ in range(2)]
            prow = mk.sb("b3prow", [1, 3 * PW], F32)
            mub = mk.sb("b3pmub", [128, PW], F32)
            rbt = mk.sb("b3prbt", [128, PW], F32)
            tmpt = [mk.sb(mk.fresh("b3ptmp"), [128, PW], F32) for _ in range(2)]
            Dv = float(D)
            for (t0, tw) in [(i * PW, PW) for i in range(T // PW)]:
                pbS, pkS = mk.pbank()
                pbQ, pkQ = mk.pbank()
                for k in range(NCH):
                    pp = k % 2
                    kq = ("b3psq", pp)
                    em.op("scalar", lambda e, k=k, t0=t0, tw=tw, pp=pp:
                          e.activation(sqp[pp][:, 0:tw],
                                       x[:, k * T + t0:k * T + t0 + tw],
                                       AF.Square),
                          reads=[xkeyf(k)], writes=[kq])
                    em.op("tensor", lambda e, k=k, t0=t0, tw=tw, pbS=pbS:
                          e.matmul(pbS[0:1, 0:tw], mk.ones_col[:, 0:1],
                                   x[:, k * T + t0:k * T + t0 + tw],
                                   start=(k == 0), stop=(k == NCH - 1)),
                          reads=[("c_oc",), xkeyf(k)], writes=[pkS])
                    em.op("tensor", lambda e, k=k, tw=tw, pp=pp, pbQ=pbQ:
                          e.matmul(pbQ[0:1, 0:tw], mk.ones_col[:, 0:1],
                                   sqp[pp][:, 0:tw],
                                   start=(k == 0), stop=(k == NCH - 1)),
                          reads=[("c_oc",), kq], writes=[pkQ])
                em.op("vector", lambda e, tw=tw, pbS=pbS: e.tensor_scalar_mul(
                    prow[0:1, 0:tw], pbS[0:1, 0:tw], 1.0 / Dv),
                    reads=[pkS], writes=[("b3pr",)])
                em.op("vector", lambda e, tw=tw, pbQ=pbQ: e.tensor_scalar_mul(
                    prow[0:1, PW:PW + tw], pbQ[0:1, 0:tw], 1.0 / Dv),
                    reads=[pkQ], writes=[("b3pr",)])
                em.op("vector", lambda e, tw=tw: e.scalar_tensor_tensor(
                    prow[0:1, 2 * PW:2 * PW + tw], prow[0:1, 0:tw], -1.0,
                    prow[0:1, 0:tw], ALU.mult, ALU.mult),
                    reads=[("b3pr",)], writes=[("b3pr",)])
                em.op("vector", lambda e, tw=tw: e.tensor_add(
                    prow[0:1, 2 * PW:2 * PW + tw],
                    prow[0:1, 2 * PW:2 * PW + tw], prow[0:1, PW:PW + tw]),
                    reads=[("b3pr",)], writes=[("b3pr",)])
                c_eps1p = mk.constf(1e-5, 1)
                c_mh1p = mk.constf(-0.5, 1)
                em.op("scalar", lambda e, tw=tw, c_eps1p=c_eps1p: e.activation(
                    prow[0:1, 2 * PW:2 * PW + tw],
                    prow[0:1, 2 * PW:2 * PW + tw], AF.Ln, bias=c_eps1p),
                    reads=[("b3pr",), ("c_cp",)], writes=[("b3pr",)])
                em.op("scalar", lambda e, tw=tw, c_mh1p=c_mh1p: e.activation(
                    prow[0:1, 2 * PW:2 * PW + tw],
                    prow[0:1, 2 * PW:2 * PW + tw], AF.Exp, scale=c_mh1p),
                    reads=[("b3pr",), ("c_cp",)], writes=[("b3pr",)])
                mk.bcast_row(prow, tw, ("b3pr",), mub, ("b3mub",), col_off=0)
                mk.bcast_row(prow, tw, ("b3pr",), rbt, ("b3rbt",),
                             col_off=2 * PW)
                for k in range(NCH):
                    pp = k % 2
                    tk = ("b3ptm", pp)
                    em.op("vector", lambda e, k=k, t0=t0, tw=tw, pp=pp:
                          e.tensor_sub(tmpt[pp][:, 0:tw],
                                       x[:, k * T + t0:k * T + t0 + tw],
                                       mub[:, 0:tw]),
                          reads=[xkeyf(k), ("b3mub",)], writes=[tk])
                    em.op("gpsimd", lambda e, k=k, t0=t0, tw=tw, pp=pp:
                          e.tensor_mul(xnb[:, k * T + t0:k * T + t0 + tw],
                                       tmpt[pp][:, 0:tw], rbt[:, 0:tw]),
                          reads=[tk, ("b3rbt",)], writes=[xnkeyf(k)])
                    em.op("vector", lambda e, k=k, t0=t0, tw=tw, pp=pp:
                          e.tensor_mul(tmpt[pp][:, 0:tw], tmpt[pp][:, 0:tw],
                                       rbt[:, 0:tw]),
                          reads=[tk, ("b3rbt",)], writes=[tk])
                    em.op("sync", lambda e, k=k, t0=t0, tw=tw, pp=pp:
                          e.dma_start(
                              out=xnsp.ap()[k * 128:(k + 1) * 128, t0:t0 + tw],
                              in_=tmpt[pp][:, 0:tw]),
                          reads=[tk], writes=[("b3xnsp",)], dom="qs")
        pro = {"x": x, "xkeyf": xkeyf, "xnb": xnb, "xnkeyf": xnkeyf}
        wbp = mk.wstream_bufs(max(NCH, 8), npar=1)
        ag_in = mk.dram("b3ag_in", [3 * D, T], BF16)
        ag_out = mk.dram("b3ag_out", [2 * 3 * D, T], BF16)
        ag2_in = mk.dram("b3ag2_in", [c.NGC * 128, S], BF16)
        with mk.scope():
            wstrip = mk.sb("b3_wstrip", [128, 128], F32)
            wstrip16 = mk.sb("b3_ws16", [128, 128], BF16)
            shipb = mk.sb("b3_shipb", [128, T], BF16)
            for sec, wn in enumerate(("wqBD", "wkBD", "wvBD")):
                for k in range(NCH):
                    em.op("sync", lambda e, wn=wn, k=k: e.dma_start(
                        out=wstrip[:, :],
                        in_=io[wn].ap()[k * 128:(k + 1) * 128, :]),
                        writes=[("b3ws",)], dom="qs")
                    em.op("vector", lambda e: e.tensor_copy(wstrip16[:, :],
                                                            wstrip[:, :]),
                          reads=[("b3ws",)], writes=[("b3ws16",)])
                    for ti in range(c.NTS):
                        t0 = ti * c.TSUB
                        pb, pk = mk.pbank()
                        em.op("tensor", lambda e, k=k, t0=t0, pb=pb: e.matmul(
                            pb[0:128, 0:c.TSUB], wstrip16[:, :],
                            xnb[:, k * T + t0:k * T + t0 + c.TSUB],
                            start=True, stop=True),
                            reads=[("b3ws16",), xnkeyf(k)], writes=[pk])
                        em.op("vector", lambda e, t0=t0, pb=pb: e.tensor_copy(
                            shipb[:, t0:t0 + c.TSUB], pb[0:128, 0:c.TSUB]),
                            reads=[pk], writes=[("b3shipb",)])
                    em.op("gpsimd", lambda e, sec=sec, k=k: e.dma_start(
                        out=ag_in.ap()[sec * D + k * 128:sec * D + (k + 1) * 128, :],
                        in_=shipb[:, :]),
                        reads=[("b3shipb",)], writes=[("b3ag_in",)], dom="qg")
            mk.allgather(ag_in, ag_out, ("b3ag_in",), ("ag_out",))
        with mk.scope():
            mask01 = mk.sb("b3_m01", [128, 128], F32)
            em.op("sync", lambda e: e.dma_start(out=mask01[:, :],
                                                in_=io["mask01"].ap()),
                  writes=[("b3m01",)], dom="qs")
            gprow = mk.sb("b3_gprow", [2 * c.NGC, (S // 128) * c.NKB], F32)
            em.op("sync", lambda e: e.dma_start(out=gprow[0:2 * c.NGC, :],
                                                in_=io["gpow"].ap()),
                  writes=[("b3gpr",)], dom="qs")
            gnsc = mk.sb("b3_gnsc", [128, 2 * c.NGC], F32)
            em.op("sync", lambda e: e.dma_start(out=gnsc[:, :],
                                                in_=io["gncols"].ap()),
                  writes=[("b3gnsc",)], dom="qs")
            rowg = mk.sb("b3_rowg", [2, 2 * 512], F32)
            sqg = mk.sb("b3_sqg", [128, 512], BF16)
            muv = mk.sb("b3_muv", [128, 512], F32)
            hselg = mk.sb("b3_hselg", [128, 2], BF16)
            em.op("vector", lambda e: e.memset(hselg[:, :], 0.0), writes=[("b3hsg",)])
            em.op("vector", lambda e: e.memset(hselg[0:64, 0:1], 1.0), writes=[("b3hsg",)])
            em.op("vector", lambda e: e.memset(hselg[64:128, 1:2], 1.0), writes=[("b3hsg",)])
            rstb = mk.sb("b3_rstb", [128, 512], F32)
            c_gt = mk.constf(1e-5, 2)
            c_gmh = mk.constf(-0.5, 2)

            def post_group(g, outT, kOT):
                for t0 in range(0, S, 512):
                    em.op("scalar", lambda e, t0=t0, outT=outT: e.activation(
                        sqg[:, 0:512], outT[:, t0:t0 + 512], AF.Square),
                        reads=[kOT], writes=[("b3sqg",)])
                    em.op("tensor", lambda e, t0=t0, outT=outT: e.matmul(
                        pb5[0:2, 0:512], hselg[:, 0:2], outT[:, t0:t0 + 512],
                        start=True, stop=True),
                        reads=[("b3hsg",), kOT], writes=[pk5])
                    em.op("vector", lambda e: e.tensor_copy(
                        rowg[0:2, 0:512], pb5[0:2, 0:512]),
                        reads=[pk5], writes=[("b3rs",)])
                    em.op("tensor", lambda e: e.matmul(
                        pb5[0:2, 0:512], hselg[:, 0:2], sqg[:, 0:512],
                        start=True, stop=True),
                        reads=[("b3hsg",), ("b3sqg",)], writes=[pk5])
                    em.op("vector", lambda e: e.tensor_copy(
                        rowg[0:2, 512:1024], pb5[0:2, 0:512]),
                        reads=[pk5], writes=[("b3rs",)])
                    em.op("vector", lambda e: e.tensor_scalar_mul(
                        rowg[0:2, 0:512], rowg[0:2, 0:512], 1.0 / 64.0),
                        reads=[("b3rs",)], writes=[("b3rs",)])
                    em.op("tensor", lambda e: e.matmul(
                        pb5[0:128, 0:512], mk.hsel32[0:2, :],
                        rowg[0:2, 0:512], start=True, stop=True),
                        reads=[("c_hs",), ("b3rs",)], writes=[pk5])
                    em.op("vector", lambda e: e.tensor_copy(
                        muv[:, 0:512], pb5[0:128, 0:512]),
                        reads=[pk5], writes=[("b3muv",)])
                    em.op("vector", lambda e: e.tensor_scalar_mul(
                        rowg[0:2, 512:1024], rowg[0:2, 512:1024], 1.0 / 64.0),
                        reads=[("b3rs",)], writes=[("b3rs",)])
                    em.op("vector", lambda e: e.scalar_tensor_tensor(
                        rowg[0:2, 0:512], rowg[0:2, 0:512], -1.0,
                        rowg[0:2, 0:512], ALU.mult, ALU.mult),
                        reads=[("b3rs",)], writes=[("b3rs",)])
                    em.op("vector", lambda e: e.tensor_add(
                        rowg[0:2, 0:512], rowg[0:2, 0:512],
                        rowg[0:2, 512:1024]),
                        reads=[("b3rs",)], writes=[("b3rs",)])
                    em.op("scalar", lambda e: e.activation(
                        rowg[0:2, 0:512], rowg[0:2, 0:512], AF.Ln, bias=c_gt),
                        reads=[("b3rs",), ("c_cp",)], writes=[("b3rs",)])
                    em.op("scalar", lambda e: e.activation(
                        rowg[0:2, 0:512], rowg[0:2, 0:512], AF.Exp,
                        scale=c_gmh),
                        reads=[("b3rs",), ("c_cp",)], writes=[("b3rs",)])
                    em.op("vector", lambda e, t0=t0, outT=outT: e.tensor_sub(
                        muv[:, 0:512], outT[:, t0:t0 + 512], muv[:, 0:512]),
                        reads=[kOT, ("b3muv",)], writes=[("b3muv",)])
                    em.op("tensor", lambda e: e.matmul(
                        pb5[0:128, 0:512], mk.hsel32[0:2, :],
                        rowg[0:2, 0:512], start=True, stop=True),
                        reads=[("c_hs",), ("b3rs",)], writes=[pk5])
                    em.op("vector", lambda e: e.tensor_copy(
                        rstb[:, 0:512], pb5[0:128, 0:512]),
                        reads=[pk5], writes=[("b3rstb",)])
                    em.op("vector", lambda e: e.tensor_mul(
                        muv[:, 0:512], muv[:, 0:512], rstb[:, 0:512]),
                        reads=[("b3muv",), ("b3rstb",)], writes=[("b3muv",)])
                    em.op("vector", lambda e, t0=t0, g=g, outT=outT:
                          e.tensor_scalar(
                              outT[:, t0:t0 + 512], muv[:, 0:512],
                              gnsc[:, 2 * g:2 * g + 1],
                              gnsc[:, 2 * g + 1:2 * g + 2],
                              ALU.mult, ALU.add),
                          reads=[("b3muv",), ("b3gnsc",)], writes=[kOT])

            mk.attn_headshard("b3a", ag_out, ag2_in, mode="retention",
                              mask01=(mask01, ("b3m01",)),
                              decay_dram=(io["qdec"], io["kdec"]),
                              gpow_src=(gprow, ("b3gpr",)),
                              post_group=post_group)
            mk.allgather(ag2_in, ag2_out, ("b3a", "ag2_in"), ("b3ag2o",))
        with mk.scope():
            g16 = mk.sb("b3_g16", [128, NCH * T], BF16)
            def evac_gate(fabs, fcw, t0, tw, pb, pk):
                kk = fabs // 128
                em.op("scalar", lambda e, kk=kk, t0=t0, tw=tw, fcw=fcw, pb=pb:
                      e.activation(g16[0:fcw, kk * T + t0:kk * T + t0 + tw],
                                   pb[0:fcw, 0:tw], AF.Silu),
                      reads=[pk], writes=[("b3g16", kk)])
            mk.proj("b3w1", io["w1b3"].ap(), D, D, xnb, xnkeyf, evac_gate,
                    wbufs=wbp)
            aob = mk.sb("b3_aob", [128, NCH * T], BF16)
            dst4 = aob[:, :].rearrange("p (k t) -> p k t", k=NCH)
            mk.dyn_load("sync", dst4, ag2_out, 0,
                        [[S, 128], [128 * S, NCH], [1, T]],
                        [("b3ag2o",)], [("b3aob", k) for k in range(NCH)],
                        base="tb")
            for k in range(NCH):
                em.op("vector", lambda e, k=k: e.tensor_add(
                    g16[:, k * T:(k + 1) * T], g16[:, k * T:(k + 1) * T],
                    aob[:, k * T:(k + 1) * T]),
                    reads=[("b3g16", k), ("b3aob", k)], writes=[("b3g16", k)])
            mk.proj("b3w2", io["w2b3"].ap(), D, D, g16,
                    lambda k: ("b3g16", k), resid_evac(mk, pro, "b3w2"),
                    wbufs=wbp)
        for k in range(NCH):
            em.op("gpsimd", lambda e, k=k: e.dma_start(
                out=xsp.ap()[k * 128:(k + 1) * 128, :],
                in_=x[:, k * T:(k + 1) * T]),
                reads=[xkeyf(k)], writes=[("b3xsp",)], dom="qg")
    emit_l3ffn(mk, io, xnsp, xsp)




def _dih(cfg):
    return cdiv(int(cfg.D * 4 * 2 / 3), 128) * 128


def emit_l3ffn(mk: MK, io, xnsp, xsp):
    c = mk.cfg
    em = mk.em
    T, D, S = c.T, c.D, c.S
    NCH = c.NCH
    TH = 1 + T
    x3 = mk.sb("b3f_x3", [128, NCH * TH], F32)
    x3keyf = lambda k: ("x3", k)
    for k in range(NCH):
        em.op("sync", lambda e, k=k: e.dma_start(
            out=x3[:, k * TH + 1:k * TH + 1 + T],
            in_=xnsp.ap()[k * 128:(k + 1) * 128, :]),
            reads=[("b3xnsp",)], writes=[x3keyf(k)], dom="qs")
    THB = 258 + T
    EXT = 256 + T
    NG3 = D // 128
    NW = T // 256
    DIH = _dih(c)
    NKH = DIH // 128
    JB = 3
    NDEP = 6
    pb5 = mk.p512[5]
    pk5 = ("p512", 5)
    O_QKV = 0
    O_RP = 3 * EXT
    O_P3 = O_RP + 512
    O_PT = O_P3 + 1024
    O_VT = O_PT + 512
    O_OT = O_VT + 512
    ACOLS = max(O_OT + T, JB * (T + D))
    arena = mk.sb("b3f_arena", [128, ACOLS], BF16)
    O_HJ = 0
    O_W2 = JB * T
    x3b = mk.sb("b3f_x3b", [128, NCH * THB], BF16)
    x3bkeyf = lambda k: ("x3b", k)
    wo_sb = mk.sb("b3f_wo", [128, D], BF16)
    wbp = mk.wstream_bufs(NCH, npar=1)
    w1sl = [[mk.sb(mk.fresh("w1s"), [128, 256], BF16) for _ in range(NCH)]
            for _ in range(2)]
    hv = mk.sb("b3f_hv", [128, 1], F32)
    xq_cos = mk.sb("b3f_xqc", [128, T], BF16)
    xq_sin = mk.sb("b3f_xqs", [128, T], BF16)
    xk_cos = mk.sb("b3f_xkc", [128, EXT], BF16)
    xk_sin = mk.sb("b3f_xks", [128, EXT], BF16)
    R3 = mk.sb("b3f_R3", [128, 128], BF16)
    maskL3 = mk.sb("b3f_mL", [128, 2 * 512], BF16)
    maskW0 = mk.sb("b3f_mW0", [128, 512], BF16)
    qscols = mk.sb("b3f_qsc", [128, 2 * NDEP], F32)
    gbcols = mk.sb("b3f_gbc", [128, 2 * NDEP * NCH], F32)
    b1cols = mk.sb("b3f_b1c", [128, 2 * NDEP * NKH], F32)
    b2cols = mk.sb("b3f_b2c", [128, NDEP * NCH], F32)
    for nm, dst in (("halo_valid", hv), ("xq_cos", xq_cos), ("xq_sin", xq_sin),
                    ("xk_cos", xk_cos), ("xk_sin", xk_sin), ("R3", R3),
                    ("maskL3", maskL3), ("maskW0", maskW0), ("qscols", qscols),
                    ("gbcols", gbcols), ("b1cols", b1cols), ("b2cols", b2cols)):
        em.op("sync", lambda e, nm=nm, dst=dst: e.dma_start(
            out=dst[:, :], in_=io[nm].ap()), writes=[("t_" + nm,)], dom="qs")
    kHV = ("t_halo_valid",)
    hsel3 = mk.sb("b3f_hs3", [128, 2], BF16)
    em.op("vector", lambda e: e.memset(hsel3[:, :], 0.0), writes=[("b3hs3",)])
    em.op("vector", lambda e: e.memset(hsel3[0:64, 0:1], 1.0), writes=[("b3hs3",)])
    em.op("vector", lambda e: e.memset(hsel3[64:128, 1:2], 1.0), writes=[("b3hs3",)])
    rn3 = mk.sb("b3f_rn3", [2, 512], F32)
    rbb = mk.sb("b3f_rbb", [128, 512], F32)
    rb2 = mk.sb("b3f_rb2", [128, 512], F32)
    tmp3 = mk.sb("b3f_tmp3", [128, 512], F32)
    abuf = [mk.sb(mk.fresh("b3fab"), [128, 512], BF16) for _ in range(2)]
    sums3 = mk.sb("b3f_sm3", [128, 8], F32)
    nm3 = mk.sb("b3f_nm3", [128, 4], F32)
    STW = min(512, T)
    frow = mk.sb("b3f_frow", [1, 3 * STW], F32)
    c_t24 = mk.constf(1e-24, 2)
    c_mh2 = mk.constf(-0.5, 2)
    c_8 = mk.constf(8.0, 128)
    c_eps1 = mk.constf(1e-5, 1)
    c_mh1 = mk.constf(-0.5, 1)
    HW = min(258, T)
    HO = 258 - HW
    agh_in = mk.dram("b3agh_in", [D, HW], F32)
    agh_out = mk.dram("b3agh_out", [2 * D, HW], F32)
    agf_in = mk.dram("b3agf_in", [D, 1], F32)
    agf_out = mk.dram("b3agf_out", [2 * D, 1], F32)
    for k in range(NCH):
        if HO:
            em.op("vector", lambda e, k=k: e.memset(
                x3b[:, k * THB:k * THB + HO], 0.0), writes=[x3bkeyf(k)])
        em.op("vector", lambda e, k=k: e.tensor_copy(
            x3b[:, k * THB + 258:k * THB + 258 + T],
            x3[:, k * TH + 1:k * TH + 1 + T]),
            reads=[x3keyf(k)], writes=[x3bkeyf(k)])
    stat_tiles = [(i * STW, STW) for i in range(T // STW)]
    ext_tsubs = [(2, 256)] + [(258 + t0, tw) for (t0, tw) in stat_tiles]
    kloc_tiles = [(0, 256)] + [(256 + t0, tw) for (t0, tw) in stat_tiles]
    qloc_tiles = [(256 + t0, tw) for (t0, tw) in stat_tiles]

    for li in range(NDEP):
        for k in range(NCH):
            em.op("gpsimd", lambda e, k=k: e.dma_start(
                out=agh_in.ap()[k * 128:(k + 1) * 128, :],
                in_=x3[:, k * TH + 1 + T - HW:k * TH + 1 + T]),
                reads=[x3keyf(k)], writes=[("aghi",)], dom="qg")
        mk.allgather(agh_in, agh_out, ("aghi",), ("agho",))
        for k in range(NCH):
            em.op("gpsimd", lambda e, k=k: e.dma_start(
                out=x3b[:, k * THB + HO:k * THB + 258],
                in_=agh_out.ap()[k * 128:(k + 1) * 128, :]),
                reads=[("agho",)], writes=[x3bkeyf(k)], dom="qg")
            em.op("vector", lambda e, k=k: e.tensor_scalar(
                x3b[:, k * THB + HO:k * THB + 258],
                x3b[:, k * THB + HO:k * THB + 258],
                hv[:, 0:1], None, ALU.mult),
                reads=[x3bkeyf(k), kHV], writes=[x3bkeyf(k)])
        for g in range(NG3):
            def evac_qkv3(fabs, fcw, t0, tw, pb, pk):
                sec = fabs // 128
                em.op("vector", lambda e, sec=sec, t0=t0, tw=tw, fcw=fcw, pb=pb:
                      e.tensor_copy(
                          arena[0:fcw, O_QKV + sec * EXT + t0 - 2:
                                O_QKV + sec * EXT + t0 - 2 + tw],
                          pb[0:fcw, 0:tw]),
                      reads=[pk], writes=[("a_qkv", sec)])
            mk.proj(f"b3f_qkv{li}_{g}",
                    io["WqkvR"].ap()[li * D:(li + 1) * D, g * 384:(g + 1) * 384],
                    D, 384, x3b, x3bkeyf, evac_qkv3, wbufs=wbp,
                    tsubs=ext_tsubs,
                    xoff_fn=lambda k, t0: k * THB + t0 - (1 if k >= NCH // 2 else 0))
            for sec in (0, 1):
                tiles = qloc_tiles if sec == 0 else kloc_tiles
                toff = 256 if sec == 0 else 0
                ctab, ck = (xq_cos, ("t_xq_cos",)) if sec == 0 else (xk_cos, ("t_xk_cos",))
                stab, sk = (xq_sin, ("t_xq_sin",)) if sec == 0 else (xk_sin, ("t_xk_sin",))
                for (t0, tw) in tiles:
                    q0 = O_QKV + sec * EXT + t0
                    em.op("scalar", lambda e, q0=q0, tw=tw: e.activation(
                        arena[:, O_RP:O_RP + tw], arena[:, q0:q0 + tw], AF.Square),
                        reads=[("a_qkv", sec)], writes=[("a_rp",)])
                    em.op("tensor", lambda e, tw=tw: e.matmul(
                        pb5[0:2, 0:tw], hsel3[:, 0:2], arena[:, O_RP:O_RP + tw],
                        start=True, stop=True),
                        reads=[("b3hs3",), ("a_rp",)], writes=[pk5])
                    em.op("vector", lambda e, tw=tw: e.tensor_copy(
                        rn3[0:2, 0:tw], pb5[0:2, 0:tw]), reads=[pk5],
                        writes=[("b3rn",)])
                    em.op("scalar", lambda e, tw=tw: e.activation(
                        rn3[0:2, 0:tw], rn3[0:2, 0:tw], AF.Ln, bias=c_t24),
                        reads=[("b3rn",), ("c_cp",)], writes=[("b3rn",)])
                    em.op("scalar", lambda e, tw=tw: e.activation(
                        rn3[0:2, 0:tw], rn3[0:2, 0:tw], AF.Exp, scale=c_mh2),
                        reads=[("b3rn",), ("c_cp",)], writes=[("b3rn",)])
                    em.op("tensor", lambda e, tw=tw: e.matmul(
                        pb5[0:128, 0:tw], mk.hsel32[0:2, :], rn3[0:2, 0:tw],
                        start=True, stop=True),
                        reads=[("c_hs",), ("b3rn",)], writes=[pk5])
                    em.op("vector", lambda e, tw=tw, sec=sec, li=li: e.tensor_scalar(
                        rbb[:, 0:tw], pb5[0:128, 0:tw],
                        qscols[:, 2 * li + sec:2 * li + sec + 1], None, ALU.mult),
                        reads=[pk5, ("t_qscols",)], writes=[("b3rbb",)])
                    em.op("vector", lambda e, q0=q0, tw=tw, sec=sec: e.tensor_mul(
                        arena[:, q0:q0 + tw], arena[:, q0:q0 + tw], rbb[:, 0:tw]),
                        reads=[("a_qkv", sec), ("b3rbb",)], writes=[("a_qkv", sec)])
                    em.op("tensor", lambda e, q0=q0, tw=tw, sec=sec: e.matmul(
                        pb5[0:128, 0:tw], R3[:, :], arena[:, q0:q0 + tw],
                        start=True, stop=True),
                        reads=[("t_R3",), ("a_qkv", sec)], writes=[pk5])
                    em.op("vector", lambda e, t0=t0, tw=tw, toff=toff, stab=stab,
                          sk=sk: e.tensor_mul(
                              arena[:, O_RP:O_RP + tw], pb5[0:128, 0:tw],
                              stab[:, t0 - toff:t0 - toff + tw]),
                          reads=[pk5, sk], writes=[("a_rp",)])
                    em.op("vector", lambda e, q0=q0, tw=tw, t0=t0, toff=toff,
                          ctab=ctab, ck=ck, sec=sec: e.tensor_mul(
                              arena[:, q0:q0 + tw], arena[:, q0:q0 + tw],
                              ctab[:, t0 - toff:t0 - toff + tw]),
                          reads=[("a_qkv", sec), ck], writes=[("a_qkv", sec)])
                    em.op("vector", lambda e, q0=q0, tw=tw, sec=sec: e.tensor_add(
                        arena[:, q0:q0 + tw], arena[:, q0:q0 + tw],
                        arena[:, O_RP:O_RP + tw]),
                        reads=[("a_qkv", sec), ("a_rp",)], writes=[("a_qkv", sec)])
            for w in range(NW):
                kb0 = w * 256
                kas = list(range(4)) if w == 0 else [w * 2 + 2, w * 2 + 3]
                for ka in kas:
                    slot = ka % 4
                    for h in range(2):
                        mk.transpose(
                            arena[64 * h:64 * h + 64,
                                  O_QKV + 2 * EXT + ka * 128:
                                  O_QKV + 2 * EXT + (ka + 1) * 128],
                            64, 128, [("a_qkv", 2)], arena,
                            O_VT + (h * 4 + slot) * 64,
                            ("a_vT", h, slot), pti=ka % 2, bp=64 * h)
                for qt2 in range(2):
                    ql0 = O_QKV + 256 + w * 256 + qt2 * 128
                    for h in range(2):
                        si = h * 2 + qt2
                        Sb, kS = mk.p512[si], ("p512", si)
                        scol = qt2 * 2 + h
                        em.op("tensor", lambda e, h=h, ql0=ql0, kb0=kb0, Sb=Sb:
                              e.matmul(
                                  Sb[0:128, 0:512],
                                  arena[64 * h:64 * h + 64, ql0:ql0 + 128],
                                  arena[64 * h:64 * h + 64,
                                        O_QKV + EXT + kb0:O_QKV + EXT + kb0 + 512],
                                  start=True, stop=True,
                                  tile_position=(64 * h, 0)),
                              reads=[("a_qkv", 0), ("a_qkv", 1)], writes=[kS])
                        em.op("vector", lambda e, qt2=qt2, Sb=Sb: e.tensor_add(
                            Sb[0:128, 0:512], maskL3[:, qt2 * 512:(qt2 + 1) * 512],
                            Sb[0:128, 0:512]),
                            reads=[("t_maskL3",), kS], writes=[kS])
                        if w == 0:
                            em.op("vector", lambda e, Sb=Sb: e.tensor_add(
                                Sb[0:128, 0:512], maskW0[:, 0:512],
                                Sb[0:128, 0:512]),
                                reads=[("t_maskW0",), kS], writes=[kS])
                        em.op("vector", lambda e, scol=scol, Sb=Sb: e.tensor_reduce(
                            sums3[:, scol:scol + 1], Sb[0:128, 0:512],
                            AX.X, ALU.max),
                            reads=[kS], writes=[("b3mx",)])
                        em.op("vector", lambda e, scol=scol: e.tensor_scalar_mul(
                            nm3[:, scol:scol + 1], sums3[:, scol:scol + 1], -8.0),
                            reads=[("b3mx",)], writes=[("b3nm",)])
                        em.op("scalar", lambda e, h=h, scol=scol, Sb=Sb:
                              e.activation(
                                  arena[:, O_P3 + h * 512:O_P3 + h * 512 + 512],
                                  Sb[0:128, 0:512], AF.Exp,
                                  bias=nm3[:, scol:scol + 1], scale=c_8,
                                  accum_out=sums3[:, 4 + scol:4 + scol + 1]),
                              reads=[kS, ("b3nm",), ("c_cp",)],
                              writes=[("a_P", h), ("b3sm",)])
                        for kt in range(4):
                            ka = w * 2 + kt
                            slot = ka % 4
                            pti = (kt + h) % 2
                            ptb, pkT = mk.pT[pti], ("pT", pti)
                            em.op("tensor", lambda e, h=h, kt=kt, ptb=ptb:
                                  e.transpose(
                                      ptb[0:128, 0:128],
                                      arena[:, O_P3 + h * 512 + kt * 128:
                                            O_P3 + h * 512 + (kt + 1) * 128],
                                      mk.ident[:, :]),
                                  reads=[("a_P", h), ("c_id",)], writes=[pkT])
                            pt_off = O_PT + (h * 2 + kt % 2) * 128
                            em.op("vector", lambda e, ptb=ptb, pt_off=pt_off:
                                  e.tensor_copy(arena[:, pt_off:pt_off + 128],
                                                ptb[0:128, 0:128]),
                                  reads=[pkT], writes=[("a_PT", h, kt % 2)])
                            em.op("tensor", lambda e, h=h, kt=kt, qt2=qt2,
                                  slot=slot, pt_off=pt_off: e.matmul(
                                      mk.p512[4][64 * h:64 * h + 64,
                                                 qt2 * 128:qt2 * 128 + 128],
                                      arena[:, O_VT + (h * 4 + slot) * 64:
                                            O_VT + (h * 4 + slot) * 64 + 64],
                                      arena[:, pt_off:pt_off + 128],
                                      start=(kt == 0), stop=(kt == 3),
                                      tile_position=(0, 64 * h)),
                                  reads=[("a_vT", h, slot), ("a_PT", h, kt % 2)],
                                  writes=[("p512", 4)])
                em.op("vector", lambda e: e.reciprocal(sums3[:, 4:8],
                                                       sums3[:, 4:8]),
                      reads=[("b3sm",)], writes=[("b3sm",)])
                for qt2 in range(2):
                    em.op("tensor", lambda e, qt2=qt2: e.transpose(
                        pb5[0:2, qt2 * 128:qt2 * 128 + 128],
                        sums3[:, 4 + qt2 * 2:4 + qt2 * 2 + 2],
                        mk.ident32[:, :]),
                        reads=[("b3sm",), ("c_id32",)], writes=[pk5])
                    em.op("vector", lambda e, qt2=qt2: e.tensor_copy(
                        rn3[0:2, qt2 * 128:qt2 * 128 + 128],
                        pb5[0:2, qt2 * 128:qt2 * 128 + 128]),
                        reads=[pk5], writes=[("b3rn",)])
                for qt2 in range(2):
                    em.op("tensor", lambda e, qt2=qt2: e.matmul(
                        pb5[0:128, qt2 * 128:qt2 * 128 + 128], mk.hsel32[0:2, :],
                        rn3[0:2, qt2 * 128:qt2 * 128 + 128],
                        start=True, stop=True),
                        reads=[("c_hs",), ("b3rn",)], writes=[pk5])
                em.op("vector", lambda e: e.tensor_copy(
                    rbb[:, 0:256], pb5[0:128, 0:256]),
                    reads=[pk5], writes=[("b3rbb",)])
                em.op("vector", lambda e, w=w: e.tensor_mul(
                    arena[:, O_OT + w * 256:O_OT + w * 256 + 256],
                    mk.p512[4][0:128, 0:256], rbb[:, 0:256]),
                    reads=[("p512", 4), ("b3rbb",)], writes=[("a_oT",)])
            em.op("gpsimd", lambda e, li=li, g=g: e.dma_start(
                out=wo_sb[:, :],
                in_=io["Wo3"].ap()[li * D + g * 128:li * D + (g + 1) * 128, :]),
                writes=[("b3wo",)], dom="qg")
            for kk in range(NCH):
                for ti, (t0, tw) in enumerate(stat_tiles):
                    pi = (kk * len(stat_tiles) + ti) % 2
                    pbF, pkF = mk.p512[pi], ("p512", pi)
                    em.op("tensor", lambda e, kk=kk, t0=t0, tw=tw, pbF=pbF:
                          e.matmul(pbF[0:128, 0:tw], wo_sb[:, kk * 128:(kk + 1) * 128],
                                   arena[:, O_OT + t0:O_OT + t0 + tw],
                                   start=True, stop=True),
                          reads=[("b3wo",), ("a_oT",)], writes=[pkF])
                    em.op("vector", lambda e, kk=kk, t0=t0, tw=tw, pbF=pbF:
                          e.tensor_add(
                              x3[:, kk * TH + 1 + t0:kk * TH + 1 + t0 + tw],
                              x3[:, kk * TH + 1 + t0:kk * TH + 1 + t0 + tw],
                              pbF[0:128, 0:tw]),
                          reads=[x3keyf(kk), pkF], writes=[x3keyf(kk)])
        for k in range(NCH):
            em.op("gpsimd", lambda e, k=k: e.dma_start(
                out=agf_in.ap()[k * 128:(k + 1) * 128, :],
                in_=x3[:, k * TH + T:k * TH + T + 1]),
                reads=[x3keyf(k)], writes=[("agfi",)], dom="qg")
        mk.allgather(agf_in, agf_out, ("agfi",), ("agfo",))
        for k in range(NCH):
            em.op("sync", lambda e, k=k: e.dma_start(
                out=x3[:, k * TH:k * TH + 1],
                in_=agf_out.ap()[k * 128:(k + 1) * 128, :]),
                reads=[("agfo",)], writes=[x3keyf(k)], dom="qs")
            em.op("vector", lambda e, k=k: e.tensor_scalar(
                x3[:, k * TH:k * TH + 1], x3[:, k * TH:k * TH + 1],
                hv[:, 0:1], None, ALU.mult),
                reads=[x3keyf(k), kHV], writes=[x3keyf(k)])
        em.fence()
        emit_l3ffn_mlp(mk, io, li, x3, x3keyf, x3b, x3bkeyf, arena,
                       dict(wbp=wbp, w1sl=w1sl, abuf=abuf, frow=frow,
                            rbb=rbb, rb2=rb2, tmp3=tmp3,
                            stat_tiles=stat_tiles, STW=STW, JB=JB,
                            NKH=NKH, DIH=DIH, O_HJ=O_HJ, O_W2=O_W2,
                            c_eps1=c_eps1, c_mh1=c_mh1,
                            gbcols=gbcols, b1cols=b1cols, b2cols=b2cols))
        for k in range(NCH):
            em.op("vector", lambda e, k=k: e.tensor_copy(
                x3b[:, k * THB + 258:k * THB + 258 + T],
                x3[:, k * TH + 1:k * TH + 1 + T]),
                reads=[x3keyf(k)], writes=[x3bkeyf(k)])
        em.fence()
    for k in range(NCH):
        for ti, (t0, tw) in enumerate(stat_tiles):
            bb = rbb if ti % 2 == 0 else rb2
            kb = ("b3rbb",) if ti % 2 == 0 else ("b3rb2",)
            em.op("sync", lambda e, k=k, t0=t0, tw=tw, bb=bb: e.dma_start(
                out=bb[:, 0:tw], in_=xsp.ap()[k * 128:(k + 1) * 128, t0:t0 + tw]),
                reads=[("b3xsp",)], writes=[kb], dom="qs")
            em.op("vector", lambda e, k=k, t0=t0, tw=tw, bb=bb: e.tensor_add(
                bb[:, 0:tw], bb[:, 0:tw],
                x3[:, k * TH + 1 + t0:k * TH + 1 + t0 + tw]),
                reads=[kb, x3keyf(k)], writes=[kb])
            em.op("gpsimd", lambda e, k=k, t0=t0, tw=tw, bb=bb: e.dma_start(
                out=io["x_out"].ap()[k * 128:(k + 1) * 128, t0:t0 + tw],
                in_=bb[:, 0:tw]),
                reads=[kb], writes=[("x_out",)], dom="qg")
    em.finalize_output_wait("sync")


def emit_l3ffn_mlp(mk: MK, io, li, x3, x3keyf, x3b, x3bkeyf, arena, sh):
    c = mk.cfg
    em = mk.em
    T, D = c.T, c.D
    NCH = c.NCH
    TH = 1 + T
    THB = 258 + T
    JB, NKH, DIH = sh["JB"], sh["NKH"], sh["DIH"]
    O_HJ, O_W2 = sh["O_HJ"], sh["O_W2"]
    STW = sh["STW"]
    stat_tiles = sh["stat_tiles"]
    frow, rbb, rb2, tmp3 = sh["frow"], sh["rbb"], sh["rb2"], sh["tmp3"]
    abuf, w1sl, wbp = sh["abuf"], sh["w1sl"], sh["wbp"]
    gbcols, b1cols, b2cols = sh["gbcols"], sh["b1cols"], sh["b2cols"]
    stage = wbp["stage"]
    Dv = float(D)
    for (t0, tw) in stat_tiles:
        pbS, pkS = mk.pbank()
        pbQ, pkQ = mk.pbank()
        for k in range(NCH):
            sOFF = k * TH + 1 + t0 - (1 if k >= NCH // 2 else 0)
            sqb = rbb if k % 2 == 0 else rb2
            kq = ("b3rbb",) if k % 2 == 0 else ("b3rb2",)
            em.op("scalar", lambda e, sOFF=sOFF, tw=tw, sqb=sqb: e.activation(
                sqb[:, 0:tw], x3[:, sOFF:sOFF + tw], AF.Square),
                reads=[x3keyf(k)], writes=[kq])
            em.op("tensor", lambda e, sOFF=sOFF, tw=tw, k=k, pbS=pbS: e.matmul(
                pbS[0:1, 0:tw], mk.ones_col[:, 0:1], x3[:, sOFF:sOFF + tw],
                start=(k == 0), stop=(k == NCH - 1)),
                reads=[("c_oc",), x3keyf(k)], writes=[pkS])
            em.op("tensor", lambda e, tw=tw, k=k, pbQ=pbQ, sqb=sqb: e.matmul(
                pbQ[0:1, 0:tw], mk.ones_col[:, 0:1], sqb[:, 0:tw],
                start=(k == 0), stop=(k == NCH - 1)),
                reads=[("c_oc",), kq], writes=[pkQ])
        em.op("vector", lambda e, tw=tw, pbS=pbS: e.tensor_scalar_mul(
            frow[0:1, 0:tw], pbS[0:1, 0:tw], 1.0 / Dv),
            reads=[pkS], writes=[("b3fr",)])
        em.op("vector", lambda e, tw=tw, pbQ=pbQ: e.tensor_scalar_mul(
            frow[0:1, STW:STW + tw], pbQ[0:1, 0:tw], 1.0 / Dv),
            reads=[pkQ], writes=[("b3fr",)])
        em.op("vector", lambda e, tw=tw: e.scalar_tensor_tensor(
            frow[0:1, 2 * STW:2 * STW + tw], frow[0:1, 0:tw], -1.0,
            frow[0:1, 0:tw], ALU.mult, ALU.mult),
            reads=[("b3fr",)], writes=[("b3fr",)])
        em.op("vector", lambda e, tw=tw: e.tensor_add(
            frow[0:1, 2 * STW:2 * STW + tw], frow[0:1, 2 * STW:2 * STW + tw],
            frow[0:1, STW:STW + tw]),
            reads=[("b3fr",)], writes=[("b3fr",)])
        em.op("scalar", lambda e, tw=tw: e.activation(
            frow[0:1, 2 * STW:2 * STW + tw], frow[0:1, 2 * STW:2 * STW + tw],
            AF.Ln, bias=sh["c_eps1"]),
            reads=[("b3fr",), ("c_cp",)], writes=[("b3fr",)])
        em.op("scalar", lambda e, tw=tw: e.activation(
            frow[0:1, 2 * STW:2 * STW + tw], frow[0:1, 2 * STW:2 * STW + tw],
            AF.Exp, scale=sh["c_mh1"]),
            reads=[("b3fr",), ("c_cp",)], writes=[("b3fr",)])
        mk.bcast_row(frow, tw, ("b3fr",), rbb, ("b3rbb",), col_off=0)
        mk.bcast_row(frow, tw, ("b3fr",), rb2, ("b3rb2",), col_off=2 * STW)
        for k in range(NCH):
            sOFF = k * TH + 1 + t0 - (1 if k >= NCH // 2 else 0)
            gc = (li * NCH + k) * 2
            em.op("vector", lambda e, sOFF=sOFF, tw=tw: e.tensor_sub(
                tmp3[:, 0:tw], x3[:, sOFF:sOFF + tw], rbb[:, 0:tw]),
                reads=[x3keyf(k), ("b3rbb",)], writes=[("b3tmp3",)])
            em.op("vector", lambda e, tw=tw: e.tensor_mul(
                tmp3[:, 0:tw], tmp3[:, 0:tw], rb2[:, 0:tw]),
                reads=[("b3tmp3",), ("b3rb2",)], writes=[("b3tmp3",)])
            em.op("gpsimd", lambda e, k=k, t0=t0, tw=tw, gc=gc: e.tensor_scalar(
                x3b[:, k * THB + 258 + t0:k * THB + 258 + t0 + tw],
                tmp3[:, 0:tw], gbcols[:, gc:gc + 1], gbcols[:, gc + 1:gc + 2],
                ALU.mult, ALU.add),
                reads=[("b3tmp3",), ("t_gbcols",)], writes=[x3bkeyf(k)])
    jblocks = [list(range(j0, min(j0 + JB, NKH))) for j0 in range(0, NKH, JB)]
    for js in jblocks:
        for ji, j in enumerate(js):
            jp = j % 2
            for k in range(NCH):
                sp = k % 2
                em.op("sync", lambda e, li=li, j=j, k=k, sp=sp: e.dma_start(
                    out=stage[sp][0:128, 0:256],
                    in_=io["W1p"].ap()[li * D + k * 128:li * D + (k + 1) * 128,
                                       j * 256:(j + 1) * 256]),
                    writes=[("w1stg", sp)], dom="qs")
                em.op("gpsimd", lambda e, k=k, sp=sp, jp=jp: e.tensor_copy(
                    w1sl[jp][k][:, 0:256], stage[sp][0:128, 0:256]),
                    reads=[("w1stg", sp)], writes=[("w1w", jp, k)])
            bcol = (li * NKH + j) * 2
            for ti, (t0, tw) in enumerate(stat_tiles):
                pbA, pkA = mk.pbank()
                pbG, pkG = mk.pbank()
                for k in range(NCH):
                    em.op("tensor", lambda e, k=k, jp=jp, t0=t0, tw=tw, pbA=pbA:
                          e.matmul(pbA[0:128, 0:tw], w1sl[jp][k][:, 0:128],
                                   x3b[:, k * THB + 258 + t0:
                                       k * THB + 258 + t0 + tw],
                                   start=(k == 0), stop=(k == NCH - 1)),
                          reads=[("w1w", jp, k), x3bkeyf(k)], writes=[pkA])
                    em.op("tensor", lambda e, k=k, jp=jp, t0=t0, tw=tw, pbG=pbG:
                          e.matmul(pbG[0:128, 0:tw], w1sl[jp][k][:, 128:256],
                                   x3b[:, k * THB + 258 + t0:
                                       k * THB + 258 + t0 + tw],
                                   start=(k == 0), stop=(k == NCH - 1)),
                          reads=[("w1w", jp, k), x3bkeyf(k)], writes=[pkG])
                pa = ti % 2
                em.op("vector", lambda e, tw=tw, bcol=bcol, pa=pa, pbA=pbA:
                      e.tensor_scalar(abuf[pa][:, 0:tw], pbA[0:128, 0:tw],
                                      b1cols[:, bcol:bcol + 1], None,
                                      ALU.add),
                      reads=[pkA, ("t_b1cols",)], writes=[("b3ab", pa)])
                em.op("scalar", lambda e, ji=ji, t0=t0, tw=tw, bcol=bcol,
                      pbG=pbG: e.activation(
                          arena[:, O_HJ + ji * T + t0:O_HJ + ji * T + t0 + tw],
                          pbG[0:128, 0:tw], AF.Gelu,
                          bias=b1cols[:, bcol + 1:bcol + 2]),
                      reads=[pkG, ("t_b1cols",)], writes=[("a_hj", ji)])
                em.op("vector", lambda e, ji=ji, t0=t0, tw=tw, pa=pa:
                      e.tensor_mul(
                          arena[:, O_HJ + ji * T + t0:O_HJ + ji * T + t0 + tw],
                          arena[:, O_HJ + ji * T + t0:O_HJ + ji * T + t0 + tw],
                          abuf[pa][:, 0:tw]),
                      reads=[("a_hj", ji), ("b3ab", pa)], writes=[("a_hj", ji)])
            em.op("gpsimd", lambda e, li=li, j=j, ji=ji: e.dma_start(
                out=arena[:, O_W2 + ji * D:O_W2 + (ji + 1) * D],
                in_=io["W2p"].ap()[li * DIH + j * 128:li * DIH + (j + 1) * 128, :]),
                writes=[("a_w2", ji)], dom="qg")
        for kk in range(NCH):
            for (t0, tw) in stat_tiles:
                pb, pk = mk.pbank()
                for ji in range(len(js)):
                    em.op("tensor", lambda e, kk=kk, ji=ji, t0=t0, tw=tw, pb=pb,
                          nj=len(js): e.matmul(
                              pb[0:128, 0:tw],
                              arena[:, O_W2 + ji * D + kk * 128:
                                    O_W2 + ji * D + (kk + 1) * 128],
                              arena[:, O_HJ + ji * T + t0:O_HJ + ji * T + t0 + tw],
                              start=(ji == 0), stop=(ji == nj - 1)),
                          reads=[("a_w2", ji), ("a_hj", ji)], writes=[pk])
                em.op("vector", lambda e, kk=kk, t0=t0, tw=tw, pb=pb:
                      e.tensor_add(
                          x3[:, kk * TH + 1 + t0:kk * TH + 1 + t0 + tw],
                          x3[:, kk * TH + 1 + t0:kk * TH + 1 + t0 + tw],
                          pb[0:128, 0:tw]),
                      reads=[pk, x3keyf(kk)], writes=[x3keyf(kk)])
    for kk in range(NCH):
        for (t0, tw) in stat_tiles:
            em.op("vector", lambda e, kk=kk, t0=t0, tw=tw, li=li: e.tensor_scalar(
                x3[:, kk * TH + 1 + t0:kk * TH + 1 + t0 + tw],
                x3[:, kk * TH + 1 + t0:kk * TH + 1 + t0 + tw],
                b2cols[:, li * NCH + kk:li * NCH + kk + 1], None, ALU.add),
                reads=[x3keyf(kk), ("t_b2cols",)], writes=[x3keyf(kk)])


BLOCK_INPUTS[3] = lambda cfg: [
    ("wqBD", [cfg.NCH * 128, 128]), ("wkBD", [cfg.NCH * 128, 128]),
    ("wvBD", [cfg.NCH * 128, 128]),
    ("w1b3", [cfg.D, cfg.D]), ("w2b3", [cfg.D, cfg.D]),
    ("mask01", [128, 128]),
    ("gpow", [2 * cfg.NGC, (cfg.S // 128) * cfg.NKB]),
    ("qdec", [cfg.NGC * 128, cfg.S], BF16),
    ("kdec", [cfg.NGC * 128, cfg.S], BF16),
    ("gncols", [128, 2 * cfg.NGC]),
    ("WqkvR", [6 * cfg.D, (cfg.D // 128) * 384]),
    ("Wo3", [6 * cfg.D, cfg.D]),
    ("W1p", [6 * cfg.D, 2 * _dih(cfg)]),
    ("W2p", [6 * _dih(cfg), cfg.D]),
    ("qscols", [128, 12]), ("gbcols", [128, 12 * cfg.NCH]),
    ("b1cols", [128, 12 * (_dih(cfg) // 128)]), ("b2cols", [128, 6 * cfg.NCH]),
    ("halo_valid", [128, 1]),
    ("maskW0", [128, 512], BF16),
    ("maskL3", [128, 1024], BF16),
    ("R3", [128, 128], BF16),
    ("xq_cos", [128, cfg.T], BF16), ("xq_sin", [128, cfg.T], BF16),
    ("xk_cos", [128, 256 + cfg.T], BF16), ("xk_sin", [128, 256 + cfg.T], BF16),
]

BLOCK_EMITTERS[3] = emit_block3


def _gammas_ret(H):
    lin = np.linspace(np.log(1.0 / 512.0), np.log(1.0 / 32.0), H)
    return 1.0 - np.exp(lin)


def _bd_strips(w):
    H = w.shape[0]
    out = np.zeros((H // 2 * 128, 128), np.float32)
    for k in range(H // 2):
        out[k * 128:k * 128 + 64, 0:64] = w[2 * k]
        out[k * 128 + 64:(k + 1) * 128, 64:128] = w[2 * k + 1]
    return np.ascontiguousarray(out)


def _mask01_np():
    p = np.arange(128)[:, None]
    f = np.arange(128)[None, :]
    return np.where(p >= f, 1.0, 0.0).astype(np.float32)


def _maskL3_np():
    import ml_dtypes
    p = np.arange(128)[:, None]
    f = np.arange(512)[None, :]
    out = np.zeros((128, 1024), np.float32)
    for qt2 in range(2):
        il = qt2 * 128 + p
        valid = (il <= f) & (f <= il + 256)
        out[:, qt2 * 512:(qt2 + 1) * 512] = np.where(valid, 0.0, -1e9)
    return out.astype(ml_dtypes.bfloat16)


def _xpos_tables(cfg, core):
    import ml_dtypes
    S, T = cfg.S, cfg.T
    HD = 64
    tb = (core % 2) * T
    half = np.arange(0, HD, 2, dtype=np.float64)
    inv = 1.0 / 10000.0 ** (half / HD)
    base = (half + 0.4 * HD) / (1.4 * HD)

    def tabs(pos):
        fr = pos[:, None] * inv[None, :]
        fr = np.concatenate([fr, fr], -1)
        power = (pos - S // 2) / (256 // 2)
        sc = base[None, :] ** power[:, None]
        sc = np.concatenate([sc, sc], -1)
        return np.cos(fr), np.sin(fr), sc
    dd = np.arange(128) % HD
    posq = np.arange(tb, tb + T, dtype=np.float64)
    cq, sq, scq = tabs(posq)
    posk = np.clip(np.arange(tb - 256, tb + T, dtype=np.float64), 0, None)
    ck, sk, sck = tabs(posk)
    bf = ml_dtypes.bfloat16
    return (np.ascontiguousarray((cq * scq)[:, dd].T).astype(bf),
            np.ascontiguousarray((sq * scq)[:, dd].T).astype(bf),
            np.ascontiguousarray((ck / sck)[:, dd].T).astype(bf),
            np.ascontiguousarray((sk / sck)[:, dd].T).astype(bf))


def _percore_block3(cfg, core, inputs=None):
    import ml_dtypes
    bf = ml_dtypes.bfloat16
    S, T, NGC, NKB = cfg.S, cfg.T, cfg.NGC, cfg.NKB
    parity = core % 2
    g_all = _gammas_ret(cfg.H)
    heads = parity * cfg.HC + np.arange(cfg.HC)
    lg = np.log(g_all[heads])
    t = np.arange(S, dtype=np.float64)
    qdec = np.zeros((NGC * 128, S))
    kdec = np.zeros((NGC * 128, S))
    NQT128 = S // 128
    gpow = np.zeros((2 * NGC, NQT128 * NKB))
    for g in range(NGC):
        for hh in range(2):
            lgh = lg[g * 2 + hh]
            rows = slice(g * 128 + hh * 64, g * 128 + (hh + 1) * 64)
            qdec[rows, :] = np.exp(lgh * (t % 128))[None, :]
            kdec[rows, :] = np.exp(lgh * (-(t % 512)))[None, :]
            qt = np.arange(NQT128)[:, None]
            jb = np.arange(NKB)[None, :]
            ex = np.clip(lgh * (128.0 * qt - 512.0 * jb), -300.0, 80.0)
            gpow[g * 2 + hh, :] = np.exp(ex).reshape(-1)
    gnsc = np.asarray(inputs["L3_attn_gn_scale"], np.float64).reshape(-1)
    gnb = np.asarray(inputs["L3_attn_gn_bias"], np.float64).reshape(-1)
    gncols = np.zeros((128, 2 * NGC), np.float32)
    for g in range(NGC):
        for hh in range(2):
            ch = heads[g * 2 + hh] * 64 + np.arange(64)
            gncols[hh * 64:(hh + 1) * 64, 2 * g] = gnsc[ch]
            gncols[hh * 64:(hh + 1) * 64, 2 * g + 1] = gnb[ch]
    xqc, xqs, xkc, xks = _xpos_tables(cfg, core)
    mW0 = np.zeros((128, 512), np.float32)
    if parity == 0:
        mW0[:, 0:256] = -1e9
    hvv = np.full((128, 1), float(parity), np.float32)
    return {
        "gpow": gpow.astype(np.float32),
        "qdec": qdec.astype(bf), "kdec": kdec.astype(bf),
        "gncols": gncols,
        "halo_valid": hvv,
        "maskW0": mW0.astype(bf),
        "xq_cos": xqc, "xq_sin": xqs, "xk_cos": xkc, "xk_sin": xks,
    }


BLOCK_PERCORE[3] = _percore_block3


def _prep_block3(cfg, inputs):
    import ml_dtypes
    f = np.float32
    D = cfg.D
    NCH = cfg.NCH
    DIH = _dih(cfg)
    NKH = DIH // 128
    DI = int(D * 4 * 2 / 3)
    NG3 = D // 128
    Wqkv = np.asarray(inputs["L3_ffn_Wqkv"], f)
    WqkvR = np.zeros((6 * D, NG3 * 384), f)
    for li in range(6):
        for g in range(NG3):
            for sec in range(3):
                WqkvR[li * D:(li + 1) * D, g * 384 + sec * 128:
                      g * 384 + (sec + 1) * 128] = \
                    Wqkv[li][:, sec * D + g * 128:sec * D + (g + 1) * 128]
    W1 = np.asarray(inputs["L3_ffn_W1"], f)
    b1 = np.asarray(inputs["L3_ffn_b1"], f)
    W1p = np.zeros((6 * D, 2 * DIH), f)
    b1cols = np.zeros((128, 12 * NKH), f)
    for li in range(6):
        a = np.zeros((D, DIH), f)
        a[:, :DI] = W1[li][:, :DI]
        g_ = np.zeros((D, DIH), f)
        g_[:, :DI] = W1[li][:, DI:]
        W1p[li * D:(li + 1) * D] = np.stack(
            [a.reshape(D, NKH, 128), g_.reshape(D, NKH, 128)],
            axis=2).reshape(D, 2 * DIH)
        ab = np.zeros(DIH, f)
        ab[:DI] = b1[li][:DI]
        gb = np.zeros(DIH, f)
        gb[:DI] = b1[li][DI:]
        for j in range(NKH):
            b1cols[:, (li * NKH + j) * 2] = ab[j * 128:(j + 1) * 128]
            b1cols[:, (li * NKH + j) * 2 + 1] = gb[j * 128:(j + 1) * 128]
    W2 = np.asarray(inputs["L3_ffn_W2"], f)
    b2 = np.asarray(inputs["L3_ffn_b2"], f)
    W2p = np.zeros((6 * DIH, D), f)
    b2cols = np.zeros((128, 6 * NCH), f)
    for li in range(6):
        W2p[li * DIH:li * DIH + DI] = W2[li]
        for k in range(NCH):
            b2cols[:, li * NCH + k] = b2[li][k * 128:(k + 1) * 128]
    lng = np.asarray(inputs["L3_ffn_ln_g"], f)
    lnb = np.asarray(inputs["L3_ffn_ln_b"], f)
    gbcols = np.zeros((128, 12 * NCH), f)
    for li in range(6):
        for k in range(NCH):
            gbcols[:, (li * NCH + k) * 2] = lng[li][k * 128:(k + 1) * 128]
            gbcols[:, (li * NCH + k) * 2 + 1] = lnb[li][k * 128:(k + 1) * 128]
    qs_ = np.asarray(inputs["L3_ffn_q_scale"], f)
    ks_ = np.asarray(inputs["L3_ffn_k_scale"], f)
    qscols = np.zeros((128, 12), f)
    for li in range(6):
        qscols[:, 2 * li] = np.tile(qs_[li], 2)
        qscols[:, 2 * li + 1] = np.tile(ks_[li], 2)
    return {
        "wqBD": _bd_strips(np.asarray(inputs["L3_attn_wq"], f)),
        "wkBD": _bd_strips(np.asarray(inputs["L3_attn_wk"], f)),
        "wvBD": _bd_strips(np.asarray(inputs["L3_attn_wv"], f)),
        "w1b3": np.ascontiguousarray(inputs["L3_attn_w1"], dtype=f),
        "w2b3": np.ascontiguousarray(inputs["L3_attn_w2"], dtype=f),
        "mask01": _mask01_np(),
        "WqkvR": WqkvR,
        "Wo3": np.ascontiguousarray(np.asarray(inputs["L3_ffn_Wo"], f)
                                    .reshape(6 * D, D)),
        "W1p": W1p, "W2p": W2p,
        "qscols": qscols, "gbcols": gbcols,
        "b1cols": b1cols, "b2cols": b2cols,
        "maskL3": _maskL3_np(),
        "R3": _r3_bf(cfg),
    }


def _r3_bf(cfg):
    import ml_dtypes
    return _rope1_R(cfg).astype(ml_dtypes.bfloat16)


N_BLOCKS = 4
